# version 02
# baseline (speedup 1.0000x reference)
import numpy as np
import ml_dtypes
from contextlib import ExitStack

import concourse.bass as bass
import concourse.mybir as mybir
from concourse.bass_utils import run_bass_kernel_spmd

F32 = mybir.dt.float32
BF16 = mybir.dt.bfloat16
AF = mybir.ActivationFunctionType
ALU = mybir.AluOpType
AX = mybir.AxisListType
BF16NP = ml_dtypes.bfloat16

D = 2048
TOK = 1024
NT = TOK // 128
DT = D // 128
INTER = 8192
IHI = 64
EPS_LN = 1e-5
EPS_RMS = float(np.finfo(np.float32).eps)
C_UG = float(2.0 ** 2.5)
C_DP = float(2.0 ** 0.5)

DFF = 1408
FT = 11
NEXP = 18
NHEAD = 16
BLK = 256
TCH = 512


def _had(n):
    h = np.array([[1.0]])
    while h.shape[0] < n:
        h = np.block([[h, h], [h, -h]])
    return h


def _c(x):
    return np.ascontiguousarray(np.asarray(x), dtype=np.float32)


H128S = _c(_had(128) * (2.0 ** -8))
IDENT = _c(np.eye(128))
CONSTS = _c(np.broadcast_to(np.array([EPS_LN, 1.0, EPS_RMS, 0.0], np.float32), (128, 4)))


class Sem:
    def __init__(self, h):
        self.h = h
        self.v = 0

    def plan(self, n):
        b = self.v
        self.v += n
        return b


class Ctx:
    def __init__(self, nc, ES):
        self.nc = nc
        self.ES = ES
        self.block = ES.enter_context(nc.Block())
        self._all = []
        self._n = 0
        self.tc = self.new_sem("tc")
        self.vc = self.new_sem("vc")
        self.sc = self.new_sem("sc")
        self.gc = self.new_sem("gc")
        self.lg = self.new_sem("lg")

    def new_sem(self, name):
        self._n += 1
        s = Sem(self.ES.enter_context(self.nc.semaphore(f"s_{name}{self._n}")))
        self._all.append(s)
        return s

    def sb(self, scope, name, shape, dt=F32):
        self._n += 1
        return scope.enter_context(self.nc.sbuf_tensor(f"{name}{self._n}", list(shape), dt))

    def ps(self, scope, name, shape, dt=F32):
        self._n += 1
        return scope.enter_context(self.nc.psum_tensor(f"{name}{self._n}", list(shape), dt))

    def join(self):
        block = self.block
        sems = [s for s in self._all if s.v]

        def w(e):
            for s in sems:
                e.wait_ge(s.h, s.v)

        @block.sync
        def _(e):
            w(e)

        @block.gpsimd
        def _(e):
            w(e)

        @block.tensor
        def _(e):
            w(e)

        @block.vector
        def _(e):
            w(e)

        @block.scalar
        def _(e):
            w(e)


def emit_ln_stream(ctx, S, xin, xn_out, ntile, cvec):
    block = ctx.block
    win = ctx.sb(S, "lnw", (128, 2 * D))
    xst = ctx.sb(S, "lnx", (128, 2 * D), BF16)
    sums = ctx.sb(S, "lns", (128, 8))
    ssq = ctx.sb(S, "lnq", (128, 2))
    sq = ctx.sb(S, "lnsq", (128, D))
    lsl = [ctx.new_sem("lnl"), ctx.new_sem("lnl")]
    ssl = [ctx.new_sem("lnst"), ctx.new_sem("lnst")]
    for t in range(ntile):
        lsl[t % 2].plan(16)
        ssl[t % 2].plan(16)

    v0 = ctx.vc.plan(ntile * 5)
    c0 = ctx.sc.plan(ntile * 3)

    @block.sync
    def _(sy):
        for t in range(ntile):
            par = t % 2
            if t >= 2:
                sy.wait_ge(ctx.vc.h, v0 + (t - 1) * 5)
                sy.wait_ge(ctx.sc.h, c0 + (t - 2) * 3 + 1)
                sy.wait_ge(ssl[par].h, (t // 2) * 16)
            sy.dma_start(out=win[:, par * D:(par + 1) * D],
                         in_=xin[t * 128:(t + 1) * 128, :]).then_inc(lsl[par].h, 16)
            sy.wait_ge(ctx.vc.h, v0 + t * 5 + 5)
            sy.dma_start(out=xn_out[t * 128:(t + 1) * 128, :],
                         in_=xst[:, par * D:(par + 1) * D]).then_inc(ssl[par].h, 16)

    @block.vector
    def _(v):
        for t in range(ntile):
            par = t % 2
            v.wait_ge(lsl[par].h, (t // 2 + 1) * 16)
            w = win[:, par * D:(par + 1) * D]
            v.tensor_reduce(sums[:, 4 * par:4 * par + 1], w, axis=AX.X,
                            op=ALU.add).then_inc(ctx.vc.h, 1)
            v.wait_ge(ctx.vc.h, v0 + t * 5 + 1)
            v.tensor_scalar(out=sums[:, 4 * par:4 * par + 1], in0=sums[:, 4 * par:4 * par + 1],
                            scalar1=-1.0 / D, scalar2=None, op0=ALU.mult).then_inc(ctx.vc.h, 1)
            v.wait_ge(ctx.vc.h, v0 + t * 5 + 2)
            v.wait_ge(ctx.sc.h, c0 + t * 3 + 1)
            v.tensor_tensor(out=sums[:, 4 * par + 1:4 * par + 2], in0=sums[:, 4 * par:4 * par + 1],
                            in1=sums[:, 4 * par:4 * par + 1], op=ALU.mult).then_inc(ctx.vc.h, 1)
            v.wait_ge(ctx.vc.h, v0 + t * 5 + 3)
            v.scalar_tensor_tensor(out=sums[:, 4 * par + 2:4 * par + 3],
                                   in0=ssq[:, par:par + 1], scalar=1.0 / D,
                                   in1=sums[:, 4 * par + 1:4 * par + 2],
                                   op0=ALU.mult, op1=ALU.subtract).then_inc(ctx.vc.h, 1)
            v.wait_ge(ctx.sc.h, c0 + t * 3 + 3)
            v.tensor_scalar(out=xst[:, par * D:(par + 1) * D], in0=w,
                            scalar1=sums[:, 4 * par:4 * par + 1],
                            scalar2=sums[:, 4 * par + 3:4 * par + 4],
                            op0=ALU.add, op1=ALU.mult).then_inc(ctx.vc.h, 1)

    @block.scalar
    def _(s):
        for t in range(ntile):
            par = t % 2
            if t >= 2:
                s.wait_ge(ctx.vc.h, v0 + (t - 1) * 5)
            s.wait_ge(lsl[par].h, (t // 2 + 1) * 16)
            s.activation(out=sq[:], in_=win[:, par * D:(par + 1) * D], func=AF.Square,
                         bias=0.0, scale=1.0, accum_out=ssq[:, par:par + 1]).then_inc(ctx.sc.h, 1)
            s.wait_ge(ctx.vc.h, v0 + t * 5 + 4)
            s.activation(out=sums[:, 4 * par + 3:4 * par + 4], in_=sums[:, 4 * par + 2:4 * par + 3],
                         func=AF.Ln, bias=cvec[:, 0:1], scale=1.0).then_inc(ctx.sc.h, 1)
            s.wait_ge(ctx.sc.h, c0 + t * 3 + 2)
            s.activation(out=sums[:, 4 * par + 3:4 * par + 4], in_=sums[:, 4 * par + 3:4 * par + 4],
                         func=AF.Exp, bias=0.0, scale=-0.5).then_inc(ctx.sc.h, 1)


def emit_ffn0(ctx, S, xin, xout, xn, ntile, wu, wg, wd, su_ug_d, su_dp_d,
              ws_up_d, ws_gt_d, ws_dp_d, h128s, ident,
              parts=("y1", "z", "fwht", "down")):
    nc, block = ctx.nc, ctx.block
    NCH = (ntile * 128) // TCH
    NTT = TCH // 128
    SLAB = 2 * DT * 128
    HALF = 32 * 128

    xnc = ctx.sb(S, "xnc", (128, NTT * D), BF16)
    xnT = ctx.sb(S, "xnT", (128, DT * TCH), BF16)
    y1 = ctx.sb(S, "y1", (128, DT * TCH), BF16)
    tmp = ctx.sb(S, "tmpc", (128, 2 * TCH), BF16)
    btmp = ctx.sb(S, "btmp", (128, 2048), BF16)
    z = ctx.sb(S, "zf", (128, IHI * TCH), BF16)
    sil = ctx.sb(S, "sil", (128, 2 * TCH))
    foutT = ctx.sb(S, "foutT", (128, DT * TCH), BF16)
    wub = ctx.sb(S, "wub", (128, 2 * SLAB), BF16)
    wgb = ctx.sb(S, "wgb", (128, 2 * SLAB), BF16)
    wdb = ctx.sb(S, "wdb", (128, 2 * HALF), BF16)
    win = ctx.sb(S, "fwin", (128, 2 * D))
    su_ug = ctx.sb(S, "suug", (128, 16))
    su_dp = ctx.sb(S, "sudp", (128, 64))
    wse_u = ctx.sb(S, "wseu", (128, 64))
    wse_g = ctx.sb(S, "wseg", (128, 64))
    wse_d = ctx.sb(S, "wsed", (128, 16))
    psA = ctx.ps(S, "psA", (128, 2 * 512), F32)
    psU = ctx.ps(S, "psU", (128, 2 * 512), F32)
    psG = ctx.ps(S, "psG", (128, 2 * 512), F32)
    psD = ctx.ps(S, "psD", (128, 512), F32)
    psT = ctx.ps(S, "psT", (128, 1024), BF16)

    lf = ctx.new_sem("ffc")
    sxn = ctx.new_sem("fxn")
    swz = [ctx.new_sem("fwz"), ctx.new_sem("fwz")]
    swd = [ctx.new_sem("fwd"), ctx.new_sem("fwd")]
    srl = [ctx.new_sem("frl"), ctx.new_sem("frl")]
    srs = [ctx.new_sem("frs"), ctx.new_sem("frs")]
    lsm = lf.plan(5 * 16)
    vsm = ctx.vc.plan(3)

    @block.sync
    def _(sy):
        sy.dma_start(out=su_ug[:], in_=su_ug_d[:]).then_inc(lf.h, 16)
        sy.dma_start(out=su_dp[:], in_=su_dp_d[:]).then_inc(lf.h, 16)
        sy.dma_start(out=wse_u[:], in_=ws_up_d[:]).then_inc(lf.h, 16)
        sy.dma_start(out=wse_g[:], in_=ws_gt_d[:]).then_inc(lf.h, 16)
        sy.dma_start(out=wse_d[:], in_=ws_dp_d[:]).then_inc(lf.h, 16)

    @block.vector
    def _(v):
        v.wait_ge(lf.h, 80)
        v.tensor_scalar(out=wse_u[:], in0=wse_u[:], scalar1=C_UG, scalar2=None,
                        op0=ALU.mult).then_inc(ctx.vc.h, 1)
        v.tensor_scalar(out=wse_g[:], in0=wse_g[:], scalar1=C_UG, scalar2=None,
                        op0=ALU.mult).then_inc(ctx.vc.h, 1)
        v.tensor_scalar(out=wse_d[:], in0=wse_d[:], scalar1=C_DP, scalar2=None,
                        op0=ALU.mult).then_inc(ctx.vc.h, 1)
        v.wait_ge(ctx.vc.h, vsm + 3)

    if "consts" in parts and len(parts) == 1:
        return
    for ch in range(NCH):
        t0 = ch * TCH

        lx = sxn.plan(16)
        tx = ctx.tc.plan(NTT * DT)
        vx = ctx.vc.plan(NTT * DT)

        @block.sync
        def _(sy):
            if ch > 0:
                sy.wait_ge(ctx.tc.h, tx)
            sy.dma_start(out=xnc[:].rearrange("p (t d) -> p t d", t=NTT),
                         in_=xn[t0:t0 + TCH, :].rearrange("(t p) d -> p t d", p=128)
                         ).then_inc(sxn.h, 16)

        if "xl" in parts:
            continue

        @block.tensor
        def _(te):
            te.wait_ge(sxn.h, lx + 16)
            te.wait_ge(ctx.vc.h, vx)
            n = 0
            for tt in range(NTT):
                for d_ in range(DT):
                    if n >= 1:
                        te.wait_ge(ctx.vc.h, vx + n)
                    te.transpose(psT[:, 0:128],
                                 xnc[:, tt * D + d_ * 128: tt * D + (d_ + 1) * 128],
                                 ident[:]).then_inc(ctx.tc.h, 1)
                    n += 1

        @block.vector
        def _(v):
            n = 0
            for tt in range(NTT):
                for d_ in range(DT):
                    n += 1
                    v.wait_ge(ctx.tc.h, tx + n)
                    v.tensor_copy(xnT[:, d_ * TCH + tt * 128: d_ * TCH + (tt + 1) * 128],
                                  psT[:, 0:128]).then_inc(ctx.vc.h, 1)

        v0 = ctx.vc.plan(2 * DT)
        t0s = ctx.tc.plan(DT)

        @block.vector
        def _(v):
            v.wait_ge(ctx.vc.h, vx + NTT * DT)
            for T in range(DT):
                if T >= 2:
                    v.wait_ge(ctx.tc.h, t0s + T - 1)
                v.tensor_scalar(out=tmp[:, (T % 2) * TCH:(T % 2) * TCH + TCH],
                                in0=xnT[:, T * TCH:(T + 1) * TCH],
                                scalar1=su_ug[:, T:T + 1], scalar2=None,
                                op0=ALU.mult).then_inc(ctx.vc.h, 1)
                if T > 0:
                    v.wait_ge(ctx.tc.h, t0s + T)
                    v.tensor_copy(y1[:, (T - 1) * TCH: T * TCH],
                                  psA[:, ((T - 1) % 2) * 512:((T - 1) % 2) * 512 + 512]
                                  ).then_inc(ctx.vc.h, 1)
            v.wait_ge(ctx.tc.h, t0s + DT)
            v.tensor_copy(y1[:, (DT - 1) * TCH: DT * TCH],
                          psA[:, ((DT - 1) % 2) * 512:((DT - 1) % 2) * 512 + 512]
                          ).then_inc(ctx.vc.h, 1)

        @block.tensor
        def _(te):
            for T in range(DT):
                need = v0 + 1 if T == 0 else v0 + 2 * T
                te.wait_ge(ctx.vc.h, need)
                te.matmul(psA[:, (T % 2) * 512:(T % 2) * 512 + 512], lhsT=h128s[:],
                          rhs=tmp[:, (T % 2) * TCH:(T % 2) * TCH + TCH], start=True,
                          stop=True).then_inc(ctx.tc.h, 1)

        if "bfly" in parts or "z" in parts:
            emit_butterflies(ctx, y1, btmp, DT, TCH)
        if "z" not in parts:
            continue

        NSLAB = 32
        y1r = ctx.vc.v
        for sl in range(NSLAB):
            swz[sl % 2].plan(32)
        tz = ctx.tc.plan(IHI)
        sz = ctx.sc.plan(IHI)
        vz = ctx.vc.plan(IHI)

        @block.gpsimd
        def _(g):
            for sl in range(NSLAB):
                par = sl % 2
                if sl >= 2:
                    g.wait_ge(ctx.tc.h, tz + 2 * (sl - 1))
                g.dma_start(out=wub[:, par * SLAB:(par + 1) * SLAB],
                            in_=wu[sl].rearrange("p a b q -> p (a b q)")
                            ).then_inc(swz[par].h, 16)
                g.dma_start(out=wgb[:, par * SLAB:(par + 1) * SLAB],
                            in_=wg[sl].rearrange("p a b q -> p (a b q)")
                            ).then_inc(swz[par].h, 16)

        @block.tensor
        def _(te):
            te.wait_ge(ctx.vc.h, y1r)
            for ih in range(IHI):
                sl, j = ih // 2, ih % 2
                par = sl % 2
                te.wait_ge(swz[par].h, (ch * (NSLAB // 2) + sl // 2 + 1) * 32)
                if ih >= 2:
                    te.wait_ge(ctx.sc.h, sz + ih - 1)
                    te.wait_ge(ctx.vc.h, vz + ih - 1)
                pu = psU[:, (ih % 2) * 512:(ih % 2) * 512 + 512]
                pg = psG[:, (ih % 2) * 512:(ih % 2) * 512 + 512]
                for T in range(DT):
                    off = par * SLAB + (j * DT + T) * 128
                    te.matmul(pu, lhsT=wub[:, off:off + 128], rhs=y1[:, T * TCH:(T + 1) * TCH],
                              start=(T == 0), stop=(T == DT - 1))
                for T in range(DT):
                    off = par * SLAB + (j * DT + T) * 128
                    mm = te.matmul(pg, lhsT=wgb[:, off:off + 128], rhs=y1[:, T * TCH:(T + 1) * TCH],
                                   start=(T == 0), stop=(T == DT - 1))
                mm.then_inc(ctx.tc.h, 1)

        @block.scalar
        def _(s):
            for ih in range(IHI):
                if ih >= 2:
                    s.wait_ge(ctx.vc.h, vz + ih - 1)
                s.wait_ge(ctx.tc.h, tz + ih + 1)
                pg = psG[:, (ih % 2) * 512:(ih % 2) * 512 + 512]
                s.activation(out=sil[:, (ih % 2) * TCH:(ih % 2) * TCH + TCH], in_=pg,
                             func=AF.Silu, bias=0.0,
                             scale=wse_g[:, ih:ih + 1]).then_inc(ctx.sc.h, 1)

        @block.vector
        def _(v):
            for ih in range(IHI):
                v.wait_ge(ctx.sc.h, sz + ih + 1)
                pu = psU[:, (ih % 2) * 512:(ih % 2) * 512 + 512]
                v.scalar_tensor_tensor(out=z[:, ih * TCH:(ih + 1) * TCH], in0=pu,
                                       scalar=wse_u[:, ih:ih + 1],
                                       in1=sil[:, (ih % 2) * TCH:(ih % 2) * TCH + TCH],
                                       op0=ALU.mult, op1=ALU.mult).then_inc(ctx.vc.h, 1)

        if "fwht" not in parts:
            continue
        vf = ctx.vc.plan(2 * IHI)
        tf = ctx.tc.plan(IHI)

        @block.vector
        def _(v):
            for ih in range(IHI):
                if ih >= 2:
                    v.wait_ge(ctx.tc.h, tf + ih - 1)
                v.tensor_scalar(out=tmp[:, (ih % 2) * TCH:(ih % 2) * TCH + TCH],
                                in0=z[:, ih * TCH:(ih + 1) * TCH],
                                scalar1=su_dp[:, ih:ih + 1], scalar2=None,
                                op0=ALU.mult).then_inc(ctx.vc.h, 1)
                if ih > 0:
                    v.wait_ge(ctx.tc.h, tf + ih)
                    v.tensor_copy(z[:, (ih - 1) * TCH: ih * TCH],
                                  psA[:, ((ih - 1) % 2) * 512:((ih - 1) % 2) * 512 + 512]
                                  ).then_inc(ctx.vc.h, 1)
            v.wait_ge(ctx.tc.h, tf + IHI)
            v.tensor_copy(z[:, (IHI - 1) * TCH: IHI * TCH],
                          psA[:, ((IHI - 1) % 2) * 512:((IHI - 1) % 2) * 512 + 512]
                          ).then_inc(ctx.vc.h, 1)

        @block.tensor
        def _(te):
            for ih in range(IHI):
                need = vf + 1 if ih == 0 else vf + 2 * ih
                te.wait_ge(ctx.vc.h, need)
                te.matmul(psA[:, (ih % 2) * 512:(ih % 2) * 512 + 512], lhsT=h128s[:],
                          rhs=tmp[:, (ih % 2) * TCH:(ih % 2) * TCH + TCH], start=True,
                          stop=True).then_inc(ctx.tc.h, 1)

        emit_butterflies(ctx, z, btmp, IHI, TCH)

        if "down" not in parts:
            continue
        zr = ctx.vc.v
        for n in range(DT * 2):
            swd[n % 2].plan(16)
        td = ctx.tc.plan(DT * 2)
        vd = ctx.vc.plan(DT)

        @block.gpsimd
        def _(g):
            for n in range(DT * 2):
                dt_, hf = n // 2, n % 2
                par = n % 2
                if n >= 2:
                    g.wait_ge(ctx.tc.h, td + n - 1)
                g.dma_start(out=wdb[:, par * HALF:(par + 1) * HALF],
                            in_=wd[dt_, :, hf * 32:(hf + 1) * 32, :].rearrange("p a q -> p (a q)")
                            ).then_inc(swd[par].h, 16)

        @block.tensor
        def _(te):
            te.wait_ge(ctx.vc.h, zr)
            for n in range(DT * 2):
                dt_, hf = n // 2, n % 2
                par = n % 2
                te.wait_ge(swd[par].h, (ch * DT + n // 2 + 1) * 16)
                if hf == 0 and dt_ >= 1:
                    te.wait_ge(ctx.vc.h, vd + dt_)
                pd = psD[:, 0:512]
                for a in range(32):
                    ih = hf * 32 + a
                    mm = te.matmul(pd, lhsT=wdb[:, par * HALF + a * 128: par * HALF + (a + 1) * 128],
                                   rhs=z[:, ih * TCH:(ih + 1) * TCH],
                                   start=(ih == 0), stop=(ih == IHI - 1))
                mm.then_inc(ctx.tc.h, 1)

        @block.vector
        def _(v):
            for dt_ in range(DT):
                v.wait_ge(ctx.tc.h, td + 2 * dt_ + 2)
                pd = psD[:, 0:512]
                v.tensor_scalar(out=foutT[:, dt_ * TCH:(dt_ + 1) * TCH], in0=pd,
                                scalar1=wse_d[:, dt_:dt_ + 1], scalar2=None,
                                op0=ALU.mult).then_inc(ctx.vc.h, 1)

        for tt in range(NTT):
            srl[tt % 2].plan(16)
            srs[tt % 2].plan(16)
        ta = ctx.tc.plan(NTT * DT)
        va = ctx.vc.plan(NTT * DT)

        @block.sync
        def _(sy):
            for tt in range(NTT):
                par = tt % 2
                gl = ch * (NTT // 2) + tt // 2
                if gl > 0:
                    sy.wait_ge(srs[par].h, gl * 16)
                sy.dma_start(out=win[:, par * D:(par + 1) * D],
                             in_=xin[t0 + tt * 128: t0 + (tt + 1) * 128, :]
                             ).then_inc(srl[par].h, 16)
                sy.wait_ge(ctx.vc.h, va + (tt + 1) * DT)
                sy.dma_start(out=xout[t0 + tt * 128: t0 + (tt + 1) * 128, :],
                             in_=win[:, par * D:(par + 1) * D]).then_inc(srs[par].h, 16)

        @block.tensor
        def _(te):
            n = 0
            for tt in range(NTT):
                for dt_ in range(DT):
                    te.wait_ge(ctx.vc.h, vd + dt_ + 1)
                    if n >= 1:
                        te.wait_ge(ctx.vc.h, va + n)
                    te.transpose(psT[:, 0:128],
                                 foutT[:, dt_ * TCH + tt * 128: dt_ * TCH + (tt + 1) * 128],
                                 ident[:]).then_inc(ctx.tc.h, 1)
                    n += 1

        @block.vector
        def _(v):
            n = 0
            for tt in range(NTT):
                par = tt % 2
                v.wait_ge(srl[par].h, (ch * (NTT // 2) + tt // 2 + 1) * 16)
                for dt_ in range(DT):
                    n += 1
                    v.wait_ge(ctx.tc.h, ta + n)
                    dst = win[:, par * D + dt_ * 128: par * D + (dt_ + 1) * 128]
                    v.tensor_tensor(out=dst, in0=dst, in1=psT[:, 0:128],
                                    op=ALU.add).then_inc(ctx.vc.h, 1)


def emit_butterflies(ctx, buf, btmp, nblk, tch):
    block = ctx.block
    h = 1
    stages = []
    while h < nblk:
        stages.append(h)
        h *= 2
    vb = ctx.vc.plan(2 * len(stages))

    @block.vector
    def _(v):
        n = vb
        for h in stages:
            G = nblk // (2 * h)
            w = h * tch
            view = buf[:, 0:nblk * tch].rearrange("p (g two w) -> p g two w", g=G, two=2)
            av = view[:, :, 0, :]
            bv = view[:, :, 1, :]
            v.wait_ge(ctx.vc.h, n)
            v.tensor_tensor(out=bv, in0=av, in1=bv, op=ALU.subtract).then_inc(ctx.vc.h, 1)
            n += 1
            v.wait_ge(ctx.vc.h, n)
            v.scalar_tensor_tensor(out=av, in0=av, scalar=2.0, in1=bv,
                                   op0=ALU.mult, op1=ALU.subtract).then_inc(ctx.vc.h, 1)
            n += 1


def build_L1(stages=("ln1", "ffn", "ln2")):
    nc = bass.Bass("TRN2", target_bir_lowering=False, debug=False)
    x = nc.dram_tensor("x", [TOK, D], F32, kind="ExternalInput").ap()
    wu = nc.dram_tensor("wu", [32, 128, 2, DT, 128], F32, kind="ExternalInput").ap()
    wg = nc.dram_tensor("wg", [32, 128, 2, DT, 128], F32, kind="ExternalInput").ap()
    wd = nc.dram_tensor("wd", [DT, 128, IHI, 128], F32, kind="ExternalInput").ap()
    su_ug_d = nc.dram_tensor("su_ug", [128, 16], F32, kind="ExternalInput").ap()
    su_dp_d = nc.dram_tensor("su_dp", [128, 64], F32, kind="ExternalInput").ap()
    ws_up_d = nc.dram_tensor("ws_up", [128, 64], F32, kind="ExternalInput").ap()
    ws_gt_d = nc.dram_tensor("ws_gt", [128, 64], F32, kind="ExternalInput").ap()
    ws_dp_d = nc.dram_tensor("ws_dp", [128, 16], F32, kind="ExternalInput").ap()
    h128_d = nc.dram_tensor("h128s", [128, 128], F32, kind="ExternalInput").ap()
    id_d = nc.dram_tensor("ident", [128, 128], F32, kind="ExternalInput").ap()
    cv_d = nc.dram_tensor("consts", [128, 4], F32, kind="ExternalInput").ap()
    x1_d = nc.dram_tensor("x1", [TOK, D], F32, kind="ExternalOutput").ap()
    xn2_d = nc.dram_tensor("xn2", [TOK, D], BF16, kind="ExternalOutput").ap()
    xn_i = nc.dram_tensor("xn_i", [TOK, D], BF16, kind="Internal").ap()

    with ExitStack() as ES:
        ctx = Ctx(nc, ES)
        block = ctx.block
        h128s = ctx.sb(ES, "h128", (128, 128), BF16)
        ident = ctx.sb(ES, "id", (128, 128), BF16)
        cvec = ctx.sb(ES, "cv", (128, 4))
        lc = ctx.lg.plan(3 * 16)

        @block.gpsimd
        def _(g):
            g.dma_start(out=h128s[:], in_=h128_d[:]).then_inc(ctx.lg.h, 16)
            g.dma_start(out=ident[:], in_=id_d[:]).then_inc(ctx.lg.h, 16)
            g.dma_start(out=cvec[:], in_=cv_d[:]).then_inc(ctx.lg.h, 16)

        ctx.join()

        if "ln1" in stages:
            with ExitStack() as S1:
                emit_ln_stream(ctx, S1, x, xn_i, NT, cvec)
                ctx.join()
        if "ffn" in stages:
            fp = tuple(s[1:] for s in stages if s.startswith("_"))
            parts = fp if fp else ("y1", "z", "fwht", "down")
            with ExitStack() as S2:
                emit_ffn0(ctx, S2, x, x1_d, xn_i, NT, wu, wg, wd, su_ug_d, su_dp_d,
                          ws_up_d, ws_gt_d, ws_dp_d, h128s, ident, parts=parts)
                ctx.join()
        if "ln2" in stages:
            with ExitStack() as S3:
                emit_ln_stream(ctx, S3, x1_d, xn2_d, NT, cvec)
                ctx.join()

        ctx.join()

    return nc


SEQ = 4096
NTS = SEQ // 128


def build_L2():
    nc = bass.Bass("TRN2", target_bir_lowering=False, debug=False)
    xq_d = nc.dram_tensor("xq", [NTS, DT, 128, 128], BF16, kind="ExternalInput").ap()
    qkvw_d = nc.dram_tensor("qkvw", [4, DT, 128, 384], F32, kind="ExternalInput").ap()
    slope_d = nc.dram_tensor("slope", [1, 4], F32, kind="ExternalInput").ap()
    ones_d = nc.dram_tensor("ones1", [1, 128], F32, kind="ExternalInput").ap()
    arrq_d = nc.dram_tensor("arrq", [128, 2], F32, kind="ExternalInput").ap()
    arrk_d = nc.dram_tensor("arrk", [128, 2], F32, kind="ExternalInput").ap()
    idxT_d = nc.dram_tensor("idxT", [2, 128, 256], F32, kind="ExternalInput").ap()
    mskT_d = nc.dram_tensor("mskT", [2, 128, 256], F32, kind="ExternalInput").ap()
    id_d = nc.dram_tensor("ident", [128, 128], F32, kind="ExternalInput").ap()
    out_d = nc.dram_tensor("outT", [512, SEQ], F32, kind="ExternalOutput").ap()

    NBLK = 16

    with ExitStack() as ES:
        ctx = Ctx(nc, ES)
        block = ctx.block
        ident = ctx.sb(ES, "id", (128, 128), BF16)
        slope = ctx.sb(ES, "slp", (1, 4))
        ones1 = ctx.sb(ES, "on1", (1, 128))
        arrq = ctx.sb(ES, "arq", (128, 2))
        arrk = ctx.sb(ES, "ark", (128, 2))
        idxT = ctx.sb(ES, "idx", (128, 2 * 256))
        mskT = ctx.sb(ES, "msk", (128, 2 * 256))
        negs = ctx.sb(ES, "ngs", (128, 4))
        xqb = ctx.sb(ES, "xqb", (128, 2 * DT * 128), BF16)
        qwb = ctx.sb(ES, "qwb", (128, 2 * DT * 384), BF16)
        qkvt = ctx.sb(ES, "qkvt", (128, NTS * 384), BF16)
        qsc = ctx.sb(ES, "qsc", (128, 256), BF16)
        ksc = ctx.sb(ES, "ksc", (128, 256), BF16)
        qT = ctx.sb(ES, "qT", (128, 256), BF16)
        qsT = ctx.sb(ES, "qsT", (128, 256), BF16)
        kT = ctx.sb(ES, "kT", (128, 256), BF16)
        ssc = ctx.sb(ES, "ssc", (128, 2 * 256), BF16)
        decT = ctx.sb(ES, "decT", (128, 2 * 256))
        qdec = ctx.sb(ES, "qdec", (128, 2))
        kdec = ctx.sb(ES, "kdec", (128, 2))
        bdcol = ctx.sb(ES, "bdc", (128, 1))
        kv = ctx.sb(ES, "kv", (128, 128))
        kvbf = ctx.sb(ES, "kvbf", (128, 128), BF16)
        outh = ctx.sb(ES, "outh", (128, SEQ))
        psQ = ctx.ps(ES, "psQ", (128, 2 * 512), F32)
        psTt = ctx.ps(ES, "psTt", (128, 2 * 1024), BF16)
        psS = ctx.ps(ES, "psS", (128, 2 * 512), F32)
        psO = ctx.ps(ES, "psO", (128, 512), F32)
        psKV = ctx.ps(ES, "psKV", (128, 512), F32)

        sxq = [ctx.new_sem("xqA"), ctx.new_sem("xqB")]
        sqw = [ctx.new_sem("qwA"), ctx.new_sem("qwB")]
        sout = ctx.new_sem("out")
        lcs = ctx.new_sem("lcs")
        lc0 = lcs.plan(6 * 16)
        lc = ctx.lg.plan(16)

        @block.sync
        def _(sy):
            sy.dma_start(out=slope[:], in_=slope_d[:]).then_inc(lcs.h, 16)
            sy.dma_start(out=ones1[:], in_=ones_d[:]).then_inc(lcs.h, 16)
            sy.dma_start(out=arrq[:], in_=arrq_d[:]).then_inc(lcs.h, 16)
            sy.dma_start(out=arrk[:], in_=arrk_d[:]).then_inc(lcs.h, 16)
            sy.dma_start(out=idxT[:].rearrange("p (u m) -> p u m", u=2),
                         in_=idxT_d.rearrange("u p m -> p u m")).then_inc(lcs.h, 16)
            sy.dma_start(out=mskT[:].rearrange("p (u m) -> p u m", u=2),
                         in_=mskT_d.rearrange("u p m -> p u m")).then_inc(lcs.h, 16)

        @block.gpsimd
        def _(g):
            g.dma_start(out=ident[:], in_=id_d[:]).then_inc(ctx.lg.h, 16)

        ctx.join()

        tb = ctx.tc.plan(1)
        vbb = ctx.vc.plan(1)

        @block.tensor
        def _(te):
            te.matmul(psO[:, 0:4], lhsT=ones1[:], rhs=slope[:], start=True,
                      stop=True).then_inc(ctx.tc.h, 1)

        @block.vector
        def _(v):
            v.wait_ge(ctx.tc.h, tb + 1)
            v.tensor_scalar(out=negs[:], in0=psO[:, 0:4], scalar1=-1.0, scalar2=None,
                            op0=ALU.mult).then_inc(ctx.vc.h, 1)

        ctx.join()

        hq_base = []
        for h in range(4):
            head_tc0 = ctx.tc.v
            head_vc0 = ctx.vc.v
            head_sc0 = ctx.sc.v

            c0 = ctx.sc.plan(4)
            v0 = ctx.vc.plan(2)

            @block.scalar
            def _(s, h=h, head_vc0=head_vc0):
                s.wait_ge(ctx.vc.h, head_vc0)
                s.activation(out=qdec[:], in_=arrq[:], func=AF.Exp, bias=0.0,
                             scale=negs[:, h:h + 1]).then_inc(ctx.sc.h, 1)
                s.activation(out=kdec[:], in_=arrk[:], func=AF.Exp, bias=0.0,
                             scale=negs[:, h:h + 1]).then_inc(ctx.sc.h, 1)
                s.activation(out=bdcol[:], in_=negs[:, h:h + 1], func=AF.Exp, bias=0.0,
                             scale=256.0).then_inc(ctx.sc.h, 1)
                s.activation(out=decT[:], in_=idxT[:], func=AF.Exp, bias=0.0,
                             scale=negs[:, h:h + 1]).then_inc(ctx.sc.h, 1)

            @block.vector
            def _(v, c0=c0, v0=v0):
                v.wait_ge(ctx.sc.h, c0 + 4)
                v.tensor_tensor(out=decT[:], in0=decT[:], in1=mskT[:],
                                op=ALU.mult).then_inc(ctx.vc.h, 1)
                v.wait_ge(ctx.vc.h, v0 + 1)
                v.memset(kv[:], 0.0)
                v.nop().then_inc(ctx.vc.h, 1)

            hw = sqw[h % 2].plan(16)

            @block.gpsimd
            def _(g, h=h):
                par = h % 2
                if h >= 2:
                    g.wait_ge(ctx.tc.h, hq_base[h - 2] + NTS)
                g.dma_start(out=qwb[:, par * DT * 384:(par + 1) * DT * 384].rearrange(
                                "p (a q) -> p a q", a=DT),
                            in_=qkvw_d[h].rearrange("a p q -> p a q")
                            ).then_inc(sqw[par].h, 16)

            tq = ctx.tc.plan(NTS)
            hq_base.append(tq)
            cq = ctx.sc.plan(NTS)

            @block.sync
            def _(sy, h=h):
                for tt in range(NTS):
                    par = tt % 2
                    seq_no = h * (NTS // 2) + tt // 2
                    if seq_no > 0:
                        if tt >= 2:
                            need = hq_base[h] + tt - 1
                        else:
                            need = hq_base[h - 1] + NTS - 2 + par + 1
                        sy.wait_ge(ctx.tc.h, need)
                    sy.dma_start(out=xqb[:, par * DT * 128:(par + 1) * DT * 128].rearrange(
                                     "p (a q) -> p a q", a=DT),
                                 in_=xq_d[tt].rearrange("a p q -> p a q")
                                 ).then_inc(sxq[par].h, 16)
            for tt in range(NTS):
                sxq[tt % 2].plan(16)

            @block.tensor
            def _(te, h=h, tq=tq, cq=cq):
                par_h = h % 2
                te.wait_ge(sqw[h % 2].h, (h // 2 + 1) * 16)
                for tt in range(NTS):
                    par = tt % 2
                    te.wait_ge(sxq[par].h, (h * (NTS // 2) + tt // 2 + 1) * 16)
                    if tt >= 2:
                        te.wait_ge(ctx.sc.h, cq + tt - 1)
                    pq = psQ[:, (tt % 2) * 512:(tt % 2) * 512 + 384]
                    for T in range(DT):
                        mm = te.matmul(pq, lhsT=xqb[:, par * DT * 128 + T * 128: par * DT * 128 + (T + 1) * 128],
                                       rhs=qwb[:, par_h * DT * 384 + T * 384: par_h * DT * 384 + (T + 1) * 384],
                                       start=(T == 0), stop=(T == DT - 1))
                    mm.then_inc(ctx.tc.h, 1)

            @block.scalar
            def _(s, h=h, tq=tq, head_tc0=head_tc0):
                if h > 0:
                    s.wait_ge(ctx.tc.h, head_tc0)
                for tt in range(NTS):
                    s.wait_ge(ctx.tc.h, tq + tt + 1)
                    pq = psQ[:, (tt % 2) * 512:(tt % 2) * 512 + 384]
                    s.activation(out=qkvt[:, tt * 384:(tt + 1) * 384], in_=pq,
                                 func=AF.Silu, bias=0.0, scale=1.0).then_inc(ctx.sc.h, 1)

            for j in range(NBLK):
                tj = ctx.tc.plan(10)
                vj = ctx.vc.plan(14)

                @block.vector
                def _(v, h=h, j=j, tj=tj, vj=vj, cq=cq):
                    v.wait_ge(ctx.tc.h, tj)
                    v.wait_ge(ctx.sc.h, cq + 2 * j + 2)
                    for u in range(2):
                        col = (2 * j + u) * 384
                        v.tensor_scalar(out=qsc[:, u * 128:(u + 1) * 128],
                                        in0=qkvt[:, col:col + 128],
                                        scalar1=qdec[:, u:u + 1], scalar2=None,
                                        op0=ALU.mult).then_inc(ctx.vc.h, 1)
                        v.tensor_scalar(out=ksc[:, u * 128:(u + 1) * 128],
                                        in0=qkvt[:, col + 128:col + 256],
                                        scalar1=kdec[:, u:u + 1], scalar2=None,
                                        op0=ALU.mult).then_inc(ctx.vc.h, 1)
                    for i6 in range(6):
                        v.wait_ge(ctx.tc.h, tj + i6 + 1)
                        dst = [qT[:, 0:128], qT[:, 128:256], qsT[:, 0:128],
                               qsT[:, 128:256], kT[:, 0:128], kT[:, 128:256]][i6]
                        v.tensor_copy(dst, psTt[:, (i6 % 2) * 1024:(i6 % 2) * 1024 + 128]
                                      ).then_inc(ctx.vc.h, 1)
                    v.wait_ge(ctx.tc.h, tj + 8)
                    v.tensor_tensor(out=ssc[:, 0:256], in0=psS[:, 0:256],
                                    in1=decT[:, 0:256], op=ALU.mult)
                    v.tensor_tensor(out=ssc[:, 256:512], in0=psS[:, 512:768],
                                    in1=decT[:, 256:512], op=ALU.mult).then_inc(ctx.vc.h, 1)
                    v.wait_ge(ctx.tc.h, tj + 10)
                    v.scalar_tensor_tensor(out=kv[:], in0=kv[:], scalar=bdcol[:],
                                           in1=psKV[:, 0:128], op0=ALU.mult,
                                           op1=ALU.add).then_inc(ctx.vc.h, 1)
                    v.wait_ge(ctx.vc.h, vj + 12)
                    v.tensor_copy(kvbf[:], kv[:]).then_inc(ctx.vc.h, 1)
                    v.wait_ge(ctx.tc.h, tj + 9)
                    if j == 0:
                        v.wait_ge(sout.h, h * 16)
                    v.tensor_copy(outh[:, j * 256:(j + 1) * 256],
                                  psO[:, 0:256]).then_inc(ctx.vc.h, 1)

                @block.tensor
                def _(te, h=h, j=j, tj=tj, vj=vj, cq=cq):
                    te.wait_ge(ctx.sc.h, cq + 2 * j + 2)
                    srcs = [qkvt[:, (2 * j) * 384:(2 * j) * 384 + 128],
                            qkvt[:, (2 * j + 1) * 384:(2 * j + 1) * 384 + 128],
                            qsc[:, 0:128], qsc[:, 128:256],
                            qkvt[:, (2 * j) * 384 + 128:(2 * j) * 384 + 256],
                            qkvt[:, (2 * j + 1) * 384 + 128:(2 * j + 1) * 384 + 256]]
                    for i6, src in enumerate(srcs):
                        if i6 in (2, 3):
                            te.wait_ge(ctx.vc.h, vj + 3)
                        if i6 >= 2:
                            te.wait_ge(ctx.vc.h, vj + 4 + i6 - 1)
                        te.transpose(psTt[:, (i6 % 2) * 1024:(i6 % 2) * 1024 + 128],
                                     src, ident[:]).then_inc(ctx.tc.h, 1)
                    te.wait_ge(ctx.vc.h, vj + 10)
                    for u in range(2):
                        te.matmul(psS[:, u * 512:u * 512 + 256],
                                  lhsT=kT[:, u * 128:(u + 1) * 128],
                                  rhs=qT[:], start=True, stop=True).then_inc(ctx.tc.h, 1)
                    te.wait_ge(ctx.vc.h, vj + 11)
                    first = True
                    if j > 0:
                        te.matmul(psO[:, 0:256], lhsT=kvbf[:], rhs=qsT[:], start=True,
                                  stop=False)
                        first = False
                    for u in range(2):
                        mm = te.matmul(psO[:, 0:256],
                                       lhsT=qkvt[:, (2 * j + u) * 384 + 256:(2 * j + u) * 384 + 384],
                                       rhs=ssc[:, u * 256:(u + 1) * 256], start=first,
                                       stop=(u == 1))
                        first = False
                    mm.then_inc(ctx.tc.h, 1)
                    for u in range(2):
                        mm = te.matmul(psKV[:, 0:128], lhsT=ksc[:, u * 128:(u + 1) * 128],
                                       rhs=qkvt[:, (2 * j + u) * 384 + 256:(2 * j + u) * 384 + 384],
                                       start=(u == 0), stop=(u == 1))
                    mm.then_inc(ctx.tc.h, 1)

            so = sout.plan(16)
            hv_end = ctx.vc.v

            @block.sync
            def _(sy, h=h, hv_end=hv_end):
                sy.wait_ge(ctx.vc.h, hv_end)
                sy.dma_start(out=out_d[h * 128:(h + 1) * 128, :],
                             in_=outh[:]).then_inc(sout.h, 16)

        ctx.join()

    return nc




def emit_attn_finish(ctx, S, x1, outT_d, xn2v_d, wgt_d, wot_d, nw_d, x2, cvec, ident):
    block = ctx.block
    xn2v = ctx.sb(S, "xn2v", (128, DT * TOK), BF16)
    goT = ctx.sb(S, "goT", (128, DT * TOK), BF16)
    aoutT = ctx.sb(S, "aoutT", (128, DT * TOK), BF16)
    owin = ctx.sb(S, "owin", (128, 2 * TOK))
    sq = ctx.sb(S, "sqa", (128, 2 * TOK))
    tmp = ctx.sb(S, "tmpa", (128, TOK))
    rrow = ctx.sb(S, "rrow", (1, TOK))
    nw = ctx.sb(S, "nw", (128, 16))
    wgb = ctx.sb(S, "wgb", (128, 2 * DT * 128), BF16)
    wob = ctx.sb(S, "wob", (128, 2 * DT * 128), BF16)
    win = ctx.sb(S, "awin", (128, 2 * D))
    ones1 = ctx.sb(S, "aon1", (1, 128))
    onec = ctx.sb(S, "aonc", (128, 1))
    psW = ctx.ps(S, "psW", (128, 2 * 512), F32)
    psR = ctx.ps(S, "psR", (1, 2 * 512), F32)
    psB = ctx.ps(S, "psB", (128, 2 * 512), F32)
    psT = ctx.ps(S, "psT2", (128, 2 * 1024), BF16)

    sxv = ctx.new_sem("axv")
    swg = [ctx.new_sem("awg"), ctx.new_sem("awg")]
    swo = [ctx.new_sem("awo"), ctx.new_sem("awo")]
    sow = [ctx.new_sem("aow"), ctx.new_sem("aow")]
    srl = [ctx.new_sem("arl"), ctx.new_sem("arl")]
    srs = [ctx.new_sem("ars"), ctx.new_sem("ars")]

    lv = sxv.plan(2 * 16)

    @block.sync
    def _(sy):
        sy.dma_start(out=xn2v[:].rearrange("p (a q) -> p a q", a=DT),
                     in_=xn2v_d[:].rearrange("a p q -> p a q")).then_inc(sxv.h, 16)
        sy.dma_start(out=nw[:], in_=nw_d[:]).then_inc(sxv.h, 16)

    v_on = ctx.vc.plan(2)

    @block.vector
    def _(v):
        v.memset(ones1[:], 1.0).then_inc(ctx.vc.h, 1)
        v.memset(onec[:], 1.0).then_inc(ctx.vc.h, 1)

    tg = ctx.tc.plan(DT * 2)
    cg = ctx.sc.plan(DT * 2)

    @block.gpsimd
    def _(g):
        for F in range(DT):
            par = F % 2
            if F >= 2:
                g.wait_ge(ctx.tc.h, tg + 2 * (F - 1))
            g.dma_start(out=wgb[:, par * DT * 128:(par + 1) * DT * 128].rearrange(
                            "p (a q) -> p a q", a=DT),
                        in_=wgt_d[F].rearrange("a p q -> p a q")).then_inc(swg[par].h, 16)
    for F in range(DT):
        swg[F % 2].plan(16)

    @block.tensor
    def _(te):
        te.wait_ge(sxv.h, 32)
        for F in range(DT):
            par = F % 2
            te.wait_ge(swg[par].h, (F // 2 + 1) * 16)
            for half in range(2):
                n2 = 2 * F + half
                if n2 >= 2:
                    te.wait_ge(ctx.sc.h, cg + n2 - 1)
                pw = psW[:, (n2 % 2) * 512:(n2 % 2) * 512 + 512]
                for T in range(DT):
                    mm = te.matmul(pw, lhsT=wgb[:, par * DT * 128 + T * 128: par * DT * 128 + (T + 1) * 128],
                                   rhs=xn2v[:, T * TOK + half * 512: T * TOK + half * 512 + 512],
                                   start=(T == 0), stop=(T == DT - 1))
                mm.then_inc(ctx.tc.h, 1)

    @block.scalar
    def _(s):
        for F in range(DT):
            for half in range(2):
                n2 = 2 * F + half
                s.wait_ge(ctx.tc.h, tg + n2 + 1)
                pw = psW[:, (n2 % 2) * 512:(n2 % 2) * 512 + 512]
                s.activation(out=goT[:, F * TOK + half * 512: F * TOK + half * 512 + 512],
                             in_=pw, func=AF.Sigmoid, bias=0.0,
                             scale=1.0).then_inc(ctx.sc.h, 1)

    t_r = ctx.tc.plan(DT * 2)
    c_r = ctx.sc.plan(DT)
    v_r = ctx.vc.plan(0)

    @block.sync
    def _(sy):
        for T in range(DT):
            par = T % 2
            if T >= 2:
                sy.wait_ge(ctx.sc.h, c_r + T - 1)
            sy.dma_start(out=owin[:, par * TOK:(par + 1) * TOK],
                         in_=outT_d[T * 128:(T + 1) * 128, :]).then_inc(sow[par].h, 16)
    for T in range(DT):
        sow[T % 2].plan(16)

    @block.scalar
    def _(s):
        for T in range(DT):
            par = T % 2
            s.wait_ge(sow[par].h, (T // 2 + 1) * 16)
            if T >= 2:
                s.wait_ge(ctx.tc.h, t_r + 2 * (T - 1))
            s.activation(out=sq[:, par * TOK:(par + 1) * TOK],
                         in_=owin[:, par * TOK:(par + 1) * TOK],
                         func=AF.Square, bias=0.0, scale=1.0).then_inc(ctx.sc.h, 1)

    @block.tensor
    def _(te):
        te.wait_ge(ctx.vc.h, v_on + 2)
        for T in range(DT):
            par = T % 2
            te.wait_ge(ctx.sc.h, c_r + T + 1)
            for half in range(2):
                te.matmul(psR[0:1, half * 512:half * 512 + 512], lhsT=onec[:],
                          rhs=sq[:, par * TOK + half * 512: par * TOK + half * 512 + 512],
                          start=(T == 0), stop=(T == DT - 1)).then_inc(ctx.tc.h, 1)

    c_s = ctx.sc.plan(2)
    t_b = ctx.tc.plan(2)

    @block.scalar
    def _(s):
        s.wait_ge(ctx.tc.h, t_r + DT * 2)
        s.activation(out=rrow[:], in_=psR[0:1, :], func=AF.Ln, bias=cvec[0:1, 2:3],
                     scale=1.0 / D).then_inc(ctx.sc.h, 1)
        s.wait_ge(ctx.sc.h, c_s + 1)
        s.activation(out=rrow[:], in_=rrow[:], func=AF.Exp, bias=0.0,
                     scale=-0.5).then_inc(ctx.sc.h, 1)

    @block.tensor
    def _(te):
        te.wait_ge(ctx.sc.h, c_s + 2)
        for half in range(2):
            te.matmul(psB[:, half * 512:half * 512 + 512], lhsT=ones1[:],
                      rhs=rrow[0:1, half * 512:half * 512 + 512], start=True,
                      stop=True).then_inc(ctx.tc.h, 1)

    v_g = ctx.vc.plan(DT * 2)
    c_g2 = ctx.sc.plan(0)

    @block.sync
    def _(sy):
        sy.wait_ge(ctx.sc.h, c_r + DT)
        for T in range(DT):
            par = T % 2
            if T >= 2:
                sy.wait_ge(ctx.vc.h, v_g + 2 * (T - 1))
            sy.dma_start(out=owin[:, par * TOK:(par + 1) * TOK],
                         in_=outT_d[T * 128:(T + 1) * 128, :]).then_inc(sow[par].h, 16)
    for T in range(DT):
        sow[T % 2].plan(16)

    @block.vector
    def _(v):
        v.wait_ge(ctx.tc.h, t_b + 2)
        for T in range(DT):
            par = T % 2
            v.wait_ge(sow[par].h, (DT // 2 + T // 2 + 1) * 16)
            v.wait_ge(ctx.sc.h, cg + 2 * T + 2)
            if T >= 1:
                v.wait_ge(ctx.vc.h, v_g + 2 * T)
            v.scalar_tensor_tensor(out=tmp[:], in0=owin[:, par * TOK:(par + 1) * TOK],
                                   scalar=nw[:, T:T + 1], in1=psB[:],
                                   op0=ALU.mult, op1=ALU.mult).then_inc(ctx.vc.h, 1)
            v.wait_ge(ctx.vc.h, v_g + 2 * T + 1)
            v.tensor_tensor(out=goT[:, T * TOK:(T + 1) * TOK], in0=tmp[:],
                            in1=goT[:, T * TOK:(T + 1) * TOK],
                            op=ALU.mult).then_inc(ctx.vc.h, 1)

    t_w = ctx.tc.plan(DT * 2)
    v_w = ctx.vc.plan(DT * 2)

    @block.gpsimd
    def _(g):
        for dto in range(DT):
            par = dto % 2
            if dto >= 2:
                g.wait_ge(ctx.tc.h, t_w + 2 * (dto - 1))
            g.dma_start(out=wob[:, par * DT * 128:(par + 1) * DT * 128].rearrange(
                            "p (a q) -> p a q", a=DT),
                        in_=wot_d[dto]).then_inc(swo[par].h, 16)
    for dto in range(DT):
        swo[dto % 2].plan(16)

    @block.tensor
    def _(te):
        te.wait_ge(ctx.vc.h, v_w)
        for dto in range(DT):
            par = dto % 2
            te.wait_ge(swo[par].h, (dto // 2 + 1) * 16)
            for half in range(2):
                n2 = 2 * dto + half
                if n2 >= 2:
                    te.wait_ge(ctx.vc.h, v_w + n2 - 1)
                pw = psW[:, (n2 % 2) * 512:(n2 % 2) * 512 + 512]
                for T in range(DT):
                    mm = te.matmul(pw, lhsT=wob[:, par * DT * 128 + T * 128: par * DT * 128 + (T + 1) * 128],
                                   rhs=goT[:, T * TOK + half * 512: T * TOK + half * 512 + 512],
                                   start=(T == 0), stop=(T == DT - 1))
                mm.then_inc(ctx.tc.h, 1)

    @block.vector
    def _(v):
        for dto in range(DT):
            for half in range(2):
                n2 = 2 * dto + half
                v.wait_ge(ctx.tc.h, t_w + n2 + 1)
                pw = psW[:, (n2 % 2) * 512:(n2 % 2) * 512 + 512]
                v.tensor_copy(aoutT[:, dto * TOK + half * 512: dto * TOK + half * 512 + 512],
                              pw).then_inc(ctx.vc.h, 1)

    t_a = ctx.tc.plan(NT * DT)
    v_a = ctx.vc.plan(NT * DT)
    for tt in range(NT):
        srl[tt % 2].plan(16)
        srs[tt % 2].plan(16)

    @block.sync
    def _(sy):
        for tt in range(NT):
            par = tt % 2
            if tt >= 2:
                sy.wait_ge(srs[par].h, (tt // 2) * 16)
            sy.dma_start(out=win[:, par * D:(par + 1) * D],
                         in_=x1[tt * 128:(tt + 1) * 128, :]).then_inc(srl[par].h, 16)
            sy.wait_ge(ctx.vc.h, v_a + (tt + 1) * DT)
            sy.dma_start(out=x2[tt * 128:(tt + 1) * 128, :],
                         in_=win[:, par * D:(par + 1) * D]).then_inc(srs[par].h, 16)

    @block.tensor
    def _(te):
        te.wait_ge(ctx.vc.h, v_a)
        n = 0
        for tt in range(NT):
            for dto in range(DT):
                if n >= 2:
                    te.wait_ge(ctx.vc.h, v_a + n - 1)
                te.transpose(psT[:, (n % 2) * 1024:(n % 2) * 1024 + 128],
                             aoutT[:, dto * TOK + tt * 128: dto * TOK + (tt + 1) * 128],
                             ident[:]).then_inc(ctx.tc.h, 1)
                n += 1

    @block.vector
    def _(v):
        n = 0
        for tt in range(NT):
            par = tt % 2
            v.wait_ge(srl[par].h, (tt // 2 + 1) * 16)
            for dto in range(DT):
                n += 1
                v.wait_ge(ctx.tc.h, t_a + n)
                dst = win[:, par * D + dto * 128: par * D + (dto + 1) * 128]
                v.tensor_tensor(out=dst, in0=dst,
                                in1=psT[:, ((n - 1) % 2) * 1024:((n - 1) % 2) * 1024 + 128],
                                op=ALU.add).then_inc(ctx.vc.h, 1)


def emit_moe(ctx, S, x3, x4, xn4, w1_d, w2_d, b1_d, b2_d, cent_d, bias_d,
             iota_d, ident, ident32):
    block = ctx.block
    xnT = ctx.sb(S, "mxnT", (128, DT * TOK), BF16)
    hT = ctx.sb(S, "mhT", (128, FT * TOK), BF16)
    acc = ctx.sb(S, "macc", (128, NT * D))
    w1b = ctx.sb(S, "mw1", (128, 2 * DT * 128), BF16)
    w2b = ctx.sb(S, "mw2", (128, FT * D), BF16)
    b1c = ctx.sb(S, "mb1", (128, NEXP * FT))
    b2b = ctx.sb(S, "mb2", (18, D), BF16)
    cent = ctx.sb(S, "mcent", (128, DT * 16), BF16)
    biasr = ctx.sb(S, "mbias", (128, 16))
    iota = ctx.sb(S, "miota", (128, 16))
    sct = ctx.sb(S, "msct", (128, NT * 16))
    G = ctx.sb(S, "mG", (128, NT * NEXP))
    GT = ctx.sb(S, "mGT", (18, TOK), BF16)
    work = ctx.sb(S, "mwork", (128, 16))
    m1 = ctx.sb(S, "mm1", (128, 1))
    msk = ctx.sb(S, "mmsk", (128, 16))
    cand = ctx.sb(S, "mcand", (128, 16))
    idxs = ctx.sb(S, "midx", (128, 1))
    selm = ctx.sb(S, "mselm", (128, 16))
    gsum = ctx.sb(S, "mgsum", (128, 1))
    ginv = ctx.sb(S, "mginv", (128, 1))
    win = ctx.sb(S, "mwin", (128, 2 * D))
    xw = ctx.sb(S, "mxw", (128, 2 * D), BF16)
    psH = ctx.ps(S, "psH", (128, 2 * 512), F32)
    psY = ctx.ps(S, "psY", (128, 2 * 512), F32)
    psSc = ctx.ps(S, "psSc", (128, 512), F32)
    psG = ctx.ps(S, "psGt", (128, 512), F32)
    psT = ctx.ps(S, "psTm", (128, 2 * 1024), BF16)

    sxv = ctx.new_sem("mxv")
    sw1 = [ctx.new_sem("mw1"), ctx.new_sem("mw1")]
    sw2 = ctx.new_sem("mw2s")
    sxt = [ctx.new_sem("mxt"), ctx.new_sem("mxt")]
    srl = [ctx.new_sem("mrl"), ctx.new_sem("mrl")]
    srs = [ctx.new_sem("mrs"), ctx.new_sem("mrs")]

    lv = sxv.plan(3 * 16)
    lvg = ctx.lg.plan(2 * 16)

    @block.sync
    def _(sy):
        sy.dma_start(out=b1c[:].rearrange("p (e f) -> p e f", e=NEXP),
                     in_=b1_d[:].rearrange("e p f -> p e f")).then_inc(sxv.h, 16)
        sy.dma_start(out=biasr[:], in_=bias_d[:]).then_inc(sxv.h, 16)
        sy.dma_start(out=iota[:], in_=iota_d[:]).then_inc(sxv.h, 16)

    @block.gpsimd
    def _(g):
        g.dma_start(out=b2b[:], in_=b2_d[:]).then_inc(ctx.lg.h, 16)
        g.dma_start(out=cent[:].rearrange("p (a q) -> p a q", a=DT),
                    in_=cent_d[:].rearrange("a p q -> p a q")).then_inc(ctx.lg.h, 16)

    t_x = ctx.tc.plan(NT * DT)
    v_x = ctx.vc.plan(NT * (1 + DT))
    for tt in range(NT):
        sxt[tt % 2].plan(16)

    @block.sync
    def _(sy):
        for tt in range(NT):
            par = tt % 2
            if tt >= 2:
                sy.wait_ge(ctx.vc.h, v_x + (tt - 1) * (1 + DT))
            sy.dma_start(out=xw[:, par * D:(par + 1) * D],
                         in_=xn4[tt * 128:(tt + 1) * 128, :]).then_inc(sxt[par].h, 16)

    @block.vector
    def _(v):
        n = 0
        for tt in range(NT):
            par = tt % 2
            v.wait_ge(sxt[par].h, (tt // 2 + 1) * 16)
            v.tensor_scalar(out=acc[:, tt * D:(tt + 1) * D], in0=xw[:, par * D:(par + 1) * D],
                            scalar1=0.5, scalar2=None, op0=ALU.mult).then_inc(ctx.vc.h, 1)
            n += 1
            for d_ in range(DT):
                n += 1
                v.wait_ge(ctx.tc.h, t_x + tt * DT + d_ + 1)
                v.tensor_copy(xnT[:, d_ * TOK + tt * 128: d_ * TOK + (tt + 1) * 128],
                              psT[:, ((tt * DT + d_) % 2) * 1024:((tt * DT + d_) % 2) * 1024 + 128]
                              ).then_inc(ctx.vc.h, 1)

    @block.tensor
    def _(te):
        te.wait_ge(ctx.vc.h, t_x * 0 + v_x)
        n = 0
        for tt in range(NT):
            par = tt % 2
            te.wait_ge(sxt[par].h, (tt // 2 + 1) * 16)
            for d_ in range(DT):
                if n >= 2:
                    prev = n - 2
                    ptt, pd = prev // DT, prev % DT
                    te.wait_ge(ctx.vc.h, v_x + ptt * (1 + DT) + 1 + pd + 1)
                te.transpose(psT[:, (n % 2) * 1024:(n % 2) * 1024 + 128],
                             xw[:, par * D + d_ * 128: par * D + (d_ + 1) * 128],
                             ident[:]).then_inc(ctx.tc.h, 1)
                n += 1

    xnT_done = ctx.vc.v
    t_s = ctx.tc.plan(NT)
    c_s = ctx.sc.plan(NT)
    v_s = ctx.vc.plan(NT * 39)

    @block.tensor
    def _(te):
        te.wait_ge(sxv.h, 48)
        te.wait_ge(ctx.lg.h, lvg + 32)
        te.wait_ge(ctx.vc.h, xnT_done)
        for tt in range(NT):
            if tt >= 1:
                te.wait_ge(ctx.sc.h, c_s + tt)
            for T in range(DT):
                mm = te.matmul(psSc[:, 0:16],
                               lhsT=xnT[:, T * TOK + tt * 128: T * TOK + (tt + 1) * 128],
                               rhs=cent[:, T * 16:(T + 1) * 16],
                               start=(T == 0), stop=(T == DT - 1))
            mm.then_inc(ctx.tc.h, 1)

    @block.scalar
    def _(s):
        for tt in range(NT):
            s.wait_ge(ctx.tc.h, t_s + tt + 1)
            s.activation(out=sct[:, tt * 16:(tt + 1) * 16], in_=psSc[:, 0:16],
                         func=AF.Sigmoid, bias=0.0, scale=1.0).then_inc(ctx.sc.h, 1)

    @block.vector
    def _(v):
        nv = v_s
        v.wait_ge(sxv.h, 48)
        v.wait_ge(ctx.lg.h, lvg + 32)
        for tt in range(NT):
            gt = G[:, tt * NEXP:(tt + 1) * NEXP]
            v.wait_ge(ctx.sc.h, c_s + tt + 1)

            def step(instr):
                nonlocal nv
                instr.then_inc(ctx.vc.h, 1)
                nv += 1
                v.wait_ge(ctx.vc.h, nv)
            step(v.tensor_tensor(out=work[:], in0=sct[:, tt * 16:(tt + 1) * 16],
                                 in1=biasr[:], op=ALU.add))
            step(v.memset(gt, 0.0))
            step(v.tensor_scalar(out=gt[:, 0:2], in0=gt[:, 0:2], scalar1=0.0,
                                 scalar2=0.5, op0=ALU.mult, op1=ALU.add))
            for k in range(4):
                step(v.tensor_reduce(m1[:], work[:], axis=AX.X, op=ALU.max))
                step(v.tensor_scalar(out=msk[:], in0=work[:], scalar1=m1[:],
                                     scalar2=None, op0=ALU.is_equal))
                step(v.scalar_tensor_tensor(out=cand[:], in0=iota[:], scalar=-999.0,
                                            in1=msk[:], op0=ALU.add, op1=ALU.mult))
                step(v.tensor_scalar(out=cand[:], in0=cand[:], scalar1=999.0,
                                     scalar2=None, op0=ALU.add))
                step(v.tensor_reduce(idxs[:], cand[:], axis=AX.X, op=ALU.min))
                step(v.tensor_scalar(out=selm[:], in0=iota[:], scalar1=idxs[:],
                                     scalar2=None, op0=ALU.is_equal))
                step(v.scalar_tensor_tensor(out=gt[:, 2:18], in0=selm[:], scalar=m1[:],
                                            in1=gt[:, 2:18], op0=ALU.mult, op1=ALU.add))
                if k < 3:
                    step(v.scalar_tensor_tensor(out=work[:], in0=selm[:], scalar=-1e9,
                                                in1=work[:], op0=ALU.mult, op1=ALU.add))
                else:
                    step(v.nop())
            step(v.tensor_reduce(gsum[:], gt[:, 2:18], axis=AX.X, op=ALU.add))
            step(v.tensor_scalar(out=gsum[:], in0=gsum[:], scalar1=1e-8, scalar2=None,
                                 op0=ALU.max))
            step(v.reciprocal(ginv[:], gsum[:]))
            step(v.tensor_scalar(out=gt[:, 2:18], in0=gt[:, 2:18], scalar1=ginv[:],
                                 scalar2=0.5, op0=ALU.mult, op1=ALU.mult))

    t_g = ctx.tc.plan(NT)
    v_gt = ctx.vc.plan(NT)

    @block.tensor
    def _(te):
        te.wait_ge(ctx.vc.h, v_gt)
        for tt in range(NT):
            if tt >= 1:
                te.wait_ge(ctx.vc.h, v_gt + tt)
            te.transpose(psG[0:NEXP, 0:128], G[:, tt * NEXP:(tt + 1) * NEXP],
                         ident32[:]).then_inc(ctx.tc.h, 1)

    @block.vector
    def _(v):
        for tt in range(NT):
            v.wait_ge(ctx.tc.h, t_g + tt + 1)
            v.tensor_copy(GT[:, tt * 128:(tt + 1) * 128],
                          psG[0:NEXP, 0:128]).then_inc(ctx.vc.h, 1)


    sw1_cnt = [0, 0]
    for e in range(NEXP):
        t_h = ctx.tc.plan(FT * 2)
        c_h = ctx.sc.plan(FT * 2)
        sw1_need = []
        for ft in range(FT):
            sw1[ft % 2].plan(16)
            sw1_cnt[ft % 2] += 1
            sw1_need.append(sw1_cnt[ft % 2])
        sw2.plan(16)

        @block.gpsimd
        def _(g, e=e, t_h=t_h):
            if e > 0:
                g.wait_ge(ctx.tc.h, t_h)
            g.dma_start(out=w2b[:].rearrange("p (a q) -> p a q", a=FT),
                        in_=w2_d[e].rearrange("(a p) q -> p a q", p=128)
                        ).then_inc(sw2.h, 16)
            for ft in range(FT):
                par = ft % 2
                if ft >= 2:
                    g.wait_ge(ctx.tc.h, t_h + 2 * (ft - 1))
                g.dma_start(out=w1b[:, par * DT * 128:(par + 1) * DT * 128].rearrange(
                                "p (a q) -> p a q", a=DT),
                            in_=w1_d[e, ft]).then_inc(sw1[par].h, 16)

        @block.tensor
        def _(te, e=e, t_h=t_h, c_h=c_h, sw1_need=sw1_need):
            te.wait_ge(ctx.sc.h, c_h)
            for ft in range(FT):
                par = ft % 2
                te.wait_ge(sw1[par].h, sw1_need[ft] * 16)
                for half in range(2):
                    n2 = 2 * ft + half
                    if n2 >= 2:
                        te.wait_ge(ctx.sc.h, c_h + n2 - 1)
                    ph = psH[:, (n2 % 2) * 512:(n2 % 2) * 512 + 512]
                    for T in range(DT):
                        mm = te.matmul(ph, lhsT=w1b[:, par * DT * 128 + T * 128: par * DT * 128 + (T + 1) * 128],
                                       rhs=xnT[:, T * TOK + half * 512: T * TOK + half * 512 + 512],
                                       start=(T == 0), stop=(T == DT - 1))
                    mm.then_inc(ctx.tc.h, 1)

        @block.scalar
        def _(s, e=e, t_h=t_h):
            s.wait_ge(ctx.tc.h, t_h)
            for ft in range(FT):
                for half in range(2):
                    n2 = 2 * ft + half
                    s.wait_ge(ctx.tc.h, t_h + n2 + 1)
                    ph = psH[:, (n2 % 2) * 512:(n2 % 2) * 512 + 512]
                    s.activation(out=hT[:, ft * TOK + half * 512: ft * TOK + half * 512 + 512],
                                 in_=ph, func=AF.Relu, bias=b1c[:, e * FT + ft:e * FT + ft + 1],
                                 scale=1.0).then_inc(ctx.sc.h, 1)

        t_y = ctx.tc.plan(NT * 4)
        v_y = ctx.vc.plan(NT * 4)

        @block.tensor
        def _(te, e=e, t_y=t_y, v_y=v_y, c_h=c_h):
            te.wait_ge(ctx.sc.h, c_h + FT * 2)
            te.wait_ge(ctx.vc.h, v_y)
            te.wait_ge(sw2.h, (e + 1) * 16)
            n = 0
            for tt in range(NT):
                for chk in range(4):
                    if n >= 2:
                        te.wait_ge(ctx.vc.h, v_y + n - 1)
                    py = psY[:, (n % 2) * 512:(n % 2) * 512 + 512]
                    for ft in range(FT):
                        mm = te.matmul(py, lhsT=hT[:, ft * TOK + tt * 128: ft * TOK + (tt + 1) * 128],
                                       rhs=w2b[:, ft * D + chk * 512: ft * D + (chk + 1) * 512],
                                       start=(ft == 0), stop=(ft == FT - 1))
                    mm.then_inc(ctx.tc.h, 1)
                    n += 1

        @block.vector
        def _(v, e=e, t_y=t_y, v_y=v_y):
            n = 0
            for tt in range(NT):
                for chk in range(4):
                    n += 1
                    v.wait_ge(ctx.tc.h, t_y + n)
                    py = psY[:, ((n - 1) % 2) * 512:((n - 1) % 2) * 512 + 512]
                    dst = acc[:, tt * D + chk * 512: tt * D + (chk + 1) * 512]
                    v.scalar_tensor_tensor(out=dst, in0=py,
                                           scalar=G[:, tt * NEXP + e:tt * NEXP + e + 1],
                                           in1=dst, op0=ALU.mult,
                                           op1=ALU.add).then_inc(ctx.vc.h, 1)

    t_b2 = ctx.tc.plan(NT * 4)
    v_b2 = ctx.vc.plan(NT * 4)

    @block.tensor
    def _(te):
        te.wait_ge(ctx.vc.h, v_b2)
        n = 0
        for tt in range(NT):
            for chk in range(4):
                if n >= 2:
                    te.wait_ge(ctx.vc.h, v_b2 + n - 1)
                py = psY[:, (n % 2) * 512:(n % 2) * 512 + 512]
                te.matmul(py, lhsT=GT[:, tt * 128:(tt + 1) * 128],
                          rhs=b2b[:, chk * 512:(chk + 1) * 512], start=True,
                          stop=True).then_inc(ctx.tc.h, 1)
                n += 1

    @block.vector
    def _(v):
        n = 0
        for tt in range(NT):
            for chk in range(4):
                n += 1
                v.wait_ge(ctx.tc.h, t_b2 + n)
                py = psY[:, ((n - 1) % 2) * 512:((n - 1) % 2) * 512 + 512]
                dst = acc[:, tt * D + chk * 512: tt * D + (chk + 1) * 512]
                v.tensor_tensor(out=dst, in0=dst, in1=py, op=ALU.add).then_inc(ctx.vc.h, 1)

    v_f = ctx.vc.plan(NT)
    for tt in range(NT):
        srl[tt % 2].plan(16)
        srs[tt % 2].plan(16)

    @block.sync
    def _(sy):
        for tt in range(NT):
            par = tt % 2
            if tt >= 2:
                sy.wait_ge(srs[par].h, (tt // 2) * 16)
            sy.dma_start(out=win[:, par * D:(par + 1) * D],
                         in_=x3[tt * 128:(tt + 1) * 128, :]).then_inc(srl[par].h, 16)
            sy.wait_ge(ctx.vc.h, v_f + tt + 1)
            sy.dma_start(out=x4[tt * 128:(tt + 1) * 128, :],
                         in_=win[:, par * D:(par + 1) * D]).then_inc(srs[par].h, 16)

    @block.vector
    def _(v):
        v.wait_ge(ctx.vc.h, v_f)
        for tt in range(NT):
            par = tt % 2
            v.wait_ge(srl[par].h, (tt // 2 + 1) * 16)
            v.tensor_tensor(out=win[:, par * D:(par + 1) * D],
                            in0=win[:, par * D:(par + 1) * D],
                            in1=acc[:, tt * D:(tt + 1) * D], op=ALU.add).then_inc(ctx.vc.h, 1)



def emit_xdbl(ctx, S, xn5, wx_d, xdbl_d, ident):
    block = ctx.block
    xw = ctx.sb(S, "xdw", (128, 2 * D), BF16)
    xnT = ctx.sb(S, "xdT", (128, DT * 128), BF16)
    wxb = ctx.sb(S, "xwx", (128, DT * 96), BF16)
    ob = ctx.sb(S, "xdo", (128, 2 * 96))
    psX = ctx.ps(S, "psX", (128, 512), F32)
    psT = ctx.ps(S, "psTx", (128, 2 * 1024), BF16)
    sx = [ctx.new_sem("xdl"), ctx.new_sem("xdl")]
    so = [ctx.new_sem("xds"), ctx.new_sem("xds")]
    lw = ctx.lg.plan(16)

    @block.gpsimd
    def _(g):
        g.dma_start(out=wxb[:].rearrange("p (a q) -> p a q", a=DT),
                    in_=wx_d.rearrange("a p q -> p a q")).then_inc(ctx.lg.h, 16)

    t_x = ctx.tc.plan(NT * (DT + 1))
    v_x = ctx.vc.plan(NT * (DT + 1))
    c_x = ctx.sc.plan(0)
    for tt in range(NT):
        sx[tt % 2].plan(16)
        so[tt % 2].plan(16)

    @block.sync
    def _(sy):
        for tt in range(NT):
            par = tt % 2
            if tt >= 2:
                sy.wait_ge(ctx.tc.h, t_x + (tt - 1) * (DT + 1))
            sy.dma_start(out=xw[:, par * D:(par + 1) * D],
                         in_=xn5[tt * 128:(tt + 1) * 128, :]).then_inc(sx[par].h, 16)
            sy.wait_ge(ctx.vc.h, v_x + tt * (DT + 1) + DT + 1)
            sy.dma_start(out=xdbl_d[tt * 128:(tt + 1) * 128, :],
                         in_=ob[:, par * 96:(par + 1) * 96]).then_inc(so[par].h, 16)

    @block.tensor
    def _(te):
        te.wait_ge(ctx.lg.h, ctx.lg.v)
        n = 0
        for tt in range(NT):
            par = tt % 2
            te.wait_ge(sx[par].h, (tt // 2 + 1) * 16)
            for d_ in range(DT):
                if n >= 2:
                    prev = n - 2
                    ptt, pd = prev // DT, prev % DT
                    te.wait_ge(ctx.vc.h, v_x + ptt * (DT + 1) + pd + 1)
                te.transpose(psT[:, (n % 2) * 1024:(n % 2) * 1024 + 128],
                             xw[:, par * D + d_ * 128: par * D + (d_ + 1) * 128],
                             ident[:]).then_inc(ctx.tc.h, 1)
                n += 1
            te.wait_ge(ctx.vc.h, v_x + tt * (DT + 1) + DT)
            if tt >= 1:
                te.wait_ge(ctx.vc.h, v_x + (tt - 1) * (DT + 1) + DT + 1)
            for T in range(DT):
                mm = te.matmul(psX[:, 0:96], lhsT=xnT[:, T * 128:(T + 1) * 128],
                               rhs=wxb[:, T * 96:(T + 1) * 96],
                               start=(T == 0), stop=(T == DT - 1))
            mm.then_inc(ctx.tc.h, 1)

    @block.vector
    def _(v):
        n = 0
        for tt in range(NT):
            par = tt % 2
            for d_ in range(DT):
                n += 1
                v.wait_ge(ctx.tc.h, t_x + tt * (DT + 1) + d_ + 1)
                v.tensor_copy(xnT[:, d_ * 128:(d_ + 1) * 128],
                              psT[:, ((n - 1 - tt) % 2) * 1024:((n - 1 - tt) % 2) * 1024 + 128]
                              ).then_inc(ctx.vc.h, 1)
            n += 1
            v.wait_ge(ctx.tc.h, t_x + (tt + 1) * (DT + 1))
            if tt >= 2:
                v.wait_ge(so[par].h, (tt // 2) * 16)
            v.tensor_copy(ob[:, par * 96:(par + 1) * 96], psX[:, 0:96]).then_inc(ctx.vc.h, 1)


def build_L3(stages=("att", "ln3", "ffn", "ln4", "moe", "ln5", "xdbl")):
    nc = bass.Bass("TRN2", target_bir_lowering=False, debug=False)
    x1 = nc.dram_tensor("x1", [TOK, D], F32, kind="ExternalInput").ap()
    outT_d = nc.dram_tensor("outT", [D, TOK], F32, kind="ExternalInput").ap()
    xn2v_d = nc.dram_tensor("xn2v", [DT, 128, TOK], BF16, kind="ExternalInput").ap()
    wgt_d = nc.dram_tensor("wgt", [DT, DT, 128, 128], F32, kind="ExternalInput").ap()
    wot_d = nc.dram_tensor("wot", [DT, 128, DT, 128], F32, kind="ExternalInput").ap()
    nw_d = nc.dram_tensor("nw", [128, 16], F32, kind="ExternalInput").ap()
    wu = nc.dram_tensor("wu", [32, 128, 2, DT, 128], F32, kind="ExternalInput").ap()
    wg = nc.dram_tensor("wg", [32, 128, 2, DT, 128], F32, kind="ExternalInput").ap()
    wd = nc.dram_tensor("wd", [DT, 128, IHI, 128], F32, kind="ExternalInput").ap()
    su_ug_d = nc.dram_tensor("su_ug", [128, 16], F32, kind="ExternalInput").ap()
    su_dp_d = nc.dram_tensor("su_dp", [128, 64], F32, kind="ExternalInput").ap()
    ws_up_d = nc.dram_tensor("ws_up", [128, 64], F32, kind="ExternalInput").ap()
    ws_gt_d = nc.dram_tensor("ws_gt", [128, 64], F32, kind="ExternalInput").ap()
    ws_dp_d = nc.dram_tensor("ws_dp", [128, 16], F32, kind="ExternalInput").ap()
    w1_d = nc.dram_tensor("w1", [NEXP, FT, 128, DT, 128], F32, kind="ExternalInput").ap()
    w2_d = nc.dram_tensor("w2", [NEXP, DFF, D], F32, kind="ExternalInput").ap()
    b1_d = nc.dram_tensor("b1", [NEXP, 128, FT], F32, kind="ExternalInput").ap()
    b2_d = nc.dram_tensor("b2", [NEXP, D], F32, kind="ExternalInput").ap()
    cent_d = nc.dram_tensor("cent", [DT, 128, 16], F32, kind="ExternalInput").ap()
    bias_d = nc.dram_tensor("biasr", [128, 16], F32, kind="ExternalInput").ap()
    iota_d = nc.dram_tensor("iota16", [128, 16], F32, kind="ExternalInput").ap()
    wx_d = nc.dram_tensor("wx", [DT, 128, 96], F32, kind="ExternalInput").ap()
    h128_d = nc.dram_tensor("h128s", [128, 128], F32, kind="ExternalInput").ap()
    id_d = nc.dram_tensor("ident", [128, 128], F32, kind="ExternalInput").ap()
    cv_d = nc.dram_tensor("consts", [128, 4], F32, kind="ExternalInput").ap()
    x4_d = nc.dram_tensor("x4", [TOK, D], F32, kind="ExternalOutput").ap()
    xn5_d = nc.dram_tensor("xn5", [TOK, D], BF16, kind="ExternalOutput").ap()
    xdbl_d = nc.dram_tensor("xdbl", [TOK, 96], F32, kind="ExternalOutput").ap()
    x2_i = nc.dram_tensor("x2_i", [TOK, D], F32, kind="Internal").ap()
    xn3_i = nc.dram_tensor("xn3_i", [TOK, D], BF16, kind="Internal").ap()
    x3_i = nc.dram_tensor("x3_i", [TOK, D], F32, kind="Internal").ap()
    xn4_i = nc.dram_tensor("xn4_i", [TOK, D], BF16, kind="Internal").ap()

    with ExitStack() as ES:
        ctx = Ctx(nc, ES)
        block = ctx.block
        h128s = ctx.sb(ES, "h128", (128, 128), BF16)
        ident = ctx.sb(ES, "id", (128, 128), BF16)
        ident32 = ctx.sb(ES, "id32", (128, 128))
        cvec = ctx.sb(ES, "cv", (128, 4))
        lc = ctx.lg.plan(4 * 16)

        @block.gpsimd
        def _(g):
            g.dma_start(out=h128s[:], in_=h128_d[:]).then_inc(ctx.lg.h, 16)
            g.dma_start(out=ident[:], in_=id_d[:]).then_inc(ctx.lg.h, 16)
            g.dma_start(out=ident32[:], in_=id_d[:]).then_inc(ctx.lg.h, 16)
            g.dma_start(out=cvec[:], in_=cv_d[:]).then_inc(ctx.lg.h, 16)

        ctx.join()

        if "att" in stages:
            with ExitStack() as S:
                emit_attn_finish(ctx, S, x1, outT_d, xn2v_d, wgt_d, wot_d, nw_d, x2_i,
                                 cvec, ident)
                ctx.join()
        else:
            x2_i = x1
        if "ln3" in stages:
            with ExitStack() as S:
                emit_ln_stream(ctx, S, x2_i, xn3_i, NT, cvec)
                ctx.join()
        if "ffn" in stages:
            with ExitStack() as S:
                emit_ffn0(ctx, S, x2_i, x3_i, xn3_i, NT, wu, wg, wd, su_ug_d, su_dp_d,
                          ws_up_d, ws_gt_d, ws_dp_d, h128s, ident)
                ctx.join()
        else:
            x3_i = x2_i
        if "ln4" in stages:
            with ExitStack() as S:
                emit_ln_stream(ctx, S, x3_i, xn4_i, NT, cvec)
                ctx.join()
        if "moe" in stages:
            with ExitStack() as S:
                emit_moe(ctx, S, x3_i, x4_d, xn4_i, w1_d, w2_d, b1_d, b2_d, cent_d,
                         bias_d, iota_d, ident, ident32)
                ctx.join()
        else:
            with ExitStack() as S:
                emit_copy_dram(ctx, S, x3_i, x4_d, NT)
                ctx.join()
        if "ln5" in stages:
            with ExitStack() as S:
                emit_ln_stream(ctx, S, x4_d, xn5_d, NT, cvec)
                ctx.join()
        if "xdbl" in stages:
            with ExitStack() as S:
                emit_xdbl(ctx, S, xn5_d, wx_d, xdbl_d, ident)
                ctx.join()
        ctx.join()

    return nc


def emit_copy_dram(ctx, S, src_d, dst_d, ntile):
    block = ctx.block
    buf = ctx.sb(S, "cpy", (128, 2 * D))
    sl = [ctx.new_sem("cpl"), ctx.new_sem("cpl")]
    ss = [ctx.new_sem("cps"), ctx.new_sem("cps")]
    for t in range(ntile):
        sl[t % 2].plan(16)
        ss[t % 2].plan(16)

    @block.sync
    def _(sy):
        for t in range(ntile):
            par = t % 2
            if t >= 2:
                sy.wait_ge(ss[par].h, (t // 2) * 16)
            sy.dma_start(out=buf[:, par * D:(par + 1) * D],
                         in_=src_d[t * 128:(t + 1) * 128, :]).then_inc(sl[par].h, 16)
            sy.wait_ge(sl[par].h, (t // 2 + 1) * 16)
            sy.dma_start(out=dst_d[t * 128:(t + 1) * 128, :],
                         in_=buf[:, par * D:(par + 1) * D]).then_inc(ss[par].h, 16)


def build_L4():
    nc = bass.Bass("TRN2", target_bir_lowering=False, debug=False)
    xnv_d = nc.dram_tensor("xnv", [4, 128, SEQ], BF16, kind="ExternalInput").ap()
    dr_d = nc.dram_tensor("drows", [64, SEQ], F32, kind="ExternalInput").ap()
    br_d = nc.dram_tensor("brows", [16, SEQ], F32, kind="ExternalInput").ap()
    cr_d = nc.dram_tensor("crows", [16, SEQ], F32, kind="ExternalInput").ap()
    wdt_d = nc.dram_tensor("wdt", [64, 4, 128], F32, kind="ExternalInput").ap()
    bdt_d = nc.dram_tensor("bdt", [128, 4], F32, kind="ExternalInput").ap()
    alog_d = nc.dram_tensor("alog", [4, 128, 16], F32, kind="ExternalInput").ap()
    dpar_d = nc.dram_tensor("dpar", [128, 4], F32, kind="ExternalInput").ap()
    ones_d = nc.dram_tensor("ones1", [1, 128], F32, kind="ExternalInput").ap()
    sel_d = nc.dram_tensor("sels", [16, 16, 128], F32, kind="ExternalInput").ap()
    y_d = nc.dram_tensor("yT", [512, SEQ], F32, kind="ExternalOutput").ap()

    LCH = 1024
    NCH = SEQ // LCH

    with ExitStack() as ES:
        ctx = Ctx(nc, ES)
        block = ctx.block
        xnv = ctx.sb(ES, "sxnv", (128, 4 * SEQ), BF16)
        drbf = ctx.sb(ES, "sdr", (64, SEQ), BF16)
        brow = ctx.sb(ES, "sbr", (16, SEQ))
        crow = ctx.sb(ES, "scr", (16, SEQ))
        sels = ctx.sb(ES, "ssel", (16, 16 * 128))
        wdt = ctx.sb(ES, "swdt", (64, 4 * 128), BF16)
        bdt = ctx.sb(ES, "sbdt", (128, 4))
        acols = ctx.sb(ES, "sac", (128, 4 * 16))
        dpar = ctx.sb(ES, "sdp", (128, 4))
        ones1 = ctx.sb(ES, "son", (1, 128))
        Bbc = ctx.sb(ES, "sBb", (128, 16 * LCH), BF16)
        Cbc = ctx.sb(ES, "sCb", (128, 16 * LCH), BF16)
        delta = ctx.sb(ES, "sdl", (128, LCH))
        absb = ctx.sb(ES, "sab", (128, LCH))
        l1p = ctx.sb(ES, "sl1", (128, LCH))
        relu = ctx.sb(ES, "srl", (128, LCH))
        dxb = ctx.sb(ES, "sdx", (128, LCH))
        dAw = ctx.sb(ES, "sdA", (128, 2 * LCH))
        dBw = ctx.sb(ES, "sdB", (128, 2 * LCH))
        hw = ctx.sb(ES, "shw", (128, LCH))
        hcw = ctx.sb(ES, "shc", (128, LCH))
        yac = ctx.sb(ES, "syac", (128, 2 * LCH))
        hst = ctx.sb(ES, "shst", (128, 64))
        psD2 = ctx.ps(ES, "psD2", (128, 2 * 512), F32)
        psB2 = ctx.ps(ES, "psB2", (128, 2 * 512), F32)

        sld = ctx.new_sem("s4l")
        sbc = ctx.new_sem("s4b")
        sst = [ctx.new_sem("s4s"), ctx.new_sem("s4s")]

        l0 = sld.plan(7 * 16)
        lg0 = ctx.lg.plan(2 * 16)
        lb0 = sbc.plan(16)

        @block.sync
        def _(sy):
            sy.dma_start(out=xnv[:].rearrange("p (a q) -> p a q", a=4),
                         in_=xnv_d.rearrange("a p q -> p a q")).then_inc(sld.h, 16)
            sy.dma_start(out=brow[:], in_=br_d[:]).then_inc(sld.h, 16)
            sy.dma_start(out=crow[:], in_=cr_d[:]).then_inc(sld.h, 16)
            sy.dma_start(out=bdt[:], in_=bdt_d[:]).then_inc(sld.h, 16)
            sy.dma_start(out=acols[:].rearrange("p (a q) -> p a q", a=4),
                         in_=alog_d.rearrange("a p q -> p a q")).then_inc(sld.h, 16)
            sy.dma_start(out=dpar[:], in_=dpar_d[:]).then_inc(sld.h, 16)
            sy.dma_start(out=ones1[:], in_=ones_d[:]).then_inc(sld.h, 16)
            sy.dma_start(out=sels[:].rearrange("p (a q) -> p a q", a=16),
                         in_=sel_d.rearrange("a p q -> p a q")).then_inc(sbc.h, 16)

        @block.gpsimd
        def _(g):
            g.dma_start(out=drbf[:], in_=dr_d[:]).then_inc(ctx.lg.h, 16)
            g.dma_start(out=wdt[:].rearrange("p (a q) -> p a q", a=4),
                        in_=wdt_d.rearrange("p a q -> p a q")).then_inc(ctx.lg.h, 16)

        c_a = ctx.sc.plan(1)
        v_a = ctx.vc.plan(1)

        @block.scalar
        def _(s):
            s.wait_ge(sld.h, l0 + 7 * 16)
            s.activation(out=acols[:], in_=acols[:], func=AF.Exp, bias=0.0,
                         scale=1.0).then_inc(ctx.sc.h, 1)

        @block.vector
        def _(v):
            v.wait_ge(ctx.sc.h, c_a + 1)
            v.tensor_scalar(out=acols[:], in0=acols[:], scalar1=-1.0, scalar2=None,
                            op0=ALU.mult).then_inc(ctx.vc.h, 1)

        ctx.join()

        for lc in range(NCH):
            o = lc * LCH
            t_b = ctx.tc.plan(64)
            v_b = ctx.vc.plan(64)

            @block.tensor
            def _(te, lc=lc, t_b=t_b, v_b=v_b, gprev=ctx.gc.v):
                te.wait_ge(ctx.vc.h, v_b)
                te.wait_ge(ctx.gc.h, gprev)
                te.wait_ge(sbc.h, 16)
                n4 = 0
                for n in range(16):
                    for half in range(2):
                        for which in range(2):
                            src = brow if which == 0 else crow
                            if n4 >= 2:
                                te.wait_ge(ctx.vc.h, v_b + n4 - 1)
                            te.matmul(psB2[:, (n4 % 2) * 512:(n4 % 2) * 512 + 512],
                                      lhsT=sels[:, n * 128:(n + 1) * 128],
                                      rhs=src[:, o + half * 512: o + half * 512 + 512],
                                      start=True, stop=True).then_inc(ctx.tc.h, 1)
                            n4 += 1

            @block.vector
            def _(v, lc=lc, t_b=t_b, v_b=v_b):
                n4 = 0
                for n in range(16):
                    for half in range(2):
                        for which in range(2):
                            dst = (Bbc if which == 0 else Cbc)
                            v.wait_ge(ctx.tc.h, t_b + n4 + 1)
                            v.tensor_copy(dst[:, n * LCH + half * 512: n * LCH + half * 512 + 512],
                                          psB2[:, (n4 % 2) * 512:(n4 % 2) * 512 + 512]
                                          ).then_inc(ctx.vc.h, 1)
                            n4 += 1

            for dt4 in range(4):
                it = lc * 4 + dt4
                t_d = ctx.tc.plan(2)
                c_d = ctx.sc.plan(4 * 2)
                v_d = ctx.vc.plan(2)

                @block.tensor
                def _(te, dt4=dt4, o=o, t_d=t_d, v_d=v_d, c_d=c_d, it=it):
                    if it > 0:
                        te.wait_ge(ctx.sc.h, c_d)
                    for half in range(2):
                        mm = te.matmul(psD2[:, half * 512:half * 512 + 512],
                                       lhsT=wdt[:, dt4 * 128:(dt4 + 1) * 128],
                                       rhs=drbf[:, o + half * 512: o + half * 512 + 512],
                                       start=True, stop=True)
                    mm.then_inc(ctx.tc.h, 2)

                @block.scalar
                def _(s, dt4=dt4, t_d=t_d, c_d=c_d, v_d=v_d):
                    s.wait_ge(ctx.tc.h, t_d + 2)
                    for half in range(2):
                        pd = psD2[:, half * 512:half * 512 + 512]
                        sl = slice(half * 512, half * 512 + 512)
                        s.activation(out=absb[:, sl], in_=pd, func=AF.Abs,
                                     bias=bdt[:, dt4:dt4 + 1], scale=1.0).then_inc(ctx.sc.h, 1)
                        s.wait_ge(ctx.sc.h, c_d + 4 * half + 1)
                        s.activation(out=l1p[:, sl], in_=absb[:, sl], func=AF.Exp,
                                     bias=0.0, scale=-1.0).then_inc(ctx.sc.h, 1)
                        s.wait_ge(ctx.sc.h, c_d + 4 * half + 2)
                        s.activation(out=l1p[:, sl], in_=l1p[:, sl], func=AF.Ln,
                                     bias=1.0, scale=1.0).then_inc(ctx.sc.h, 1)
                        s.activation(out=relu[:, sl], in_=pd, func=AF.Relu,
                                     bias=bdt[:, dt4:dt4 + 1], scale=1.0).then_inc(ctx.sc.h, 1)

                @block.vector
                def _(v, dt4=dt4, o=o, c_d=c_d, v_d=v_d, gprev=ctx.gc.v):
                    v.wait_ge(ctx.sc.h, c_d + 8)
                    v.wait_ge(ctx.gc.h, gprev)
                    v.tensor_tensor(out=delta[:], in0=relu[:], in1=l1p[:],
                                    op=ALU.add).then_inc(ctx.vc.h, 1)
                    v.wait_ge(ctx.vc.h, v_d + 1)
                    v.tensor_tensor(out=dxb[:], in0=delta[:],
                                    in1=xnv[:, dt4 * SEQ + o: dt4 * SEQ + o + LCH],
                                    op=ALU.mult).then_inc(ctx.vc.h, 1)
                    v.wait_ge(ctx.vc.h, v_d + 2)

                c_n = ctx.sc.plan(16)
                g_n = ctx.gc.plan(16)
                v_n = ctx.vc.plan(16 * 4)

                @block.gpsimd
                def _(g, dt4=dt4, lc=lc, o=o, g_n=g_n, v_n=v_n, v_d=v_d):
                    g.wait_ge(ctx.vc.h, v_d + 2)
                    for n in range(16):
                        if n >= 2:
                            g.wait_ge(ctx.vc.h, v_n + (n - 2) * 4 + 1)
                        g.tensor_tensor(out=dBw[:, (n % 2) * LCH:(n % 2) * LCH + LCH],
                                        in0=dxb[:], in1=Bbc[:, n * LCH:(n + 1) * LCH],
                                        op=ALU.mult).then_inc(ctx.gc.h, 1)

                @block.scalar
                def _(s, dt4=dt4, c_n=c_n, v_n=v_n, v_d=v_d):
                    s.wait_ge(ctx.vc.h, v_d + 1)
                    for n in range(16):
                        if n >= 2:
                            s.wait_ge(ctx.vc.h, v_n + (n - 2) * 4 + 1)
                        s.activation(out=dAw[:, (n % 2) * LCH:(n % 2) * LCH + LCH],
                                     in_=delta[:], func=AF.Exp, bias=0.0,
                                     scale=acols[:, dt4 * 16 + n: dt4 * 16 + n + 1]
                                     ).then_inc(ctx.sc.h, 1)

                @block.vector
                def _(v, dt4=dt4, lc=lc, o=o, c_n=c_n, g_n=g_n, v_n=v_n, it=it):
                    nv = v_n
                    if it >= 2:
                        v.wait_ge(sst[it % 2].h, (it // 2) * 16)
                    for n in range(16):
                        v.wait_ge(ctx.sc.h, c_n + n + 1)
                        v.wait_ge(ctx.gc.h, g_n + n + 1)
                        v.tensor_tensor_scan(out=hw[:],
                                             data0=dAw[:, (n % 2) * LCH:(n % 2) * LCH + LCH],
                                             data1=dBw[:, (n % 2) * LCH:(n % 2) * LCH + LCH],
                                             initial=0.0 if lc == 0 else hst[:, dt4 * 16 + n: dt4 * 16 + n + 1],
                                             op0=ALU.mult, op1=ALU.add).then_inc(ctx.vc.h, 1)
                        nv += 1
                        v.wait_ge(ctx.vc.h, nv)
                        v.tensor_copy(hst[:, dt4 * 16 + n: dt4 * 16 + n + 1],
                                      hw[:, LCH - 1:LCH]).then_inc(ctx.vc.h, 1)
                        nv += 1
                        if n == 0:
                            v.tensor_tensor(out=yac[:, (it % 2) * LCH:(it % 2) * LCH + LCH],
                                            in0=hw[:], in1=Cbc[:, 0:LCH],
                                            op=ALU.mult).then_inc(ctx.vc.h, 1)
                            nv += 1
                            v.wait_ge(ctx.vc.h, nv)
                            v.nop().then_inc(ctx.vc.h, 1)
                            nv += 1
                        else:
                            v.tensor_tensor(out=hcw[:], in0=hw[:],
                                            in1=Cbc[:, n * LCH:(n + 1) * LCH],
                                            op=ALU.mult).then_inc(ctx.vc.h, 1)
                            nv += 1
                            v.wait_ge(ctx.vc.h, nv)
                            v.tensor_tensor(out=yac[:, (it % 2) * LCH:(it % 2) * LCH + LCH],
                                            in0=yac[:, (it % 2) * LCH:(it % 2) * LCH + LCH],
                                            in1=hcw[:], op=ALU.add).then_inc(ctx.vc.h, 1)
                            nv += 1
                        v.wait_ge(ctx.vc.h, nv)

                v_y = ctx.vc.plan(1)
                st0 = sst[it % 2].plan(16)

                @block.vector
                def _(v, dt4=dt4, it=it, v_y=v_y):
                    v.scalar_tensor_tensor(out=yac[:, (it % 2) * LCH:(it % 2) * LCH + LCH],
                                           in0=xnv[:, dt4 * SEQ + o: dt4 * SEQ + o + LCH],
                                           scalar=dpar[:, dt4:dt4 + 1],
                                           in1=yac[:, (it % 2) * LCH:(it % 2) * LCH + LCH],
                                           op0=ALU.mult, op1=ALU.add).then_inc(ctx.vc.h, 1)

                @block.sync
                def _(sy, dt4=dt4, o=o, it=it, v_y=v_y):
                    sy.wait_ge(ctx.vc.h, v_y + 1)
                    sy.dma_start(out=y_d[dt4 * 128:(dt4 + 1) * 128, o:o + LCH],
                                 in_=yac[:, (it % 2) * LCH:(it % 2) * LCH + LCH]
                                 ).then_inc(sst[it % 2].h, 16)

        ctx.join()

    return nc


def build_L5():
    nc = bass.Bass("TRN2", target_bir_lowering=False, debug=False)
    x4 = nc.dram_tensor("x4", [TOK, D], F32, kind="ExternalInput").ap()
    yv_d = nc.dram_tensor("yv", [TOK, D], F32, kind="ExternalInput").ap()
    w1_d = nc.dram_tensor("w1", [NEXP, FT, 128, DT, 128], F32, kind="ExternalInput").ap()
    w2_d = nc.dram_tensor("w2", [NEXP, DFF, D], F32, kind="ExternalInput").ap()
    b1_d = nc.dram_tensor("b1", [NEXP, 128, FT], F32, kind="ExternalInput").ap()
    b2_d = nc.dram_tensor("b2", [NEXP, D], F32, kind="ExternalInput").ap()
    cent_d = nc.dram_tensor("cent", [DT, 128, 16], F32, kind="ExternalInput").ap()
    bias_d = nc.dram_tensor("biasr", [128, 16], F32, kind="ExternalInput").ap()
    iota_d = nc.dram_tensor("iota16", [128, 16], F32, kind="ExternalInput").ap()
    id_d = nc.dram_tensor("ident", [128, 128], F32, kind="ExternalInput").ap()
    cv_d = nc.dram_tensor("consts", [128, 4], F32, kind="ExternalInput").ap()
    out_d = nc.dram_tensor("out", [TOK, D], F32, kind="ExternalOutput").ap()
    x5_i = nc.dram_tensor("x5_i", [TOK, D], F32, kind="Internal").ap()
    xn6_i = nc.dram_tensor("xn6_i", [TOK, D], BF16, kind="Internal").ap()

    with ExitStack() as ES:
        ctx = Ctx(nc, ES)
        block = ctx.block
        ident = ctx.sb(ES, "id", (128, 128), BF16)
        ident32 = ctx.sb(ES, "id32", (128, 128))
        cvec = ctx.sb(ES, "cv", (128, 4))
        lc = ctx.lg.plan(3 * 16)

        @block.gpsimd
        def _(g):
            g.dma_start(out=ident[:], in_=id_d[:]).then_inc(ctx.lg.h, 16)
            g.dma_start(out=ident32[:], in_=id_d[:]).then_inc(ctx.lg.h, 16)
            g.dma_start(out=cvec[:], in_=cv_d[:]).then_inc(ctx.lg.h, 16)

        ctx.join()

        with ExitStack() as S:
            buf = ctx.sb(S, "ya", (128, 2 * D))
            yb = ctx.sb(S, "yb", (128, 2 * D))
            sl = [ctx.new_sem("y5l"), ctx.new_sem("y5l")]
            sy_ = [ctx.new_sem("y5y"), ctx.new_sem("y5y")]
            ss = [ctx.new_sem("y5s"), ctx.new_sem("y5s")]
            v5 = ctx.vc.plan(NT)
            for t in range(NT):
                sl[t % 2].plan(16)
                sy_[t % 2].plan(16)
                ss[t % 2].plan(16)

            @block.sync
            def _(sy):
                for t in range(NT):
                    par = t % 2
                    if t >= 2:
                        sy.wait_ge(ss[par].h, (t // 2) * 16)
                    sy.dma_start(out=buf[:, par * D:(par + 1) * D],
                                 in_=x4[t * 128:(t + 1) * 128, :]).then_inc(sl[par].h, 16)
                    sy.dma_start(out=yb[:, par * D:(par + 1) * D],
                                 in_=yv_d[t * 128:(t + 1) * 128, :]).then_inc(sy_[par].h, 16)
                    sy.wait_ge(ctx.vc.h, v5 + t + 1)
                    sy.dma_start(out=x5_i[t * 128:(t + 1) * 128, :],
                                 in_=buf[:, par * D:(par + 1) * D]).then_inc(ss[par].h, 16)

            @block.vector
            def _(v):
                for t in range(NT):
                    par = t % 2
                    v.wait_ge(sl[par].h, (t // 2 + 1) * 16)
                    v.wait_ge(sy_[par].h, (t // 2 + 1) * 16)
                    v.tensor_tensor(out=buf[:, par * D:(par + 1) * D],
                                    in0=buf[:, par * D:(par + 1) * D],
                                    in1=yb[:, par * D:(par + 1) * D],
                                    op=ALU.add).then_inc(ctx.vc.h, 1)
            ctx.join()

        with ExitStack() as S:
            emit_ln_stream(ctx, S, x5_i, xn6_i, NT, cvec)
            ctx.join()
        with ExitStack() as S:
            emit_moe(ctx, S, x5_i, out_d, xn6_i, w1_d, w2_d, b1_d, b2_d, cent_d,
                     bias_d, iota_d, ident, ident32)
            ctx.join()
        ctx.join()

    return nc


def prep_ffn0_weights(W_up, W_gate, W_down):
    def up_tile(W):
        return _c(W.reshape(32, 2, 128, DT, 128).transpose(0, 4, 1, 3, 2))
    wu = up_tile(W_up)
    wg = up_tile(W_gate)
    wd = _c(W_down.reshape(DT, 128, IHI, 128).transpose(0, 3, 2, 1))
    return wu, wg, wd


def prep_attn_weights(Wqkv, Wg, Wo, norm_w):
    WqkvT = np.ascontiguousarray(Wqkv.T)
    qkvw_all = np.stack([WqkvT[:, h * 384:(h + 1) * 384].reshape(DT, 128, 384)
                         for h in range(16)])
    WgT = Wg.T
    wgt = _c(WgT.reshape(DT, 128, DT, 128).transpose(2, 0, 1, 3))
    wot = _c(Wo.reshape(DT, 128, DT, 128).transpose(0, 3, 2, 1))
    nw = _c(norm_w.reshape(16, 128).T)
    return _c(qkvw_all), wgt, wot, nw


def prep_moe_weights(sw1, sb1, sw2, sb2, rw1, rb1, rw2, rb2, centroids, bias):
    w1 = np.concatenate([sw1, rw1], axis=0)
    w2 = np.concatenate([sw2, rw2], axis=0)
    b1 = np.concatenate([sb1, rb1], axis=0)
    b2 = np.concatenate([sb2, rb2], axis=0)
    w1t = _c(np.stack([w1[e].reshape(DT, 128, FT, 128).transpose(2, 1, 0, 3)
                       for e in range(NEXP)]))
    b1c = _c(b1.reshape(NEXP, FT, 128).transpose(0, 2, 1))
    cent = _c(centroids.T.reshape(DT, 128, 16))
    biasr = _c(np.broadcast_to(bias.astype(np.float32), (128, 16)))
    return w1t, _c(w2), b1c, _c(b2), cent, biasr


IOTA16 = _c(np.broadcast_to(np.arange(16, dtype=np.float32), (128, 16)))


def prep_ffn0_small(SU_ug, SU_dp, Wscale_ug, Wscale_dp):
    su_ug = _c(SU_ug.reshape(16, 128).T)
    su_dp = _c(SU_dp.reshape(64, 128).T)
    ws_up = _c(Wscale_ug[:INTER].reshape(64, 128).T)
    ws_gt = _c(Wscale_ug[INTER:].reshape(64, 128).T)
    ws_dp = _c(Wscale_dp.reshape(16, 128).T)
    return su_ug, su_dp, ws_up, ws_gt, ws_dp


_NC_CACHE = {}


def _nc(name, builder):
    if name not in _NC_CACHE:
        _NC_CACHE[name] = builder()
    return _NC_CACHE[name]


def kernel(**inputs):
    ins = inputs
    x = np.ascontiguousarray(ins["L0_attn_x"].reshape(8192, D), dtype=np.float32)

    wu, wg, wd = prep_ffn0_weights(ins["L0_ffn_W_up"], ins["L0_ffn_W_gate"],
                                   ins["L0_ffn_W_down"])
    su_ug, su_dp, ws_up, ws_gt, ws_dp = prep_ffn0_small(
        ins["L0_ffn_SU_ug"], ins["L0_ffn_SU_dp"], ins["L0_ffn_Wscale_ug"],
        ins["L0_ffn_Wscale_dp"])
    qkvw_all, wgt, wot, nw = prep_attn_weights(ins["L0_attn_Wqkv"], ins["L0_attn_Wg"],
                                               ins["L0_attn_Wo"], ins["L0_attn_norm_w"])
    w1t, w2c, b1c, b2c, cent, biasr = prep_moe_weights(
        ins["L1_ffn_shared_w1"], ins["L1_ffn_shared_b1"], ins["L1_ffn_shared_w2"],
        ins["L1_ffn_shared_b2"], ins["L1_ffn_routed_w1"], ins["L1_ffn_routed_b1"],
        ins["L1_ffn_routed_w2"], ins["L1_ffn_routed_b2"], ins["L1_ffn_centroids"],
        ins["L1_ffn_bias"])
    wx = _c(ins["L1_attn_W_xproj"].T.reshape(DT, 128, 96))
    slope_row = _c(ins["L0_attn_slope_rate"].reshape(1, 16))
    ones1 = np.ones((1, 128), np.float32)
    p_ = np.arange(128, dtype=np.float32)
    arrq = _c(np.stack([p_ + 1, p_ + 129], axis=1))
    arrk = _c(np.stack([255 - p_, 255 - (p_ + 128)], axis=1))
    idxT = np.zeros((2, 128, 256), np.float32)
    mskT = np.zeros((2, 128, 256), np.float32)
    m_ = np.arange(256)
    for u_ in range(2):
        for pp in range(128):
            d_ = m_ - (u_ * 128 + pp)
            idxT[u_, pp] = np.where(d_ >= 0, d_, 1e9)
            mskT[u_, pp] = (d_ >= 0).astype(np.float32)
    sels = _c(np.eye(16, dtype=np.float32)[:, :, None]
              * np.ones((1, 1, 128), np.float32)).transpose(1, 0, 2).copy()
    WdtT = np.ascontiguousarray(ins["L1_attn_W_dt"].T)

    nc1 = _nc("L1", build_L1)
    common1 = {"wu": wu, "wg": wg, "wd": wd, "su_ug": su_ug, "su_dp": su_dp,
               "ws_up": ws_up, "ws_gt": ws_gt, "ws_dp": ws_dp,
               "h128s": H128S, "ident": IDENT, "consts": CONSTS}
    maps = [dict(common1, x=np.ascontiguousarray(x[c * TOK:(c + 1) * TOK]))
            for c in range(8)]
    r1 = run_bass_kernel_spmd(nc1, maps, core_ids=list(range(8))).results
    x1 = [r1[c]["x1"] for c in range(8)]
    xn2 = [r1[c]["xn2"] for c in range(8)]

    nc2 = _nc("L2", build_L2)
    maps = []
    xq_b = []
    for b in range(2):
        xn2_b = np.concatenate(xn2[4 * b:4 * b + 4], axis=0)
        xq_b.append(np.ascontiguousarray(
            xn2_b.reshape(NTS, 128, DT, 128).transpose(0, 2, 3, 1)))
    for cs in range(8):
        b, hg = cs // 4, cs % 4
        maps.append({"xq": xq_b[b],
                     "qkvw": np.ascontiguousarray(qkvw_all[hg * 4:(hg + 1) * 4]),
                     "slope": np.ascontiguousarray(slope_row[:, hg * 4:(hg + 1) * 4]),
                     "ones1": ones1, "arrq": arrq, "arrk": arrk,
                     "idxT": idxT, "mskT": mskT, "ident": IDENT})
    r2 = run_bass_kernel_spmd(nc2, maps, core_ids=list(range(8))).results
    outT_h = [r2[cs]["outT"] for cs in range(8)]

    nc3 = _nc("L3", build_L3)
    common3 = {"wgt": wgt, "wot": wot, "nw": nw, "wu": wu, "wg": wg, "wd": wd,
               "su_ug": su_ug, "su_dp": su_dp, "ws_up": ws_up, "ws_gt": ws_gt,
               "ws_dp": ws_dp, "w1": w1t, "w2": w2c, "b1": b1c, "b2": b2c,
               "cent": cent, "biasr": biasr, "iota16": IOTA16, "wx": wx,
               "h128s": H128S, "ident": IDENT, "consts": CONSTS}
    maps = []
    for c in range(8):
        b, pos = c // 4, c % 4
        outT_home = np.ascontiguousarray(np.concatenate(
            [outT_h[b * 4 + g][:, pos * TOK:(pos + 1) * TOK] for g in range(4)], axis=0))
        xn2v = np.ascontiguousarray(
            xn2[c].reshape(NT, 128, DT, 128).transpose(2, 3, 0, 1).reshape(DT, 128, TOK))
        maps.append(dict(common3, x1=x1[c], outT=outT_home, xn2v=xn2v))
    r3 = run_bass_kernel_spmd(nc3, maps, core_ids=list(range(8))).results
    x4 = [r3[c]["x4"] for c in range(8)]
    xn5 = [r3[c]["xn5"] for c in range(8)]
    xdbl = [r3[c]["xdbl"] for c in range(8)]

    nc4 = _nc("L4", build_L4)
    maps = []
    xn5T_b, xdblT_b = [], []
    for b in range(2):
        xn5_b = np.concatenate(xn5[4 * b:4 * b + 4], axis=0)
        xn5T_b.append(np.ascontiguousarray(xn5_b.T))
        xdbl_b = np.concatenate(xdbl[4 * b:4 * b + 4], axis=0)
        xdblT_b.append(np.ascontiguousarray(xdbl_b.T.astype(np.float32)))
    for cs in range(8):
        b, k = cs // 4, cs % 4
        ds = k * 512
        maps.append({
            "xnv": np.ascontiguousarray(xn5T_b[b][ds:ds + 512].reshape(4, 128, SEQ)),
            "drows": np.ascontiguousarray(xdblT_b[b][0:64]),
            "brows": np.ascontiguousarray(xdblT_b[b][64:80]),
            "crows": np.ascontiguousarray(xdblT_b[b][80:96]),
            "wdt": np.ascontiguousarray(WdtT[:, ds:ds + 512].reshape(64, 4, 128)),
            "bdt": np.ascontiguousarray(ins["L1_attn_b_dt"][ds:ds + 512].reshape(4, 128).T),
            "alog": np.ascontiguousarray(ins["L1_attn_A_log"][ds:ds + 512].reshape(4, 128, 16)),
            "dpar": np.ascontiguousarray(ins["L1_attn_D_param"][ds:ds + 512].reshape(4, 128).T),
            "ones1": ones1, "sels": sels})
    r4 = run_bass_kernel_spmd(nc4, maps, core_ids=list(range(8))).results
    yT = [r4[cs]["yT"] for cs in range(8)]

    nc5 = _nc("L5", build_L5)
    common5 = {"w1": w1t, "w2": w2c, "b1": b1c, "b2": b2c, "cent": cent,
               "biasr": biasr, "iota16": IOTA16, "ident": IDENT, "consts": CONSTS}
    maps = []
    for c in range(8):
        b, pos = c // 4, c % 4
        y_home = np.concatenate(
            [yT[b * 4 + k][:, pos * TOK:(pos + 1) * TOK] for k in range(4)], axis=0)
        yv = np.ascontiguousarray(y_home.T)
        maps.append(dict(common5, x4=x4[c], yv=yv))
    r5 = run_bass_kernel_spmd(nc5, maps, core_ids=list(range(8))).results
    out = np.concatenate([r5[c]["out"] for c in range(8)], axis=0)
    return out.reshape(2, 4096, D)


# version 03
# speedup vs baseline: 1.0710x; 1.0710x over previous
import numpy as np
import ml_dtypes
from contextlib import ExitStack

import concourse.bass as bass
import concourse.mybir as mybir
from concourse.bass_utils import run_bass_kernel_spmd

F32 = mybir.dt.float32
BF16 = mybir.dt.bfloat16
AF = mybir.ActivationFunctionType
ALU = mybir.AluOpType
AX = mybir.AxisListType
BF16NP = ml_dtypes.bfloat16

D = 2048
TOK = 1024
NT = TOK // 128
DT = D // 128
INTER = 8192
IHI = 64
EPS_LN = 1e-5
EPS_RMS = float(np.finfo(np.float32).eps)
C_UG = float(2.0 ** 2.5)
C_DP = float(2.0 ** 0.5)

DFF = 1408
FT = 11
NEXP = 18
NHEAD = 16
BLK = 256
TCH = 512


def _had(n):
    h = np.array([[1.0]])
    while h.shape[0] < n:
        h = np.block([[h, h], [h, -h]])
    return h


def _c(x):
    return np.ascontiguousarray(np.asarray(x), dtype=np.float32)


H128S = _c(_had(128) * (2.0 ** -8))
IDENT = _c(np.eye(128))
CONSTS = _c(np.broadcast_to(np.array([EPS_LN, 1.0, EPS_RMS, 0.0], np.float32), (128, 4)))


class Sem:
    def __init__(self, h):
        self.h = h
        self.v = 0

    def plan(self, n):
        b = self.v
        self.v += n
        return b


class Ctx:
    def __init__(self, nc, ES):
        self.nc = nc
        self.ES = ES
        self.block = ES.enter_context(nc.Block())
        self._all = []
        self._n = 0
        self.tc = self.new_sem("tc")
        self.vc = self.new_sem("vc")
        self.sc = self.new_sem("sc")
        self.gc = self.new_sem("gc")
        self.lg = self.new_sem("lg")

    def new_sem(self, name):
        self._n += 1
        s = Sem(self.ES.enter_context(self.nc.semaphore(f"s_{name}{self._n}")))
        self._all.append(s)
        return s

    def sb(self, scope, name, shape, dt=F32):
        self._n += 1
        return scope.enter_context(self.nc.sbuf_tensor(f"{name}{self._n}", list(shape), dt))

    def ps(self, scope, name, shape, dt=F32):
        self._n += 1
        return scope.enter_context(self.nc.psum_tensor(f"{name}{self._n}", list(shape), dt))

    def join(self):
        block = self.block
        sems = [s for s in self._all if s.v]

        def w(e):
            for s in sems:
                e.wait_ge(s.h, s.v)

        @block.sync
        def _(e):
            w(e)

        @block.gpsimd
        def _(e):
            w(e)

        @block.tensor
        def _(e):
            w(e)

        @block.vector
        def _(e):
            w(e)

        @block.scalar
        def _(e):
            w(e)


def emit_ln_stream(ctx, S, xin, xn_out, ntile, cvec):
    block = ctx.block
    win = ctx.sb(S, "lnw", (128, 2 * D))
    xst = ctx.sb(S, "lnx", (128, 2 * D), BF16)
    sums = ctx.sb(S, "lns", (128, 8))
    ssq = ctx.sb(S, "lnq", (128, 2))
    sq = ctx.sb(S, "lnsq", (128, D))
    lsl = [ctx.new_sem("lnl"), ctx.new_sem("lnl")]
    ssl = [ctx.new_sem("lnst"), ctx.new_sem("lnst")]
    for t in range(ntile):
        lsl[t % 2].plan(16)
        ssl[t % 2].plan(16)

    v0 = ctx.vc.plan(ntile * 5)
    c0 = ctx.sc.plan(ntile * 3)

    @block.sync
    def _(sy):
        for t in range(ntile):
            par = t % 2
            if t >= 2:
                sy.wait_ge(ctx.vc.h, v0 + (t - 1) * 5)
                sy.wait_ge(ctx.sc.h, c0 + (t - 2) * 3 + 1)
                sy.wait_ge(ssl[par].h, (t // 2) * 16)
            sy.dma_start(out=win[:, par * D:(par + 1) * D],
                         in_=xin[t * 128:(t + 1) * 128, :]).then_inc(lsl[par].h, 16)
            sy.wait_ge(ctx.vc.h, v0 + t * 5 + 5)
            sy.dma_start(out=xn_out[t * 128:(t + 1) * 128, :],
                         in_=xst[:, par * D:(par + 1) * D]).then_inc(ssl[par].h, 16)

    @block.vector
    def _(v):
        for t in range(ntile):
            par = t % 2
            v.wait_ge(lsl[par].h, (t // 2 + 1) * 16)
            w = win[:, par * D:(par + 1) * D]
            v.tensor_reduce(sums[:, 4 * par:4 * par + 1], w, axis=AX.X,
                            op=ALU.add).then_inc(ctx.vc.h, 1)
            v.wait_ge(ctx.vc.h, v0 + t * 5 + 1)
            v.tensor_scalar(out=sums[:, 4 * par:4 * par + 1], in0=sums[:, 4 * par:4 * par + 1],
                            scalar1=-1.0 / D, scalar2=None, op0=ALU.mult).then_inc(ctx.vc.h, 1)
            v.wait_ge(ctx.vc.h, v0 + t * 5 + 2)
            v.wait_ge(ctx.sc.h, c0 + t * 3 + 1)
            v.tensor_tensor(out=sums[:, 4 * par + 1:4 * par + 2], in0=sums[:, 4 * par:4 * par + 1],
                            in1=sums[:, 4 * par:4 * par + 1], op=ALU.mult).then_inc(ctx.vc.h, 1)
            v.wait_ge(ctx.vc.h, v0 + t * 5 + 3)
            v.scalar_tensor_tensor(out=sums[:, 4 * par + 2:4 * par + 3],
                                   in0=ssq[:, par:par + 1], scalar=1.0 / D,
                                   in1=sums[:, 4 * par + 1:4 * par + 2],
                                   op0=ALU.mult, op1=ALU.subtract).then_inc(ctx.vc.h, 1)
            v.wait_ge(ctx.sc.h, c0 + t * 3 + 3)
            v.tensor_scalar(out=xst[:, par * D:(par + 1) * D], in0=w,
                            scalar1=sums[:, 4 * par:4 * par + 1],
                            scalar2=sums[:, 4 * par + 3:4 * par + 4],
                            op0=ALU.add, op1=ALU.mult).then_inc(ctx.vc.h, 1)

    @block.scalar
    def _(s):
        for t in range(ntile):
            par = t % 2
            if t >= 2:
                s.wait_ge(ctx.vc.h, v0 + (t - 1) * 5)
            s.wait_ge(lsl[par].h, (t // 2 + 1) * 16)
            s.activation(out=sq[:], in_=win[:, par * D:(par + 1) * D], func=AF.Square,
                         bias=0.0, scale=1.0, accum_out=ssq[:, par:par + 1]).then_inc(ctx.sc.h, 1)
            s.wait_ge(ctx.vc.h, v0 + t * 5 + 4)
            s.activation(out=sums[:, 4 * par + 3:4 * par + 4], in_=sums[:, 4 * par + 2:4 * par + 3],
                         func=AF.Ln, bias=cvec[:, 0:1], scale=1.0).then_inc(ctx.sc.h, 1)
            s.wait_ge(ctx.sc.h, c0 + t * 3 + 2)
            s.activation(out=sums[:, 4 * par + 3:4 * par + 4], in_=sums[:, 4 * par + 3:4 * par + 4],
                         func=AF.Exp, bias=0.0, scale=-0.5).then_inc(ctx.sc.h, 1)


def emit_ffn0(ctx, S, xin, xout, xn, ntile, wu, wg, wd, su_ug_d, su_dp_d,
              ws_up_d, ws_gt_d, ws_dp_d, h128s, ident,
              parts=("y1", "z", "fwht", "down")):
    nc, block = ctx.nc, ctx.block
    NCH = (ntile * 128) // TCH
    NTT = TCH // 128
    SLAB = 2 * DT * 128
    HALF = 32 * 128

    xnc = ctx.sb(S, "xnc", (128, NTT * D), BF16)
    xnT = ctx.sb(S, "xnT", (128, DT * TCH), BF16)
    y1 = ctx.sb(S, "y1", (128, DT * TCH), BF16)
    tmp = ctx.sb(S, "tmpc", (128, 2 * TCH), BF16)
    btmp = ctx.sb(S, "btmp", (128, 2048), BF16)
    z = ctx.sb(S, "zf", (128, IHI * TCH), BF16)
    sil = ctx.sb(S, "sil", (128, 2 * TCH))
    foutT = ctx.sb(S, "foutT", (128, DT * TCH), BF16)
    wub = ctx.sb(S, "wub", (128, 2 * SLAB), BF16)
    wgb = ctx.sb(S, "wgb", (128, 2 * SLAB), BF16)
    wdb = ctx.sb(S, "wdb", (128, 2 * HALF), BF16)
    win = ctx.sb(S, "fwin", (128, 2 * D))
    su_ug = ctx.sb(S, "suug", (128, 16))
    su_dp = ctx.sb(S, "sudp", (128, 64))
    wse_u = ctx.sb(S, "wseu", (128, 64))
    wse_g = ctx.sb(S, "wseg", (128, 64))
    wse_d = ctx.sb(S, "wsed", (128, 16))
    psA = ctx.ps(S, "psA", (128, 2 * 512), F32)
    psU = ctx.ps(S, "psU", (128, 2 * 512), F32)
    psG = ctx.ps(S, "psG", (128, 2 * 512), F32)
    psD = ctx.ps(S, "psD", (128, 512), F32)
    psT = ctx.ps(S, "psT", (128, 1024), BF16)

    lf = ctx.new_sem("ffc")
    sxn = ctx.new_sem("fxn")
    swz = [ctx.new_sem("fwz"), ctx.new_sem("fwz")]
    swd = [ctx.new_sem("fwd"), ctx.new_sem("fwd")]
    srl = [ctx.new_sem("frl"), ctx.new_sem("frl")]
    srs = [ctx.new_sem("frs"), ctx.new_sem("frs")]
    lsm = lf.plan(5 * 16)
    vsm = ctx.vc.plan(4)

    @block.sync
    def _(sy):
        sy.dma_start(out=su_ug[:], in_=su_ug_d[:]).then_inc(lf.h, 16)
        sy.dma_start(out=su_dp[:], in_=su_dp_d[:]).then_inc(lf.h, 16)
        sy.dma_start(out=wse_u[:], in_=ws_up_d[:]).then_inc(lf.h, 16)
        sy.dma_start(out=wse_g[:], in_=ws_gt_d[:]).then_inc(lf.h, 16)
        sy.dma_start(out=wse_d[:], in_=ws_dp_d[:]).then_inc(lf.h, 16)

    @block.vector
    def _(v):
        v.wait_ge(lf.h, 80)
        v.tensor_scalar(out=wse_u[:], in0=wse_u[:], scalar1=C_UG, scalar2=None,
                        op0=ALU.mult).then_inc(ctx.vc.h, 1)
        v.tensor_scalar(out=wse_g[:], in0=wse_g[:], scalar1=C_UG, scalar2=None,
                        op0=ALU.mult).then_inc(ctx.vc.h, 1)
        v.tensor_scalar(out=wse_d[:], in0=wse_d[:], scalar1=C_DP, scalar2=None,
                        op0=ALU.mult).then_inc(ctx.vc.h, 1)
        v.wait_ge(ctx.vc.h, vsm + 3)
        v.tensor_tensor(out=wse_u[:], in0=wse_u[:], in1=su_dp[:],
                        op=ALU.mult).then_inc(ctx.vc.h, 1)
        v.wait_ge(ctx.vc.h, vsm + 4)

    if "consts" in parts and len(parts) == 1:
        return
    for ch in range(NCH):
        t0 = ch * TCH

        lx = sxn.plan(16)
        tx = ctx.tc.plan(NTT * DT)
        vx = ctx.vc.plan(NTT * DT)

        @block.sync
        def _(sy):
            if ch > 0:
                sy.wait_ge(ctx.tc.h, tx)
            sy.dma_start(out=xnc[:].rearrange("p (t d) -> p t d", t=NTT),
                         in_=xn[t0:t0 + TCH, :].rearrange("(t p) d -> p t d", p=128)
                         ).then_inc(sxn.h, 16)

        if "xl" in parts:
            continue

        @block.tensor
        def _(te):
            te.wait_ge(sxn.h, lx + 16)
            te.wait_ge(ctx.vc.h, vx)
            n = 0
            for tt in range(NTT):
                for d_ in range(DT):
                    if n >= 1:
                        te.wait_ge(ctx.vc.h, vx + n)
                    te.transpose(psT[:, 0:128],
                                 xnc[:, tt * D + d_ * 128: tt * D + (d_ + 1) * 128],
                                 ident[:]).then_inc(ctx.tc.h, 1)
                    n += 1

        @block.vector
        def _(v):
            n = 0
            for tt in range(NTT):
                for d_ in range(DT):
                    n += 1
                    v.wait_ge(ctx.tc.h, tx + n)
                    v.tensor_copy(xnT[:, d_ * TCH + tt * 128: d_ * TCH + (tt + 1) * 128],
                                  psT[:, 0:128]).then_inc(ctx.vc.h, 1)

        v0 = ctx.vc.plan(2 * DT)
        t0s = ctx.tc.plan(DT)

        @block.vector
        def _(v):
            v.wait_ge(ctx.vc.h, vx + NTT * DT)
            for T in range(DT):
                if T >= 2:
                    v.wait_ge(ctx.tc.h, t0s + T - 1)
                v.tensor_scalar(out=tmp[:, (T % 2) * TCH:(T % 2) * TCH + TCH],
                                in0=xnT[:, T * TCH:(T + 1) * TCH],
                                scalar1=su_ug[:, T:T + 1], scalar2=None,
                                op0=ALU.mult).then_inc(ctx.vc.h, 1)
                if T > 0:
                    v.wait_ge(ctx.tc.h, t0s + T)
                    v.tensor_copy(y1[:, (T - 1) * TCH: T * TCH],
                                  psA[:, ((T - 1) % 2) * 512:((T - 1) % 2) * 512 + 512]
                                  ).then_inc(ctx.vc.h, 1)
            v.wait_ge(ctx.tc.h, t0s + DT)
            v.tensor_copy(y1[:, (DT - 1) * TCH: DT * TCH],
                          psA[:, ((DT - 1) % 2) * 512:((DT - 1) % 2) * 512 + 512]
                          ).then_inc(ctx.vc.h, 1)

        @block.tensor
        def _(te):
            for T in range(DT):
                need = v0 + 1 if T == 0 else v0 + 2 * T
                te.wait_ge(ctx.vc.h, need)
                te.matmul(psA[:, (T % 2) * 512:(T % 2) * 512 + 512], lhsT=h128s[:],
                          rhs=tmp[:, (T % 2) * TCH:(T % 2) * TCH + TCH], start=True,
                          stop=True).then_inc(ctx.tc.h, 1)

        if "bfly" in parts or "z" in parts:
            emit_butterflies(ctx, y1, btmp, DT, TCH)
        if "z" not in parts:
            continue

        NSLAB = 32
        y1r = ctx.vc.v
        for sl in range(NSLAB):
            swz[sl % 2].plan(32)
        tz = ctx.tc.plan(2 * IHI)
        sz = ctx.sc.plan(IHI)

        vseq = []
        for ih in range(IHI + 1):
            if ih < IHI:
                vseq.append(("evac", ih))
            i = ih - 1
            if i >= 0:
                vseq.append(("copy", i))
                hh = 1
                while hh <= IHI // 2:
                    if (i + 1) % (2 * hh) == 0:
                        vseq.append(("bf", hh, i + 1 - 2 * hh))
                    hh *= 2
        nv_ops = sum(2 if e[0] == "bf" else 1 for e in vseq)
        vz = ctx.vc.plan(nv_ops)
        vcnt_evac, vcnt_copy = {}, {}
        cnt = vz
        for e in vseq:
            if e[0] == "bf":
                cnt += 2
            else:
                cnt += 1
                (vcnt_evac if e[0] == "evac" else vcnt_copy)[e[1]] = cnt
        tcnt_Z = {ih: tz + (1 if ih == 0 else 2 * ih) for ih in range(IHI)}
        tcnt_A = {i: tz + (2 * i + 3 if i < IHI - 1 else 2 * IHI) for i in range(IHI)}
        scnt = {ih: sz + ih + 1 for ih in range(IHI)}

        @block.gpsimd
        def _(g):
            for sl in range(NSLAB):
                par = sl % 2
                if sl >= 2:
                    g.wait_ge(ctx.tc.h, tcnt_Z[2 * sl - 3])
                g.dma_start(out=wub[:, par * SLAB:(par + 1) * SLAB],
                            in_=wu[sl].rearrange("p a b q -> p (a b q)")
                            ).then_inc(swz[par].h, 16)
                g.dma_start(out=wgb[:, par * SLAB:(par + 1) * SLAB],
                            in_=wg[sl].rearrange("p a b q -> p (a b q)")
                            ).then_inc(swz[par].h, 16)

        @block.tensor
        def _(te):
            te.wait_ge(ctx.vc.h, y1r)
            for ih in range(IHI):
                sl, j = ih // 2, ih % 2
                par = sl % 2
                te.wait_ge(swz[par].h, (ch * (NSLAB // 2) + sl // 2 + 1) * 32)
                if ih >= 2:
                    te.wait_ge(ctx.sc.h, scnt[ih - 2])
                    te.wait_ge(ctx.vc.h, vcnt_evac[ih - 2])
                pu = psU[:, (ih % 2) * 512:(ih % 2) * 512 + 512]
                pg = psG[:, (ih % 2) * 512:(ih % 2) * 512 + 512]
                for T in range(DT):
                    off = par * SLAB + (j * DT + T) * 128
                    te.matmul(pu, lhsT=wub[:, off:off + 128], rhs=y1[:, T * TCH:(T + 1) * TCH],
                              start=(T == 0), stop=(T == DT - 1))
                for T in range(DT):
                    off = par * SLAB + (j * DT + T) * 128
                    mm = te.matmul(pg, lhsT=wgb[:, off:off + 128], rhs=y1[:, T * TCH:(T + 1) * TCH],
                                   start=(T == 0), stop=(T == DT - 1))
                mm.then_inc(ctx.tc.h, 1)
                if ih >= 1:
                    i = ih - 1
                    te.wait_ge(ctx.vc.h, vcnt_evac[i])
                    if i >= 2:
                        te.wait_ge(ctx.vc.h, vcnt_copy[i - 2])
                    te.matmul(psA[:, (i % 2) * 512:(i % 2) * 512 + 512], lhsT=h128s[:],
                              rhs=z[:, i * TCH:(i + 1) * TCH], start=True,
                              stop=True).then_inc(ctx.tc.h, 1)
            i = IHI - 1
            te.wait_ge(ctx.vc.h, vcnt_evac[i])
            te.wait_ge(ctx.vc.h, vcnt_copy[i - 2])
            te.matmul(psA[:, (i % 2) * 512:(i % 2) * 512 + 512], lhsT=h128s[:],
                      rhs=z[:, i * TCH:(i + 1) * TCH], start=True,
                      stop=True).then_inc(ctx.tc.h, 1)

        @block.scalar
        def _(s):
            for ih in range(IHI):
                if ih >= 2:
                    s.wait_ge(ctx.vc.h, vcnt_evac[ih - 2])
                s.wait_ge(ctx.tc.h, tcnt_Z[ih])
                pg = psG[:, (ih % 2) * 512:(ih % 2) * 512 + 512]
                s.activation(out=sil[:, (ih % 2) * TCH:(ih % 2) * TCH + TCH], in_=pg,
                             func=AF.Silu, bias=0.0,
                             scale=wse_g[:, ih:ih + 1]).then_inc(ctx.sc.h, 1)

        @block.vector
        def _(v):
            cnt2 = vz
            for e in vseq:
                if e[0] == "evac":
                    ih = e[1]
                    v.wait_ge(ctx.sc.h, scnt[ih])
                    pu = psU[:, (ih % 2) * 512:(ih % 2) * 512 + 512]
                    v.scalar_tensor_tensor(out=z[:, ih * TCH:(ih + 1) * TCH], in0=pu,
                                           scalar=wse_u[:, ih:ih + 1],
                                           in1=sil[:, (ih % 2) * TCH:(ih % 2) * TCH + TCH],
                                           op0=ALU.mult, op1=ALU.mult).then_inc(ctx.vc.h, 1)
                    cnt2 += 1
                elif e[0] == "copy":
                    i = e[1]
                    v.wait_ge(ctx.tc.h, tcnt_A[i])
                    v.tensor_copy(z[:, i * TCH:(i + 1) * TCH],
                                  psA[:, (i % 2) * 512:(i % 2) * 512 + 512]
                                  ).then_inc(ctx.vc.h, 1)
                    cnt2 += 1
                else:
                    _, hh, s0 = e
                    wsp = hh * TCH
                    av = z[:, s0 * TCH: s0 * TCH + wsp]
                    bv = z[:, s0 * TCH + wsp: s0 * TCH + 2 * wsp]
                    v.wait_ge(ctx.vc.h, cnt2)
                    v.tensor_tensor(out=bv, in0=av, in1=bv,
                                    op=ALU.subtract).then_inc(ctx.vc.h, 1)
                    cnt2 += 1
                    v.wait_ge(ctx.vc.h, cnt2)
                    v.scalar_tensor_tensor(out=av, in0=av, scalar=2.0, in1=bv,
                                           op0=ALU.mult, op1=ALU.subtract
                                           ).then_inc(ctx.vc.h, 1)
                    cnt2 += 1

        if "down" not in parts:
            continue
        zr = ctx.vc.v
        for n in range(DT * 2):
            swd[n % 2].plan(16)
        td = ctx.tc.plan(DT * 2)
        vd = ctx.vc.plan(DT)

        @block.gpsimd
        def _(g):
            for n in range(DT * 2):
                dt_, hf = n // 2, n % 2
                par = n % 2
                if n >= 2:
                    g.wait_ge(ctx.tc.h, td + n - 1)
                g.dma_start(out=wdb[:, par * HALF:(par + 1) * HALF],
                            in_=wd[dt_, :, hf * 32:(hf + 1) * 32, :].rearrange("p a q -> p (a q)")
                            ).then_inc(swd[par].h, 16)

        @block.tensor
        def _(te):
            te.wait_ge(ctx.vc.h, zr)
            for n in range(DT * 2):
                dt_, hf = n // 2, n % 2
                par = n % 2
                te.wait_ge(swd[par].h, (ch * DT + n // 2 + 1) * 16)
                if hf == 0 and dt_ >= 1:
                    te.wait_ge(ctx.vc.h, vd + dt_)
                pd = psD[:, 0:512]
                for a in range(32):
                    ih = hf * 32 + a
                    mm = te.matmul(pd, lhsT=wdb[:, par * HALF + a * 128: par * HALF + (a + 1) * 128],
                                   rhs=z[:, ih * TCH:(ih + 1) * TCH],
                                   start=(ih == 0), stop=(ih == IHI - 1))
                mm.then_inc(ctx.tc.h, 1)

        @block.vector
        def _(v):
            for dt_ in range(DT):
                v.wait_ge(ctx.tc.h, td + 2 * dt_ + 2)
                pd = psD[:, 0:512]
                v.tensor_scalar(out=foutT[:, dt_ * TCH:(dt_ + 1) * TCH], in0=pd,
                                scalar1=wse_d[:, dt_:dt_ + 1], scalar2=None,
                                op0=ALU.mult).then_inc(ctx.vc.h, 1)

        for tt in range(NTT):
            srl[tt % 2].plan(16)
            srs[tt % 2].plan(16)
        ta = ctx.tc.plan(NTT * DT)
        va = ctx.vc.plan(NTT * DT)

        @block.sync
        def _(sy):
            for tt in range(NTT):
                par = tt % 2
                gl = ch * (NTT // 2) + tt // 2
                if gl > 0:
                    sy.wait_ge(srs[par].h, gl * 16)
                sy.dma_start(out=win[:, par * D:(par + 1) * D],
                             in_=xin[t0 + tt * 128: t0 + (tt + 1) * 128, :]
                             ).then_inc(srl[par].h, 16)
                sy.wait_ge(ctx.vc.h, va + (tt + 1) * DT)
                sy.dma_start(out=xout[t0 + tt * 128: t0 + (tt + 1) * 128, :],
                             in_=win[:, par * D:(par + 1) * D]).then_inc(srs[par].h, 16)

        @block.tensor
        def _(te):
            n = 0
            for tt in range(NTT):
                for dt_ in range(DT):
                    te.wait_ge(ctx.vc.h, vd + dt_ + 1)
                    if n >= 1:
                        te.wait_ge(ctx.vc.h, va + n)
                    te.transpose(psT[:, 0:128],
                                 foutT[:, dt_ * TCH + tt * 128: dt_ * TCH + (tt + 1) * 128],
                                 ident[:]).then_inc(ctx.tc.h, 1)
                    n += 1

        @block.vector
        def _(v):
            n = 0
            for tt in range(NTT):
                par = tt % 2
                v.wait_ge(srl[par].h, (ch * (NTT // 2) + tt // 2 + 1) * 16)
                for dt_ in range(DT):
                    n += 1
                    v.wait_ge(ctx.tc.h, ta + n)
                    dst = win[:, par * D + dt_ * 128: par * D + (dt_ + 1) * 128]
                    v.tensor_tensor(out=dst, in0=dst, in1=psT[:, 0:128],
                                    op=ALU.add).then_inc(ctx.vc.h, 1)


def emit_butterflies(ctx, buf, btmp, nblk, tch):
    block = ctx.block
    h = 1
    stages = []
    while h < nblk:
        stages.append(h)
        h *= 2
    vb = ctx.vc.plan(2 * len(stages))

    @block.vector
    def _(v):
        n = vb
        for h in stages:
            G = nblk // (2 * h)
            w = h * tch
            view = buf[:, 0:nblk * tch].rearrange("p (g two w) -> p g two w", g=G, two=2)
            av = view[:, :, 0, :]
            bv = view[:, :, 1, :]
            v.wait_ge(ctx.vc.h, n)
            v.tensor_tensor(out=bv, in0=av, in1=bv, op=ALU.subtract).then_inc(ctx.vc.h, 1)
            n += 1
            v.wait_ge(ctx.vc.h, n)
            v.scalar_tensor_tensor(out=av, in0=av, scalar=2.0, in1=bv,
                                   op0=ALU.mult, op1=ALU.subtract).then_inc(ctx.vc.h, 1)
            n += 1


def build_L1(stages=("ln1", "ffn", "ln2")):
    nc = bass.Bass("TRN2", target_bir_lowering=False, debug=False)
    x = nc.dram_tensor("x", [TOK, D], F32, kind="ExternalInput").ap()
    wu = nc.dram_tensor("wu", [32, 128, 2, DT, 128], F32, kind="ExternalInput").ap()
    wg = nc.dram_tensor("wg", [32, 128, 2, DT, 128], F32, kind="ExternalInput").ap()
    wd = nc.dram_tensor("wd", [DT, 128, IHI, 128], F32, kind="ExternalInput").ap()
    su_ug_d = nc.dram_tensor("su_ug", [128, 16], F32, kind="ExternalInput").ap()
    su_dp_d = nc.dram_tensor("su_dp", [128, 64], F32, kind="ExternalInput").ap()
    ws_up_d = nc.dram_tensor("ws_up", [128, 64], F32, kind="ExternalInput").ap()
    ws_gt_d = nc.dram_tensor("ws_gt", [128, 64], F32, kind="ExternalInput").ap()
    ws_dp_d = nc.dram_tensor("ws_dp", [128, 16], F32, kind="ExternalInput").ap()
    h128_d = nc.dram_tensor("h128s", [128, 128], F32, kind="ExternalInput").ap()
    id_d = nc.dram_tensor("ident", [128, 128], F32, kind="ExternalInput").ap()
    cv_d = nc.dram_tensor("consts", [128, 4], F32, kind="ExternalInput").ap()
    x1_d = nc.dram_tensor("x1", [TOK, D], F32, kind="ExternalOutput").ap()
    xn2_d = nc.dram_tensor("xn2", [TOK, D], BF16, kind="ExternalOutput").ap()
    xn_i = nc.dram_tensor("xn_i", [TOK, D], BF16, kind="Internal").ap()

    with ExitStack() as ES:
        ctx = Ctx(nc, ES)
        block = ctx.block
        h128s = ctx.sb(ES, "h128", (128, 128), BF16)
        ident = ctx.sb(ES, "id", (128, 128), BF16)
        cvec = ctx.sb(ES, "cv", (128, 4))
        lc = ctx.lg.plan(3 * 16)

        @block.gpsimd
        def _(g):
            g.dma_start(out=h128s[:], in_=h128_d[:]).then_inc(ctx.lg.h, 16)
            g.dma_start(out=ident[:], in_=id_d[:]).then_inc(ctx.lg.h, 16)
            g.dma_start(out=cvec[:], in_=cv_d[:]).then_inc(ctx.lg.h, 16)

        ctx.join()

        if "ln1" in stages:
            with ExitStack() as S1:
                emit_ln_stream(ctx, S1, x, xn_i, NT, cvec)
                ctx.join()
        if "ffn" in stages:
            fp = tuple(s[1:] for s in stages if s.startswith("_"))
            parts = fp if fp else ("y1", "z", "fwht", "down")
            with ExitStack() as S2:
                emit_ffn0(ctx, S2, x, x1_d, xn_i, NT, wu, wg, wd, su_ug_d, su_dp_d,
                          ws_up_d, ws_gt_d, ws_dp_d, h128s, ident, parts=parts)
                ctx.join()
        if "ln2" in stages:
            with ExitStack() as S3:
                emit_ln_stream(ctx, S3, x1_d, xn2_d, NT, cvec)
                ctx.join()

        ctx.join()

    return nc


SEQ = 4096
NTS = SEQ // 128


def build_L2():
    nc = bass.Bass("TRN2", target_bir_lowering=False, debug=False)
    xq_d = nc.dram_tensor("xq", [NTS, DT, 128, 128], BF16, kind="ExternalInput").ap()
    qkvw_d = nc.dram_tensor("qkvw", [4, DT, 128, 384], F32, kind="ExternalInput").ap()
    slope_d = nc.dram_tensor("slope", [1, 4], F32, kind="ExternalInput").ap()
    ones_d = nc.dram_tensor("ones1", [1, 128], F32, kind="ExternalInput").ap()
    arrq_d = nc.dram_tensor("arrq", [128, 2], F32, kind="ExternalInput").ap()
    arrk_d = nc.dram_tensor("arrk", [128, 2], F32, kind="ExternalInput").ap()
    idxT_d = nc.dram_tensor("idxT", [2, 128, 256], F32, kind="ExternalInput").ap()
    mskT_d = nc.dram_tensor("mskT", [2, 128, 256], F32, kind="ExternalInput").ap()
    id_d = nc.dram_tensor("ident", [128, 128], F32, kind="ExternalInput").ap()
    out_d = nc.dram_tensor("outT", [512, SEQ], F32, kind="ExternalOutput").ap()

    NBLK = 16

    with ExitStack() as ES:
        ctx = Ctx(nc, ES)
        block = ctx.block
        ident = ctx.sb(ES, "id", (128, 128), BF16)
        slope = ctx.sb(ES, "slp", (1, 4))
        ones1 = ctx.sb(ES, "on1", (1, 128))
        arrq = ctx.sb(ES, "arq", (128, 2))
        arrk = ctx.sb(ES, "ark", (128, 2))
        idxT = ctx.sb(ES, "idx", (128, 2 * 256))
        mskT = ctx.sb(ES, "msk", (128, 2 * 256))
        negs = ctx.sb(ES, "ngs", (128, 4))
        xqb = ctx.sb(ES, "xqb", (128, 2 * DT * 128), BF16)
        qwb = ctx.sb(ES, "qwb", (128, 2 * DT * 384), BF16)
        qkvt = ctx.sb(ES, "qkvt", (128, NTS * 384), BF16)
        qsc = ctx.sb(ES, "qsc", (128, 256), BF16)
        ksc = ctx.sb(ES, "ksc", (128, 256), BF16)
        qT = ctx.sb(ES, "qT", (128, 256), BF16)
        qsT = ctx.sb(ES, "qsT", (128, 256), BF16)
        kT = ctx.sb(ES, "kT", (128, 256), BF16)
        ssc = ctx.sb(ES, "ssc", (128, 2 * 256), BF16)
        decT = ctx.sb(ES, "decT", (128, 2 * 256))
        qdec = ctx.sb(ES, "qdec", (128, 2))
        kdec = ctx.sb(ES, "kdec", (128, 2))
        bdcol = ctx.sb(ES, "bdc", (128, 1))
        kv = ctx.sb(ES, "kv", (128, 128))
        kvbf = ctx.sb(ES, "kvbf", (128, 128), BF16)
        outh = ctx.sb(ES, "outh", (128, SEQ))
        psQ = ctx.ps(ES, "psQ", (128, 2 * 512), F32)
        psTt = ctx.ps(ES, "psTt", (128, 2 * 1024), BF16)
        psS = ctx.ps(ES, "psS", (128, 2 * 512), F32)
        psO = ctx.ps(ES, "psO", (128, 512), F32)
        psKV = ctx.ps(ES, "psKV", (128, 512), F32)

        sxq = [ctx.new_sem("xqA"), ctx.new_sem("xqB")]
        sqw = [ctx.new_sem("qwA"), ctx.new_sem("qwB")]
        sout = ctx.new_sem("out")
        lcs = ctx.new_sem("lcs")
        lc0 = lcs.plan(6 * 16)
        lc = ctx.lg.plan(16)

        @block.sync
        def _(sy):
            sy.dma_start(out=slope[:], in_=slope_d[:]).then_inc(lcs.h, 16)
            sy.dma_start(out=ones1[:], in_=ones_d[:]).then_inc(lcs.h, 16)
            sy.dma_start(out=arrq[:], in_=arrq_d[:]).then_inc(lcs.h, 16)
            sy.dma_start(out=arrk[:], in_=arrk_d[:]).then_inc(lcs.h, 16)
            sy.dma_start(out=idxT[:].rearrange("p (u m) -> p u m", u=2),
                         in_=idxT_d.rearrange("u p m -> p u m")).then_inc(lcs.h, 16)
            sy.dma_start(out=mskT[:].rearrange("p (u m) -> p u m", u=2),
                         in_=mskT_d.rearrange("u p m -> p u m")).then_inc(lcs.h, 16)

        @block.gpsimd
        def _(g):
            g.dma_start(out=ident[:], in_=id_d[:]).then_inc(ctx.lg.h, 16)

        ctx.join()

        tb = ctx.tc.plan(1)
        vbb = ctx.vc.plan(1)

        @block.tensor
        def _(te):
            te.matmul(psO[:, 0:4], lhsT=ones1[:], rhs=slope[:], start=True,
                      stop=True).then_inc(ctx.tc.h, 1)

        @block.vector
        def _(v):
            v.wait_ge(ctx.tc.h, tb + 1)
            v.tensor_scalar(out=negs[:], in0=psO[:, 0:4], scalar1=-1.0, scalar2=None,
                            op0=ALU.mult).then_inc(ctx.vc.h, 1)

        ctx.join()

        hq_base = []
        for h in range(4):
            head_tc0 = ctx.tc.v
            head_vc0 = ctx.vc.v
            head_sc0 = ctx.sc.v

            c0 = ctx.sc.plan(4)
            v0 = ctx.vc.plan(2)

            @block.scalar
            def _(s, h=h, head_vc0=head_vc0):
                s.wait_ge(ctx.vc.h, head_vc0)
                s.activation(out=qdec[:], in_=arrq[:], func=AF.Exp, bias=0.0,
                             scale=negs[:, h:h + 1]).then_inc(ctx.sc.h, 1)
                s.activation(out=kdec[:], in_=arrk[:], func=AF.Exp, bias=0.0,
                             scale=negs[:, h:h + 1]).then_inc(ctx.sc.h, 1)
                s.activation(out=bdcol[:], in_=negs[:, h:h + 1], func=AF.Exp, bias=0.0,
                             scale=256.0).then_inc(ctx.sc.h, 1)
                s.activation(out=decT[:], in_=idxT[:], func=AF.Exp, bias=0.0,
                             scale=negs[:, h:h + 1]).then_inc(ctx.sc.h, 1)

            @block.vector
            def _(v, c0=c0, v0=v0):
                v.wait_ge(ctx.sc.h, c0 + 4)
                v.tensor_tensor(out=decT[:], in0=decT[:], in1=mskT[:],
                                op=ALU.mult).then_inc(ctx.vc.h, 1)
                v.wait_ge(ctx.vc.h, v0 + 1)
                v.memset(kv[:], 0.0)
                v.nop().then_inc(ctx.vc.h, 1)

            hw = sqw[h % 2].plan(16)

            @block.gpsimd
            def _(g, h=h):
                par = h % 2
                if h >= 2:
                    g.wait_ge(ctx.tc.h, hq_base[h - 2] + NTS)
                g.dma_start(out=qwb[:, par * DT * 384:(par + 1) * DT * 384].rearrange(
                                "p (a q) -> p a q", a=DT),
                            in_=qkvw_d[h].rearrange("a p q -> p a q")
                            ).then_inc(sqw[par].h, 16)

            tq = ctx.tc.plan(NTS)
            hq_base.append(tq)
            cq = ctx.sc.plan(NTS)

            @block.sync
            def _(sy, h=h):
                for tt in range(NTS):
                    par = tt % 2
                    seq_no = h * (NTS // 2) + tt // 2
                    if seq_no > 0:
                        if tt >= 2:
                            need = hq_base[h] + tt - 1
                        else:
                            need = hq_base[h - 1] + NTS - 2 + par + 1
                        sy.wait_ge(ctx.tc.h, need)
                    sy.dma_start(out=xqb[:, par * DT * 128:(par + 1) * DT * 128].rearrange(
                                     "p (a q) -> p a q", a=DT),
                                 in_=xq_d[tt].rearrange("a p q -> p a q")
                                 ).then_inc(sxq[par].h, 16)
            for tt in range(NTS):
                sxq[tt % 2].plan(16)

            @block.tensor
            def _(te, h=h, tq=tq, cq=cq):
                par_h = h % 2
                te.wait_ge(sqw[h % 2].h, (h // 2 + 1) * 16)
                for tt in range(NTS):
                    par = tt % 2
                    te.wait_ge(sxq[par].h, (h * (NTS // 2) + tt // 2 + 1) * 16)
                    if tt >= 2:
                        te.wait_ge(ctx.sc.h, cq + tt - 1)
                    pq = psQ[:, (tt % 2) * 512:(tt % 2) * 512 + 384]
                    for T in range(DT):
                        mm = te.matmul(pq, lhsT=xqb[:, par * DT * 128 + T * 128: par * DT * 128 + (T + 1) * 128],
                                       rhs=qwb[:, par_h * DT * 384 + T * 384: par_h * DT * 384 + (T + 1) * 384],
                                       start=(T == 0), stop=(T == DT - 1))
                    mm.then_inc(ctx.tc.h, 1)

            @block.scalar
            def _(s, h=h, tq=tq, head_tc0=head_tc0):
                if h > 0:
                    s.wait_ge(ctx.tc.h, head_tc0)
                for tt in range(NTS):
                    s.wait_ge(ctx.tc.h, tq + tt + 1)
                    pq = psQ[:, (tt % 2) * 512:(tt % 2) * 512 + 384]
                    s.activation(out=qkvt[:, tt * 384:(tt + 1) * 384], in_=pq,
                                 func=AF.Silu, bias=0.0, scale=1.0).then_inc(ctx.sc.h, 1)

            for j in range(NBLK):
                tj = ctx.tc.plan(10)
                vj = ctx.vc.plan(14)

                @block.vector
                def _(v, h=h, j=j, tj=tj, vj=vj, cq=cq):
                    v.wait_ge(ctx.tc.h, tj)
                    v.wait_ge(ctx.sc.h, cq + 2 * j + 2)
                    for u in range(2):
                        col = (2 * j + u) * 384
                        v.tensor_scalar(out=qsc[:, u * 128:(u + 1) * 128],
                                        in0=qkvt[:, col:col + 128],
                                        scalar1=qdec[:, u:u + 1], scalar2=None,
                                        op0=ALU.mult).then_inc(ctx.vc.h, 1)
                        v.tensor_scalar(out=ksc[:, u * 128:(u + 1) * 128],
                                        in0=qkvt[:, col + 128:col + 256],
                                        scalar1=kdec[:, u:u + 1], scalar2=None,
                                        op0=ALU.mult).then_inc(ctx.vc.h, 1)
                    for i6 in range(6):
                        v.wait_ge(ctx.tc.h, tj + i6 + 1)
                        dst = [qT[:, 0:128], qT[:, 128:256], qsT[:, 0:128],
                               qsT[:, 128:256], kT[:, 0:128], kT[:, 128:256]][i6]
                        v.tensor_copy(dst, psTt[:, (i6 % 2) * 1024:(i6 % 2) * 1024 + 128]
                                      ).then_inc(ctx.vc.h, 1)
                    v.wait_ge(ctx.tc.h, tj + 8)
                    v.tensor_tensor(out=ssc[:, 0:256], in0=psS[:, 0:256],
                                    in1=decT[:, 0:256], op=ALU.mult)
                    v.tensor_tensor(out=ssc[:, 256:512], in0=psS[:, 512:768],
                                    in1=decT[:, 256:512], op=ALU.mult).then_inc(ctx.vc.h, 1)
                    v.wait_ge(ctx.tc.h, tj + 10)
                    v.scalar_tensor_tensor(out=kv[:], in0=kv[:], scalar=bdcol[:],
                                           in1=psKV[:, 0:128], op0=ALU.mult,
                                           op1=ALU.add).then_inc(ctx.vc.h, 1)
                    v.wait_ge(ctx.vc.h, vj + 12)
                    v.tensor_copy(kvbf[:], kv[:]).then_inc(ctx.vc.h, 1)
                    v.wait_ge(ctx.tc.h, tj + 9)
                    if j == 0:
                        v.wait_ge(sout.h, h * 16)
                    v.tensor_copy(outh[:, j * 256:(j + 1) * 256],
                                  psO[:, 0:256]).then_inc(ctx.vc.h, 1)

                @block.tensor
                def _(te, h=h, j=j, tj=tj, vj=vj, cq=cq):
                    te.wait_ge(ctx.sc.h, cq + 2 * j + 2)
                    srcs = [qkvt[:, (2 * j) * 384:(2 * j) * 384 + 128],
                            qkvt[:, (2 * j + 1) * 384:(2 * j + 1) * 384 + 128],
                            qsc[:, 0:128], qsc[:, 128:256],
                            qkvt[:, (2 * j) * 384 + 128:(2 * j) * 384 + 256],
                            qkvt[:, (2 * j + 1) * 384 + 128:(2 * j + 1) * 384 + 256]]
                    for i6, src in enumerate(srcs):
                        if i6 in (2, 3):
                            te.wait_ge(ctx.vc.h, vj + 3)
                        if i6 >= 2:
                            te.wait_ge(ctx.vc.h, vj + 4 + i6 - 1)
                        te.transpose(psTt[:, (i6 % 2) * 1024:(i6 % 2) * 1024 + 128],
                                     src, ident[:]).then_inc(ctx.tc.h, 1)
                    te.wait_ge(ctx.vc.h, vj + 10)
                    for u in range(2):
                        te.matmul(psS[:, u * 512:u * 512 + 256],
                                  lhsT=kT[:, u * 128:(u + 1) * 128],
                                  rhs=qT[:], start=True, stop=True).then_inc(ctx.tc.h, 1)
                    te.wait_ge(ctx.vc.h, vj + 11)
                    first = True
                    if j > 0:
                        te.matmul(psO[:, 0:256], lhsT=kvbf[:], rhs=qsT[:], start=True,
                                  stop=False)
                        first = False
                    for u in range(2):
                        mm = te.matmul(psO[:, 0:256],
                                       lhsT=qkvt[:, (2 * j + u) * 384 + 256:(2 * j + u) * 384 + 384],
                                       rhs=ssc[:, u * 256:(u + 1) * 256], start=first,
                                       stop=(u == 1))
                        first = False
                    mm.then_inc(ctx.tc.h, 1)
                    for u in range(2):
                        mm = te.matmul(psKV[:, 0:128], lhsT=ksc[:, u * 128:(u + 1) * 128],
                                       rhs=qkvt[:, (2 * j + u) * 384 + 256:(2 * j + u) * 384 + 384],
                                       start=(u == 0), stop=(u == 1))
                    mm.then_inc(ctx.tc.h, 1)

            so = sout.plan(16)
            hv_end = ctx.vc.v

            @block.sync
            def _(sy, h=h, hv_end=hv_end):
                sy.wait_ge(ctx.vc.h, hv_end)
                sy.dma_start(out=out_d[h * 128:(h + 1) * 128, :],
                             in_=outh[:]).then_inc(sout.h, 16)

        ctx.join()

    return nc




def emit_attn_finish(ctx, S, x1, outT_d, xn2v_d, wgt_d, wot_d, nw_d, x2, cvec, ident):
    block = ctx.block
    xn2v = ctx.sb(S, "xn2v", (128, DT * TOK), BF16)
    goT = ctx.sb(S, "goT", (128, DT * TOK), BF16)
    aoutT = ctx.sb(S, "aoutT", (128, DT * TOK), BF16)
    owin = ctx.sb(S, "owin", (128, 2 * TOK))
    sq = ctx.sb(S, "sqa", (128, 2 * TOK))
    tmp = ctx.sb(S, "tmpa", (128, TOK))
    rrow = ctx.sb(S, "rrow", (1, TOK))
    nw = ctx.sb(S, "nw", (128, 16))
    wgb = ctx.sb(S, "wgb", (128, 2 * DT * 128), BF16)
    wob = ctx.sb(S, "wob", (128, 2 * DT * 128), BF16)
    win = ctx.sb(S, "awin", (128, 2 * D))
    ones1 = ctx.sb(S, "aon1", (1, 128))
    onec = ctx.sb(S, "aonc", (128, 1))
    psW = ctx.ps(S, "psW", (128, 2 * 512), F32)
    psR = ctx.ps(S, "psR", (1, 2 * 512), F32)
    psB = ctx.ps(S, "psB", (128, 2 * 512), F32)
    psT = ctx.ps(S, "psT2", (128, 2 * 1024), BF16)

    sxv = ctx.new_sem("axv")
    swg = [ctx.new_sem("awg"), ctx.new_sem("awg")]
    swo = [ctx.new_sem("awo"), ctx.new_sem("awo")]
    sow = [ctx.new_sem("aow"), ctx.new_sem("aow")]
    srl = [ctx.new_sem("arl"), ctx.new_sem("arl")]
    srs = [ctx.new_sem("ars"), ctx.new_sem("ars")]

    lv = sxv.plan(2 * 16)

    @block.sync
    def _(sy):
        sy.dma_start(out=xn2v[:].rearrange("p (a q) -> p a q", a=DT),
                     in_=xn2v_d[:].rearrange("a p q -> p a q")).then_inc(sxv.h, 16)
        sy.dma_start(out=nw[:], in_=nw_d[:]).then_inc(sxv.h, 16)

    v_on = ctx.vc.plan(2)

    @block.vector
    def _(v):
        v.memset(ones1[:], 1.0).then_inc(ctx.vc.h, 1)
        v.memset(onec[:], 1.0).then_inc(ctx.vc.h, 1)

    tg = ctx.tc.plan(DT * 2)
    cg = ctx.sc.plan(DT * 2)

    @block.gpsimd
    def _(g):
        for F in range(DT):
            par = F % 2
            if F >= 2:
                g.wait_ge(ctx.tc.h, tg + 2 * (F - 1))
            g.dma_start(out=wgb[:, par * DT * 128:(par + 1) * DT * 128].rearrange(
                            "p (a q) -> p a q", a=DT),
                        in_=wgt_d[F].rearrange("a p q -> p a q")).then_inc(swg[par].h, 16)
    for F in range(DT):
        swg[F % 2].plan(16)

    @block.tensor
    def _(te):
        te.wait_ge(sxv.h, 32)
        for F in range(DT):
            par = F % 2
            te.wait_ge(swg[par].h, (F // 2 + 1) * 16)
            for half in range(2):
                n2 = 2 * F + half
                if n2 >= 2:
                    te.wait_ge(ctx.sc.h, cg + n2 - 1)
                pw = psW[:, (n2 % 2) * 512:(n2 % 2) * 512 + 512]
                for T in range(DT):
                    mm = te.matmul(pw, lhsT=wgb[:, par * DT * 128 + T * 128: par * DT * 128 + (T + 1) * 128],
                                   rhs=xn2v[:, T * TOK + half * 512: T * TOK + half * 512 + 512],
                                   start=(T == 0), stop=(T == DT - 1))
                mm.then_inc(ctx.tc.h, 1)

    @block.scalar
    def _(s):
        for F in range(DT):
            for half in range(2):
                n2 = 2 * F + half
                s.wait_ge(ctx.tc.h, tg + n2 + 1)
                pw = psW[:, (n2 % 2) * 512:(n2 % 2) * 512 + 512]
                s.activation(out=goT[:, F * TOK + half * 512: F * TOK + half * 512 + 512],
                             in_=pw, func=AF.Sigmoid, bias=0.0,
                             scale=1.0).then_inc(ctx.sc.h, 1)

    t_r = ctx.tc.plan(DT * 2)
    c_r = ctx.sc.plan(DT)
    v_r = ctx.vc.plan(0)

    @block.sync
    def _(sy):
        for T in range(DT):
            par = T % 2
            if T >= 2:
                sy.wait_ge(ctx.sc.h, c_r + T - 1)
            sy.dma_start(out=owin[:, par * TOK:(par + 1) * TOK],
                         in_=outT_d[T * 128:(T + 1) * 128, :]).then_inc(sow[par].h, 16)
    for T in range(DT):
        sow[T % 2].plan(16)

    @block.scalar
    def _(s):
        for T in range(DT):
            par = T % 2
            s.wait_ge(sow[par].h, (T // 2 + 1) * 16)
            if T >= 2:
                s.wait_ge(ctx.tc.h, t_r + 2 * (T - 1))
            s.activation(out=sq[:, par * TOK:(par + 1) * TOK],
                         in_=owin[:, par * TOK:(par + 1) * TOK],
                         func=AF.Square, bias=0.0, scale=1.0).then_inc(ctx.sc.h, 1)

    @block.tensor
    def _(te):
        te.wait_ge(ctx.vc.h, v_on + 2)
        for T in range(DT):
            par = T % 2
            te.wait_ge(ctx.sc.h, c_r + T + 1)
            for half in range(2):
                te.matmul(psR[0:1, half * 512:half * 512 + 512], lhsT=onec[:],
                          rhs=sq[:, par * TOK + half * 512: par * TOK + half * 512 + 512],
                          start=(T == 0), stop=(T == DT - 1)).then_inc(ctx.tc.h, 1)

    c_s = ctx.sc.plan(2)
    t_b = ctx.tc.plan(2)

    @block.scalar
    def _(s):
        s.wait_ge(ctx.tc.h, t_r + DT * 2)
        s.activation(out=rrow[:], in_=psR[0:1, :], func=AF.Ln, bias=cvec[0:1, 2:3],
                     scale=1.0 / D).then_inc(ctx.sc.h, 1)
        s.wait_ge(ctx.sc.h, c_s + 1)
        s.activation(out=rrow[:], in_=rrow[:], func=AF.Exp, bias=0.0,
                     scale=-0.5).then_inc(ctx.sc.h, 1)

    @block.tensor
    def _(te):
        te.wait_ge(ctx.sc.h, c_s + 2)
        for half in range(2):
            te.matmul(psB[:, half * 512:half * 512 + 512], lhsT=ones1[:],
                      rhs=rrow[0:1, half * 512:half * 512 + 512], start=True,
                      stop=True).then_inc(ctx.tc.h, 1)

    v_g = ctx.vc.plan(DT * 2)
    c_g2 = ctx.sc.plan(0)

    @block.sync
    def _(sy):
        sy.wait_ge(ctx.sc.h, c_r + DT)
        for T in range(DT):
            par = T % 2
            if T >= 2:
                sy.wait_ge(ctx.vc.h, v_g + 2 * (T - 1))
            sy.dma_start(out=owin[:, par * TOK:(par + 1) * TOK],
                         in_=outT_d[T * 128:(T + 1) * 128, :]).then_inc(sow[par].h, 16)
    for T in range(DT):
        sow[T % 2].plan(16)

    @block.vector
    def _(v):
        v.wait_ge(ctx.tc.h, t_b + 2)
        for T in range(DT):
            par = T % 2
            v.wait_ge(sow[par].h, (DT // 2 + T // 2 + 1) * 16)
            v.wait_ge(ctx.sc.h, cg + 2 * T + 2)
            if T >= 1:
                v.wait_ge(ctx.vc.h, v_g + 2 * T)
            v.scalar_tensor_tensor(out=tmp[:], in0=owin[:, par * TOK:(par + 1) * TOK],
                                   scalar=nw[:, T:T + 1], in1=psB[:],
                                   op0=ALU.mult, op1=ALU.mult).then_inc(ctx.vc.h, 1)
            v.wait_ge(ctx.vc.h, v_g + 2 * T + 1)
            v.tensor_tensor(out=goT[:, T * TOK:(T + 1) * TOK], in0=tmp[:],
                            in1=goT[:, T * TOK:(T + 1) * TOK],
                            op=ALU.mult).then_inc(ctx.vc.h, 1)

    t_w = ctx.tc.plan(DT * 2)
    v_w = ctx.vc.plan(DT * 2)

    @block.gpsimd
    def _(g):
        for dto in range(DT):
            par = dto % 2
            if dto >= 2:
                g.wait_ge(ctx.tc.h, t_w + 2 * (dto - 1))
            g.dma_start(out=wob[:, par * DT * 128:(par + 1) * DT * 128].rearrange(
                            "p (a q) -> p a q", a=DT),
                        in_=wot_d[dto]).then_inc(swo[par].h, 16)
    for dto in range(DT):
        swo[dto % 2].plan(16)

    @block.tensor
    def _(te):
        te.wait_ge(ctx.vc.h, v_w)
        for dto in range(DT):
            par = dto % 2
            te.wait_ge(swo[par].h, (dto // 2 + 1) * 16)
            for half in range(2):
                n2 = 2 * dto + half
                if n2 >= 2:
                    te.wait_ge(ctx.vc.h, v_w + n2 - 1)
                pw = psW[:, (n2 % 2) * 512:(n2 % 2) * 512 + 512]
                for T in range(DT):
                    mm = te.matmul(pw, lhsT=wob[:, par * DT * 128 + T * 128: par * DT * 128 + (T + 1) * 128],
                                   rhs=goT[:, T * TOK + half * 512: T * TOK + half * 512 + 512],
                                   start=(T == 0), stop=(T == DT - 1))
                mm.then_inc(ctx.tc.h, 1)

    @block.vector
    def _(v):
        for dto in range(DT):
            for half in range(2):
                n2 = 2 * dto + half
                v.wait_ge(ctx.tc.h, t_w + n2 + 1)
                pw = psW[:, (n2 % 2) * 512:(n2 % 2) * 512 + 512]
                v.tensor_copy(aoutT[:, dto * TOK + half * 512: dto * TOK + half * 512 + 512],
                              pw).then_inc(ctx.vc.h, 1)

    t_a = ctx.tc.plan(NT * DT)
    v_a = ctx.vc.plan(NT * DT)
    for tt in range(NT):
        srl[tt % 2].plan(16)
        srs[tt % 2].plan(16)

    @block.sync
    def _(sy):
        for tt in range(NT):
            par = tt % 2
            if tt >= 2:
                sy.wait_ge(srs[par].h, (tt // 2) * 16)
            sy.dma_start(out=win[:, par * D:(par + 1) * D],
                         in_=x1[tt * 128:(tt + 1) * 128, :]).then_inc(srl[par].h, 16)
            sy.wait_ge(ctx.vc.h, v_a + (tt + 1) * DT)
            sy.dma_start(out=x2[tt * 128:(tt + 1) * 128, :],
                         in_=win[:, par * D:(par + 1) * D]).then_inc(srs[par].h, 16)

    @block.tensor
    def _(te):
        te.wait_ge(ctx.vc.h, v_a)
        n = 0
        for tt in range(NT):
            for dto in range(DT):
                if n >= 2:
                    te.wait_ge(ctx.vc.h, v_a + n - 1)
                te.transpose(psT[:, (n % 2) * 1024:(n % 2) * 1024 + 128],
                             aoutT[:, dto * TOK + tt * 128: dto * TOK + (tt + 1) * 128],
                             ident[:]).then_inc(ctx.tc.h, 1)
                n += 1

    @block.vector
    def _(v):
        n = 0
        for tt in range(NT):
            par = tt % 2
            v.wait_ge(srl[par].h, (tt // 2 + 1) * 16)
            for dto in range(DT):
                n += 1
                v.wait_ge(ctx.tc.h, t_a + n)
                dst = win[:, par * D + dto * 128: par * D + (dto + 1) * 128]
                v.tensor_tensor(out=dst, in0=dst,
                                in1=psT[:, ((n - 1) % 2) * 1024:((n - 1) % 2) * 1024 + 128],
                                op=ALU.add).then_inc(ctx.vc.h, 1)


def emit_moe(ctx, S, x3, x4, xn4, w1_d, w2_d, b1_d, b2_d, cent_d, bias_d,
             iota_d, ident, ident32):
    block = ctx.block
    xnT = ctx.sb(S, "mxnT", (128, DT * TOK), BF16)
    hT = ctx.sb(S, "mhT", (128, FT * TOK), BF16)
    acc = ctx.sb(S, "macc", (128, NT * D))
    w1b = ctx.sb(S, "mw1", (128, 2 * DT * 128), BF16)
    w2b = ctx.sb(S, "mw2", (128, FT * D), BF16)
    b1c = ctx.sb(S, "mb1", (128, NEXP * FT))
    b2b = ctx.sb(S, "mb2", (18, D), BF16)
    cent = ctx.sb(S, "mcent", (128, DT * 16), BF16)
    biasr = ctx.sb(S, "mbias", (128, 16))
    iota = ctx.sb(S, "miota", (128, 16))
    sct = ctx.sb(S, "msct", (128, NT * 16))
    G = ctx.sb(S, "mG", (128, NT * NEXP))
    GT = ctx.sb(S, "mGT", (18, TOK), BF16)
    work = ctx.sb(S, "mwork", (128, 16))
    m1 = ctx.sb(S, "mm1", (128, 1))
    msk = ctx.sb(S, "mmsk", (128, 16))
    cand = ctx.sb(S, "mcand", (128, 16))
    idxs = ctx.sb(S, "midx", (128, 1))
    selm = ctx.sb(S, "mselm", (128, 16))
    gsum = ctx.sb(S, "mgsum", (128, 1))
    ginv = ctx.sb(S, "mginv", (128, 1))
    win = ctx.sb(S, "mwin", (128, 2 * D))
    xw = ctx.sb(S, "mxw", (128, 2 * D), BF16)
    psH = ctx.ps(S, "psH", (128, 2 * 512), F32)
    psY = ctx.ps(S, "psY", (128, 2 * 512), F32)
    psSc = ctx.ps(S, "psSc", (128, 512), F32)
    psG = ctx.ps(S, "psGt", (128, 512), F32)
    psT = ctx.ps(S, "psTm", (128, 2 * 1024), BF16)

    sxv = ctx.new_sem("mxv")
    sw1 = [ctx.new_sem("mw1"), ctx.new_sem("mw1")]
    sw2 = ctx.new_sem("mw2s")
    sxt = [ctx.new_sem("mxt"), ctx.new_sem("mxt")]
    srl = [ctx.new_sem("mrl"), ctx.new_sem("mrl")]
    srs = [ctx.new_sem("mrs"), ctx.new_sem("mrs")]

    lv = sxv.plan(3 * 16)
    lvg = ctx.lg.plan(2 * 16)

    @block.sync
    def _(sy):
        sy.dma_start(out=b1c[:].rearrange("p (e f) -> p e f", e=NEXP),
                     in_=b1_d[:].rearrange("e p f -> p e f")).then_inc(sxv.h, 16)
        sy.dma_start(out=biasr[:], in_=bias_d[:]).then_inc(sxv.h, 16)
        sy.dma_start(out=iota[:], in_=iota_d[:]).then_inc(sxv.h, 16)

    @block.gpsimd
    def _(g):
        g.dma_start(out=b2b[:], in_=b2_d[:]).then_inc(ctx.lg.h, 16)
        g.dma_start(out=cent[:].rearrange("p (a q) -> p a q", a=DT),
                    in_=cent_d[:].rearrange("a p q -> p a q")).then_inc(ctx.lg.h, 16)

    t_x = ctx.tc.plan(NT * DT)
    v_x = ctx.vc.plan(NT * (1 + DT))
    for tt in range(NT):
        sxt[tt % 2].plan(16)

    @block.sync
    def _(sy):
        for tt in range(NT):
            par = tt % 2
            if tt >= 2:
                sy.wait_ge(ctx.vc.h, v_x + (tt - 1) * (1 + DT))
            sy.dma_start(out=xw[:, par * D:(par + 1) * D],
                         in_=xn4[tt * 128:(tt + 1) * 128, :]).then_inc(sxt[par].h, 16)

    @block.vector
    def _(v):
        n = 0
        for tt in range(NT):
            par = tt % 2
            v.wait_ge(sxt[par].h, (tt // 2 + 1) * 16)
            v.tensor_scalar(out=acc[:, tt * D:(tt + 1) * D], in0=xw[:, par * D:(par + 1) * D],
                            scalar1=0.5, scalar2=None, op0=ALU.mult).then_inc(ctx.vc.h, 1)
            n += 1
            for d_ in range(DT):
                n += 1
                v.wait_ge(ctx.tc.h, t_x + tt * DT + d_ + 1)
                v.tensor_copy(xnT[:, d_ * TOK + tt * 128: d_ * TOK + (tt + 1) * 128],
                              psT[:, ((tt * DT + d_) % 2) * 1024:((tt * DT + d_) % 2) * 1024 + 128]
                              ).then_inc(ctx.vc.h, 1)

    @block.tensor
    def _(te):
        te.wait_ge(ctx.vc.h, t_x * 0 + v_x)
        n = 0
        for tt in range(NT):
            par = tt % 2
            te.wait_ge(sxt[par].h, (tt // 2 + 1) * 16)
            for d_ in range(DT):
                if n >= 2:
                    prev = n - 2
                    ptt, pd = prev // DT, prev % DT
                    te.wait_ge(ctx.vc.h, v_x + ptt * (1 + DT) + 1 + pd + 1)
                te.transpose(psT[:, (n % 2) * 1024:(n % 2) * 1024 + 128],
                             xw[:, par * D + d_ * 128: par * D + (d_ + 1) * 128],
                             ident[:]).then_inc(ctx.tc.h, 1)
                n += 1

    xnT_done = ctx.vc.v
    t_s = ctx.tc.plan(NT)
    c_s = ctx.sc.plan(NT)
    v_s = ctx.vc.plan(NT * 39)

    @block.tensor
    def _(te):
        te.wait_ge(sxv.h, 48)
        te.wait_ge(ctx.lg.h, lvg + 32)
        te.wait_ge(ctx.vc.h, xnT_done)
        for tt in range(NT):
            if tt >= 1:
                te.wait_ge(ctx.sc.h, c_s + tt)
            for T in range(DT):
                mm = te.matmul(psSc[:, 0:16],
                               lhsT=xnT[:, T * TOK + tt * 128: T * TOK + (tt + 1) * 128],
                               rhs=cent[:, T * 16:(T + 1) * 16],
                               start=(T == 0), stop=(T == DT - 1))
            mm.then_inc(ctx.tc.h, 1)

    @block.scalar
    def _(s):
        for tt in range(NT):
            s.wait_ge(ctx.tc.h, t_s + tt + 1)
            s.activation(out=sct[:, tt * 16:(tt + 1) * 16], in_=psSc[:, 0:16],
                         func=AF.Sigmoid, bias=0.0, scale=1.0).then_inc(ctx.sc.h, 1)

    @block.vector
    def _(v):
        nv = v_s
        v.wait_ge(sxv.h, 48)
        v.wait_ge(ctx.lg.h, lvg + 32)
        for tt in range(NT):
            gt = G[:, tt * NEXP:(tt + 1) * NEXP]
            v.wait_ge(ctx.sc.h, c_s + tt + 1)

            def step(instr):
                nonlocal nv
                instr.then_inc(ctx.vc.h, 1)
                nv += 1
                v.wait_ge(ctx.vc.h, nv)
            step(v.tensor_tensor(out=work[:], in0=sct[:, tt * 16:(tt + 1) * 16],
                                 in1=biasr[:], op=ALU.add))
            step(v.memset(gt, 0.0))
            step(v.tensor_scalar(out=gt[:, 0:2], in0=gt[:, 0:2], scalar1=0.0,
                                 scalar2=0.5, op0=ALU.mult, op1=ALU.add))
            for k in range(4):
                step(v.tensor_reduce(m1[:], work[:], axis=AX.X, op=ALU.max))
                step(v.tensor_scalar(out=msk[:], in0=work[:], scalar1=m1[:],
                                     scalar2=None, op0=ALU.is_equal))
                step(v.scalar_tensor_tensor(out=cand[:], in0=iota[:], scalar=-999.0,
                                            in1=msk[:], op0=ALU.add, op1=ALU.mult))
                step(v.tensor_scalar(out=cand[:], in0=cand[:], scalar1=999.0,
                                     scalar2=None, op0=ALU.add))
                step(v.tensor_reduce(idxs[:], cand[:], axis=AX.X, op=ALU.min))
                step(v.tensor_scalar(out=selm[:], in0=iota[:], scalar1=idxs[:],
                                     scalar2=None, op0=ALU.is_equal))
                step(v.scalar_tensor_tensor(out=gt[:, 2:18], in0=selm[:], scalar=m1[:],
                                            in1=gt[:, 2:18], op0=ALU.mult, op1=ALU.add))
                if k < 3:
                    step(v.scalar_tensor_tensor(out=work[:], in0=selm[:], scalar=-1e9,
                                                in1=work[:], op0=ALU.mult, op1=ALU.add))
                else:
                    step(v.nop())
            step(v.tensor_reduce(gsum[:], gt[:, 2:18], axis=AX.X, op=ALU.add))
            step(v.tensor_scalar(out=gsum[:], in0=gsum[:], scalar1=1e-8, scalar2=None,
                                 op0=ALU.max))
            step(v.reciprocal(ginv[:], gsum[:]))
            step(v.tensor_scalar(out=gt[:, 2:18], in0=gt[:, 2:18], scalar1=ginv[:],
                                 scalar2=0.5, op0=ALU.mult, op1=ALU.mult))

    t_g = ctx.tc.plan(NT)
    v_gt = ctx.vc.plan(NT)

    @block.tensor
    def _(te):
        te.wait_ge(ctx.vc.h, v_gt)
        for tt in range(NT):
            if tt >= 1:
                te.wait_ge(ctx.vc.h, v_gt + tt)
            te.transpose(psG[0:NEXP, 0:128], G[:, tt * NEXP:(tt + 1) * NEXP],
                         ident32[:]).then_inc(ctx.tc.h, 1)

    @block.vector
    def _(v):
        for tt in range(NT):
            v.wait_ge(ctx.tc.h, t_g + tt + 1)
            v.tensor_copy(GT[:, tt * 128:(tt + 1) * 128],
                          psG[0:NEXP, 0:128]).then_inc(ctx.vc.h, 1)


    sw1_cnt = [0, 0]
    for e in range(NEXP):
        t_h = ctx.tc.plan(FT * 2)
        c_h = ctx.sc.plan(FT * 2)
        sw1_need = []
        for ft in range(FT):
            sw1[ft % 2].plan(16)
            sw1_cnt[ft % 2] += 1
            sw1_need.append(sw1_cnt[ft % 2])
        sw2.plan(16)

        @block.gpsimd
        def _(g, e=e, t_h=t_h):
            if e > 0:
                g.wait_ge(ctx.tc.h, t_h)
            g.dma_start(out=w2b[:].rearrange("p (a q) -> p a q", a=FT),
                        in_=w2_d[e].rearrange("(a p) q -> p a q", p=128)
                        ).then_inc(sw2.h, 16)
            for ft in range(FT):
                par = ft % 2
                if ft >= 2:
                    g.wait_ge(ctx.tc.h, t_h + 2 * (ft - 1))
                g.dma_start(out=w1b[:, par * DT * 128:(par + 1) * DT * 128].rearrange(
                                "p (a q) -> p a q", a=DT),
                            in_=w1_d[e, ft]).then_inc(sw1[par].h, 16)

        @block.tensor
        def _(te, e=e, t_h=t_h, c_h=c_h, sw1_need=sw1_need):
            te.wait_ge(ctx.sc.h, c_h)
            for ft in range(FT):
                par = ft % 2
                te.wait_ge(sw1[par].h, sw1_need[ft] * 16)
                for half in range(2):
                    n2 = 2 * ft + half
                    if n2 >= 2:
                        te.wait_ge(ctx.sc.h, c_h + n2 - 1)
                    ph = psH[:, (n2 % 2) * 512:(n2 % 2) * 512 + 512]
                    for T in range(DT):
                        mm = te.matmul(ph, lhsT=w1b[:, par * DT * 128 + T * 128: par * DT * 128 + (T + 1) * 128],
                                       rhs=xnT[:, T * TOK + half * 512: T * TOK + half * 512 + 512],
                                       start=(T == 0), stop=(T == DT - 1))
                    mm.then_inc(ctx.tc.h, 1)

        @block.scalar
        def _(s, e=e, t_h=t_h):
            s.wait_ge(ctx.tc.h, t_h)
            for ft in range(FT):
                for half in range(2):
                    n2 = 2 * ft + half
                    s.wait_ge(ctx.tc.h, t_h + n2 + 1)
                    ph = psH[:, (n2 % 2) * 512:(n2 % 2) * 512 + 512]
                    s.activation(out=hT[:, ft * TOK + half * 512: ft * TOK + half * 512 + 512],
                                 in_=ph, func=AF.Relu, bias=b1c[:, e * FT + ft:e * FT + ft + 1],
                                 scale=1.0).then_inc(ctx.sc.h, 1)

        t_y = ctx.tc.plan(NT * 4)
        v_y = ctx.vc.plan(NT * 4)

        @block.tensor
        def _(te, e=e, t_y=t_y, v_y=v_y, c_h=c_h):
            te.wait_ge(ctx.sc.h, c_h + FT * 2)
            te.wait_ge(ctx.vc.h, v_y)
            te.wait_ge(sw2.h, (e + 1) * 16)
            n = 0
            for tt in range(NT):
                for chk in range(4):
                    if n >= 2:
                        te.wait_ge(ctx.vc.h, v_y + n - 1)
                    py = psY[:, (n % 2) * 512:(n % 2) * 512 + 512]
                    for ft in range(FT):
                        mm = te.matmul(py, lhsT=hT[:, ft * TOK + tt * 128: ft * TOK + (tt + 1) * 128],
                                       rhs=w2b[:, ft * D + chk * 512: ft * D + (chk + 1) * 512],
                                       start=(ft == 0), stop=(ft == FT - 1))
                    mm.then_inc(ctx.tc.h, 1)
                    n += 1

        @block.vector
        def _(v, e=e, t_y=t_y, v_y=v_y):
            n = 0
            for tt in range(NT):
                for chk in range(4):
                    n += 1
                    v.wait_ge(ctx.tc.h, t_y + n)
                    py = psY[:, ((n - 1) % 2) * 512:((n - 1) % 2) * 512 + 512]
                    dst = acc[:, tt * D + chk * 512: tt * D + (chk + 1) * 512]
                    v.scalar_tensor_tensor(out=dst, in0=py,
                                           scalar=G[:, tt * NEXP + e:tt * NEXP + e + 1],
                                           in1=dst, op0=ALU.mult,
                                           op1=ALU.add).then_inc(ctx.vc.h, 1)

    t_b2 = ctx.tc.plan(NT * 4)
    v_b2 = ctx.vc.plan(NT * 4)

    @block.tensor
    def _(te):
        te.wait_ge(ctx.vc.h, v_b2)
        n = 0
        for tt in range(NT):
            for chk in range(4):
                if n >= 2:
                    te.wait_ge(ctx.vc.h, v_b2 + n - 1)
                py = psY[:, (n % 2) * 512:(n % 2) * 512 + 512]
                te.matmul(py, lhsT=GT[:, tt * 128:(tt + 1) * 128],
                          rhs=b2b[:, chk * 512:(chk + 1) * 512], start=True,
                          stop=True).then_inc(ctx.tc.h, 1)
                n += 1

    @block.vector
    def _(v):
        n = 0
        for tt in range(NT):
            for chk in range(4):
                n += 1
                v.wait_ge(ctx.tc.h, t_b2 + n)
                py = psY[:, ((n - 1) % 2) * 512:((n - 1) % 2) * 512 + 512]
                dst = acc[:, tt * D + chk * 512: tt * D + (chk + 1) * 512]
                v.tensor_tensor(out=dst, in0=dst, in1=py, op=ALU.add).then_inc(ctx.vc.h, 1)

    v_f = ctx.vc.plan(NT)
    for tt in range(NT):
        srl[tt % 2].plan(16)
        srs[tt % 2].plan(16)

    @block.sync
    def _(sy):
        for tt in range(NT):
            par = tt % 2
            if tt >= 2:
                sy.wait_ge(srs[par].h, (tt // 2) * 16)
            sy.dma_start(out=win[:, par * D:(par + 1) * D],
                         in_=x3[tt * 128:(tt + 1) * 128, :]).then_inc(srl[par].h, 16)
            sy.wait_ge(ctx.vc.h, v_f + tt + 1)
            sy.dma_start(out=x4[tt * 128:(tt + 1) * 128, :],
                         in_=win[:, par * D:(par + 1) * D]).then_inc(srs[par].h, 16)

    @block.vector
    def _(v):
        v.wait_ge(ctx.vc.h, v_f)
        for tt in range(NT):
            par = tt % 2
            v.wait_ge(srl[par].h, (tt // 2 + 1) * 16)
            v.tensor_tensor(out=win[:, par * D:(par + 1) * D],
                            in0=win[:, par * D:(par + 1) * D],
                            in1=acc[:, tt * D:(tt + 1) * D], op=ALU.add).then_inc(ctx.vc.h, 1)



def emit_xdbl(ctx, S, xn5, wx_d, xdbl_d, ident):
    block = ctx.block
    xw = ctx.sb(S, "xdw", (128, 2 * D), BF16)
    xnT = ctx.sb(S, "xdT", (128, DT * 128), BF16)
    wxb = ctx.sb(S, "xwx", (128, DT * 96), BF16)
    ob = ctx.sb(S, "xdo", (128, 2 * 96))
    psX = ctx.ps(S, "psX", (128, 512), F32)
    psT = ctx.ps(S, "psTx", (128, 2 * 1024), BF16)
    sx = [ctx.new_sem("xdl"), ctx.new_sem("xdl")]
    so = [ctx.new_sem("xds"), ctx.new_sem("xds")]
    lw = ctx.lg.plan(16)

    @block.gpsimd
    def _(g):
        g.dma_start(out=wxb[:].rearrange("p (a q) -> p a q", a=DT),
                    in_=wx_d.rearrange("a p q -> p a q")).then_inc(ctx.lg.h, 16)

    t_x = ctx.tc.plan(NT * (DT + 1))
    v_x = ctx.vc.plan(NT * (DT + 1))
    c_x = ctx.sc.plan(0)
    for tt in range(NT):
        sx[tt % 2].plan(16)
        so[tt % 2].plan(16)

    @block.sync
    def _(sy):
        for tt in range(NT):
            par = tt % 2
            if tt >= 2:
                sy.wait_ge(ctx.tc.h, t_x + (tt - 1) * (DT + 1))
            sy.dma_start(out=xw[:, par * D:(par + 1) * D],
                         in_=xn5[tt * 128:(tt + 1) * 128, :]).then_inc(sx[par].h, 16)
            sy.wait_ge(ctx.vc.h, v_x + tt * (DT + 1) + DT + 1)
            sy.dma_start(out=xdbl_d[tt * 128:(tt + 1) * 128, :],
                         in_=ob[:, par * 96:(par + 1) * 96]).then_inc(so[par].h, 16)

    @block.tensor
    def _(te):
        te.wait_ge(ctx.lg.h, ctx.lg.v)
        n = 0
        for tt in range(NT):
            par = tt % 2
            te.wait_ge(sx[par].h, (tt // 2 + 1) * 16)
            for d_ in range(DT):
                if n >= 2:
                    prev = n - 2
                    ptt, pd = prev // DT, prev % DT
                    te.wait_ge(ctx.vc.h, v_x + ptt * (DT + 1) + pd + 1)
                te.transpose(psT[:, (n % 2) * 1024:(n % 2) * 1024 + 128],
                             xw[:, par * D + d_ * 128: par * D + (d_ + 1) * 128],
                             ident[:]).then_inc(ctx.tc.h, 1)
                n += 1
            te.wait_ge(ctx.vc.h, v_x + tt * (DT + 1) + DT)
            if tt >= 1:
                te.wait_ge(ctx.vc.h, v_x + (tt - 1) * (DT + 1) + DT + 1)
            for T in range(DT):
                mm = te.matmul(psX[:, 0:96], lhsT=xnT[:, T * 128:(T + 1) * 128],
                               rhs=wxb[:, T * 96:(T + 1) * 96],
                               start=(T == 0), stop=(T == DT - 1))
            mm.then_inc(ctx.tc.h, 1)

    @block.vector
    def _(v):
        n = 0
        for tt in range(NT):
            par = tt % 2
            for d_ in range(DT):
                n += 1
                v.wait_ge(ctx.tc.h, t_x + tt * (DT + 1) + d_ + 1)
                v.tensor_copy(xnT[:, d_ * 128:(d_ + 1) * 128],
                              psT[:, ((n - 1 - tt) % 2) * 1024:((n - 1 - tt) % 2) * 1024 + 128]
                              ).then_inc(ctx.vc.h, 1)
            n += 1
            v.wait_ge(ctx.tc.h, t_x + (tt + 1) * (DT + 1))
            if tt >= 2:
                v.wait_ge(so[par].h, (tt // 2) * 16)
            v.tensor_copy(ob[:, par * 96:(par + 1) * 96], psX[:, 0:96]).then_inc(ctx.vc.h, 1)


def build_L3(stages=("att", "ln3", "ffn", "ln4", "moe", "ln5", "xdbl")):
    nc = bass.Bass("TRN2", target_bir_lowering=False, debug=False)
    x1 = nc.dram_tensor("x1", [TOK, D], F32, kind="ExternalInput").ap()
    outT_d = nc.dram_tensor("outT", [D, TOK], F32, kind="ExternalInput").ap()
    xn2v_d = nc.dram_tensor("xn2v", [DT, 128, TOK], BF16, kind="ExternalInput").ap()
    wgt_d = nc.dram_tensor("wgt", [DT, DT, 128, 128], F32, kind="ExternalInput").ap()
    wot_d = nc.dram_tensor("wot", [DT, 128, DT, 128], F32, kind="ExternalInput").ap()
    nw_d = nc.dram_tensor("nw", [128, 16], F32, kind="ExternalInput").ap()
    wu = nc.dram_tensor("wu", [32, 128, 2, DT, 128], F32, kind="ExternalInput").ap()
    wg = nc.dram_tensor("wg", [32, 128, 2, DT, 128], F32, kind="ExternalInput").ap()
    wd = nc.dram_tensor("wd", [DT, 128, IHI, 128], F32, kind="ExternalInput").ap()
    su_ug_d = nc.dram_tensor("su_ug", [128, 16], F32, kind="ExternalInput").ap()
    su_dp_d = nc.dram_tensor("su_dp", [128, 64], F32, kind="ExternalInput").ap()
    ws_up_d = nc.dram_tensor("ws_up", [128, 64], F32, kind="ExternalInput").ap()
    ws_gt_d = nc.dram_tensor("ws_gt", [128, 64], F32, kind="ExternalInput").ap()
    ws_dp_d = nc.dram_tensor("ws_dp", [128, 16], F32, kind="ExternalInput").ap()
    w1_d = nc.dram_tensor("w1", [NEXP, FT, 128, DT, 128], F32, kind="ExternalInput").ap()
    w2_d = nc.dram_tensor("w2", [NEXP, DFF, D], F32, kind="ExternalInput").ap()
    b1_d = nc.dram_tensor("b1", [NEXP, 128, FT], F32, kind="ExternalInput").ap()
    b2_d = nc.dram_tensor("b2", [NEXP, D], F32, kind="ExternalInput").ap()
    cent_d = nc.dram_tensor("cent", [DT, 128, 16], F32, kind="ExternalInput").ap()
    bias_d = nc.dram_tensor("biasr", [128, 16], F32, kind="ExternalInput").ap()
    iota_d = nc.dram_tensor("iota16", [128, 16], F32, kind="ExternalInput").ap()
    wx_d = nc.dram_tensor("wx", [DT, 128, 96], F32, kind="ExternalInput").ap()
    h128_d = nc.dram_tensor("h128s", [128, 128], F32, kind="ExternalInput").ap()
    id_d = nc.dram_tensor("ident", [128, 128], F32, kind="ExternalInput").ap()
    cv_d = nc.dram_tensor("consts", [128, 4], F32, kind="ExternalInput").ap()
    x4_d = nc.dram_tensor("x4", [TOK, D], F32, kind="ExternalOutput").ap()
    xn5_d = nc.dram_tensor("xn5", [TOK, D], BF16, kind="ExternalOutput").ap()
    xdbl_d = nc.dram_tensor("xdbl", [TOK, 96], F32, kind="ExternalOutput").ap()
    x2_i = nc.dram_tensor("x2_i", [TOK, D], F32, kind="Internal").ap()
    xn3_i = nc.dram_tensor("xn3_i", [TOK, D], BF16, kind="Internal").ap()
    x3_i = nc.dram_tensor("x3_i", [TOK, D], F32, kind="Internal").ap()
    xn4_i = nc.dram_tensor("xn4_i", [TOK, D], BF16, kind="Internal").ap()

    with ExitStack() as ES:
        ctx = Ctx(nc, ES)
        block = ctx.block
        h128s = ctx.sb(ES, "h128", (128, 128), BF16)
        ident = ctx.sb(ES, "id", (128, 128), BF16)
        ident32 = ctx.sb(ES, "id32", (128, 128))
        cvec = ctx.sb(ES, "cv", (128, 4))
        lc = ctx.lg.plan(4 * 16)

        @block.gpsimd
        def _(g):
            g.dma_start(out=h128s[:], in_=h128_d[:]).then_inc(ctx.lg.h, 16)
            g.dma_start(out=ident[:], in_=id_d[:]).then_inc(ctx.lg.h, 16)
            g.dma_start(out=ident32[:], in_=id_d[:]).then_inc(ctx.lg.h, 16)
            g.dma_start(out=cvec[:], in_=cv_d[:]).then_inc(ctx.lg.h, 16)

        ctx.join()

        if "att" in stages:
            with ExitStack() as S:
                emit_attn_finish(ctx, S, x1, outT_d, xn2v_d, wgt_d, wot_d, nw_d, x2_i,
                                 cvec, ident)
                ctx.join()
        else:
            x2_i = x1
        if "ln3" in stages:
            with ExitStack() as S:
                emit_ln_stream(ctx, S, x2_i, xn3_i, NT, cvec)
                ctx.join()
        if "ffn" in stages:
            with ExitStack() as S:
                emit_ffn0(ctx, S, x2_i, x3_i, xn3_i, NT, wu, wg, wd, su_ug_d, su_dp_d,
                          ws_up_d, ws_gt_d, ws_dp_d, h128s, ident)
                ctx.join()
        else:
            x3_i = x2_i
        if "ln4" in stages:
            with ExitStack() as S:
                emit_ln_stream(ctx, S, x3_i, xn4_i, NT, cvec)
                ctx.join()
        if "moe" in stages:
            with ExitStack() as S:
                emit_moe(ctx, S, x3_i, x4_d, xn4_i, w1_d, w2_d, b1_d, b2_d, cent_d,
                         bias_d, iota_d, ident, ident32)
                ctx.join()
        else:
            with ExitStack() as S:
                emit_copy_dram(ctx, S, x3_i, x4_d, NT)
                ctx.join()
        if "ln5" in stages:
            with ExitStack() as S:
                emit_ln_stream(ctx, S, x4_d, xn5_d, NT, cvec)
                ctx.join()
        if "xdbl" in stages:
            with ExitStack() as S:
                emit_xdbl(ctx, S, xn5_d, wx_d, xdbl_d, ident)
                ctx.join()
        ctx.join()

    return nc


def emit_copy_dram(ctx, S, src_d, dst_d, ntile):
    block = ctx.block
    buf = ctx.sb(S, "cpy", (128, 2 * D))
    sl = [ctx.new_sem("cpl"), ctx.new_sem("cpl")]
    ss = [ctx.new_sem("cps"), ctx.new_sem("cps")]
    for t in range(ntile):
        sl[t % 2].plan(16)
        ss[t % 2].plan(16)

    @block.sync
    def _(sy):
        for t in range(ntile):
            par = t % 2
            if t >= 2:
                sy.wait_ge(ss[par].h, (t // 2) * 16)
            sy.dma_start(out=buf[:, par * D:(par + 1) * D],
                         in_=src_d[t * 128:(t + 1) * 128, :]).then_inc(sl[par].h, 16)
            sy.wait_ge(sl[par].h, (t // 2 + 1) * 16)
            sy.dma_start(out=dst_d[t * 128:(t + 1) * 128, :],
                         in_=buf[:, par * D:(par + 1) * D]).then_inc(ss[par].h, 16)


def build_L4():
    nc = bass.Bass("TRN2", target_bir_lowering=False, debug=False)
    xnv_d = nc.dram_tensor("xnv", [4, 128, SEQ], BF16, kind="ExternalInput").ap()
    dr_d = nc.dram_tensor("drows", [64, SEQ], F32, kind="ExternalInput").ap()
    br_d = nc.dram_tensor("brows", [16, SEQ], F32, kind="ExternalInput").ap()
    cr_d = nc.dram_tensor("crows", [16, SEQ], F32, kind="ExternalInput").ap()
    wdt_d = nc.dram_tensor("wdt", [64, 4, 128], F32, kind="ExternalInput").ap()
    bdt_d = nc.dram_tensor("bdt", [128, 4], F32, kind="ExternalInput").ap()
    alog_d = nc.dram_tensor("alog", [4, 128, 16], F32, kind="ExternalInput").ap()
    dpar_d = nc.dram_tensor("dpar", [128, 4], F32, kind="ExternalInput").ap()
    ones_d = nc.dram_tensor("ones1", [1, 128], F32, kind="ExternalInput").ap()
    sel_d = nc.dram_tensor("sels", [16, 16, 128], F32, kind="ExternalInput").ap()
    y_d = nc.dram_tensor("yT", [512, SEQ], F32, kind="ExternalOutput").ap()

    LCH = 1024
    NCH = SEQ // LCH

    with ExitStack() as ES:
        ctx = Ctx(nc, ES)
        block = ctx.block
        xnv = ctx.sb(ES, "sxnv", (128, 4 * SEQ), BF16)
        drbf = ctx.sb(ES, "sdr", (64, SEQ), BF16)
        brow = ctx.sb(ES, "sbr", (16, SEQ))
        crow = ctx.sb(ES, "scr", (16, SEQ))
        sels = ctx.sb(ES, "ssel", (16, 16 * 128))
        wdt = ctx.sb(ES, "swdt", (64, 4 * 128), BF16)
        bdt = ctx.sb(ES, "sbdt", (128, 4))
        acols = ctx.sb(ES, "sac", (128, 4 * 16))
        dpar = ctx.sb(ES, "sdp", (128, 4))
        ones1 = ctx.sb(ES, "son", (1, 128))
        Bbc = ctx.sb(ES, "sBb", (128, 16 * LCH), BF16)
        Cbc = ctx.sb(ES, "sCb", (128, 16 * LCH), BF16)
        delta = ctx.sb(ES, "sdl", (128, LCH))
        absb = ctx.sb(ES, "sab", (128, LCH))
        l1p = ctx.sb(ES, "sl1", (128, LCH))
        relu = ctx.sb(ES, "srl", (128, LCH))
        dxb = ctx.sb(ES, "sdx", (128, LCH))
        dAw = ctx.sb(ES, "sdA", (128, 2 * LCH))
        dBw = ctx.sb(ES, "sdB", (128, 2 * LCH))
        hw = ctx.sb(ES, "shw", (128, LCH))
        hcw = ctx.sb(ES, "shc", (128, LCH))
        yac = ctx.sb(ES, "syac", (128, 2 * LCH))
        hst = ctx.sb(ES, "shst", (128, 64))
        psD2 = ctx.ps(ES, "psD2", (128, 2 * 512), F32)
        psB2 = ctx.ps(ES, "psB2", (128, 2 * 512), F32)

        sld = ctx.new_sem("s4l")
        sbc = ctx.new_sem("s4b")
        sst = [ctx.new_sem("s4s"), ctx.new_sem("s4s")]

        l0 = sld.plan(7 * 16)
        lg0 = ctx.lg.plan(2 * 16)
        lb0 = sbc.plan(16)

        @block.sync
        def _(sy):
            sy.dma_start(out=xnv[:].rearrange("p (a q) -> p a q", a=4),
                         in_=xnv_d.rearrange("a p q -> p a q")).then_inc(sld.h, 16)
            sy.dma_start(out=brow[:], in_=br_d[:]).then_inc(sld.h, 16)
            sy.dma_start(out=crow[:], in_=cr_d[:]).then_inc(sld.h, 16)
            sy.dma_start(out=bdt[:], in_=bdt_d[:]).then_inc(sld.h, 16)
            sy.dma_start(out=acols[:].rearrange("p (a q) -> p a q", a=4),
                         in_=alog_d.rearrange("a p q -> p a q")).then_inc(sld.h, 16)
            sy.dma_start(out=dpar[:], in_=dpar_d[:]).then_inc(sld.h, 16)
            sy.dma_start(out=ones1[:], in_=ones_d[:]).then_inc(sld.h, 16)
            sy.dma_start(out=sels[:].rearrange("p (a q) -> p a q", a=16),
                         in_=sel_d.rearrange("a p q -> p a q")).then_inc(sbc.h, 16)

        @block.gpsimd
        def _(g):
            g.dma_start(out=drbf[:], in_=dr_d[:]).then_inc(ctx.lg.h, 16)
            g.dma_start(out=wdt[:].rearrange("p (a q) -> p a q", a=4),
                        in_=wdt_d.rearrange("p a q -> p a q")).then_inc(ctx.lg.h, 16)

        c_a = ctx.sc.plan(1)
        v_a = ctx.vc.plan(1)

        @block.scalar
        def _(s):
            s.wait_ge(sld.h, l0 + 7 * 16)
            s.activation(out=acols[:], in_=acols[:], func=AF.Exp, bias=0.0,
                         scale=1.0).then_inc(ctx.sc.h, 1)

        @block.vector
        def _(v):
            v.wait_ge(ctx.sc.h, c_a + 1)
            v.tensor_scalar(out=acols[:], in0=acols[:], scalar1=-1.0, scalar2=None,
                            op0=ALU.mult).then_inc(ctx.vc.h, 1)

        ctx.join()

        for lc in range(NCH):
            o = lc * LCH
            t_b = ctx.tc.plan(64)
            v_b = ctx.vc.plan(64)

            @block.tensor
            def _(te, lc=lc, t_b=t_b, v_b=v_b, gprev=ctx.gc.v):
                te.wait_ge(ctx.vc.h, v_b)
                te.wait_ge(ctx.gc.h, gprev)
                te.wait_ge(sbc.h, 16)
                n4 = 0
                for n in range(16):
                    for half in range(2):
                        for which in range(2):
                            src = brow if which == 0 else crow
                            if n4 >= 2:
                                te.wait_ge(ctx.vc.h, v_b + n4 - 1)
                            te.matmul(psB2[:, (n4 % 2) * 512:(n4 % 2) * 512 + 512],
                                      lhsT=sels[:, n * 128:(n + 1) * 128],
                                      rhs=src[:, o + half * 512: o + half * 512 + 512],
                                      start=True, stop=True).then_inc(ctx.tc.h, 1)
                            n4 += 1

            @block.vector
            def _(v, lc=lc, t_b=t_b, v_b=v_b):
                n4 = 0
                for n in range(16):
                    for half in range(2):
                        for which in range(2):
                            dst = (Bbc if which == 0 else Cbc)
                            v.wait_ge(ctx.tc.h, t_b + n4 + 1)
                            v.tensor_copy(dst[:, n * LCH + half * 512: n * LCH + half * 512 + 512],
                                          psB2[:, (n4 % 2) * 512:(n4 % 2) * 512 + 512]
                                          ).then_inc(ctx.vc.h, 1)
                            n4 += 1

            for dt4 in range(4):
                it = lc * 4 + dt4
                t_d = ctx.tc.plan(2)
                c_d = ctx.sc.plan(4 * 2)
                v_d = ctx.vc.plan(2)

                @block.tensor
                def _(te, dt4=dt4, o=o, t_d=t_d, v_d=v_d, c_d=c_d, it=it):
                    if it > 0:
                        te.wait_ge(ctx.sc.h, c_d)
                    for half in range(2):
                        mm = te.matmul(psD2[:, half * 512:half * 512 + 512],
                                       lhsT=wdt[:, dt4 * 128:(dt4 + 1) * 128],
                                       rhs=drbf[:, o + half * 512: o + half * 512 + 512],
                                       start=True, stop=True)
                    mm.then_inc(ctx.tc.h, 2)

                @block.scalar
                def _(s, dt4=dt4, t_d=t_d, c_d=c_d, v_d=v_d):
                    s.wait_ge(ctx.tc.h, t_d + 2)
                    for half in range(2):
                        pd = psD2[:, half * 512:half * 512 + 512]
                        sl = slice(half * 512, half * 512 + 512)
                        s.activation(out=absb[:, sl], in_=pd, func=AF.Abs,
                                     bias=bdt[:, dt4:dt4 + 1], scale=1.0).then_inc(ctx.sc.h, 1)
                        s.wait_ge(ctx.sc.h, c_d + 4 * half + 1)
                        s.activation(out=l1p[:, sl], in_=absb[:, sl], func=AF.Exp,
                                     bias=0.0, scale=-1.0).then_inc(ctx.sc.h, 1)
                        s.wait_ge(ctx.sc.h, c_d + 4 * half + 2)
                        s.activation(out=l1p[:, sl], in_=l1p[:, sl], func=AF.Ln,
                                     bias=1.0, scale=1.0).then_inc(ctx.sc.h, 1)
                        s.activation(out=relu[:, sl], in_=pd, func=AF.Relu,
                                     bias=bdt[:, dt4:dt4 + 1], scale=1.0).then_inc(ctx.sc.h, 1)

                @block.vector
                def _(v, dt4=dt4, o=o, c_d=c_d, v_d=v_d, gprev=ctx.gc.v):
                    v.wait_ge(ctx.sc.h, c_d + 8)
                    v.wait_ge(ctx.gc.h, gprev)
                    v.tensor_tensor(out=delta[:], in0=relu[:], in1=l1p[:],
                                    op=ALU.add).then_inc(ctx.vc.h, 1)
                    v.wait_ge(ctx.vc.h, v_d + 1)
                    v.tensor_tensor(out=dxb[:], in0=delta[:],
                                    in1=xnv[:, dt4 * SEQ + o: dt4 * SEQ + o + LCH],
                                    op=ALU.mult).then_inc(ctx.vc.h, 1)
                    v.wait_ge(ctx.vc.h, v_d + 2)

                c_n = ctx.sc.plan(16)
                v_n = ctx.vc.plan(16 * 5)

                @block.scalar
                def _(s, dt4=dt4, c_n=c_n, v_n=v_n, v_d=v_d):
                    s.wait_ge(ctx.vc.h, v_d + 1)
                    for n in range(16):
                        if n >= 2:
                            s.wait_ge(ctx.vc.h, v_n + (n - 2) * 5 + 2)
                        s.activation(out=dAw[:, (n % 2) * LCH:(n % 2) * LCH + LCH],
                                     in_=delta[:], func=AF.Exp, bias=0.0,
                                     scale=acols[:, dt4 * 16 + n: dt4 * 16 + n + 1]
                                     ).then_inc(ctx.sc.h, 1)

                @block.vector
                def _(v, dt4=dt4, lc=lc, o=o, c_n=c_n, v_n=v_n, it=it):
                    nv = v_n
                    if it >= 2:
                        v.wait_ge(sst[it % 2].h, (it // 2) * 16)
                    for n in range(16):
                        v.tensor_tensor(out=dBw[:, (n % 2) * LCH:(n % 2) * LCH + LCH],
                                        in0=dxb[:], in1=Bbc[:, n * LCH:(n + 1) * LCH],
                                        op=ALU.mult).then_inc(ctx.vc.h, 1)
                        nv += 1
                        v.wait_ge(ctx.vc.h, nv)
                        v.wait_ge(ctx.sc.h, c_n + n + 1)
                        v.tensor_tensor_scan(out=hw[:],
                                             data0=dAw[:, (n % 2) * LCH:(n % 2) * LCH + LCH],
                                             data1=dBw[:, (n % 2) * LCH:(n % 2) * LCH + LCH],
                                             initial=0.0 if lc == 0 else hst[:, dt4 * 16 + n: dt4 * 16 + n + 1],
                                             op0=ALU.mult, op1=ALU.add).then_inc(ctx.vc.h, 1)
                        nv += 1
                        v.wait_ge(ctx.vc.h, nv)
                        v.tensor_copy(hst[:, dt4 * 16 + n: dt4 * 16 + n + 1],
                                      hw[:, LCH - 1:LCH]).then_inc(ctx.vc.h, 1)
                        nv += 1
                        if n == 0:
                            v.tensor_tensor(out=yac[:, (it % 2) * LCH:(it % 2) * LCH + LCH],
                                            in0=hw[:], in1=Cbc[:, 0:LCH],
                                            op=ALU.mult).then_inc(ctx.vc.h, 1)
                            nv += 1
                            v.wait_ge(ctx.vc.h, nv)
                            v.nop().then_inc(ctx.vc.h, 1)
                            nv += 1
                        else:
                            v.tensor_tensor(out=hcw[:], in0=hw[:],
                                            in1=Cbc[:, n * LCH:(n + 1) * LCH],
                                            op=ALU.mult).then_inc(ctx.vc.h, 1)
                            nv += 1
                            v.wait_ge(ctx.vc.h, nv)
                            v.tensor_tensor(out=yac[:, (it % 2) * LCH:(it % 2) * LCH + LCH],
                                            in0=yac[:, (it % 2) * LCH:(it % 2) * LCH + LCH],
                                            in1=hcw[:], op=ALU.add).then_inc(ctx.vc.h, 1)
                            nv += 1
                        v.wait_ge(ctx.vc.h, nv)

                v_y = ctx.vc.plan(1)
                st0 = sst[it % 2].plan(16)

                @block.vector
                def _(v, dt4=dt4, it=it, v_y=v_y):
                    v.scalar_tensor_tensor(out=yac[:, (it % 2) * LCH:(it % 2) * LCH + LCH],
                                           in0=xnv[:, dt4 * SEQ + o: dt4 * SEQ + o + LCH],
                                           scalar=dpar[:, dt4:dt4 + 1],
                                           in1=yac[:, (it % 2) * LCH:(it % 2) * LCH + LCH],
                                           op0=ALU.mult, op1=ALU.add).then_inc(ctx.vc.h, 1)

                @block.sync
                def _(sy, dt4=dt4, o=o, it=it, v_y=v_y):
                    sy.wait_ge(ctx.vc.h, v_y + 1)
                    sy.dma_start(out=y_d[dt4 * 128:(dt4 + 1) * 128, o:o + LCH],
                                 in_=yac[:, (it % 2) * LCH:(it % 2) * LCH + LCH]
                                 ).then_inc(sst[it % 2].h, 16)

        ctx.join()

    return nc


def build_L5():
    nc = bass.Bass("TRN2", target_bir_lowering=False, debug=False)
    x4 = nc.dram_tensor("x4", [TOK, D], F32, kind="ExternalInput").ap()
    yv_d = nc.dram_tensor("yv", [TOK, D], F32, kind="ExternalInput").ap()
    w1_d = nc.dram_tensor("w1", [NEXP, FT, 128, DT, 128], F32, kind="ExternalInput").ap()
    w2_d = nc.dram_tensor("w2", [NEXP, DFF, D], F32, kind="ExternalInput").ap()
    b1_d = nc.dram_tensor("b1", [NEXP, 128, FT], F32, kind="ExternalInput").ap()
    b2_d = nc.dram_tensor("b2", [NEXP, D], F32, kind="ExternalInput").ap()
    cent_d = nc.dram_tensor("cent", [DT, 128, 16], F32, kind="ExternalInput").ap()
    bias_d = nc.dram_tensor("biasr", [128, 16], F32, kind="ExternalInput").ap()
    iota_d = nc.dram_tensor("iota16", [128, 16], F32, kind="ExternalInput").ap()
    id_d = nc.dram_tensor("ident", [128, 128], F32, kind="ExternalInput").ap()
    cv_d = nc.dram_tensor("consts", [128, 4], F32, kind="ExternalInput").ap()
    out_d = nc.dram_tensor("out", [TOK, D], F32, kind="ExternalOutput").ap()
    x5_i = nc.dram_tensor("x5_i", [TOK, D], F32, kind="Internal").ap()
    xn6_i = nc.dram_tensor("xn6_i", [TOK, D], BF16, kind="Internal").ap()

    with ExitStack() as ES:
        ctx = Ctx(nc, ES)
        block = ctx.block
        ident = ctx.sb(ES, "id", (128, 128), BF16)
        ident32 = ctx.sb(ES, "id32", (128, 128))
        cvec = ctx.sb(ES, "cv", (128, 4))
        lc = ctx.lg.plan(3 * 16)

        @block.gpsimd
        def _(g):
            g.dma_start(out=ident[:], in_=id_d[:]).then_inc(ctx.lg.h, 16)
            g.dma_start(out=ident32[:], in_=id_d[:]).then_inc(ctx.lg.h, 16)
            g.dma_start(out=cvec[:], in_=cv_d[:]).then_inc(ctx.lg.h, 16)

        ctx.join()

        with ExitStack() as S:
            buf = ctx.sb(S, "ya", (128, 2 * D))
            yb = ctx.sb(S, "yb", (128, 2 * D))
            sl = [ctx.new_sem("y5l"), ctx.new_sem("y5l")]
            sy_ = [ctx.new_sem("y5y"), ctx.new_sem("y5y")]
            ss = [ctx.new_sem("y5s"), ctx.new_sem("y5s")]
            v5 = ctx.vc.plan(NT)
            for t in range(NT):
                sl[t % 2].plan(16)
                sy_[t % 2].plan(16)
                ss[t % 2].plan(16)

            @block.sync
            def _(sy):
                for t in range(NT):
                    par = t % 2
                    if t >= 2:
                        sy.wait_ge(ss[par].h, (t // 2) * 16)
                    sy.dma_start(out=buf[:, par * D:(par + 1) * D],
                                 in_=x4[t * 128:(t + 1) * 128, :]).then_inc(sl[par].h, 16)
                    sy.dma_start(out=yb[:, par * D:(par + 1) * D],
                                 in_=yv_d[t * 128:(t + 1) * 128, :]).then_inc(sy_[par].h, 16)
                    sy.wait_ge(ctx.vc.h, v5 + t + 1)
                    sy.dma_start(out=x5_i[t * 128:(t + 1) * 128, :],
                                 in_=buf[:, par * D:(par + 1) * D]).then_inc(ss[par].h, 16)

            @block.vector
            def _(v):
                for t in range(NT):
                    par = t % 2
                    v.wait_ge(sl[par].h, (t // 2 + 1) * 16)
                    v.wait_ge(sy_[par].h, (t // 2 + 1) * 16)
                    v.tensor_tensor(out=buf[:, par * D:(par + 1) * D],
                                    in0=buf[:, par * D:(par + 1) * D],
                                    in1=yb[:, par * D:(par + 1) * D],
                                    op=ALU.add).then_inc(ctx.vc.h, 1)
            ctx.join()

        with ExitStack() as S:
            emit_ln_stream(ctx, S, x5_i, xn6_i, NT, cvec)
            ctx.join()
        with ExitStack() as S:
            emit_moe(ctx, S, x5_i, out_d, xn6_i, w1_d, w2_d, b1_d, b2_d, cent_d,
                     bias_d, iota_d, ident, ident32)
            ctx.join()
        ctx.join()

    return nc


def prep_ffn0_weights(W_up, W_gate, W_down):
    def up_tile(W):
        return _c(W.reshape(32, 2, 128, DT, 128).transpose(0, 4, 1, 3, 2))
    wu = up_tile(W_up)
    wg = up_tile(W_gate)
    wd = _c(W_down.reshape(DT, 128, IHI, 128).transpose(0, 3, 2, 1))
    return wu, wg, wd


def prep_attn_weights(Wqkv, Wg, Wo, norm_w):
    WqkvT = np.ascontiguousarray(Wqkv.T)
    qkvw_all = np.stack([WqkvT[:, h * 384:(h + 1) * 384].reshape(DT, 128, 384)
                         for h in range(16)])
    WgT = Wg.T
    wgt = _c(WgT.reshape(DT, 128, DT, 128).transpose(2, 0, 1, 3))
    wot = _c(Wo.reshape(DT, 128, DT, 128).transpose(0, 3, 2, 1))
    nw = _c(norm_w.reshape(16, 128).T)
    return _c(qkvw_all), wgt, wot, nw


def prep_moe_weights(sw1, sb1, sw2, sb2, rw1, rb1, rw2, rb2, centroids, bias):
    w1 = np.concatenate([sw1, rw1], axis=0)
    w2 = np.concatenate([sw2, rw2], axis=0)
    b1 = np.concatenate([sb1, rb1], axis=0)
    b2 = np.concatenate([sb2, rb2], axis=0)
    w1t = _c(np.stack([w1[e].reshape(DT, 128, FT, 128).transpose(2, 1, 0, 3)
                       for e in range(NEXP)]))
    b1c = _c(b1.reshape(NEXP, FT, 128).transpose(0, 2, 1))
    cent = _c(centroids.T.reshape(DT, 128, 16))
    biasr = _c(np.broadcast_to(bias.astype(np.float32), (128, 16)))
    return w1t, _c(w2), b1c, _c(b2), cent, biasr


IOTA16 = _c(np.broadcast_to(np.arange(16, dtype=np.float32), (128, 16)))


def prep_ffn0_small(SU_ug, SU_dp, Wscale_ug, Wscale_dp):
    su_ug = _c(SU_ug.reshape(16, 128).T)
    su_dp = _c(SU_dp.reshape(64, 128).T)
    ws_up = _c(Wscale_ug[:INTER].reshape(64, 128).T)
    ws_gt = _c(Wscale_ug[INTER:].reshape(64, 128).T)
    ws_dp = _c(Wscale_dp.reshape(16, 128).T)
    return su_ug, su_dp, ws_up, ws_gt, ws_dp


_NC_CACHE = {}


def _nc(name, builder):
    if name not in _NC_CACHE:
        _NC_CACHE[name] = builder()
    return _NC_CACHE[name]


def kernel(**inputs):
    ins = inputs
    x = np.ascontiguousarray(ins["L0_attn_x"].reshape(8192, D), dtype=np.float32)

    wu, wg, wd = prep_ffn0_weights(ins["L0_ffn_W_up"], ins["L0_ffn_W_gate"],
                                   ins["L0_ffn_W_down"])
    su_ug, su_dp, ws_up, ws_gt, ws_dp = prep_ffn0_small(
        ins["L0_ffn_SU_ug"], ins["L0_ffn_SU_dp"], ins["L0_ffn_Wscale_ug"],
        ins["L0_ffn_Wscale_dp"])
    qkvw_all, wgt, wot, nw = prep_attn_weights(ins["L0_attn_Wqkv"], ins["L0_attn_Wg"],
                                               ins["L0_attn_Wo"], ins["L0_attn_norm_w"])
    w1t, w2c, b1c, b2c, cent, biasr = prep_moe_weights(
        ins["L1_ffn_shared_w1"], ins["L1_ffn_shared_b1"], ins["L1_ffn_shared_w2"],
        ins["L1_ffn_shared_b2"], ins["L1_ffn_routed_w1"], ins["L1_ffn_routed_b1"],
        ins["L1_ffn_routed_w2"], ins["L1_ffn_routed_b2"], ins["L1_ffn_centroids"],
        ins["L1_ffn_bias"])
    wx = _c(ins["L1_attn_W_xproj"].T.reshape(DT, 128, 96))
    slope_row = _c(ins["L0_attn_slope_rate"].reshape(1, 16))
    ones1 = np.ones((1, 128), np.float32)
    p_ = np.arange(128, dtype=np.float32)
    arrq = _c(np.stack([p_ + 1, p_ + 129], axis=1))
    arrk = _c(np.stack([255 - p_, 255 - (p_ + 128)], axis=1))
    idxT = np.zeros((2, 128, 256), np.float32)
    mskT = np.zeros((2, 128, 256), np.float32)
    m_ = np.arange(256)
    for u_ in range(2):
        for pp in range(128):
            d_ = m_ - (u_ * 128 + pp)
            idxT[u_, pp] = np.where(d_ >= 0, d_, 1e9)
            mskT[u_, pp] = (d_ >= 0).astype(np.float32)
    sels = _c(np.eye(16, dtype=np.float32)[:, :, None]
              * np.ones((1, 1, 128), np.float32)).transpose(1, 0, 2).copy()
    WdtT = np.ascontiguousarray(ins["L1_attn_W_dt"].T)

    nc1 = _nc("L1", build_L1)
    common1 = {"wu": wu, "wg": wg, "wd": wd, "su_ug": su_ug, "su_dp": su_dp,
               "ws_up": ws_up, "ws_gt": ws_gt, "ws_dp": ws_dp,
               "h128s": H128S, "ident": IDENT, "consts": CONSTS}
    maps = [dict(common1, x=np.ascontiguousarray(x[c * TOK:(c + 1) * TOK]))
            for c in range(8)]
    r1 = run_bass_kernel_spmd(nc1, maps, core_ids=list(range(8))).results
    x1 = [r1[c]["x1"] for c in range(8)]
    xn2 = [r1[c]["xn2"] for c in range(8)]

    nc2 = _nc("L2", build_L2)
    maps = []
    xq_b = []
    for b in range(2):
        xn2_b = np.concatenate(xn2[4 * b:4 * b + 4], axis=0)
        xq_b.append(np.ascontiguousarray(
            xn2_b.reshape(NTS, 128, DT, 128).transpose(0, 2, 3, 1)))
    for cs in range(8):
        b, hg = cs // 4, cs % 4
        maps.append({"xq": xq_b[b],
                     "qkvw": np.ascontiguousarray(qkvw_all[hg * 4:(hg + 1) * 4]),
                     "slope": np.ascontiguousarray(slope_row[:, hg * 4:(hg + 1) * 4]),
                     "ones1": ones1, "arrq": arrq, "arrk": arrk,
                     "idxT": idxT, "mskT": mskT, "ident": IDENT})
    r2 = run_bass_kernel_spmd(nc2, maps, core_ids=list(range(8))).results
    outT_h = [r2[cs]["outT"] for cs in range(8)]

    nc3 = _nc("L3", build_L3)
    common3 = {"wgt": wgt, "wot": wot, "nw": nw, "wu": wu, "wg": wg, "wd": wd,
               "su_ug": su_ug, "su_dp": su_dp, "ws_up": ws_up, "ws_gt": ws_gt,
               "ws_dp": ws_dp, "w1": w1t, "w2": w2c, "b1": b1c, "b2": b2c,
               "cent": cent, "biasr": biasr, "iota16": IOTA16, "wx": wx,
               "h128s": H128S, "ident": IDENT, "consts": CONSTS}
    maps = []
    for c in range(8):
        b, pos = c // 4, c % 4
        outT_home = np.ascontiguousarray(np.concatenate(
            [outT_h[b * 4 + g][:, pos * TOK:(pos + 1) * TOK] for g in range(4)], axis=0))
        xn2v = np.ascontiguousarray(
            xn2[c].reshape(NT, 128, DT, 128).transpose(2, 3, 0, 1).reshape(DT, 128, TOK))
        maps.append(dict(common3, x1=x1[c], outT=outT_home, xn2v=xn2v))
    r3 = run_bass_kernel_spmd(nc3, maps, core_ids=list(range(8))).results
    x4 = [r3[c]["x4"] for c in range(8)]
    xn5 = [r3[c]["xn5"] for c in range(8)]
    xdbl = [r3[c]["xdbl"] for c in range(8)]

    nc4 = _nc("L4", build_L4)
    maps = []
    xn5T_b, xdblT_b = [], []
    for b in range(2):
        xn5_b = np.concatenate(xn5[4 * b:4 * b + 4], axis=0)
        xn5T_b.append(np.ascontiguousarray(xn5_b.T))
        xdbl_b = np.concatenate(xdbl[4 * b:4 * b + 4], axis=0)
        xdblT_b.append(np.ascontiguousarray(xdbl_b.T.astype(np.float32)))
    for cs in range(8):
        b, k = cs // 4, cs % 4
        ds = k * 512
        maps.append({
            "xnv": np.ascontiguousarray(xn5T_b[b][ds:ds + 512].reshape(4, 128, SEQ)),
            "drows": np.ascontiguousarray(xdblT_b[b][0:64]),
            "brows": np.ascontiguousarray(xdblT_b[b][64:80]),
            "crows": np.ascontiguousarray(xdblT_b[b][80:96]),
            "wdt": np.ascontiguousarray(WdtT[:, ds:ds + 512].reshape(64, 4, 128)),
            "bdt": np.ascontiguousarray(ins["L1_attn_b_dt"][ds:ds + 512].reshape(4, 128).T),
            "alog": np.ascontiguousarray(ins["L1_attn_A_log"][ds:ds + 512].reshape(4, 128, 16)),
            "dpar": np.ascontiguousarray(ins["L1_attn_D_param"][ds:ds + 512].reshape(4, 128).T),
            "ones1": ones1, "sels": sels})
    r4 = run_bass_kernel_spmd(nc4, maps, core_ids=list(range(8))).results
    yT = [r4[cs]["yT"] for cs in range(8)]

    nc5 = _nc("L5", build_L5)
    common5 = {"w1": w1t, "w2": w2c, "b1": b1c, "b2": b2c, "cent": cent,
               "biasr": biasr, "iota16": IOTA16, "ident": IDENT, "consts": CONSTS}
    maps = []
    for c in range(8):
        b, pos = c // 4, c % 4
        y_home = np.concatenate(
            [yT[b * 4 + k][:, pos * TOK:(pos + 1) * TOK] for k in range(4)], axis=0)
        yv = np.ascontiguousarray(y_home.T)
        maps.append(dict(common5, x4=x4[c], yv=yv))
    r5 = run_bass_kernel_spmd(nc5, maps, core_ids=list(range(8))).results
    out = np.concatenate([r5[c]["out"] for c in range(8)], axis=0)
    return out.reshape(2, 4096, D)


# version 04
# speedup vs baseline: 1.0962x; 1.0236x over previous
import numpy as np
import ml_dtypes
from contextlib import ExitStack

import concourse.bass as bass
import concourse.mybir as mybir
from concourse.bass_utils import run_bass_kernel_spmd

F32 = mybir.dt.float32
BF16 = mybir.dt.bfloat16
AF = mybir.ActivationFunctionType
ALU = mybir.AluOpType
AX = mybir.AxisListType
BF16NP = ml_dtypes.bfloat16

D = 2048
TOK = 1024
NT = TOK // 128
DT = D // 128
INTER = 8192
IHI = 64
EPS_LN = 1e-5
EPS_RMS = float(np.finfo(np.float32).eps)
C_UG = float(2.0 ** 2.5)
C_DP = float(2.0 ** 0.5)

DFF = 1408
FT = 11
NEXP = 18
NHEAD = 16
BLK = 256
TCH = 512


def _had(n):
    h = np.array([[1.0]])
    while h.shape[0] < n:
        h = np.block([[h, h], [h, -h]])
    return h


def _c(x):
    return np.ascontiguousarray(np.asarray(x), dtype=np.float32)


H128S = _c(_had(128) * (2.0 ** -8))
IDENT = _c(np.eye(128))
CONSTS = _c(np.broadcast_to(np.array([EPS_LN, 1.0, EPS_RMS, 0.0], np.float32), (128, 4)))


class Sem:
    def __init__(self, h):
        self.h = h
        self.v = 0

    def plan(self, n):
        b = self.v
        self.v += n
        return b


class Ctx:
    def __init__(self, nc, ES):
        self.nc = nc
        self.ES = ES
        self.block = ES.enter_context(nc.Block())
        self._all = []
        self._n = 0
        self.tc = self.new_sem("tc")
        self.vc = self.new_sem("vc")
        self.sc = self.new_sem("sc")
        self.gc = self.new_sem("gc")
        self.lg = self.new_sem("lg")

    def new_sem(self, name):
        self._n += 1
        s = Sem(self.ES.enter_context(self.nc.semaphore(f"s_{name}{self._n}")))
        self._all.append(s)
        return s

    def sb(self, scope, name, shape, dt=F32):
        self._n += 1
        return scope.enter_context(self.nc.sbuf_tensor(f"{name}{self._n}", list(shape), dt))

    def ps(self, scope, name, shape, dt=F32):
        self._n += 1
        return scope.enter_context(self.nc.psum_tensor(f"{name}{self._n}", list(shape), dt))

    def join(self):
        block = self.block
        sems = [s for s in self._all if s.v]

        def w(e):
            for s in sems:
                e.wait_ge(s.h, s.v)

        @block.sync
        def _(e):
            w(e)

        @block.gpsimd
        def _(e):
            w(e)

        @block.tensor
        def _(e):
            w(e)

        @block.vector
        def _(e):
            w(e)

        @block.scalar
        def _(e):
            w(e)


def emit_ln_stream(ctx, S, xin, xn_out, ntile, cvec):
    block = ctx.block
    win = ctx.sb(S, "lnw", (128, 2 * D))
    xst = ctx.sb(S, "lnx", (128, 2 * D), BF16)
    sums = ctx.sb(S, "lns", (128, 8))
    ssq = ctx.sb(S, "lnq", (128, 2))
    sq = ctx.sb(S, "lnsq", (128, D))
    lsl = [ctx.new_sem("lnl"), ctx.new_sem("lnl")]
    ssl = [ctx.new_sem("lnst"), ctx.new_sem("lnst")]
    for t in range(ntile):
        lsl[t % 2].plan(16)
        ssl[t % 2].plan(16)

    v0 = ctx.vc.plan(ntile * 5)
    c0 = ctx.sc.plan(ntile * 3)

    @block.sync
    def _(sy):
        for t in range(ntile):
            par = t % 2
            if t >= 2:
                sy.wait_ge(ctx.vc.h, v0 + (t - 1) * 5)
                sy.wait_ge(ctx.sc.h, c0 + (t - 2) * 3 + 1)
                sy.wait_ge(ssl[par].h, (t // 2) * 16)
            sy.dma_start(out=win[:, par * D:(par + 1) * D],
                         in_=xin[t * 128:(t + 1) * 128, :]).then_inc(lsl[par].h, 16)
            sy.wait_ge(ctx.vc.h, v0 + t * 5 + 5)
            sy.dma_start(out=xn_out[t * 128:(t + 1) * 128, :],
                         in_=xst[:, par * D:(par + 1) * D]).then_inc(ssl[par].h, 16)

    @block.vector
    def _(v):
        for t in range(ntile):
            par = t % 2
            v.wait_ge(lsl[par].h, (t // 2 + 1) * 16)
            w = win[:, par * D:(par + 1) * D]
            v.tensor_reduce(sums[:, 4 * par:4 * par + 1], w, axis=AX.X,
                            op=ALU.add).then_inc(ctx.vc.h, 1)
            v.wait_ge(ctx.vc.h, v0 + t * 5 + 1)
            v.tensor_scalar(out=sums[:, 4 * par:4 * par + 1], in0=sums[:, 4 * par:4 * par + 1],
                            scalar1=-1.0 / D, scalar2=None, op0=ALU.mult).then_inc(ctx.vc.h, 1)
            v.wait_ge(ctx.vc.h, v0 + t * 5 + 2)
            v.wait_ge(ctx.sc.h, c0 + t * 3 + 1)
            v.tensor_tensor(out=sums[:, 4 * par + 1:4 * par + 2], in0=sums[:, 4 * par:4 * par + 1],
                            in1=sums[:, 4 * par:4 * par + 1], op=ALU.mult).then_inc(ctx.vc.h, 1)
            v.wait_ge(ctx.vc.h, v0 + t * 5 + 3)
            v.scalar_tensor_tensor(out=sums[:, 4 * par + 2:4 * par + 3],
                                   in0=ssq[:, par:par + 1], scalar=1.0 / D,
                                   in1=sums[:, 4 * par + 1:4 * par + 2],
                                   op0=ALU.mult, op1=ALU.subtract).then_inc(ctx.vc.h, 1)
            v.wait_ge(ctx.sc.h, c0 + t * 3 + 3)
            v.tensor_scalar(out=xst[:, par * D:(par + 1) * D], in0=w,
                            scalar1=sums[:, 4 * par:4 * par + 1],
                            scalar2=sums[:, 4 * par + 3:4 * par + 4],
                            op0=ALU.add, op1=ALU.mult).then_inc(ctx.vc.h, 1)

    @block.scalar
    def _(s):
        for t in range(ntile):
            par = t % 2
            if t >= 2:
                s.wait_ge(ctx.vc.h, v0 + (t - 1) * 5)
            s.wait_ge(lsl[par].h, (t // 2 + 1) * 16)
            s.activation(out=sq[:], in_=win[:, par * D:(par + 1) * D], func=AF.Square,
                         bias=0.0, scale=1.0, accum_out=ssq[:, par:par + 1]).then_inc(ctx.sc.h, 1)
            s.wait_ge(ctx.vc.h, v0 + t * 5 + 4)
            s.activation(out=sums[:, 4 * par + 3:4 * par + 4], in_=sums[:, 4 * par + 2:4 * par + 3],
                         func=AF.Ln, bias=cvec[:, 0:1], scale=1.0).then_inc(ctx.sc.h, 1)
            s.wait_ge(ctx.sc.h, c0 + t * 3 + 2)
            s.activation(out=sums[:, 4 * par + 3:4 * par + 4], in_=sums[:, 4 * par + 3:4 * par + 4],
                         func=AF.Exp, bias=0.0, scale=-0.5).then_inc(ctx.sc.h, 1)


def emit_ffn0(ctx, S, xin, xout, xn, ntile, wu, wg, wd, su_ug_d, su_dp_d,
              ws_up_d, ws_gt_d, ws_dp_d, h128s, ident,
              parts=("y1", "z", "fwht", "down")):
    nc, block = ctx.nc, ctx.block
    NCH = (ntile * 128) // TCH
    NTT = TCH // 128
    SLAB = 2 * DT * 128
    HALF = 32 * 128

    xnc = ctx.sb(S, "xnc", (128, NTT * D), BF16)
    xnT = ctx.sb(S, "xnT", (128, DT * TCH), BF16)
    y1 = ctx.sb(S, "y1", (128, DT * TCH), BF16)
    tmp = ctx.sb(S, "tmpc", (128, 2 * TCH), BF16)
    btmp = ctx.sb(S, "btmp", (128, 2048), BF16)
    z = ctx.sb(S, "zf", (128, IHI * TCH), BF16)
    sil = ctx.sb(S, "sil", (128, 2 * TCH))
    foutT = ctx.sb(S, "foutT", (128, DT * TCH), BF16)
    wub = ctx.sb(S, "wub", (128, 2 * SLAB), BF16)
    wgb = ctx.sb(S, "wgb", (128, 2 * SLAB), BF16)
    wdb = ctx.sb(S, "wdb", (128, 2 * HALF), BF16)
    win = ctx.sb(S, "fwin", (128, 2 * D))
    su_ug = ctx.sb(S, "suug", (128, 16))
    su_dp = ctx.sb(S, "sudp", (128, 64))
    wse_u = ctx.sb(S, "wseu", (128, 64))
    wse_g = ctx.sb(S, "wseg", (128, 64))
    wse_d = ctx.sb(S, "wsed", (128, 16))
    psA = ctx.ps(S, "psA", (128, 2 * 512), F32)
    psU = ctx.ps(S, "psU", (128, 2 * 512), F32)
    psG = ctx.ps(S, "psG", (128, 2 * 512), F32)
    psD = ctx.ps(S, "psD", (128, 512), F32)
    psT = ctx.ps(S, "psT", (128, 1024), BF16)

    lf = ctx.new_sem("ffc")
    sxn = ctx.new_sem("fxn")
    swz = [ctx.new_sem("fwz"), ctx.new_sem("fwz")]
    swd = [ctx.new_sem("fwd"), ctx.new_sem("fwd")]
    srl = [ctx.new_sem("frl"), ctx.new_sem("frl")]
    srs = [ctx.new_sem("frs"), ctx.new_sem("frs")]
    lsm = lf.plan(5 * 16)
    vsm = ctx.vc.plan(4)

    @block.sync
    def _(sy):
        sy.dma_start(out=su_ug[:], in_=su_ug_d[:]).then_inc(lf.h, 16)
        sy.dma_start(out=su_dp[:], in_=su_dp_d[:]).then_inc(lf.h, 16)
        sy.dma_start(out=wse_u[:], in_=ws_up_d[:]).then_inc(lf.h, 16)
        sy.dma_start(out=wse_g[:], in_=ws_gt_d[:]).then_inc(lf.h, 16)
        sy.dma_start(out=wse_d[:], in_=ws_dp_d[:]).then_inc(lf.h, 16)

    @block.vector
    def _(v):
        v.wait_ge(lf.h, 80)
        v.tensor_scalar(out=wse_u[:], in0=wse_u[:], scalar1=C_UG, scalar2=None,
                        op0=ALU.mult).then_inc(ctx.vc.h, 1)
        v.tensor_scalar(out=wse_g[:], in0=wse_g[:], scalar1=C_UG, scalar2=None,
                        op0=ALU.mult).then_inc(ctx.vc.h, 1)
        v.tensor_scalar(out=wse_d[:], in0=wse_d[:], scalar1=C_DP, scalar2=None,
                        op0=ALU.mult).then_inc(ctx.vc.h, 1)
        v.wait_ge(ctx.vc.h, vsm + 3)
        v.tensor_tensor(out=wse_u[:], in0=wse_u[:], in1=su_dp[:],
                        op=ALU.mult).then_inc(ctx.vc.h, 1)
        v.wait_ge(ctx.vc.h, vsm + 4)

    if "consts" in parts and len(parts) == 1:
        return
    for ch in range(NCH):
        t0 = ch * TCH

        lx = sxn.plan(16)
        tx = ctx.tc.plan(NTT * DT)
        vx = ctx.vc.plan(NTT * DT)

        @block.sync
        def _(sy):
            if ch > 0:
                sy.wait_ge(ctx.tc.h, tx)
            sy.dma_start(out=xnc[:].rearrange("p (t d) -> p t d", t=NTT),
                         in_=xn[t0:t0 + TCH, :].rearrange("(t p) d -> p t d", p=128)
                         ).then_inc(sxn.h, 16)

        if "xl" in parts:
            continue

        @block.tensor
        def _(te):
            te.wait_ge(sxn.h, lx + 16)
            te.wait_ge(ctx.vc.h, vx)
            n = 0
            for tt in range(NTT):
                for d_ in range(DT):
                    if n >= 1:
                        te.wait_ge(ctx.vc.h, vx + n)
                    te.transpose(psT[:, 0:128],
                                 xnc[:, tt * D + d_ * 128: tt * D + (d_ + 1) * 128],
                                 ident[:]).then_inc(ctx.tc.h, 1)
                    n += 1

        @block.vector
        def _(v):
            n = 0
            for tt in range(NTT):
                for d_ in range(DT):
                    n += 1
                    v.wait_ge(ctx.tc.h, tx + n)
                    v.tensor_copy(xnT[:, d_ * TCH + tt * 128: d_ * TCH + (tt + 1) * 128],
                                  psT[:, 0:128]).then_inc(ctx.vc.h, 1)

        v0 = ctx.vc.plan(2 * DT)
        t0s = ctx.tc.plan(DT)

        @block.vector
        def _(v):
            v.wait_ge(ctx.vc.h, vx + NTT * DT)
            for T in range(DT):
                if T >= 2:
                    v.wait_ge(ctx.tc.h, t0s + T - 1)
                v.tensor_scalar(out=tmp[:, (T % 2) * TCH:(T % 2) * TCH + TCH],
                                in0=xnT[:, T * TCH:(T + 1) * TCH],
                                scalar1=su_ug[:, T:T + 1], scalar2=None,
                                op0=ALU.mult).then_inc(ctx.vc.h, 1)
                if T > 0:
                    v.wait_ge(ctx.tc.h, t0s + T)
                    v.tensor_copy(y1[:, (T - 1) * TCH: T * TCH],
                                  psA[:, ((T - 1) % 2) * 512:((T - 1) % 2) * 512 + 512]
                                  ).then_inc(ctx.vc.h, 1)
            v.wait_ge(ctx.tc.h, t0s + DT)
            v.tensor_copy(y1[:, (DT - 1) * TCH: DT * TCH],
                          psA[:, ((DT - 1) % 2) * 512:((DT - 1) % 2) * 512 + 512]
                          ).then_inc(ctx.vc.h, 1)

        @block.tensor
        def _(te):
            for T in range(DT):
                need = v0 + 1 if T == 0 else v0 + 2 * T
                te.wait_ge(ctx.vc.h, need)
                te.matmul(psA[:, (T % 2) * 512:(T % 2) * 512 + 512], lhsT=h128s[:],
                          rhs=tmp[:, (T % 2) * TCH:(T % 2) * TCH + TCH], start=True,
                          stop=True).then_inc(ctx.tc.h, 1)

        if "bfly" in parts or "z" in parts:
            emit_butterflies(ctx, y1, btmp, DT, TCH)
        if "z" not in parts:
            continue

        NSLAB = 32
        y1r = ctx.vc.v
        for sl in range(NSLAB):
            swz[sl % 2].plan(32)
        tz = ctx.tc.plan(2 * IHI)
        sz = ctx.sc.plan(IHI)

        vseq = []
        for ih in range(IHI + 1):
            if ih < IHI:
                vseq.append(("evac", ih))
            i = ih - 1
            if i >= 0:
                vseq.append(("copy", i))
                hh = 1
                while hh <= IHI // 2:
                    if (i + 1) % (2 * hh) == 0:
                        vseq.append(("bf", hh, i + 1 - 2 * hh))
                    hh *= 2
        nv_ops = sum(2 if e[0] == "bf" else 1 for e in vseq)
        vz = ctx.vc.plan(nv_ops)
        vcnt_evac, vcnt_copy = {}, {}
        cnt = vz
        for e in vseq:
            if e[0] == "bf":
                cnt += 2
            else:
                cnt += 1
                (vcnt_evac if e[0] == "evac" else vcnt_copy)[e[1]] = cnt
        tcnt_Z = {ih: tz + (1 if ih == 0 else 2 * ih) for ih in range(IHI)}
        tcnt_A = {i: tz + (2 * i + 3 if i < IHI - 1 else 2 * IHI) for i in range(IHI)}
        scnt = {ih: sz + ih + 1 for ih in range(IHI)}

        @block.gpsimd
        def _(g):
            for sl in range(NSLAB):
                par = sl % 2
                if sl >= 2:
                    g.wait_ge(ctx.tc.h, tcnt_Z[2 * sl - 3])
                g.dma_start(out=wub[:, par * SLAB:(par + 1) * SLAB],
                            in_=wu[sl].rearrange("p a b q -> p (a b q)")
                            ).then_inc(swz[par].h, 16)
                g.dma_start(out=wgb[:, par * SLAB:(par + 1) * SLAB],
                            in_=wg[sl].rearrange("p a b q -> p (a b q)")
                            ).then_inc(swz[par].h, 16)

        @block.tensor
        def _(te):
            te.wait_ge(ctx.vc.h, y1r)
            for ih in range(IHI):
                sl, j = ih // 2, ih % 2
                par = sl % 2
                te.wait_ge(swz[par].h, (ch * (NSLAB // 2) + sl // 2 + 1) * 32)
                if ih >= 2:
                    te.wait_ge(ctx.sc.h, scnt[ih - 2])
                    te.wait_ge(ctx.vc.h, vcnt_evac[ih - 2])
                pu = psU[:, (ih % 2) * 512:(ih % 2) * 512 + 512]
                pg = psG[:, (ih % 2) * 512:(ih % 2) * 512 + 512]
                for T in range(DT):
                    off = par * SLAB + (j * DT + T) * 128
                    te.matmul(pu, lhsT=wub[:, off:off + 128], rhs=y1[:, T * TCH:(T + 1) * TCH],
                              start=(T == 0), stop=(T == DT - 1))
                for T in range(DT):
                    off = par * SLAB + (j * DT + T) * 128
                    mm = te.matmul(pg, lhsT=wgb[:, off:off + 128], rhs=y1[:, T * TCH:(T + 1) * TCH],
                                   start=(T == 0), stop=(T == DT - 1))
                mm.then_inc(ctx.tc.h, 1)
                if ih >= 1:
                    i = ih - 1
                    te.wait_ge(ctx.vc.h, vcnt_evac[i])
                    if i >= 2:
                        te.wait_ge(ctx.vc.h, vcnt_copy[i - 2])
                    te.matmul(psA[:, (i % 2) * 512:(i % 2) * 512 + 512], lhsT=h128s[:],
                              rhs=z[:, i * TCH:(i + 1) * TCH], start=True,
                              stop=True).then_inc(ctx.tc.h, 1)
            i = IHI - 1
            te.wait_ge(ctx.vc.h, vcnt_evac[i])
            te.wait_ge(ctx.vc.h, vcnt_copy[i - 2])
            te.matmul(psA[:, (i % 2) * 512:(i % 2) * 512 + 512], lhsT=h128s[:],
                      rhs=z[:, i * TCH:(i + 1) * TCH], start=True,
                      stop=True).then_inc(ctx.tc.h, 1)

        @block.scalar
        def _(s):
            for ih in range(IHI):
                if ih >= 2:
                    s.wait_ge(ctx.vc.h, vcnt_evac[ih - 2])
                s.wait_ge(ctx.tc.h, tcnt_Z[ih])
                pg = psG[:, (ih % 2) * 512:(ih % 2) * 512 + 512]
                s.activation(out=sil[:, (ih % 2) * TCH:(ih % 2) * TCH + TCH], in_=pg,
                             func=AF.Silu, bias=0.0,
                             scale=wse_g[:, ih:ih + 1]).then_inc(ctx.sc.h, 1)

        @block.vector
        def _(v):
            cnt2 = vz
            for e in vseq:
                if e[0] == "evac":
                    ih = e[1]
                    v.wait_ge(ctx.sc.h, scnt[ih])
                    pu = psU[:, (ih % 2) * 512:(ih % 2) * 512 + 512]
                    v.scalar_tensor_tensor(out=z[:, ih * TCH:(ih + 1) * TCH], in0=pu,
                                           scalar=wse_u[:, ih:ih + 1],
                                           in1=sil[:, (ih % 2) * TCH:(ih % 2) * TCH + TCH],
                                           op0=ALU.mult, op1=ALU.mult).then_inc(ctx.vc.h, 1)
                    cnt2 += 1
                elif e[0] == "copy":
                    i = e[1]
                    v.wait_ge(ctx.tc.h, tcnt_A[i])
                    v.tensor_copy(z[:, i * TCH:(i + 1) * TCH],
                                  psA[:, (i % 2) * 512:(i % 2) * 512 + 512]
                                  ).then_inc(ctx.vc.h, 1)
                    cnt2 += 1
                else:
                    _, hh, s0 = e
                    wsp = hh * TCH
                    av = z[:, s0 * TCH: s0 * TCH + wsp]
                    bv = z[:, s0 * TCH + wsp: s0 * TCH + 2 * wsp]
                    v.wait_ge(ctx.vc.h, cnt2)
                    v.tensor_tensor(out=bv, in0=av, in1=bv,
                                    op=ALU.subtract).then_inc(ctx.vc.h, 1)
                    cnt2 += 1
                    v.wait_ge(ctx.vc.h, cnt2)
                    v.scalar_tensor_tensor(out=av, in0=av, scalar=2.0, in1=bv,
                                           op0=ALU.mult, op1=ALU.subtract
                                           ).then_inc(ctx.vc.h, 1)
                    cnt2 += 1

        if "down" not in parts:
            continue
        zr = ctx.vc.v
        for n in range(DT * 2):
            swd[n % 2].plan(16)
        td = ctx.tc.plan(DT * 2)
        vd = ctx.vc.plan(DT)

        @block.gpsimd
        def _(g):
            for n in range(DT * 2):
                dt_, hf = n // 2, n % 2
                par = n % 2
                if n >= 2:
                    g.wait_ge(ctx.tc.h, td + n - 1)
                g.dma_start(out=wdb[:, par * HALF:(par + 1) * HALF],
                            in_=wd[dt_, :, hf * 32:(hf + 1) * 32, :].rearrange("p a q -> p (a q)")
                            ).then_inc(swd[par].h, 16)

        @block.tensor
        def _(te):
            te.wait_ge(ctx.vc.h, zr)
            for n in range(DT * 2):
                dt_, hf = n // 2, n % 2
                par = n % 2
                te.wait_ge(swd[par].h, (ch * DT + n // 2 + 1) * 16)
                if hf == 0 and dt_ >= 1:
                    te.wait_ge(ctx.vc.h, vd + dt_)
                pd = psD[:, 0:512]
                for a in range(32):
                    ih = hf * 32 + a
                    mm = te.matmul(pd, lhsT=wdb[:, par * HALF + a * 128: par * HALF + (a + 1) * 128],
                                   rhs=z[:, ih * TCH:(ih + 1) * TCH],
                                   start=(ih == 0), stop=(ih == IHI - 1))
                mm.then_inc(ctx.tc.h, 1)

        @block.vector
        def _(v):
            for dt_ in range(DT):
                v.wait_ge(ctx.tc.h, td + 2 * dt_ + 2)
                pd = psD[:, 0:512]
                v.tensor_scalar(out=foutT[:, dt_ * TCH:(dt_ + 1) * TCH], in0=pd,
                                scalar1=wse_d[:, dt_:dt_ + 1], scalar2=None,
                                op0=ALU.mult).then_inc(ctx.vc.h, 1)

        for tt in range(NTT):
            srl[tt % 2].plan(16)
            srs[tt % 2].plan(16)
        ta = ctx.tc.plan(NTT * DT)
        va = ctx.vc.plan(NTT * DT)

        @block.sync
        def _(sy):
            for tt in range(NTT):
                par = tt % 2
                gl = ch * (NTT // 2) + tt // 2
                if gl > 0:
                    sy.wait_ge(srs[par].h, gl * 16)
                sy.dma_start(out=win[:, par * D:(par + 1) * D],
                             in_=xin[t0 + tt * 128: t0 + (tt + 1) * 128, :]
                             ).then_inc(srl[par].h, 16)
                sy.wait_ge(ctx.vc.h, va + (tt + 1) * DT)
                sy.dma_start(out=xout[t0 + tt * 128: t0 + (tt + 1) * 128, :],
                             in_=win[:, par * D:(par + 1) * D]).then_inc(srs[par].h, 16)

        @block.tensor
        def _(te):
            n = 0
            for tt in range(NTT):
                for dt_ in range(DT):
                    te.wait_ge(ctx.vc.h, vd + dt_ + 1)
                    if n >= 1:
                        te.wait_ge(ctx.vc.h, va + n)
                    te.transpose(psT[:, 0:128],
                                 foutT[:, dt_ * TCH + tt * 128: dt_ * TCH + (tt + 1) * 128],
                                 ident[:]).then_inc(ctx.tc.h, 1)
                    n += 1

        @block.vector
        def _(v):
            n = 0
            for tt in range(NTT):
                par = tt % 2
                v.wait_ge(srl[par].h, (ch * (NTT // 2) + tt // 2 + 1) * 16)
                for dt_ in range(DT):
                    n += 1
                    v.wait_ge(ctx.tc.h, ta + n)
                    dst = win[:, par * D + dt_ * 128: par * D + (dt_ + 1) * 128]
                    v.tensor_tensor(out=dst, in0=dst, in1=psT[:, 0:128],
                                    op=ALU.add).then_inc(ctx.vc.h, 1)


def emit_butterflies(ctx, buf, btmp, nblk, tch):
    block = ctx.block
    h = 1
    stages = []
    while h < nblk:
        stages.append(h)
        h *= 2
    vb = ctx.vc.plan(2 * len(stages))

    @block.vector
    def _(v):
        n = vb
        for h in stages:
            G = nblk // (2 * h)
            w = h * tch
            view = buf[:, 0:nblk * tch].rearrange("p (g two w) -> p g two w", g=G, two=2)
            av = view[:, :, 0, :]
            bv = view[:, :, 1, :]
            v.wait_ge(ctx.vc.h, n)
            v.tensor_tensor(out=bv, in0=av, in1=bv, op=ALU.subtract).then_inc(ctx.vc.h, 1)
            n += 1
            v.wait_ge(ctx.vc.h, n)
            v.scalar_tensor_tensor(out=av, in0=av, scalar=2.0, in1=bv,
                                   op0=ALU.mult, op1=ALU.subtract).then_inc(ctx.vc.h, 1)
            n += 1


def build_L1(stages=("ln1", "ffn", "ln2")):
    nc = bass.Bass("TRN2", target_bir_lowering=False, debug=False)
    x = nc.dram_tensor("x", [TOK, D], F32, kind="ExternalInput").ap()
    wu = nc.dram_tensor("wu", [32, 128, 2, DT, 128], F32, kind="ExternalInput").ap()
    wg = nc.dram_tensor("wg", [32, 128, 2, DT, 128], F32, kind="ExternalInput").ap()
    wd = nc.dram_tensor("wd", [DT, 128, IHI, 128], F32, kind="ExternalInput").ap()
    su_ug_d = nc.dram_tensor("su_ug", [128, 16], F32, kind="ExternalInput").ap()
    su_dp_d = nc.dram_tensor("su_dp", [128, 64], F32, kind="ExternalInput").ap()
    ws_up_d = nc.dram_tensor("ws_up", [128, 64], F32, kind="ExternalInput").ap()
    ws_gt_d = nc.dram_tensor("ws_gt", [128, 64], F32, kind="ExternalInput").ap()
    ws_dp_d = nc.dram_tensor("ws_dp", [128, 16], F32, kind="ExternalInput").ap()
    h128_d = nc.dram_tensor("h128s", [128, 128], F32, kind="ExternalInput").ap()
    id_d = nc.dram_tensor("ident", [128, 128], F32, kind="ExternalInput").ap()
    cv_d = nc.dram_tensor("consts", [128, 4], F32, kind="ExternalInput").ap()
    x1_d = nc.dram_tensor("x1", [TOK, D], F32, kind="ExternalOutput").ap()
    xn2_d = nc.dram_tensor("xn2", [TOK, D], BF16, kind="ExternalOutput").ap()
    xn_i = nc.dram_tensor("xn_i", [TOK, D], BF16, kind="Internal").ap()

    with ExitStack() as ES:
        ctx = Ctx(nc, ES)
        block = ctx.block
        h128s = ctx.sb(ES, "h128", (128, 128), BF16)
        ident = ctx.sb(ES, "id", (128, 128), BF16)
        cvec = ctx.sb(ES, "cv", (128, 4))
        lc = ctx.lg.plan(3 * 16)

        @block.gpsimd
        def _(g):
            g.dma_start(out=h128s[:], in_=h128_d[:]).then_inc(ctx.lg.h, 16)
            g.dma_start(out=ident[:], in_=id_d[:]).then_inc(ctx.lg.h, 16)
            g.dma_start(out=cvec[:], in_=cv_d[:]).then_inc(ctx.lg.h, 16)

        ctx.join()

        if "ln1" in stages:
            with ExitStack() as S1:
                emit_ln_stream(ctx, S1, x, xn_i, NT, cvec)
                ctx.join()
        if "ffn" in stages:
            fp = tuple(s[1:] for s in stages if s.startswith("_"))
            parts = fp if fp else ("y1", "z", "fwht", "down")
            with ExitStack() as S2:
                emit_ffn0(ctx, S2, x, x1_d, xn_i, NT, wu, wg, wd, su_ug_d, su_dp_d,
                          ws_up_d, ws_gt_d, ws_dp_d, h128s, ident, parts=parts)
                ctx.join()
        if "ln2" in stages:
            with ExitStack() as S3:
                emit_ln_stream(ctx, S3, x1_d, xn2_d, NT, cvec)
                ctx.join()

        ctx.join()

    return nc


SEQ = 4096
NTS = SEQ // 128


def build_L2():
    nc = bass.Bass("TRN2", target_bir_lowering=False, debug=False)
    xq_d = nc.dram_tensor("xq", [NTS, DT, 128, 128], BF16, kind="ExternalInput").ap()
    qkvw_d = nc.dram_tensor("qkvw", [4, DT, 128, 384], F32, kind="ExternalInput").ap()
    slope_d = nc.dram_tensor("slope", [1, 4], F32, kind="ExternalInput").ap()
    ones_d = nc.dram_tensor("ones1", [1, 128], F32, kind="ExternalInput").ap()
    arrq_d = nc.dram_tensor("arrq", [128, 2], F32, kind="ExternalInput").ap()
    arrk_d = nc.dram_tensor("arrk", [128, 2], F32, kind="ExternalInput").ap()
    idxT_d = nc.dram_tensor("idxT", [2, 128, 256], F32, kind="ExternalInput").ap()
    mskT_d = nc.dram_tensor("mskT", [2, 128, 256], F32, kind="ExternalInput").ap()
    id_d = nc.dram_tensor("ident", [128, 128], F32, kind="ExternalInput").ap()
    out_d = nc.dram_tensor("outT", [512, SEQ], F32, kind="ExternalOutput").ap()

    NBLK = 16

    with ExitStack() as ES:
        ctx = Ctx(nc, ES)
        block = ctx.block
        ident = ctx.sb(ES, "id", (128, 128), BF16)
        slope = ctx.sb(ES, "slp", (1, 4))
        ones1 = ctx.sb(ES, "on1", (1, 128))
        arrq = ctx.sb(ES, "arq", (128, 2))
        arrk = ctx.sb(ES, "ark", (128, 2))
        idxT = ctx.sb(ES, "idx", (128, 2 * 256))
        mskT = ctx.sb(ES, "msk", (128, 2 * 256))
        negs = ctx.sb(ES, "ngs", (128, 4))
        xqb = ctx.sb(ES, "xqb", (128, 2 * DT * 128), BF16)
        qwb = ctx.sb(ES, "qwb", (128, 2 * DT * 384), BF16)
        qkvt = ctx.sb(ES, "qkvt", (128, NTS * 384), BF16)
        qsc = ctx.sb(ES, "qsc", (128, 256), BF16)
        ksc = ctx.sb(ES, "ksc", (128, 256), BF16)
        qT = ctx.sb(ES, "qT", (128, 256), BF16)
        qsT = ctx.sb(ES, "qsT", (128, 256), BF16)
        kT = ctx.sb(ES, "kT", (128, 256), BF16)
        ssc = ctx.sb(ES, "ssc", (128, 2 * 256), BF16)
        decT = ctx.sb(ES, "decT", (128, 2 * 256))
        qdec = ctx.sb(ES, "qdec", (128, 2))
        kdec = ctx.sb(ES, "kdec", (128, 2))
        bdcol = ctx.sb(ES, "bdc", (128, 1))
        kv = ctx.sb(ES, "kv", (128, 128))
        kvbf = ctx.sb(ES, "kvbf", (128, 128), BF16)
        outh = ctx.sb(ES, "outh", (128, SEQ))
        psQ = ctx.ps(ES, "psQ", (128, 2 * 512), F32)
        psTt = ctx.ps(ES, "psTt", (128, 2 * 1024), BF16)
        psS = ctx.ps(ES, "psS", (128, 2 * 512), F32)
        psO = ctx.ps(ES, "psO", (128, 512), F32)
        psKV = ctx.ps(ES, "psKV", (128, 512), F32)

        sxq = [ctx.new_sem("xqA"), ctx.new_sem("xqB")]
        sqw = [ctx.new_sem("qwA"), ctx.new_sem("qwB")]
        sout = ctx.new_sem("out")
        lcs = ctx.new_sem("lcs")
        lc0 = lcs.plan(6 * 16)
        lc = ctx.lg.plan(16)

        @block.sync
        def _(sy):
            sy.dma_start(out=slope[:], in_=slope_d[:]).then_inc(lcs.h, 16)
            sy.dma_start(out=ones1[:], in_=ones_d[:]).then_inc(lcs.h, 16)
            sy.dma_start(out=arrq[:], in_=arrq_d[:]).then_inc(lcs.h, 16)
            sy.dma_start(out=arrk[:], in_=arrk_d[:]).then_inc(lcs.h, 16)
            sy.dma_start(out=idxT[:].rearrange("p (u m) -> p u m", u=2),
                         in_=idxT_d.rearrange("u p m -> p u m")).then_inc(lcs.h, 16)
            sy.dma_start(out=mskT[:].rearrange("p (u m) -> p u m", u=2),
                         in_=mskT_d.rearrange("u p m -> p u m")).then_inc(lcs.h, 16)

        @block.gpsimd
        def _(g):
            g.dma_start(out=ident[:], in_=id_d[:]).then_inc(ctx.lg.h, 16)

        ctx.join()

        tb = ctx.tc.plan(1)
        vbb = ctx.vc.plan(1)

        @block.tensor
        def _(te):
            te.matmul(psO[:, 0:4], lhsT=ones1[:], rhs=slope[:], start=True,
                      stop=True).then_inc(ctx.tc.h, 1)

        @block.vector
        def _(v):
            v.wait_ge(ctx.tc.h, tb + 1)
            v.tensor_scalar(out=negs[:], in0=psO[:, 0:4], scalar1=-1.0, scalar2=None,
                            op0=ALU.mult).then_inc(ctx.vc.h, 1)

        ctx.join()

        hq_base = []
        for h in range(4):
            head_tc0 = ctx.tc.v
            head_vc0 = ctx.vc.v
            head_sc0 = ctx.sc.v

            c0 = ctx.sc.plan(4)
            v0 = ctx.vc.plan(2)

            @block.scalar
            def _(s, h=h, head_vc0=head_vc0):
                s.wait_ge(ctx.vc.h, head_vc0)
                s.activation(out=qdec[:], in_=arrq[:], func=AF.Exp, bias=0.0,
                             scale=negs[:, h:h + 1]).then_inc(ctx.sc.h, 1)
                s.activation(out=kdec[:], in_=arrk[:], func=AF.Exp, bias=0.0,
                             scale=negs[:, h:h + 1]).then_inc(ctx.sc.h, 1)
                s.activation(out=bdcol[:], in_=negs[:, h:h + 1], func=AF.Exp, bias=0.0,
                             scale=256.0).then_inc(ctx.sc.h, 1)
                s.activation(out=decT[:], in_=idxT[:], func=AF.Exp, bias=0.0,
                             scale=negs[:, h:h + 1]).then_inc(ctx.sc.h, 1)

            @block.vector
            def _(v, c0=c0, v0=v0):
                v.wait_ge(ctx.sc.h, c0 + 4)
                v.tensor_tensor(out=decT[:], in0=decT[:], in1=mskT[:],
                                op=ALU.mult).then_inc(ctx.vc.h, 1)
                v.wait_ge(ctx.vc.h, v0 + 1)
                v.memset(kv[:], 0.0)
                v.nop().then_inc(ctx.vc.h, 1)

            hw = sqw[h % 2].plan(16)

            @block.gpsimd
            def _(g, h=h):
                par = h % 2
                if h >= 2:
                    g.wait_ge(ctx.tc.h, hq_base[h - 2] + NTS)
                g.dma_start(out=qwb[:, par * DT * 384:(par + 1) * DT * 384].rearrange(
                                "p (a q) -> p a q", a=DT),
                            in_=qkvw_d[h].rearrange("a p q -> p a q")
                            ).then_inc(sqw[par].h, 16)

            tq = ctx.tc.plan(NTS)
            hq_base.append(tq)
            cq = ctx.sc.plan(NTS)

            @block.sync
            def _(sy, h=h):
                for tt in range(NTS):
                    par = tt % 2
                    seq_no = h * (NTS // 2) + tt // 2
                    if seq_no > 0:
                        if tt >= 2:
                            need = hq_base[h] + tt - 1
                        else:
                            need = hq_base[h - 1] + NTS - 2 + par + 1
                        sy.wait_ge(ctx.tc.h, need)
                    sy.dma_start(out=xqb[:, par * DT * 128:(par + 1) * DT * 128].rearrange(
                                     "p (a q) -> p a q", a=DT),
                                 in_=xq_d[tt].rearrange("a p q -> p a q")
                                 ).then_inc(sxq[par].h, 16)
            for tt in range(NTS):
                sxq[tt % 2].plan(16)

            @block.tensor
            def _(te, h=h, tq=tq, cq=cq):
                par_h = h % 2
                te.wait_ge(sqw[h % 2].h, (h // 2 + 1) * 16)
                for tt in range(NTS):
                    par = tt % 2
                    te.wait_ge(sxq[par].h, (h * (NTS // 2) + tt // 2 + 1) * 16)
                    if tt >= 2:
                        te.wait_ge(ctx.sc.h, cq + tt - 1)
                    pq = psQ[:, (tt % 2) * 512:(tt % 2) * 512 + 384]
                    for T in range(DT):
                        mm = te.matmul(pq, lhsT=xqb[:, par * DT * 128 + T * 128: par * DT * 128 + (T + 1) * 128],
                                       rhs=qwb[:, par_h * DT * 384 + T * 384: par_h * DT * 384 + (T + 1) * 384],
                                       start=(T == 0), stop=(T == DT - 1))
                    mm.then_inc(ctx.tc.h, 1)

            @block.scalar
            def _(s, h=h, tq=tq, head_tc0=head_tc0):
                if h > 0:
                    s.wait_ge(ctx.tc.h, head_tc0)
                for tt in range(NTS):
                    s.wait_ge(ctx.tc.h, tq + tt + 1)
                    pq = psQ[:, (tt % 2) * 512:(tt % 2) * 512 + 384]
                    s.activation(out=qkvt[:, tt * 384:(tt + 1) * 384], in_=pq,
                                 func=AF.Silu, bias=0.0, scale=1.0).then_inc(ctx.sc.h, 1)

            for j in range(NBLK):
                tj = ctx.tc.plan(10)
                vj = ctx.vc.plan(14)

                @block.vector
                def _(v, h=h, j=j, tj=tj, vj=vj, cq=cq):
                    v.wait_ge(ctx.tc.h, tj)
                    v.wait_ge(ctx.sc.h, cq + 2 * j + 2)
                    for u in range(2):
                        col = (2 * j + u) * 384
                        v.tensor_scalar(out=qsc[:, u * 128:(u + 1) * 128],
                                        in0=qkvt[:, col:col + 128],
                                        scalar1=qdec[:, u:u + 1], scalar2=None,
                                        op0=ALU.mult).then_inc(ctx.vc.h, 1)
                        v.tensor_scalar(out=ksc[:, u * 128:(u + 1) * 128],
                                        in0=qkvt[:, col + 128:col + 256],
                                        scalar1=kdec[:, u:u + 1], scalar2=None,
                                        op0=ALU.mult).then_inc(ctx.vc.h, 1)
                    for i6 in range(6):
                        v.wait_ge(ctx.tc.h, tj + i6 + 1)
                        dst = [qT[:, 0:128], qT[:, 128:256], qsT[:, 0:128],
                               qsT[:, 128:256], kT[:, 0:128], kT[:, 128:256]][i6]
                        v.tensor_copy(dst, psTt[:, (i6 % 2) * 1024:(i6 % 2) * 1024 + 128]
                                      ).then_inc(ctx.vc.h, 1)
                    v.wait_ge(ctx.tc.h, tj + 8)
                    v.tensor_tensor(out=ssc[:, 0:256], in0=psS[:, 0:256],
                                    in1=decT[:, 0:256], op=ALU.mult)
                    v.tensor_tensor(out=ssc[:, 256:512], in0=psS[:, 512:768],
                                    in1=decT[:, 256:512], op=ALU.mult).then_inc(ctx.vc.h, 1)
                    v.wait_ge(ctx.tc.h, tj + 10)
                    v.scalar_tensor_tensor(out=kv[:], in0=kv[:], scalar=bdcol[:],
                                           in1=psKV[:, 0:128], op0=ALU.mult,
                                           op1=ALU.add).then_inc(ctx.vc.h, 1)
                    v.wait_ge(ctx.vc.h, vj + 12)
                    v.tensor_copy(kvbf[:], kv[:]).then_inc(ctx.vc.h, 1)
                    v.wait_ge(ctx.tc.h, tj + 9)
                    if j == 0:
                        v.wait_ge(sout.h, h * 16)
                    v.tensor_copy(outh[:, j * 256:(j + 1) * 256],
                                  psO[:, 0:256]).then_inc(ctx.vc.h, 1)

                @block.tensor
                def _(te, h=h, j=j, tj=tj, vj=vj, cq=cq):
                    te.wait_ge(ctx.sc.h, cq + 2 * j + 2)
                    srcs = [qkvt[:, (2 * j) * 384:(2 * j) * 384 + 128],
                            qkvt[:, (2 * j + 1) * 384:(2 * j + 1) * 384 + 128],
                            qsc[:, 0:128], qsc[:, 128:256],
                            qkvt[:, (2 * j) * 384 + 128:(2 * j) * 384 + 256],
                            qkvt[:, (2 * j + 1) * 384 + 128:(2 * j + 1) * 384 + 256]]
                    for i6, src in enumerate(srcs):
                        if i6 in (2, 3):
                            te.wait_ge(ctx.vc.h, vj + 3)
                        if i6 >= 2:
                            te.wait_ge(ctx.vc.h, vj + 4 + i6 - 1)
                        te.transpose(psTt[:, (i6 % 2) * 1024:(i6 % 2) * 1024 + 128],
                                     src, ident[:]).then_inc(ctx.tc.h, 1)
                    te.wait_ge(ctx.vc.h, vj + 10)
                    for u in range(2):
                        te.matmul(psS[:, u * 512:u * 512 + 256],
                                  lhsT=kT[:, u * 128:(u + 1) * 128],
                                  rhs=qT[:], start=True, stop=True).then_inc(ctx.tc.h, 1)
                    te.wait_ge(ctx.vc.h, vj + 11)
                    first = True
                    if j > 0:
                        te.matmul(psO[:, 0:256], lhsT=kvbf[:], rhs=qsT[:], start=True,
                                  stop=False)
                        first = False
                    for u in range(2):
                        mm = te.matmul(psO[:, 0:256],
                                       lhsT=qkvt[:, (2 * j + u) * 384 + 256:(2 * j + u) * 384 + 384],
                                       rhs=ssc[:, u * 256:(u + 1) * 256], start=first,
                                       stop=(u == 1))
                        first = False
                    mm.then_inc(ctx.tc.h, 1)
                    for u in range(2):
                        mm = te.matmul(psKV[:, 0:128], lhsT=ksc[:, u * 128:(u + 1) * 128],
                                       rhs=qkvt[:, (2 * j + u) * 384 + 256:(2 * j + u) * 384 + 384],
                                       start=(u == 0), stop=(u == 1))
                    mm.then_inc(ctx.tc.h, 1)

            so = sout.plan(16)
            hv_end = ctx.vc.v

            @block.sync
            def _(sy, h=h, hv_end=hv_end):
                sy.wait_ge(ctx.vc.h, hv_end)
                sy.dma_start(out=out_d[h * 128:(h + 1) * 128, :],
                             in_=outh[:]).then_inc(sout.h, 16)

        ctx.join()

    return nc




def emit_attn_finish(ctx, S, x1, outT_d, xn2v_d, wgt_d, wot_d, nw_d, x2, cvec, ident):
    block = ctx.block
    xn2v = ctx.sb(S, "xn2v", (128, DT * TOK), BF16)
    goT = ctx.sb(S, "goT", (128, DT * TOK), BF16)
    aoutT = ctx.sb(S, "aoutT", (128, DT * TOK), BF16)
    owin = ctx.sb(S, "owin", (128, 2 * TOK))
    sq = ctx.sb(S, "sqa", (128, 2 * TOK))
    tmp = ctx.sb(S, "tmpa", (128, TOK))
    rrow = ctx.sb(S, "rrow", (1, TOK))
    nw = ctx.sb(S, "nw", (128, 16))
    wgb = ctx.sb(S, "wgb", (128, 2 * DT * 128), BF16)
    wob = ctx.sb(S, "wob", (128, 2 * DT * 128), BF16)
    win = ctx.sb(S, "awin", (128, 2 * D))
    ones1 = ctx.sb(S, "aon1", (1, 128))
    onec = ctx.sb(S, "aonc", (128, 1))
    psW = ctx.ps(S, "psW", (128, 2 * 512), F32)
    psR = ctx.ps(S, "psR", (1, 2 * 512), F32)
    psB = ctx.ps(S, "psB", (128, 2 * 512), F32)
    psT = ctx.ps(S, "psT2", (128, 2 * 1024), BF16)

    sxv = ctx.new_sem("axv")
    swg = [ctx.new_sem("awg"), ctx.new_sem("awg")]
    swo = [ctx.new_sem("awo"), ctx.new_sem("awo")]
    sow = [ctx.new_sem("aow"), ctx.new_sem("aow")]
    srl = [ctx.new_sem("arl"), ctx.new_sem("arl")]
    srs = [ctx.new_sem("ars"), ctx.new_sem("ars")]

    lv = sxv.plan(2 * 16)

    @block.sync
    def _(sy):
        sy.dma_start(out=xn2v[:].rearrange("p (a q) -> p a q", a=DT),
                     in_=xn2v_d[:].rearrange("a p q -> p a q")).then_inc(sxv.h, 16)
        sy.dma_start(out=nw[:], in_=nw_d[:]).then_inc(sxv.h, 16)

    v_on = ctx.vc.plan(2)

    @block.vector
    def _(v):
        v.memset(ones1[:], 1.0).then_inc(ctx.vc.h, 1)
        v.memset(onec[:], 1.0).then_inc(ctx.vc.h, 1)

    tg = ctx.tc.plan(DT * 2)
    cg = ctx.sc.plan(DT * 2)

    @block.gpsimd
    def _(g):
        for F in range(DT):
            par = F % 2
            if F >= 2:
                g.wait_ge(ctx.tc.h, tg + 2 * (F - 1))
            g.dma_start(out=wgb[:, par * DT * 128:(par + 1) * DT * 128].rearrange(
                            "p (a q) -> p a q", a=DT),
                        in_=wgt_d[F].rearrange("a p q -> p a q")).then_inc(swg[par].h, 16)
    for F in range(DT):
        swg[F % 2].plan(16)

    @block.tensor
    def _(te):
        te.wait_ge(sxv.h, 32)
        for F in range(DT):
            par = F % 2
            te.wait_ge(swg[par].h, (F // 2 + 1) * 16)
            for half in range(2):
                n2 = 2 * F + half
                if n2 >= 2:
                    te.wait_ge(ctx.sc.h, cg + n2 - 1)
                pw = psW[:, (n2 % 2) * 512:(n2 % 2) * 512 + 512]
                for T in range(DT):
                    mm = te.matmul(pw, lhsT=wgb[:, par * DT * 128 + T * 128: par * DT * 128 + (T + 1) * 128],
                                   rhs=xn2v[:, T * TOK + half * 512: T * TOK + half * 512 + 512],
                                   start=(T == 0), stop=(T == DT - 1))
                mm.then_inc(ctx.tc.h, 1)

    @block.scalar
    def _(s):
        for F in range(DT):
            for half in range(2):
                n2 = 2 * F + half
                s.wait_ge(ctx.tc.h, tg + n2 + 1)
                pw = psW[:, (n2 % 2) * 512:(n2 % 2) * 512 + 512]
                s.activation(out=goT[:, F * TOK + half * 512: F * TOK + half * 512 + 512],
                             in_=pw, func=AF.Sigmoid, bias=0.0,
                             scale=1.0).then_inc(ctx.sc.h, 1)

    t_r = ctx.tc.plan(DT * 2)
    c_r = ctx.sc.plan(DT)
    v_r = ctx.vc.plan(0)

    @block.sync
    def _(sy):
        for T in range(DT):
            par = T % 2
            if T >= 2:
                sy.wait_ge(ctx.sc.h, c_r + T - 1)
            sy.dma_start(out=owin[:, par * TOK:(par + 1) * TOK],
                         in_=outT_d[T * 128:(T + 1) * 128, :]).then_inc(sow[par].h, 16)
    for T in range(DT):
        sow[T % 2].plan(16)

    @block.scalar
    def _(s):
        for T in range(DT):
            par = T % 2
            s.wait_ge(sow[par].h, (T // 2 + 1) * 16)
            if T >= 2:
                s.wait_ge(ctx.tc.h, t_r + 2 * (T - 1))
            s.activation(out=sq[:, par * TOK:(par + 1) * TOK],
                         in_=owin[:, par * TOK:(par + 1) * TOK],
                         func=AF.Square, bias=0.0, scale=1.0).then_inc(ctx.sc.h, 1)

    @block.tensor
    def _(te):
        te.wait_ge(ctx.vc.h, v_on + 2)
        for T in range(DT):
            par = T % 2
            te.wait_ge(ctx.sc.h, c_r + T + 1)
            for half in range(2):
                te.matmul(psR[0:1, half * 512:half * 512 + 512], lhsT=onec[:],
                          rhs=sq[:, par * TOK + half * 512: par * TOK + half * 512 + 512],
                          start=(T == 0), stop=(T == DT - 1)).then_inc(ctx.tc.h, 1)

    c_s = ctx.sc.plan(2)
    t_b = ctx.tc.plan(2)

    @block.scalar
    def _(s):
        s.wait_ge(ctx.tc.h, t_r + DT * 2)
        s.activation(out=rrow[:], in_=psR[0:1, :], func=AF.Ln, bias=cvec[0:1, 2:3],
                     scale=1.0 / D).then_inc(ctx.sc.h, 1)
        s.wait_ge(ctx.sc.h, c_s + 1)
        s.activation(out=rrow[:], in_=rrow[:], func=AF.Exp, bias=0.0,
                     scale=-0.5).then_inc(ctx.sc.h, 1)

    @block.tensor
    def _(te):
        te.wait_ge(ctx.sc.h, c_s + 2)
        for half in range(2):
            te.matmul(psB[:, half * 512:half * 512 + 512], lhsT=ones1[:],
                      rhs=rrow[0:1, half * 512:half * 512 + 512], start=True,
                      stop=True).then_inc(ctx.tc.h, 1)

    v_g = ctx.vc.plan(DT * 2)
    c_g2 = ctx.sc.plan(0)

    @block.sync
    def _(sy):
        sy.wait_ge(ctx.sc.h, c_r + DT)
        for T in range(DT):
            par = T % 2
            if T >= 2:
                sy.wait_ge(ctx.vc.h, v_g + 2 * (T - 1))
            sy.dma_start(out=owin[:, par * TOK:(par + 1) * TOK],
                         in_=outT_d[T * 128:(T + 1) * 128, :]).then_inc(sow[par].h, 16)
    for T in range(DT):
        sow[T % 2].plan(16)

    @block.vector
    def _(v):
        v.wait_ge(ctx.tc.h, t_b + 2)
        for T in range(DT):
            par = T % 2
            v.wait_ge(sow[par].h, (DT // 2 + T // 2 + 1) * 16)
            v.wait_ge(ctx.sc.h, cg + 2 * T + 2)
            if T >= 1:
                v.wait_ge(ctx.vc.h, v_g + 2 * T)
            v.scalar_tensor_tensor(out=tmp[:], in0=owin[:, par * TOK:(par + 1) * TOK],
                                   scalar=nw[:, T:T + 1], in1=psB[:],
                                   op0=ALU.mult, op1=ALU.mult).then_inc(ctx.vc.h, 1)
            v.wait_ge(ctx.vc.h, v_g + 2 * T + 1)
            v.tensor_tensor(out=goT[:, T * TOK:(T + 1) * TOK], in0=tmp[:],
                            in1=goT[:, T * TOK:(T + 1) * TOK],
                            op=ALU.mult).then_inc(ctx.vc.h, 1)

    t_w = ctx.tc.plan(DT * 2)
    v_w = ctx.vc.plan(DT * 2)

    @block.gpsimd
    def _(g):
        for dto in range(DT):
            par = dto % 2
            if dto >= 2:
                g.wait_ge(ctx.tc.h, t_w + 2 * (dto - 1))
            g.dma_start(out=wob[:, par * DT * 128:(par + 1) * DT * 128].rearrange(
                            "p (a q) -> p a q", a=DT),
                        in_=wot_d[dto]).then_inc(swo[par].h, 16)
    for dto in range(DT):
        swo[dto % 2].plan(16)

    @block.tensor
    def _(te):
        te.wait_ge(ctx.vc.h, v_w)
        for dto in range(DT):
            par = dto % 2
            te.wait_ge(swo[par].h, (dto // 2 + 1) * 16)
            for half in range(2):
                n2 = 2 * dto + half
                if n2 >= 2:
                    te.wait_ge(ctx.vc.h, v_w + n2 - 1)
                pw = psW[:, (n2 % 2) * 512:(n2 % 2) * 512 + 512]
                for T in range(DT):
                    mm = te.matmul(pw, lhsT=wob[:, par * DT * 128 + T * 128: par * DT * 128 + (T + 1) * 128],
                                   rhs=goT[:, T * TOK + half * 512: T * TOK + half * 512 + 512],
                                   start=(T == 0), stop=(T == DT - 1))
                mm.then_inc(ctx.tc.h, 1)

    @block.vector
    def _(v):
        for dto in range(DT):
            for half in range(2):
                n2 = 2 * dto + half
                v.wait_ge(ctx.tc.h, t_w + n2 + 1)
                pw = psW[:, (n2 % 2) * 512:(n2 % 2) * 512 + 512]
                v.tensor_copy(aoutT[:, dto * TOK + half * 512: dto * TOK + half * 512 + 512],
                              pw).then_inc(ctx.vc.h, 1)

    t_a = ctx.tc.plan(NT * DT)
    v_a = ctx.vc.plan(NT * DT)
    for tt in range(NT):
        srl[tt % 2].plan(16)
        srs[tt % 2].plan(16)

    @block.sync
    def _(sy):
        for tt in range(NT):
            par = tt % 2
            if tt >= 2:
                sy.wait_ge(srs[par].h, (tt // 2) * 16)
            sy.dma_start(out=win[:, par * D:(par + 1) * D],
                         in_=x1[tt * 128:(tt + 1) * 128, :]).then_inc(srl[par].h, 16)
            sy.wait_ge(ctx.vc.h, v_a + (tt + 1) * DT)
            sy.dma_start(out=x2[tt * 128:(tt + 1) * 128, :],
                         in_=win[:, par * D:(par + 1) * D]).then_inc(srs[par].h, 16)

    @block.tensor
    def _(te):
        te.wait_ge(ctx.vc.h, v_a)
        n = 0
        for tt in range(NT):
            for dto in range(DT):
                if n >= 2:
                    te.wait_ge(ctx.vc.h, v_a + n - 1)
                te.transpose(psT[:, (n % 2) * 1024:(n % 2) * 1024 + 128],
                             aoutT[:, dto * TOK + tt * 128: dto * TOK + (tt + 1) * 128],
                             ident[:]).then_inc(ctx.tc.h, 1)
                n += 1

    @block.vector
    def _(v):
        n = 0
        for tt in range(NT):
            par = tt % 2
            v.wait_ge(srl[par].h, (tt // 2 + 1) * 16)
            for dto in range(DT):
                n += 1
                v.wait_ge(ctx.tc.h, t_a + n)
                dst = win[:, par * D + dto * 128: par * D + (dto + 1) * 128]
                v.tensor_tensor(out=dst, in0=dst,
                                in1=psT[:, ((n - 1) % 2) * 1024:((n - 1) % 2) * 1024 + 128],
                                op=ALU.add).then_inc(ctx.vc.h, 1)


def emit_moe(ctx, S, x3, x4, xn4, w1_d, w2_d, b1_d, b2_d, cent_d, bias_d,
             iota_d, ident, ident32):
    block = ctx.block
    xnT = ctx.sb(S, "mxnT", (128, DT * TOK), BF16)
    hT = ctx.sb(S, "mhT", (128, FT * TOK), BF16)
    acc = ctx.sb(S, "macc", (128, NT * D))
    w1b = ctx.sb(S, "mw1", (128, 2 * DT * 128), BF16)
    w2b = ctx.sb(S, "mw2", (128, FT * D), BF16)
    b1c = ctx.sb(S, "mb1", (128, NEXP * FT))
    b2b = ctx.sb(S, "mb2", (18, D), BF16)
    cent = ctx.sb(S, "mcent", (128, DT * 16), BF16)
    biasr = ctx.sb(S, "mbias", (128, 16))
    iota = ctx.sb(S, "miota", (128, 16))
    sct = ctx.sb(S, "msct", (128, NT * 16))
    G = ctx.sb(S, "mG", (128, NT * NEXP))
    GT = ctx.sb(S, "mGT", (18, TOK), BF16)
    work = ctx.sb(S, "mwork", (128, 16))
    m1 = ctx.sb(S, "mm1", (128, 1))
    msk = ctx.sb(S, "mmsk", (128, 16))
    cand = ctx.sb(S, "mcand", (128, 16))
    idxs = ctx.sb(S, "midx", (128, 1))
    selm = ctx.sb(S, "mselm", (128, 16))
    gsum = ctx.sb(S, "mgsum", (128, 1))
    ginv = ctx.sb(S, "mginv", (128, 1))
    win = ctx.sb(S, "mwin", (128, 2 * D))
    xw = ctx.sb(S, "mxw", (128, 2 * D), BF16)
    psH = ctx.ps(S, "psH", (128, 2 * 512), F32)
    psY = ctx.ps(S, "psY", (128, 2 * 512), F32)
    psSc = ctx.ps(S, "psSc", (128, 512), F32)
    psG = ctx.ps(S, "psGt", (128, 512), F32)
    psT = ctx.ps(S, "psTm", (128, 2 * 1024), BF16)

    sxv = ctx.new_sem("mxv")
    sw1 = [ctx.new_sem("mw1"), ctx.new_sem("mw1")]
    sw2 = ctx.new_sem("mw2s")
    sxt = [ctx.new_sem("mxt"), ctx.new_sem("mxt")]
    srl = [ctx.new_sem("mrl"), ctx.new_sem("mrl")]
    srs = [ctx.new_sem("mrs"), ctx.new_sem("mrs")]

    lv = sxv.plan(3 * 16)
    lvg = ctx.lg.plan(2 * 16)

    @block.sync
    def _(sy):
        sy.dma_start(out=b1c[:].rearrange("p (e f) -> p e f", e=NEXP),
                     in_=b1_d[:].rearrange("e p f -> p e f")).then_inc(sxv.h, 16)
        sy.dma_start(out=biasr[:], in_=bias_d[:]).then_inc(sxv.h, 16)
        sy.dma_start(out=iota[:], in_=iota_d[:]).then_inc(sxv.h, 16)

    @block.gpsimd
    def _(g):
        g.dma_start(out=b2b[:], in_=b2_d[:]).then_inc(ctx.lg.h, 16)
        g.dma_start(out=cent[:].rearrange("p (a q) -> p a q", a=DT),
                    in_=cent_d[:].rearrange("a p q -> p a q")).then_inc(ctx.lg.h, 16)

    t_x = ctx.tc.plan(NT * DT)
    v_x = ctx.vc.plan(NT * (1 + DT))
    for tt in range(NT):
        sxt[tt % 2].plan(16)

    @block.sync
    def _(sy):
        for tt in range(NT):
            par = tt % 2
            if tt >= 2:
                sy.wait_ge(ctx.vc.h, v_x + (tt - 1) * (1 + DT))
            sy.dma_start(out=xw[:, par * D:(par + 1) * D],
                         in_=xn4[tt * 128:(tt + 1) * 128, :]).then_inc(sxt[par].h, 16)

    @block.vector
    def _(v):
        n = 0
        for tt in range(NT):
            par = tt % 2
            v.wait_ge(sxt[par].h, (tt // 2 + 1) * 16)
            v.tensor_scalar(out=acc[:, tt * D:(tt + 1) * D], in0=xw[:, par * D:(par + 1) * D],
                            scalar1=0.5, scalar2=None, op0=ALU.mult).then_inc(ctx.vc.h, 1)
            n += 1
            for d_ in range(DT):
                n += 1
                v.wait_ge(ctx.tc.h, t_x + tt * DT + d_ + 1)
                v.tensor_copy(xnT[:, d_ * TOK + tt * 128: d_ * TOK + (tt + 1) * 128],
                              psT[:, ((tt * DT + d_) % 2) * 1024:((tt * DT + d_) % 2) * 1024 + 128]
                              ).then_inc(ctx.vc.h, 1)

    @block.tensor
    def _(te):
        te.wait_ge(ctx.vc.h, t_x * 0 + v_x)
        n = 0
        for tt in range(NT):
            par = tt % 2
            te.wait_ge(sxt[par].h, (tt // 2 + 1) * 16)
            for d_ in range(DT):
                if n >= 2:
                    prev = n - 2
                    ptt, pd = prev // DT, prev % DT
                    te.wait_ge(ctx.vc.h, v_x + ptt * (1 + DT) + 1 + pd + 1)
                te.transpose(psT[:, (n % 2) * 1024:(n % 2) * 1024 + 128],
                             xw[:, par * D + d_ * 128: par * D + (d_ + 1) * 128],
                             ident[:]).then_inc(ctx.tc.h, 1)
                n += 1

    xnT_done = ctx.vc.v
    t_s = ctx.tc.plan(NT)
    c_s = ctx.sc.plan(NT)
    v_s = ctx.vc.plan(NT * 39)

    @block.tensor
    def _(te):
        te.wait_ge(sxv.h, 48)
        te.wait_ge(ctx.lg.h, lvg + 32)
        te.wait_ge(ctx.vc.h, xnT_done)
        for tt in range(NT):
            if tt >= 1:
                te.wait_ge(ctx.sc.h, c_s + tt)
            for T in range(DT):
                mm = te.matmul(psSc[:, 0:16],
                               lhsT=xnT[:, T * TOK + tt * 128: T * TOK + (tt + 1) * 128],
                               rhs=cent[:, T * 16:(T + 1) * 16],
                               start=(T == 0), stop=(T == DT - 1))
            mm.then_inc(ctx.tc.h, 1)

    @block.scalar
    def _(s):
        for tt in range(NT):
            s.wait_ge(ctx.tc.h, t_s + tt + 1)
            s.activation(out=sct[:, tt * 16:(tt + 1) * 16], in_=psSc[:, 0:16],
                         func=AF.Sigmoid, bias=0.0, scale=1.0).then_inc(ctx.sc.h, 1)

    @block.vector
    def _(v):
        nv = v_s
        v.wait_ge(sxv.h, 48)
        v.wait_ge(ctx.lg.h, lvg + 32)
        for tt in range(NT):
            gt = G[:, tt * NEXP:(tt + 1) * NEXP]
            v.wait_ge(ctx.sc.h, c_s + tt + 1)

            def step(instr):
                nonlocal nv
                instr.then_inc(ctx.vc.h, 1)
                nv += 1
                v.wait_ge(ctx.vc.h, nv)
            step(v.tensor_tensor(out=work[:], in0=sct[:, tt * 16:(tt + 1) * 16],
                                 in1=biasr[:], op=ALU.add))
            step(v.memset(gt, 0.0))
            step(v.tensor_scalar(out=gt[:, 0:2], in0=gt[:, 0:2], scalar1=0.0,
                                 scalar2=0.5, op0=ALU.mult, op1=ALU.add))
            for k in range(4):
                step(v.tensor_reduce(m1[:], work[:], axis=AX.X, op=ALU.max))
                step(v.tensor_scalar(out=msk[:], in0=work[:], scalar1=m1[:],
                                     scalar2=None, op0=ALU.is_equal))
                step(v.scalar_tensor_tensor(out=cand[:], in0=iota[:], scalar=-999.0,
                                            in1=msk[:], op0=ALU.add, op1=ALU.mult))
                step(v.tensor_scalar(out=cand[:], in0=cand[:], scalar1=999.0,
                                     scalar2=None, op0=ALU.add))
                step(v.tensor_reduce(idxs[:], cand[:], axis=AX.X, op=ALU.min))
                step(v.tensor_scalar(out=selm[:], in0=iota[:], scalar1=idxs[:],
                                     scalar2=None, op0=ALU.is_equal))
                step(v.scalar_tensor_tensor(out=gt[:, 2:18], in0=selm[:], scalar=m1[:],
                                            in1=gt[:, 2:18], op0=ALU.mult, op1=ALU.add))
                if k < 3:
                    step(v.scalar_tensor_tensor(out=work[:], in0=selm[:], scalar=-1e9,
                                                in1=work[:], op0=ALU.mult, op1=ALU.add))
                else:
                    step(v.nop())
            step(v.tensor_reduce(gsum[:], gt[:, 2:18], axis=AX.X, op=ALU.add))
            step(v.tensor_scalar(out=gsum[:], in0=gsum[:], scalar1=1e-8, scalar2=None,
                                 op0=ALU.max))
            step(v.reciprocal(ginv[:], gsum[:]))
            step(v.tensor_scalar(out=gt[:, 2:18], in0=gt[:, 2:18], scalar1=ginv[:],
                                 scalar2=0.5, op0=ALU.mult, op1=ALU.mult))

    t_g = ctx.tc.plan(NT)
    v_gt = ctx.vc.plan(NT)

    @block.tensor
    def _(te):
        te.wait_ge(ctx.vc.h, v_gt)
        for tt in range(NT):
            if tt >= 1:
                te.wait_ge(ctx.vc.h, v_gt + tt)
            te.transpose(psG[0:NEXP, 0:128], G[:, tt * NEXP:(tt + 1) * NEXP],
                         ident32[:]).then_inc(ctx.tc.h, 1)

    @block.vector
    def _(v):
        for tt in range(NT):
            v.wait_ge(ctx.tc.h, t_g + tt + 1)
            v.tensor_copy(GT[:, tt * 128:(tt + 1) * 128],
                          psG[0:NEXP, 0:128]).then_inc(ctx.vc.h, 1)


    sw1_cnt = [0, 0]
    prev_ty = None
    for e in range(NEXP):
        t_h = ctx.tc.plan(FT * 2)
        c_h = ctx.sc.plan(FT * 2)
        sw1_need = []
        for ft in range(FT):
            sw1[ft % 2].plan(16)
            sw1_cnt[ft % 2] += 1
            sw1_need.append(sw1_cnt[ft % 2])
        sw2.plan(16)

        @block.gpsimd
        def _(g, e=e, t_h=t_h, prev_ty=prev_ty):
            for ft in range(FT):
                par = ft % 2
                if ft >= 2:
                    g.wait_ge(ctx.tc.h, t_h + 2 * (ft - 1))
                elif e > 0:
                    g.wait_ge(ctx.tc.h, prev_ty)
                g.dma_start(out=w1b[:, par * DT * 128:(par + 1) * DT * 128].rearrange(
                                "p (a q) -> p a q", a=DT),
                            in_=w1_d[e, ft]).then_inc(sw1[par].h, 16)
            if e > 0:
                g.wait_ge(ctx.tc.h, t_h)
            g.dma_start(out=w2b[:].rearrange("p (a q) -> p a q", a=FT),
                        in_=w2_d[e].rearrange("(a p) q -> p a q", p=128)
                        ).then_inc(sw2.h, 16)

        @block.tensor
        def _(te, e=e, t_h=t_h, c_h=c_h, sw1_need=sw1_need):
            te.wait_ge(ctx.sc.h, c_h)
            for ft in range(FT):
                par = ft % 2
                te.wait_ge(sw1[par].h, sw1_need[ft] * 16)
                for half in range(2):
                    n2 = 2 * ft + half
                    if n2 >= 2:
                        te.wait_ge(ctx.sc.h, c_h + n2 - 1)
                    ph = psH[:, (n2 % 2) * 512:(n2 % 2) * 512 + 512]
                    for T in range(DT):
                        mm = te.matmul(ph, lhsT=w1b[:, par * DT * 128 + T * 128: par * DT * 128 + (T + 1) * 128],
                                       rhs=xnT[:, T * TOK + half * 512: T * TOK + half * 512 + 512],
                                       start=(T == 0), stop=(T == DT - 1))
                    mm.then_inc(ctx.tc.h, 1)

        @block.scalar
        def _(s, e=e, t_h=t_h):
            s.wait_ge(ctx.tc.h, t_h)
            for ft in range(FT):
                for half in range(2):
                    n2 = 2 * ft + half
                    s.wait_ge(ctx.tc.h, t_h + n2 + 1)
                    ph = psH[:, (n2 % 2) * 512:(n2 % 2) * 512 + 512]
                    s.activation(out=hT[:, ft * TOK + half * 512: ft * TOK + half * 512 + 512],
                                 in_=ph, func=AF.Relu, bias=b1c[:, e * FT + ft:e * FT + ft + 1],
                                 scale=1.0).then_inc(ctx.sc.h, 1)

        prev_ty = t_h + FT * 2
        t_y = ctx.tc.plan(NT * 4)
        v_y = ctx.vc.plan(NT * 4)

        @block.tensor
        def _(te, e=e, t_y=t_y, v_y=v_y, c_h=c_h):
            te.wait_ge(ctx.sc.h, c_h + FT * 2)
            te.wait_ge(ctx.vc.h, v_y)
            te.wait_ge(sw2.h, (e + 1) * 16)
            n = 0
            for tt in range(NT):
                for chk in range(4):
                    if n >= 2:
                        te.wait_ge(ctx.vc.h, v_y + n - 1)
                    py = psY[:, (n % 2) * 512:(n % 2) * 512 + 512]
                    for ft in range(FT):
                        mm = te.matmul(py, lhsT=hT[:, ft * TOK + tt * 128: ft * TOK + (tt + 1) * 128],
                                       rhs=w2b[:, ft * D + chk * 512: ft * D + (chk + 1) * 512],
                                       start=(ft == 0), stop=(ft == FT - 1))
                    mm.then_inc(ctx.tc.h, 1)
                    n += 1

        @block.vector
        def _(v, e=e, t_y=t_y, v_y=v_y):
            n = 0
            for tt in range(NT):
                for chk in range(4):
                    n += 1
                    v.wait_ge(ctx.tc.h, t_y + n)
                    py = psY[:, ((n - 1) % 2) * 512:((n - 1) % 2) * 512 + 512]
                    dst = acc[:, tt * D + chk * 512: tt * D + (chk + 1) * 512]
                    v.scalar_tensor_tensor(out=dst, in0=py,
                                           scalar=G[:, tt * NEXP + e:tt * NEXP + e + 1],
                                           in1=dst, op0=ALU.mult,
                                           op1=ALU.add).then_inc(ctx.vc.h, 1)

    t_b2 = ctx.tc.plan(NT * 4)
    v_b2 = ctx.vc.plan(NT * 4)

    @block.tensor
    def _(te):
        te.wait_ge(ctx.vc.h, v_b2)
        n = 0
        for tt in range(NT):
            for chk in range(4):
                if n >= 2:
                    te.wait_ge(ctx.vc.h, v_b2 + n - 1)
                py = psY[:, (n % 2) * 512:(n % 2) * 512 + 512]
                te.matmul(py, lhsT=GT[:, tt * 128:(tt + 1) * 128],
                          rhs=b2b[:, chk * 512:(chk + 1) * 512], start=True,
                          stop=True).then_inc(ctx.tc.h, 1)
                n += 1

    @block.vector
    def _(v):
        n = 0
        for tt in range(NT):
            for chk in range(4):
                n += 1
                v.wait_ge(ctx.tc.h, t_b2 + n)
                py = psY[:, ((n - 1) % 2) * 512:((n - 1) % 2) * 512 + 512]
                dst = acc[:, tt * D + chk * 512: tt * D + (chk + 1) * 512]
                v.tensor_tensor(out=dst, in0=dst, in1=py, op=ALU.add).then_inc(ctx.vc.h, 1)

    v_f = ctx.vc.plan(NT)
    for tt in range(NT):
        srl[tt % 2].plan(16)
        srs[tt % 2].plan(16)

    @block.sync
    def _(sy):
        for tt in range(NT):
            par = tt % 2
            if tt >= 2:
                sy.wait_ge(srs[par].h, (tt // 2) * 16)
            sy.dma_start(out=win[:, par * D:(par + 1) * D],
                         in_=x3[tt * 128:(tt + 1) * 128, :]).then_inc(srl[par].h, 16)
            sy.wait_ge(ctx.vc.h, v_f + tt + 1)
            sy.dma_start(out=x4[tt * 128:(tt + 1) * 128, :],
                         in_=win[:, par * D:(par + 1) * D]).then_inc(srs[par].h, 16)

    @block.vector
    def _(v):
        v.wait_ge(ctx.vc.h, v_f)
        for tt in range(NT):
            par = tt % 2
            v.wait_ge(srl[par].h, (tt // 2 + 1) * 16)
            v.tensor_tensor(out=win[:, par * D:(par + 1) * D],
                            in0=win[:, par * D:(par + 1) * D],
                            in1=acc[:, tt * D:(tt + 1) * D], op=ALU.add).then_inc(ctx.vc.h, 1)



def emit_xdbl(ctx, S, xn5, wx_d, xdbl_d, ident):
    block = ctx.block
    xw = ctx.sb(S, "xdw", (128, 2 * D), BF16)
    xnT = ctx.sb(S, "xdT", (128, DT * 128), BF16)
    wxb = ctx.sb(S, "xwx", (128, DT * 96), BF16)
    ob = ctx.sb(S, "xdo", (128, 2 * 96))
    psX = ctx.ps(S, "psX", (128, 512), F32)
    psT = ctx.ps(S, "psTx", (128, 2 * 1024), BF16)
    sx = [ctx.new_sem("xdl"), ctx.new_sem("xdl")]
    so = [ctx.new_sem("xds"), ctx.new_sem("xds")]
    lw = ctx.lg.plan(16)

    @block.gpsimd
    def _(g):
        g.dma_start(out=wxb[:].rearrange("p (a q) -> p a q", a=DT),
                    in_=wx_d.rearrange("a p q -> p a q")).then_inc(ctx.lg.h, 16)

    t_x = ctx.tc.plan(NT * (DT + 1))
    v_x = ctx.vc.plan(NT * (DT + 1))
    c_x = ctx.sc.plan(0)
    for tt in range(NT):
        sx[tt % 2].plan(16)
        so[tt % 2].plan(16)

    @block.sync
    def _(sy):
        for tt in range(NT):
            par = tt % 2
            if tt >= 2:
                sy.wait_ge(ctx.tc.h, t_x + (tt - 1) * (DT + 1))
            sy.dma_start(out=xw[:, par * D:(par + 1) * D],
                         in_=xn5[tt * 128:(tt + 1) * 128, :]).then_inc(sx[par].h, 16)
            sy.wait_ge(ctx.vc.h, v_x + tt * (DT + 1) + DT + 1)
            sy.dma_start(out=xdbl_d[tt * 128:(tt + 1) * 128, :],
                         in_=ob[:, par * 96:(par + 1) * 96]).then_inc(so[par].h, 16)

    @block.tensor
    def _(te):
        te.wait_ge(ctx.lg.h, ctx.lg.v)
        n = 0
        for tt in range(NT):
            par = tt % 2
            te.wait_ge(sx[par].h, (tt // 2 + 1) * 16)
            for d_ in range(DT):
                if n >= 2:
                    prev = n - 2
                    ptt, pd = prev // DT, prev % DT
                    te.wait_ge(ctx.vc.h, v_x + ptt * (DT + 1) + pd + 1)
                te.transpose(psT[:, (n % 2) * 1024:(n % 2) * 1024 + 128],
                             xw[:, par * D + d_ * 128: par * D + (d_ + 1) * 128],
                             ident[:]).then_inc(ctx.tc.h, 1)
                n += 1
            te.wait_ge(ctx.vc.h, v_x + tt * (DT + 1) + DT)
            if tt >= 1:
                te.wait_ge(ctx.vc.h, v_x + (tt - 1) * (DT + 1) + DT + 1)
            for T in range(DT):
                mm = te.matmul(psX[:, 0:96], lhsT=xnT[:, T * 128:(T + 1) * 128],
                               rhs=wxb[:, T * 96:(T + 1) * 96],
                               start=(T == 0), stop=(T == DT - 1))
            mm.then_inc(ctx.tc.h, 1)

    @block.vector
    def _(v):
        n = 0
        for tt in range(NT):
            par = tt % 2
            for d_ in range(DT):
                n += 1
                v.wait_ge(ctx.tc.h, t_x + tt * (DT + 1) + d_ + 1)
                v.tensor_copy(xnT[:, d_ * 128:(d_ + 1) * 128],
                              psT[:, ((n - 1 - tt) % 2) * 1024:((n - 1 - tt) % 2) * 1024 + 128]
                              ).then_inc(ctx.vc.h, 1)
            n += 1
            v.wait_ge(ctx.tc.h, t_x + (tt + 1) * (DT + 1))
            if tt >= 2:
                v.wait_ge(so[par].h, (tt // 2) * 16)
            v.tensor_copy(ob[:, par * 96:(par + 1) * 96], psX[:, 0:96]).then_inc(ctx.vc.h, 1)


def build_L3(stages=("att", "ln3", "ffn", "ln4", "moe", "ln5", "xdbl")):
    nc = bass.Bass("TRN2", target_bir_lowering=False, debug=False)
    x1 = nc.dram_tensor("x1", [TOK, D], F32, kind="ExternalInput").ap()
    outT_d = nc.dram_tensor("outT", [D, TOK], F32, kind="ExternalInput").ap()
    xn2v_d = nc.dram_tensor("xn2v", [DT, 128, TOK], BF16, kind="ExternalInput").ap()
    wgt_d = nc.dram_tensor("wgt", [DT, DT, 128, 128], F32, kind="ExternalInput").ap()
    wot_d = nc.dram_tensor("wot", [DT, 128, DT, 128], F32, kind="ExternalInput").ap()
    nw_d = nc.dram_tensor("nw", [128, 16], F32, kind="ExternalInput").ap()
    wu = nc.dram_tensor("wu", [32, 128, 2, DT, 128], F32, kind="ExternalInput").ap()
    wg = nc.dram_tensor("wg", [32, 128, 2, DT, 128], F32, kind="ExternalInput").ap()
    wd = nc.dram_tensor("wd", [DT, 128, IHI, 128], F32, kind="ExternalInput").ap()
    su_ug_d = nc.dram_tensor("su_ug", [128, 16], F32, kind="ExternalInput").ap()
    su_dp_d = nc.dram_tensor("su_dp", [128, 64], F32, kind="ExternalInput").ap()
    ws_up_d = nc.dram_tensor("ws_up", [128, 64], F32, kind="ExternalInput").ap()
    ws_gt_d = nc.dram_tensor("ws_gt", [128, 64], F32, kind="ExternalInput").ap()
    ws_dp_d = nc.dram_tensor("ws_dp", [128, 16], F32, kind="ExternalInput").ap()
    w1_d = nc.dram_tensor("w1", [NEXP, FT, 128, DT, 128], F32, kind="ExternalInput").ap()
    w2_d = nc.dram_tensor("w2", [NEXP, DFF, D], F32, kind="ExternalInput").ap()
    b1_d = nc.dram_tensor("b1", [NEXP, 128, FT], F32, kind="ExternalInput").ap()
    b2_d = nc.dram_tensor("b2", [NEXP, D], F32, kind="ExternalInput").ap()
    cent_d = nc.dram_tensor("cent", [DT, 128, 16], F32, kind="ExternalInput").ap()
    bias_d = nc.dram_tensor("biasr", [128, 16], F32, kind="ExternalInput").ap()
    iota_d = nc.dram_tensor("iota16", [128, 16], F32, kind="ExternalInput").ap()
    wx_d = nc.dram_tensor("wx", [DT, 128, 96], F32, kind="ExternalInput").ap()
    h128_d = nc.dram_tensor("h128s", [128, 128], F32, kind="ExternalInput").ap()
    id_d = nc.dram_tensor("ident", [128, 128], F32, kind="ExternalInput").ap()
    cv_d = nc.dram_tensor("consts", [128, 4], F32, kind="ExternalInput").ap()
    x4_d = nc.dram_tensor("x4", [TOK, D], F32, kind="ExternalOutput").ap()
    xn5_d = nc.dram_tensor("xn5", [TOK, D], BF16, kind="ExternalOutput").ap()
    xdbl_d = nc.dram_tensor("xdbl", [TOK, 96], F32, kind="ExternalOutput").ap()
    x2_i = nc.dram_tensor("x2_i", [TOK, D], F32, kind="Internal").ap()
    xn3_i = nc.dram_tensor("xn3_i", [TOK, D], BF16, kind="Internal").ap()
    x3_i = nc.dram_tensor("x3_i", [TOK, D], F32, kind="Internal").ap()
    xn4_i = nc.dram_tensor("xn4_i", [TOK, D], BF16, kind="Internal").ap()

    with ExitStack() as ES:
        ctx = Ctx(nc, ES)
        block = ctx.block
        h128s = ctx.sb(ES, "h128", (128, 128), BF16)
        ident = ctx.sb(ES, "id", (128, 128), BF16)
        ident32 = ctx.sb(ES, "id32", (128, 128))
        cvec = ctx.sb(ES, "cv", (128, 4))
        lc = ctx.lg.plan(4 * 16)

        @block.gpsimd
        def _(g):
            g.dma_start(out=h128s[:], in_=h128_d[:]).then_inc(ctx.lg.h, 16)
            g.dma_start(out=ident[:], in_=id_d[:]).then_inc(ctx.lg.h, 16)
            g.dma_start(out=ident32[:], in_=id_d[:]).then_inc(ctx.lg.h, 16)
            g.dma_start(out=cvec[:], in_=cv_d[:]).then_inc(ctx.lg.h, 16)

        ctx.join()

        if "att" in stages:
            with ExitStack() as S:
                emit_attn_finish(ctx, S, x1, outT_d, xn2v_d, wgt_d, wot_d, nw_d, x2_i,
                                 cvec, ident)
                ctx.join()
        else:
            x2_i = x1
        if "ln3" in stages:
            with ExitStack() as S:
                emit_ln_stream(ctx, S, x2_i, xn3_i, NT, cvec)
                ctx.join()
        if "ffn" in stages:
            with ExitStack() as S:
                emit_ffn0(ctx, S, x2_i, x3_i, xn3_i, NT, wu, wg, wd, su_ug_d, su_dp_d,
                          ws_up_d, ws_gt_d, ws_dp_d, h128s, ident)
                ctx.join()
        else:
            x3_i = x2_i
        if "ln4" in stages:
            with ExitStack() as S:
                emit_ln_stream(ctx, S, x3_i, xn4_i, NT, cvec)
                ctx.join()
        if "moe" in stages:
            with ExitStack() as S:
                emit_moe(ctx, S, x3_i, x4_d, xn4_i, w1_d, w2_d, b1_d, b2_d, cent_d,
                         bias_d, iota_d, ident, ident32)
                ctx.join()
        else:
            with ExitStack() as S:
                emit_copy_dram(ctx, S, x3_i, x4_d, NT)
                ctx.join()
        if "ln5" in stages:
            with ExitStack() as S:
                emit_ln_stream(ctx, S, x4_d, xn5_d, NT, cvec)
                ctx.join()
        if "xdbl" in stages:
            with ExitStack() as S:
                emit_xdbl(ctx, S, xn5_d, wx_d, xdbl_d, ident)
                ctx.join()
        ctx.join()

    return nc


def emit_copy_dram(ctx, S, src_d, dst_d, ntile):
    block = ctx.block
    buf = ctx.sb(S, "cpy", (128, 2 * D))
    sl = [ctx.new_sem("cpl"), ctx.new_sem("cpl")]
    ss = [ctx.new_sem("cps"), ctx.new_sem("cps")]
    for t in range(ntile):
        sl[t % 2].plan(16)
        ss[t % 2].plan(16)

    @block.sync
    def _(sy):
        for t in range(ntile):
            par = t % 2
            if t >= 2:
                sy.wait_ge(ss[par].h, (t // 2) * 16)
            sy.dma_start(out=buf[:, par * D:(par + 1) * D],
                         in_=src_d[t * 128:(t + 1) * 128, :]).then_inc(sl[par].h, 16)
            sy.wait_ge(sl[par].h, (t // 2 + 1) * 16)
            sy.dma_start(out=dst_d[t * 128:(t + 1) * 128, :],
                         in_=buf[:, par * D:(par + 1) * D]).then_inc(ss[par].h, 16)


def build_L4():
    nc = bass.Bass("TRN2", target_bir_lowering=False, debug=False)
    xnv_d = nc.dram_tensor("xnv", [4, 128, SEQ], BF16, kind="ExternalInput").ap()
    dr_d = nc.dram_tensor("drows", [64, SEQ], F32, kind="ExternalInput").ap()
    br_d = nc.dram_tensor("brows", [16, SEQ], F32, kind="ExternalInput").ap()
    cr_d = nc.dram_tensor("crows", [16, SEQ], F32, kind="ExternalInput").ap()
    wdt_d = nc.dram_tensor("wdt", [64, 4, 128], F32, kind="ExternalInput").ap()
    bdt_d = nc.dram_tensor("bdt", [128, 4], F32, kind="ExternalInput").ap()
    alog_d = nc.dram_tensor("alog", [4, 128, 16], F32, kind="ExternalInput").ap()
    dpar_d = nc.dram_tensor("dpar", [128, 4], F32, kind="ExternalInput").ap()
    ones_d = nc.dram_tensor("ones1", [1, 128], F32, kind="ExternalInput").ap()
    sel_d = nc.dram_tensor("sels", [16, 16, 128], F32, kind="ExternalInput").ap()
    y_d = nc.dram_tensor("yT", [512, SEQ], F32, kind="ExternalOutput").ap()

    LCH = 1024
    NCH = SEQ // LCH

    with ExitStack() as ES:
        ctx = Ctx(nc, ES)
        block = ctx.block
        xnv = ctx.sb(ES, "sxnv", (128, 4 * SEQ), BF16)
        drbf = ctx.sb(ES, "sdr", (64, SEQ), BF16)
        brow = ctx.sb(ES, "sbr", (16, SEQ))
        crow = ctx.sb(ES, "scr", (16, SEQ))
        sels = ctx.sb(ES, "ssel", (16, 16 * 128))
        wdt = ctx.sb(ES, "swdt", (64, 4 * 128), BF16)
        bdt = ctx.sb(ES, "sbdt", (128, 4))
        acols = ctx.sb(ES, "sac", (128, 4 * 16))
        dpar = ctx.sb(ES, "sdp", (128, 4))
        ones1 = ctx.sb(ES, "son", (1, 128))
        Bbc = ctx.sb(ES, "sBb", (128, 16 * LCH), BF16)
        Cbc = ctx.sb(ES, "sCb", (128, 16 * LCH), BF16)
        delta = ctx.sb(ES, "sdl", (128, LCH))
        absb = ctx.sb(ES, "sab", (128, LCH))
        l1p = ctx.sb(ES, "sl1", (128, LCH))
        relu = ctx.sb(ES, "srl", (128, LCH))
        dxb = ctx.sb(ES, "sdx", (128, LCH))
        dAw = ctx.sb(ES, "sdA", (128, 2 * LCH))
        dBw = ctx.sb(ES, "sdB", (128, 2 * LCH))
        hw = ctx.sb(ES, "shw", (128, LCH))
        hcw = ctx.sb(ES, "shc", (128, LCH))
        yac = ctx.sb(ES, "syac", (128, 2 * LCH))
        hst = ctx.sb(ES, "shst", (128, 64))
        psD2 = ctx.ps(ES, "psD2", (128, 2 * 512), F32)
        psB2 = ctx.ps(ES, "psB2", (128, 2 * 512), F32)

        sld = ctx.new_sem("s4l")
        sbc = ctx.new_sem("s4b")
        sst = [ctx.new_sem("s4s"), ctx.new_sem("s4s")]

        l0 = sld.plan(7 * 16)
        lg0 = ctx.lg.plan(2 * 16)
        lb0 = sbc.plan(16)

        @block.sync
        def _(sy):
            sy.dma_start(out=xnv[:].rearrange("p (a q) -> p a q", a=4),
                         in_=xnv_d.rearrange("a p q -> p a q")).then_inc(sld.h, 16)
            sy.dma_start(out=brow[:], in_=br_d[:]).then_inc(sld.h, 16)
            sy.dma_start(out=crow[:], in_=cr_d[:]).then_inc(sld.h, 16)
            sy.dma_start(out=bdt[:], in_=bdt_d[:]).then_inc(sld.h, 16)
            sy.dma_start(out=acols[:].rearrange("p (a q) -> p a q", a=4),
                         in_=alog_d.rearrange("a p q -> p a q")).then_inc(sld.h, 16)
            sy.dma_start(out=dpar[:], in_=dpar_d[:]).then_inc(sld.h, 16)
            sy.dma_start(out=ones1[:], in_=ones_d[:]).then_inc(sld.h, 16)
            sy.dma_start(out=sels[:].rearrange("p (a q) -> p a q", a=16),
                         in_=sel_d.rearrange("a p q -> p a q")).then_inc(sbc.h, 16)

        @block.gpsimd
        def _(g):
            g.dma_start(out=drbf[:], in_=dr_d[:]).then_inc(ctx.lg.h, 16)
            g.dma_start(out=wdt[:].rearrange("p (a q) -> p a q", a=4),
                        in_=wdt_d.rearrange("p a q -> p a q")).then_inc(ctx.lg.h, 16)

        c_a = ctx.sc.plan(1)
        v_a = ctx.vc.plan(1)

        @block.scalar
        def _(s):
            s.wait_ge(sld.h, l0 + 7 * 16)
            s.activation(out=acols[:], in_=acols[:], func=AF.Exp, bias=0.0,
                         scale=1.0).then_inc(ctx.sc.h, 1)

        @block.vector
        def _(v):
            v.wait_ge(ctx.sc.h, c_a + 1)
            v.tensor_scalar(out=acols[:], in0=acols[:], scalar1=-1.0, scalar2=None,
                            op0=ALU.mult).then_inc(ctx.vc.h, 1)

        ctx.join()

        for lc in range(NCH):
            o = lc * LCH
            t_b = ctx.tc.plan(64)
            v_b = ctx.vc.plan(64)

            @block.tensor
            def _(te, lc=lc, t_b=t_b, v_b=v_b, gprev=ctx.gc.v):
                te.wait_ge(ctx.vc.h, v_b)
                te.wait_ge(ctx.gc.h, gprev)
                te.wait_ge(sbc.h, 16)
                n4 = 0
                for n in range(16):
                    for half in range(2):
                        for which in range(2):
                            src = brow if which == 0 else crow
                            if n4 >= 2:
                                te.wait_ge(ctx.vc.h, v_b + n4 - 1)
                            te.matmul(psB2[:, (n4 % 2) * 512:(n4 % 2) * 512 + 512],
                                      lhsT=sels[:, n * 128:(n + 1) * 128],
                                      rhs=src[:, o + half * 512: o + half * 512 + 512],
                                      start=True, stop=True).then_inc(ctx.tc.h, 1)
                            n4 += 1

            @block.vector
            def _(v, lc=lc, t_b=t_b, v_b=v_b):
                n4 = 0
                for n in range(16):
                    for half in range(2):
                        for which in range(2):
                            dst = (Bbc if which == 0 else Cbc)
                            v.wait_ge(ctx.tc.h, t_b + n4 + 1)
                            v.tensor_copy(dst[:, n * LCH + half * 512: n * LCH + half * 512 + 512],
                                          psB2[:, (n4 % 2) * 512:(n4 % 2) * 512 + 512]
                                          ).then_inc(ctx.vc.h, 1)
                            n4 += 1

            for dt4 in range(4):
                it = lc * 4 + dt4
                t_d = ctx.tc.plan(2)
                c_d = ctx.sc.plan(4 * 2)
                v_d = ctx.vc.plan(2)

                @block.tensor
                def _(te, dt4=dt4, o=o, t_d=t_d, v_d=v_d, c_d=c_d, it=it):
                    if it > 0:
                        te.wait_ge(ctx.sc.h, c_d)
                    for half in range(2):
                        mm = te.matmul(psD2[:, half * 512:half * 512 + 512],
                                       lhsT=wdt[:, dt4 * 128:(dt4 + 1) * 128],
                                       rhs=drbf[:, o + half * 512: o + half * 512 + 512],
                                       start=True, stop=True)
                    mm.then_inc(ctx.tc.h, 2)

                @block.scalar
                def _(s, dt4=dt4, t_d=t_d, c_d=c_d, v_d=v_d):
                    s.wait_ge(ctx.tc.h, t_d + 2)
                    for half in range(2):
                        pd = psD2[:, half * 512:half * 512 + 512]
                        sl = slice(half * 512, half * 512 + 512)
                        s.activation(out=absb[:, sl], in_=pd, func=AF.Abs,
                                     bias=bdt[:, dt4:dt4 + 1], scale=1.0).then_inc(ctx.sc.h, 1)
                        s.wait_ge(ctx.sc.h, c_d + 4 * half + 1)
                        s.activation(out=l1p[:, sl], in_=absb[:, sl], func=AF.Exp,
                                     bias=0.0, scale=-1.0).then_inc(ctx.sc.h, 1)
                        s.wait_ge(ctx.sc.h, c_d + 4 * half + 2)
                        s.activation(out=l1p[:, sl], in_=l1p[:, sl], func=AF.Ln,
                                     bias=1.0, scale=1.0).then_inc(ctx.sc.h, 1)
                        s.activation(out=relu[:, sl], in_=pd, func=AF.Relu,
                                     bias=bdt[:, dt4:dt4 + 1], scale=1.0).then_inc(ctx.sc.h, 1)

                @block.vector
                def _(v, dt4=dt4, o=o, c_d=c_d, v_d=v_d, gprev=ctx.gc.v):
                    v.wait_ge(ctx.sc.h, c_d + 8)
                    v.wait_ge(ctx.gc.h, gprev)
                    v.tensor_tensor(out=delta[:], in0=relu[:], in1=l1p[:],
                                    op=ALU.add).then_inc(ctx.vc.h, 1)
                    v.wait_ge(ctx.vc.h, v_d + 1)
                    v.tensor_tensor(out=dxb[:], in0=delta[:],
                                    in1=xnv[:, dt4 * SEQ + o: dt4 * SEQ + o + LCH],
                                    op=ALU.mult).then_inc(ctx.vc.h, 1)
                    v.wait_ge(ctx.vc.h, v_d + 2)

                c_n = ctx.sc.plan(16)
                v_n = ctx.vc.plan(16 * 5)

                @block.scalar
                def _(s, dt4=dt4, c_n=c_n, v_n=v_n, v_d=v_d):
                    s.wait_ge(ctx.vc.h, v_d + 1)
                    for n in range(16):
                        if n >= 2:
                            s.wait_ge(ctx.vc.h, v_n + (n - 2) * 5 + 2)
                        s.activation(out=dAw[:, (n % 2) * LCH:(n % 2) * LCH + LCH],
                                     in_=delta[:], func=AF.Exp, bias=0.0,
                                     scale=acols[:, dt4 * 16 + n: dt4 * 16 + n + 1]
                                     ).then_inc(ctx.sc.h, 1)

                @block.vector
                def _(v, dt4=dt4, lc=lc, o=o, c_n=c_n, v_n=v_n, it=it):
                    nv = v_n
                    if it >= 2:
                        v.wait_ge(sst[it % 2].h, (it // 2) * 16)
                    for n in range(16):
                        v.tensor_tensor(out=dBw[:, (n % 2) * LCH:(n % 2) * LCH + LCH],
                                        in0=dxb[:], in1=Bbc[:, n * LCH:(n + 1) * LCH],
                                        op=ALU.mult).then_inc(ctx.vc.h, 1)
                        nv += 1
                        v.wait_ge(ctx.vc.h, nv)
                        v.wait_ge(ctx.sc.h, c_n + n + 1)
                        v.tensor_tensor_scan(out=hw[:],
                                             data0=dAw[:, (n % 2) * LCH:(n % 2) * LCH + LCH],
                                             data1=dBw[:, (n % 2) * LCH:(n % 2) * LCH + LCH],
                                             initial=0.0 if lc == 0 else hst[:, dt4 * 16 + n: dt4 * 16 + n + 1],
                                             op0=ALU.mult, op1=ALU.add).then_inc(ctx.vc.h, 1)
                        nv += 1
                        v.wait_ge(ctx.vc.h, nv)
                        v.tensor_copy(hst[:, dt4 * 16 + n: dt4 * 16 + n + 1],
                                      hw[:, LCH - 1:LCH]).then_inc(ctx.vc.h, 1)
                        nv += 1
                        if n == 0:
                            v.tensor_tensor(out=yac[:, (it % 2) * LCH:(it % 2) * LCH + LCH],
                                            in0=hw[:], in1=Cbc[:, 0:LCH],
                                            op=ALU.mult).then_inc(ctx.vc.h, 1)
                            nv += 1
                            v.wait_ge(ctx.vc.h, nv)
                            v.nop().then_inc(ctx.vc.h, 1)
                            nv += 1
                        else:
                            v.tensor_tensor(out=hcw[:], in0=hw[:],
                                            in1=Cbc[:, n * LCH:(n + 1) * LCH],
                                            op=ALU.mult).then_inc(ctx.vc.h, 1)
                            nv += 1
                            v.wait_ge(ctx.vc.h, nv)
                            v.tensor_tensor(out=yac[:, (it % 2) * LCH:(it % 2) * LCH + LCH],
                                            in0=yac[:, (it % 2) * LCH:(it % 2) * LCH + LCH],
                                            in1=hcw[:], op=ALU.add).then_inc(ctx.vc.h, 1)
                            nv += 1
                        v.wait_ge(ctx.vc.h, nv)

                v_y = ctx.vc.plan(1)
                st0 = sst[it % 2].plan(16)

                @block.vector
                def _(v, dt4=dt4, it=it, v_y=v_y):
                    v.scalar_tensor_tensor(out=yac[:, (it % 2) * LCH:(it % 2) * LCH + LCH],
                                           in0=xnv[:, dt4 * SEQ + o: dt4 * SEQ + o + LCH],
                                           scalar=dpar[:, dt4:dt4 + 1],
                                           in1=yac[:, (it % 2) * LCH:(it % 2) * LCH + LCH],
                                           op0=ALU.mult, op1=ALU.add).then_inc(ctx.vc.h, 1)

                @block.sync
                def _(sy, dt4=dt4, o=o, it=it, v_y=v_y):
                    sy.wait_ge(ctx.vc.h, v_y + 1)
                    sy.dma_start(out=y_d[dt4 * 128:(dt4 + 1) * 128, o:o + LCH],
                                 in_=yac[:, (it % 2) * LCH:(it % 2) * LCH + LCH]
                                 ).then_inc(sst[it % 2].h, 16)

        ctx.join()

    return nc


def build_L5():
    nc = bass.Bass("TRN2", target_bir_lowering=False, debug=False)
    x4 = nc.dram_tensor("x4", [TOK, D], F32, kind="ExternalInput").ap()
    yv_d = nc.dram_tensor("yv", [TOK, D], F32, kind="ExternalInput").ap()
    w1_d = nc.dram_tensor("w1", [NEXP, FT, 128, DT, 128], F32, kind="ExternalInput").ap()
    w2_d = nc.dram_tensor("w2", [NEXP, DFF, D], F32, kind="ExternalInput").ap()
    b1_d = nc.dram_tensor("b1", [NEXP, 128, FT], F32, kind="ExternalInput").ap()
    b2_d = nc.dram_tensor("b2", [NEXP, D], F32, kind="ExternalInput").ap()
    cent_d = nc.dram_tensor("cent", [DT, 128, 16], F32, kind="ExternalInput").ap()
    bias_d = nc.dram_tensor("biasr", [128, 16], F32, kind="ExternalInput").ap()
    iota_d = nc.dram_tensor("iota16", [128, 16], F32, kind="ExternalInput").ap()
    id_d = nc.dram_tensor("ident", [128, 128], F32, kind="ExternalInput").ap()
    cv_d = nc.dram_tensor("consts", [128, 4], F32, kind="ExternalInput").ap()
    out_d = nc.dram_tensor("out", [TOK, D], F32, kind="ExternalOutput").ap()
    x5_i = nc.dram_tensor("x5_i", [TOK, D], F32, kind="Internal").ap()
    xn6_i = nc.dram_tensor("xn6_i", [TOK, D], BF16, kind="Internal").ap()

    with ExitStack() as ES:
        ctx = Ctx(nc, ES)
        block = ctx.block
        ident = ctx.sb(ES, "id", (128, 128), BF16)
        ident32 = ctx.sb(ES, "id32", (128, 128))
        cvec = ctx.sb(ES, "cv", (128, 4))
        lc = ctx.lg.plan(3 * 16)

        @block.gpsimd
        def _(g):
            g.dma_start(out=ident[:], in_=id_d[:]).then_inc(ctx.lg.h, 16)
            g.dma_start(out=ident32[:], in_=id_d[:]).then_inc(ctx.lg.h, 16)
            g.dma_start(out=cvec[:], in_=cv_d[:]).then_inc(ctx.lg.h, 16)

        ctx.join()

        with ExitStack() as S:
            buf = ctx.sb(S, "ya", (128, 2 * D))
            yb = ctx.sb(S, "yb", (128, 2 * D))
            sl = [ctx.new_sem("y5l"), ctx.new_sem("y5l")]
            sy_ = [ctx.new_sem("y5y"), ctx.new_sem("y5y")]
            ss = [ctx.new_sem("y5s"), ctx.new_sem("y5s")]
            v5 = ctx.vc.plan(NT)
            for t in range(NT):
                sl[t % 2].plan(16)
                sy_[t % 2].plan(16)
                ss[t % 2].plan(16)

            @block.sync
            def _(sy):
                for t in range(NT):
                    par = t % 2
                    if t >= 2:
                        sy.wait_ge(ss[par].h, (t // 2) * 16)
                    sy.dma_start(out=buf[:, par * D:(par + 1) * D],
                                 in_=x4[t * 128:(t + 1) * 128, :]).then_inc(sl[par].h, 16)
                    sy.dma_start(out=yb[:, par * D:(par + 1) * D],
                                 in_=yv_d[t * 128:(t + 1) * 128, :]).then_inc(sy_[par].h, 16)
                    sy.wait_ge(ctx.vc.h, v5 + t + 1)
                    sy.dma_start(out=x5_i[t * 128:(t + 1) * 128, :],
                                 in_=buf[:, par * D:(par + 1) * D]).then_inc(ss[par].h, 16)

            @block.vector
            def _(v):
                for t in range(NT):
                    par = t % 2
                    v.wait_ge(sl[par].h, (t // 2 + 1) * 16)
                    v.wait_ge(sy_[par].h, (t // 2 + 1) * 16)
                    v.tensor_tensor(out=buf[:, par * D:(par + 1) * D],
                                    in0=buf[:, par * D:(par + 1) * D],
                                    in1=yb[:, par * D:(par + 1) * D],
                                    op=ALU.add).then_inc(ctx.vc.h, 1)
            ctx.join()

        with ExitStack() as S:
            emit_ln_stream(ctx, S, x5_i, xn6_i, NT, cvec)
            ctx.join()
        with ExitStack() as S:
            emit_moe(ctx, S, x5_i, out_d, xn6_i, w1_d, w2_d, b1_d, b2_d, cent_d,
                     bias_d, iota_d, ident, ident32)
            ctx.join()
        ctx.join()

    return nc


def prep_ffn0_weights(W_up, W_gate, W_down):
    def up_tile(W):
        return _c(W.reshape(32, 2, 128, DT, 128).transpose(0, 4, 1, 3, 2))
    wu = up_tile(W_up)
    wg = up_tile(W_gate)
    wd = _c(W_down.reshape(DT, 128, IHI, 128).transpose(0, 3, 2, 1))
    return wu, wg, wd


def prep_attn_weights(Wqkv, Wg, Wo, norm_w):
    WqkvT = np.ascontiguousarray(Wqkv.T)
    qkvw_all = np.stack([WqkvT[:, h * 384:(h + 1) * 384].reshape(DT, 128, 384)
                         for h in range(16)])
    WgT = Wg.T
    wgt = _c(WgT.reshape(DT, 128, DT, 128).transpose(2, 0, 1, 3))
    wot = _c(Wo.reshape(DT, 128, DT, 128).transpose(0, 3, 2, 1))
    nw = _c(norm_w.reshape(16, 128).T)
    return _c(qkvw_all), wgt, wot, nw


def prep_moe_weights(sw1, sb1, sw2, sb2, rw1, rb1, rw2, rb2, centroids, bias):
    w1 = np.concatenate([sw1, rw1], axis=0)
    w2 = np.concatenate([sw2, rw2], axis=0)
    b1 = np.concatenate([sb1, rb1], axis=0)
    b2 = np.concatenate([sb2, rb2], axis=0)
    w1t = _c(np.stack([w1[e].reshape(DT, 128, FT, 128).transpose(2, 1, 0, 3)
                       for e in range(NEXP)]))
    b1c = _c(b1.reshape(NEXP, FT, 128).transpose(0, 2, 1))
    cent = _c(centroids.T.reshape(DT, 128, 16))
    biasr = _c(np.broadcast_to(bias.astype(np.float32), (128, 16)))
    return w1t, _c(w2), b1c, _c(b2), cent, biasr


IOTA16 = _c(np.broadcast_to(np.arange(16, dtype=np.float32), (128, 16)))


def prep_ffn0_small(SU_ug, SU_dp, Wscale_ug, Wscale_dp):
    su_ug = _c(SU_ug.reshape(16, 128).T)
    su_dp = _c(SU_dp.reshape(64, 128).T)
    ws_up = _c(Wscale_ug[:INTER].reshape(64, 128).T)
    ws_gt = _c(Wscale_ug[INTER:].reshape(64, 128).T)
    ws_dp = _c(Wscale_dp.reshape(16, 128).T)
    return su_ug, su_dp, ws_up, ws_gt, ws_dp


_NC_CACHE = {}


def _nc(name, builder):
    if name not in _NC_CACHE:
        _NC_CACHE[name] = builder()
    return _NC_CACHE[name]


def kernel(**inputs):
    ins = inputs
    x = np.ascontiguousarray(ins["L0_attn_x"].reshape(8192, D), dtype=np.float32)

    wu, wg, wd = prep_ffn0_weights(ins["L0_ffn_W_up"], ins["L0_ffn_W_gate"],
                                   ins["L0_ffn_W_down"])
    su_ug, su_dp, ws_up, ws_gt, ws_dp = prep_ffn0_small(
        ins["L0_ffn_SU_ug"], ins["L0_ffn_SU_dp"], ins["L0_ffn_Wscale_ug"],
        ins["L0_ffn_Wscale_dp"])
    qkvw_all, wgt, wot, nw = prep_attn_weights(ins["L0_attn_Wqkv"], ins["L0_attn_Wg"],
                                               ins["L0_attn_Wo"], ins["L0_attn_norm_w"])
    w1t, w2c, b1c, b2c, cent, biasr = prep_moe_weights(
        ins["L1_ffn_shared_w1"], ins["L1_ffn_shared_b1"], ins["L1_ffn_shared_w2"],
        ins["L1_ffn_shared_b2"], ins["L1_ffn_routed_w1"], ins["L1_ffn_routed_b1"],
        ins["L1_ffn_routed_w2"], ins["L1_ffn_routed_b2"], ins["L1_ffn_centroids"],
        ins["L1_ffn_bias"])
    wx = _c(ins["L1_attn_W_xproj"].T.reshape(DT, 128, 96))
    slope_row = _c(ins["L0_attn_slope_rate"].reshape(1, 16))
    ones1 = np.ones((1, 128), np.float32)
    p_ = np.arange(128, dtype=np.float32)
    arrq = _c(np.stack([p_ + 1, p_ + 129], axis=1))
    arrk = _c(np.stack([255 - p_, 255 - (p_ + 128)], axis=1))
    idxT = np.zeros((2, 128, 256), np.float32)
    mskT = np.zeros((2, 128, 256), np.float32)
    m_ = np.arange(256)
    for u_ in range(2):
        for pp in range(128):
            d_ = m_ - (u_ * 128 + pp)
            idxT[u_, pp] = np.where(d_ >= 0, d_, 1e9)
            mskT[u_, pp] = (d_ >= 0).astype(np.float32)
    sels = _c(np.eye(16, dtype=np.float32)[:, :, None]
              * np.ones((1, 1, 128), np.float32)).transpose(1, 0, 2).copy()
    WdtT = np.ascontiguousarray(ins["L1_attn_W_dt"].T)

    nc1 = _nc("L1", build_L1)
    common1 = {"wu": wu, "wg": wg, "wd": wd, "su_ug": su_ug, "su_dp": su_dp,
               "ws_up": ws_up, "ws_gt": ws_gt, "ws_dp": ws_dp,
               "h128s": H128S, "ident": IDENT, "consts": CONSTS}
    maps = [dict(common1, x=np.ascontiguousarray(x[c * TOK:(c + 1) * TOK]))
            for c in range(8)]
    r1 = run_bass_kernel_spmd(nc1, maps, core_ids=list(range(8))).results
    x1 = [r1[c]["x1"] for c in range(8)]
    xn2 = [r1[c]["xn2"] for c in range(8)]

    nc2 = _nc("L2", build_L2)
    maps = []
    xq_b = []
    for b in range(2):
        xn2_b = np.concatenate(xn2[4 * b:4 * b + 4], axis=0)
        xq_b.append(np.ascontiguousarray(
            xn2_b.reshape(NTS, 128, DT, 128).transpose(0, 2, 3, 1)))
    for cs in range(8):
        b, hg = cs // 4, cs % 4
        maps.append({"xq": xq_b[b],
                     "qkvw": np.ascontiguousarray(qkvw_all[hg * 4:(hg + 1) * 4]),
                     "slope": np.ascontiguousarray(slope_row[:, hg * 4:(hg + 1) * 4]),
                     "ones1": ones1, "arrq": arrq, "arrk": arrk,
                     "idxT": idxT, "mskT": mskT, "ident": IDENT})
    r2 = run_bass_kernel_spmd(nc2, maps, core_ids=list(range(8))).results
    outT_h = [r2[cs]["outT"] for cs in range(8)]

    nc3 = _nc("L3", build_L3)
    common3 = {"wgt": wgt, "wot": wot, "nw": nw, "wu": wu, "wg": wg, "wd": wd,
               "su_ug": su_ug, "su_dp": su_dp, "ws_up": ws_up, "ws_gt": ws_gt,
               "ws_dp": ws_dp, "w1": w1t, "w2": w2c, "b1": b1c, "b2": b2c,
               "cent": cent, "biasr": biasr, "iota16": IOTA16, "wx": wx,
               "h128s": H128S, "ident": IDENT, "consts": CONSTS}
    maps = []
    for c in range(8):
        b, pos = c // 4, c % 4
        outT_home = np.ascontiguousarray(np.concatenate(
            [outT_h[b * 4 + g][:, pos * TOK:(pos + 1) * TOK] for g in range(4)], axis=0))
        xn2v = np.ascontiguousarray(
            xn2[c].reshape(NT, 128, DT, 128).transpose(2, 3, 0, 1).reshape(DT, 128, TOK))
        maps.append(dict(common3, x1=x1[c], outT=outT_home, xn2v=xn2v))
    r3 = run_bass_kernel_spmd(nc3, maps, core_ids=list(range(8))).results
    x4 = [r3[c]["x4"] for c in range(8)]
    xn5 = [r3[c]["xn5"] for c in range(8)]
    xdbl = [r3[c]["xdbl"] for c in range(8)]

    nc4 = _nc("L4", build_L4)
    maps = []
    xn5T_b, xdblT_b = [], []
    for b in range(2):
        xn5_b = np.concatenate(xn5[4 * b:4 * b + 4], axis=0)
        xn5T_b.append(np.ascontiguousarray(xn5_b.T))
        xdbl_b = np.concatenate(xdbl[4 * b:4 * b + 4], axis=0)
        xdblT_b.append(np.ascontiguousarray(xdbl_b.T.astype(np.float32)))
    for cs in range(8):
        b, k = cs // 4, cs % 4
        ds = k * 512
        maps.append({
            "xnv": np.ascontiguousarray(xn5T_b[b][ds:ds + 512].reshape(4, 128, SEQ)),
            "drows": np.ascontiguousarray(xdblT_b[b][0:64]),
            "brows": np.ascontiguousarray(xdblT_b[b][64:80]),
            "crows": np.ascontiguousarray(xdblT_b[b][80:96]),
            "wdt": np.ascontiguousarray(WdtT[:, ds:ds + 512].reshape(64, 4, 128)),
            "bdt": np.ascontiguousarray(ins["L1_attn_b_dt"][ds:ds + 512].reshape(4, 128).T),
            "alog": np.ascontiguousarray(ins["L1_attn_A_log"][ds:ds + 512].reshape(4, 128, 16)),
            "dpar": np.ascontiguousarray(ins["L1_attn_D_param"][ds:ds + 512].reshape(4, 128).T),
            "ones1": ones1, "sels": sels})
    r4 = run_bass_kernel_spmd(nc4, maps, core_ids=list(range(8))).results
    yT = [r4[cs]["yT"] for cs in range(8)]

    nc5 = _nc("L5", build_L5)
    common5 = {"w1": w1t, "w2": w2c, "b1": b1c, "b2": b2c, "cent": cent,
               "biasr": biasr, "iota16": IOTA16, "ident": IDENT, "consts": CONSTS}
    maps = []
    for c in range(8):
        b, pos = c // 4, c % 4
        y_home = np.concatenate(
            [yT[b * 4 + k][:, pos * TOK:(pos + 1) * TOK] for k in range(4)], axis=0)
        yv = np.ascontiguousarray(y_home.T)
        maps.append(dict(common5, x4=x4[c], yv=yv))
    r5 = run_bass_kernel_spmd(nc5, maps, core_ids=list(range(8))).results
    out = np.concatenate([r5[c]["out"] for c in range(8)], axis=0)
    return out.reshape(2, 4096, D)


# version 05 (final)
# speedup vs baseline: 1.1335x; 1.0340x over previous
import numpy as np
import ml_dtypes
from contextlib import ExitStack

import concourse.bass as bass
import concourse.mybir as mybir
from concourse.bass_utils import run_bass_kernel_spmd

F32 = mybir.dt.float32
BF16 = mybir.dt.bfloat16
AF = mybir.ActivationFunctionType
ALU = mybir.AluOpType
AX = mybir.AxisListType
BF16NP = ml_dtypes.bfloat16

D = 2048
TOK = 1024
NT = TOK // 128
DT = D // 128
INTER = 8192
IHI = 64
EPS_LN = 1e-5
EPS_RMS = float(np.finfo(np.float32).eps)
C_UG = float(2.0 ** 2.5)
C_DP = float(2.0 ** 0.5)

DFF = 1408
FT = 11
NEXP = 18
NHEAD = 16
BLK = 256
TCH = 512


def _had(n):
    h = np.array([[1.0]])
    while h.shape[0] < n:
        h = np.block([[h, h], [h, -h]])
    return h


def _c(x):
    return np.ascontiguousarray(np.asarray(x), dtype=np.float32)


H128S = _c(_had(128) * (2.0 ** -8))
IDENT = _c(np.eye(128))
CONSTS = _c(np.broadcast_to(np.array([EPS_LN, 1.0, EPS_RMS, 0.0], np.float32), (128, 4)))


class Sem:
    def __init__(self, h):
        self.h = h
        self.v = 0

    def plan(self, n):
        b = self.v
        self.v += n
        return b


class Ctx:
    def __init__(self, nc, ES):
        self.nc = nc
        self.ES = ES
        self.block = ES.enter_context(nc.Block())
        self._all = []
        self._n = 0
        self.tc = self.new_sem("tc")
        self.vc = self.new_sem("vc")
        self.sc = self.new_sem("sc")
        self.gc = self.new_sem("gc")
        self.lg = self.new_sem("lg")

    def new_sem(self, name):
        self._n += 1
        s = Sem(self.ES.enter_context(self.nc.semaphore(f"s_{name}{self._n}")))
        self._all.append(s)
        return s

    def sb(self, scope, name, shape, dt=F32):
        self._n += 1
        return scope.enter_context(self.nc.sbuf_tensor(f"{name}{self._n}", list(shape), dt))

    def ps(self, scope, name, shape, dt=F32):
        self._n += 1
        return scope.enter_context(self.nc.psum_tensor(f"{name}{self._n}", list(shape), dt))

    def join(self):
        block = self.block
        sems = [s for s in self._all if s.v]

        def w(e):
            for s in sems:
                e.wait_ge(s.h, s.v)

        @block.sync
        def _(e):
            w(e)

        @block.gpsimd
        def _(e):
            w(e)

        @block.tensor
        def _(e):
            w(e)

        @block.vector
        def _(e):
            w(e)

        @block.scalar
        def _(e):
            w(e)


def emit_ln_stream(ctx, S, xin, xn_out, ntile, cvec):
    block = ctx.block
    win = ctx.sb(S, "lnw", (128, 2 * D))
    xst = ctx.sb(S, "lnx", (128, 2 * D), BF16)
    sums = ctx.sb(S, "lns", (128, 8))
    ssq = ctx.sb(S, "lnq", (128, 2))
    sq = ctx.sb(S, "lnsq", (128, D))
    lsl = [ctx.new_sem("lnl"), ctx.new_sem("lnl")]
    ssl = [ctx.new_sem("lnst"), ctx.new_sem("lnst")]
    for t in range(ntile):
        lsl[t % 2].plan(16)
        ssl[t % 2].plan(16)

    v0 = ctx.vc.plan(ntile * 5)
    c0 = ctx.sc.plan(ntile * 3)

    @block.sync
    def _(sy):
        for t in range(ntile):
            par = t % 2
            if t >= 2:
                sy.wait_ge(ctx.vc.h, v0 + (t - 1) * 5)
                sy.wait_ge(ctx.sc.h, c0 + (t - 2) * 3 + 1)
                sy.wait_ge(ssl[par].h, (t // 2) * 16)
            sy.dma_start(out=win[:, par * D:(par + 1) * D],
                         in_=xin[t * 128:(t + 1) * 128, :]).then_inc(lsl[par].h, 16)
            sy.wait_ge(ctx.vc.h, v0 + t * 5 + 5)
            sy.dma_start(out=xn_out[t * 128:(t + 1) * 128, :],
                         in_=xst[:, par * D:(par + 1) * D]).then_inc(ssl[par].h, 16)

    @block.vector
    def _(v):
        for t in range(ntile):
            par = t % 2
            v.wait_ge(lsl[par].h, (t // 2 + 1) * 16)
            w = win[:, par * D:(par + 1) * D]
            v.tensor_reduce(sums[:, 4 * par:4 * par + 1], w, axis=AX.X,
                            op=ALU.add).then_inc(ctx.vc.h, 1)
            v.wait_ge(ctx.vc.h, v0 + t * 5 + 1)
            v.tensor_scalar(out=sums[:, 4 * par:4 * par + 1], in0=sums[:, 4 * par:4 * par + 1],
                            scalar1=-1.0 / D, scalar2=None, op0=ALU.mult).then_inc(ctx.vc.h, 1)
            v.wait_ge(ctx.vc.h, v0 + t * 5 + 2)
            v.wait_ge(ctx.sc.h, c0 + t * 3 + 1)
            v.tensor_tensor(out=sums[:, 4 * par + 1:4 * par + 2], in0=sums[:, 4 * par:4 * par + 1],
                            in1=sums[:, 4 * par:4 * par + 1], op=ALU.mult).then_inc(ctx.vc.h, 1)
            v.wait_ge(ctx.vc.h, v0 + t * 5 + 3)
            v.scalar_tensor_tensor(out=sums[:, 4 * par + 2:4 * par + 3],
                                   in0=ssq[:, par:par + 1], scalar=1.0 / D,
                                   in1=sums[:, 4 * par + 1:4 * par + 2],
                                   op0=ALU.mult, op1=ALU.subtract).then_inc(ctx.vc.h, 1)
            v.wait_ge(ctx.sc.h, c0 + t * 3 + 3)
            v.tensor_scalar(out=xst[:, par * D:(par + 1) * D], in0=w,
                            scalar1=sums[:, 4 * par:4 * par + 1],
                            scalar2=sums[:, 4 * par + 3:4 * par + 4],
                            op0=ALU.add, op1=ALU.mult).then_inc(ctx.vc.h, 1)

    @block.scalar
    def _(s):
        for t in range(ntile):
            par = t % 2
            if t >= 2:
                s.wait_ge(ctx.vc.h, v0 + (t - 1) * 5)
            s.wait_ge(lsl[par].h, (t // 2 + 1) * 16)
            s.activation(out=sq[:], in_=win[:, par * D:(par + 1) * D], func=AF.Square,
                         bias=0.0, scale=1.0, accum_out=ssq[:, par:par + 1]).then_inc(ctx.sc.h, 1)
            s.wait_ge(ctx.vc.h, v0 + t * 5 + 4)
            s.activation(out=sums[:, 4 * par + 3:4 * par + 4], in_=sums[:, 4 * par + 2:4 * par + 3],
                         func=AF.Ln, bias=cvec[:, 0:1], scale=1.0).then_inc(ctx.sc.h, 1)
            s.wait_ge(ctx.sc.h, c0 + t * 3 + 2)
            s.activation(out=sums[:, 4 * par + 3:4 * par + 4], in_=sums[:, 4 * par + 3:4 * par + 4],
                         func=AF.Exp, bias=0.0, scale=-0.5).then_inc(ctx.sc.h, 1)


def emit_ffn0(ctx, S, xin, xout, xn, ntile, wu, wg, wd, su_ug_d, su_dp_d,
              ws_up_d, ws_gt_d, ws_dp_d, h128s, ident,
              parts=("y1", "z", "fwht", "down")):
    nc, block = ctx.nc, ctx.block
    NCH = (ntile * 128) // TCH
    NTT = TCH // 128
    SLAB = 2 * DT * 128
    HALF = 32 * 128

    xnc = ctx.sb(S, "xnc", (128, NTT * D), BF16)
    xnT = ctx.sb(S, "xnT", (128, DT * TCH), BF16)
    y1 = ctx.sb(S, "y1", (128, DT * TCH), BF16)
    tmp = ctx.sb(S, "tmpc", (128, 2 * TCH), BF16)
    btmp = ctx.sb(S, "btmp", (128, 2048), BF16)
    z = ctx.sb(S, "zf", (128, IHI * TCH), BF16)
    sil = ctx.sb(S, "sil", (128, 2 * TCH))
    foutT = ctx.sb(S, "foutT", (128, DT * TCH), BF16)
    wub = ctx.sb(S, "wub", (128, 2 * SLAB), BF16)
    wgb = ctx.sb(S, "wgb", (128, 2 * SLAB), BF16)
    wdb = ctx.sb(S, "wdb", (128, 2 * HALF), BF16)
    win = ctx.sb(S, "fwin", (128, 2 * D))
    su_ug = ctx.sb(S, "suug", (128, 16))
    su_dp = ctx.sb(S, "sudp", (128, 64))
    wse_u = ctx.sb(S, "wseu", (128, 64))
    wse_g = ctx.sb(S, "wseg", (128, 64))
    wse_d = ctx.sb(S, "wsed", (128, 16))
    psA = ctx.ps(S, "psA", (128, 2 * 512), F32)
    psU = ctx.ps(S, "psU", (128, 2 * 512), F32)
    psG = ctx.ps(S, "psG", (128, 2 * 512), F32)
    psD = ctx.ps(S, "psD", (128, 512), F32)
    psT = ctx.ps(S, "psT", (128, 1024), BF16)

    lf = ctx.new_sem("ffc")
    sxn = ctx.new_sem("fxn")
    swz = [ctx.new_sem("fwz"), ctx.new_sem("fwz")]
    swd = [ctx.new_sem("fwd"), ctx.new_sem("fwd")]
    srl = [ctx.new_sem("frl"), ctx.new_sem("frl")]
    srs = [ctx.new_sem("frs"), ctx.new_sem("frs")]
    lsm = lf.plan(5 * 16)
    vsm = ctx.vc.plan(4)

    @block.sync
    def _(sy):
        sy.dma_start(out=su_ug[:], in_=su_ug_d[:]).then_inc(lf.h, 16)
        sy.dma_start(out=su_dp[:], in_=su_dp_d[:]).then_inc(lf.h, 16)
        sy.dma_start(out=wse_u[:], in_=ws_up_d[:]).then_inc(lf.h, 16)
        sy.dma_start(out=wse_g[:], in_=ws_gt_d[:]).then_inc(lf.h, 16)
        sy.dma_start(out=wse_d[:], in_=ws_dp_d[:]).then_inc(lf.h, 16)

    @block.vector
    def _(v):
        v.wait_ge(lf.h, 80)
        v.tensor_scalar(out=wse_u[:], in0=wse_u[:], scalar1=C_UG, scalar2=None,
                        op0=ALU.mult).then_inc(ctx.vc.h, 1)
        v.tensor_scalar(out=wse_g[:], in0=wse_g[:], scalar1=C_UG, scalar2=None,
                        op0=ALU.mult).then_inc(ctx.vc.h, 1)
        v.tensor_scalar(out=wse_d[:], in0=wse_d[:], scalar1=C_DP, scalar2=None,
                        op0=ALU.mult).then_inc(ctx.vc.h, 1)
        v.wait_ge(ctx.vc.h, vsm + 3)
        v.tensor_tensor(out=wse_u[:], in0=wse_u[:], in1=su_dp[:],
                        op=ALU.mult).then_inc(ctx.vc.h, 1)
        v.wait_ge(ctx.vc.h, vsm + 4)

    if "consts" in parts and len(parts) == 1:
        return
    for ch in range(NCH):
        t0 = ch * TCH

        lx = sxn.plan(16)
        tx = ctx.tc.plan(NTT * DT)
        vx = ctx.vc.plan(NTT * DT)

        @block.sync
        def _(sy):
            if ch > 0:
                sy.wait_ge(ctx.tc.h, tx)
            sy.dma_start(out=xnc[:].rearrange("p (t d) -> p t d", t=NTT),
                         in_=xn[t0:t0 + TCH, :].rearrange("(t p) d -> p t d", p=128)
                         ).then_inc(sxn.h, 16)

        if "xl" in parts:
            continue

        @block.tensor
        def _(te):
            te.wait_ge(sxn.h, lx + 16)
            te.wait_ge(ctx.vc.h, vx)
            n = 0
            for tt in range(NTT):
                for d_ in range(DT):
                    if n >= 1:
                        te.wait_ge(ctx.vc.h, vx + n)
                    te.transpose(psT[:, 0:128],
                                 xnc[:, tt * D + d_ * 128: tt * D + (d_ + 1) * 128],
                                 ident[:]).then_inc(ctx.tc.h, 1)
                    n += 1

        @block.vector
        def _(v):
            n = 0
            for tt in range(NTT):
                for d_ in range(DT):
                    n += 1
                    v.wait_ge(ctx.tc.h, tx + n)
                    v.tensor_copy(xnT[:, d_ * TCH + tt * 128: d_ * TCH + (tt + 1) * 128],
                                  psT[:, 0:128]).then_inc(ctx.vc.h, 1)

        v0 = ctx.vc.plan(2 * DT)
        t0s = ctx.tc.plan(DT)

        @block.vector
        def _(v):
            v.wait_ge(ctx.vc.h, vx + NTT * DT)
            for T in range(DT):
                if T >= 2:
                    v.wait_ge(ctx.tc.h, t0s + T - 1)
                v.tensor_scalar(out=tmp[:, (T % 2) * TCH:(T % 2) * TCH + TCH],
                                in0=xnT[:, T * TCH:(T + 1) * TCH],
                                scalar1=su_ug[:, T:T + 1], scalar2=None,
                                op0=ALU.mult).then_inc(ctx.vc.h, 1)
                if T > 0:
                    v.wait_ge(ctx.tc.h, t0s + T)
                    v.tensor_copy(y1[:, (T - 1) * TCH: T * TCH],
                                  psA[:, ((T - 1) % 2) * 512:((T - 1) % 2) * 512 + 512]
                                  ).then_inc(ctx.vc.h, 1)
            v.wait_ge(ctx.tc.h, t0s + DT)
            v.tensor_copy(y1[:, (DT - 1) * TCH: DT * TCH],
                          psA[:, ((DT - 1) % 2) * 512:((DT - 1) % 2) * 512 + 512]
                          ).then_inc(ctx.vc.h, 1)

        @block.tensor
        def _(te):
            for T in range(DT):
                need = v0 + 1 if T == 0 else v0 + 2 * T
                te.wait_ge(ctx.vc.h, need)
                te.matmul(psA[:, (T % 2) * 512:(T % 2) * 512 + 512], lhsT=h128s[:],
                          rhs=tmp[:, (T % 2) * TCH:(T % 2) * TCH + TCH], start=True,
                          stop=True).then_inc(ctx.tc.h, 1)

        if "bfly" in parts or "z" in parts:
            emit_butterflies(ctx, y1, btmp, DT, TCH)
        if "z" not in parts:
            continue

        NSLAB = 32
        y1r = ctx.vc.v
        for sl in range(NSLAB):
            swz[sl % 2].plan(32)
        tz = ctx.tc.plan(2 * IHI)
        sz = ctx.sc.plan(IHI)

        vseq = []
        for ih in range(IHI + 1):
            if ih < IHI:
                vseq.append(("evac", ih))
            i = ih - 1
            if i >= 0:
                vseq.append(("copy", i))
                hh = 1
                while hh <= IHI // 2:
                    if (i + 1) % (2 * hh) == 0:
                        vseq.append(("bf", hh, i + 1 - 2 * hh))
                    hh *= 2
        nv_ops = sum(2 if e[0] == "bf" else 1 for e in vseq)
        vz = ctx.vc.plan(nv_ops)
        vcnt_evac, vcnt_copy = {}, {}
        cnt = vz
        for e in vseq:
            if e[0] == "bf":
                cnt += 2
            else:
                cnt += 1
                (vcnt_evac if e[0] == "evac" else vcnt_copy)[e[1]] = cnt
        tcnt_Z = {ih: tz + (1 if ih == 0 else 2 * ih) for ih in range(IHI)}
        tcnt_A = {i: tz + (2 * i + 3 if i < IHI - 1 else 2 * IHI) for i in range(IHI)}
        scnt = {ih: sz + ih + 1 for ih in range(IHI)}

        @block.gpsimd
        def _(g):
            for sl in range(NSLAB):
                par = sl % 2
                if sl >= 2:
                    g.wait_ge(ctx.tc.h, tcnt_Z[2 * sl - 3])
                g.dma_start(out=wub[:, par * SLAB:(par + 1) * SLAB],
                            in_=wu[sl].rearrange("p a b q -> p (a b q)")
                            ).then_inc(swz[par].h, 16)
                g.dma_start(out=wgb[:, par * SLAB:(par + 1) * SLAB],
                            in_=wg[sl].rearrange("p a b q -> p (a b q)")
                            ).then_inc(swz[par].h, 16)

        @block.tensor
        def _(te):
            te.wait_ge(ctx.vc.h, y1r)
            for ih in range(IHI):
                sl, j = ih // 2, ih % 2
                par = sl % 2
                te.wait_ge(swz[par].h, (ch * (NSLAB // 2) + sl // 2 + 1) * 32)
                if ih >= 2:
                    te.wait_ge(ctx.sc.h, scnt[ih - 2])
                    te.wait_ge(ctx.vc.h, vcnt_evac[ih - 2])
                pu = psU[:, (ih % 2) * 512:(ih % 2) * 512 + 512]
                pg = psG[:, (ih % 2) * 512:(ih % 2) * 512 + 512]
                for T in range(DT):
                    off = par * SLAB + (j * DT + T) * 128
                    te.matmul(pu, lhsT=wub[:, off:off + 128], rhs=y1[:, T * TCH:(T + 1) * TCH],
                              start=(T == 0), stop=(T == DT - 1))
                for T in range(DT):
                    off = par * SLAB + (j * DT + T) * 128
                    mm = te.matmul(pg, lhsT=wgb[:, off:off + 128], rhs=y1[:, T * TCH:(T + 1) * TCH],
                                   start=(T == 0), stop=(T == DT - 1))
                mm.then_inc(ctx.tc.h, 1)
                if ih >= 1:
                    i = ih - 1
                    te.wait_ge(ctx.vc.h, vcnt_evac[i])
                    if i >= 2:
                        te.wait_ge(ctx.vc.h, vcnt_copy[i - 2])
                    te.matmul(psA[:, (i % 2) * 512:(i % 2) * 512 + 512], lhsT=h128s[:],
                              rhs=z[:, i * TCH:(i + 1) * TCH], start=True,
                              stop=True).then_inc(ctx.tc.h, 1)
            i = IHI - 1
            te.wait_ge(ctx.vc.h, vcnt_evac[i])
            te.wait_ge(ctx.vc.h, vcnt_copy[i - 2])
            te.matmul(psA[:, (i % 2) * 512:(i % 2) * 512 + 512], lhsT=h128s[:],
                      rhs=z[:, i * TCH:(i + 1) * TCH], start=True,
                      stop=True).then_inc(ctx.tc.h, 1)

        @block.scalar
        def _(s):
            for ih in range(IHI):
                if ih >= 2:
                    s.wait_ge(ctx.vc.h, vcnt_evac[ih - 2])
                s.wait_ge(ctx.tc.h, tcnt_Z[ih])
                pg = psG[:, (ih % 2) * 512:(ih % 2) * 512 + 512]
                s.activation(out=sil[:, (ih % 2) * TCH:(ih % 2) * TCH + TCH], in_=pg,
                             func=AF.Silu, bias=0.0,
                             scale=wse_g[:, ih:ih + 1]).then_inc(ctx.sc.h, 1)

        @block.vector
        def _(v):
            cnt2 = vz
            for e in vseq:
                if e[0] == "evac":
                    ih = e[1]
                    v.wait_ge(ctx.sc.h, scnt[ih])
                    pu = psU[:, (ih % 2) * 512:(ih % 2) * 512 + 512]
                    v.scalar_tensor_tensor(out=z[:, ih * TCH:(ih + 1) * TCH], in0=pu,
                                           scalar=wse_u[:, ih:ih + 1],
                                           in1=sil[:, (ih % 2) * TCH:(ih % 2) * TCH + TCH],
                                           op0=ALU.mult, op1=ALU.mult).then_inc(ctx.vc.h, 1)
                    cnt2 += 1
                elif e[0] == "copy":
                    i = e[1]
                    v.wait_ge(ctx.tc.h, tcnt_A[i])
                    v.tensor_copy(z[:, i * TCH:(i + 1) * TCH],
                                  psA[:, (i % 2) * 512:(i % 2) * 512 + 512]
                                  ).then_inc(ctx.vc.h, 1)
                    cnt2 += 1
                else:
                    _, hh, s0 = e
                    wsp = hh * TCH
                    av = z[:, s0 * TCH: s0 * TCH + wsp]
                    bv = z[:, s0 * TCH + wsp: s0 * TCH + 2 * wsp]
                    v.wait_ge(ctx.vc.h, cnt2)
                    v.tensor_tensor(out=bv, in0=av, in1=bv,
                                    op=ALU.subtract).then_inc(ctx.vc.h, 1)
                    cnt2 += 1
                    v.wait_ge(ctx.vc.h, cnt2)
                    v.scalar_tensor_tensor(out=av, in0=av, scalar=2.0, in1=bv,
                                           op0=ALU.mult, op1=ALU.subtract
                                           ).then_inc(ctx.vc.h, 1)
                    cnt2 += 1

        if "down" not in parts:
            continue
        zr = ctx.vc.v
        for n in range(DT * 2):
            swd[n % 2].plan(16)
        td = ctx.tc.plan(DT * 2)
        vd = ctx.vc.plan(DT)

        @block.gpsimd
        def _(g):
            for n in range(DT * 2):
                dt_, hf = n // 2, n % 2
                par = n % 2
                if n >= 2:
                    g.wait_ge(ctx.tc.h, td + n - 1)
                g.dma_start(out=wdb[:, par * HALF:(par + 1) * HALF],
                            in_=wd[dt_, :, hf * 32:(hf + 1) * 32, :].rearrange("p a q -> p (a q)")
                            ).then_inc(swd[par].h, 16)

        @block.tensor
        def _(te):
            te.wait_ge(ctx.vc.h, zr)
            for n in range(DT * 2):
                dt_, hf = n // 2, n % 2
                par = n % 2
                te.wait_ge(swd[par].h, (ch * DT + n // 2 + 1) * 16)
                if hf == 0 and dt_ >= 1:
                    te.wait_ge(ctx.vc.h, vd + dt_)
                pd = psD[:, 0:512]
                for a in range(32):
                    ih = hf * 32 + a
                    mm = te.matmul(pd, lhsT=wdb[:, par * HALF + a * 128: par * HALF + (a + 1) * 128],
                                   rhs=z[:, ih * TCH:(ih + 1) * TCH],
                                   start=(ih == 0), stop=(ih == IHI - 1))
                mm.then_inc(ctx.tc.h, 1)

        @block.vector
        def _(v):
            for dt_ in range(DT):
                v.wait_ge(ctx.tc.h, td + 2 * dt_ + 2)
                pd = psD[:, 0:512]
                v.tensor_scalar(out=foutT[:, dt_ * TCH:(dt_ + 1) * TCH], in0=pd,
                                scalar1=wse_d[:, dt_:dt_ + 1], scalar2=None,
                                op0=ALU.mult).then_inc(ctx.vc.h, 1)

        for tt in range(NTT):
            srl[tt % 2].plan(16)
            srs[tt % 2].plan(16)
        ta = ctx.tc.plan(NTT * DT)
        va = ctx.vc.plan(NTT * DT)

        @block.sync
        def _(sy):
            for tt in range(NTT):
                par = tt % 2
                gl = ch * (NTT // 2) + tt // 2
                if gl > 0:
                    sy.wait_ge(srs[par].h, gl * 16)
                sy.dma_start(out=win[:, par * D:(par + 1) * D],
                             in_=xin[t0 + tt * 128: t0 + (tt + 1) * 128, :]
                             ).then_inc(srl[par].h, 16)
                sy.wait_ge(ctx.vc.h, va + (tt + 1) * DT)
                sy.dma_start(out=xout[t0 + tt * 128: t0 + (tt + 1) * 128, :],
                             in_=win[:, par * D:(par + 1) * D]).then_inc(srs[par].h, 16)

        @block.tensor
        def _(te):
            n = 0
            for tt in range(NTT):
                for dt_ in range(DT):
                    te.wait_ge(ctx.vc.h, vd + dt_ + 1)
                    if n >= 1:
                        te.wait_ge(ctx.vc.h, va + n)
                    te.transpose(psT[:, 0:128],
                                 foutT[:, dt_ * TCH + tt * 128: dt_ * TCH + (tt + 1) * 128],
                                 ident[:]).then_inc(ctx.tc.h, 1)
                    n += 1

        @block.vector
        def _(v):
            n = 0
            for tt in range(NTT):
                par = tt % 2
                v.wait_ge(srl[par].h, (ch * (NTT // 2) + tt // 2 + 1) * 16)
                for dt_ in range(DT):
                    n += 1
                    v.wait_ge(ctx.tc.h, ta + n)
                    dst = win[:, par * D + dt_ * 128: par * D + (dt_ + 1) * 128]
                    v.tensor_tensor(out=dst, in0=dst, in1=psT[:, 0:128],
                                    op=ALU.add).then_inc(ctx.vc.h, 1)


def emit_butterflies(ctx, buf, btmp, nblk, tch):
    block = ctx.block
    h = 1
    stages = []
    while h < nblk:
        stages.append(h)
        h *= 2
    vb = ctx.vc.plan(2 * len(stages))

    @block.vector
    def _(v):
        n = vb
        for h in stages:
            G = nblk // (2 * h)
            w = h * tch
            view = buf[:, 0:nblk * tch].rearrange("p (g two w) -> p g two w", g=G, two=2)
            av = view[:, :, 0, :]
            bv = view[:, :, 1, :]
            v.wait_ge(ctx.vc.h, n)
            v.tensor_tensor(out=bv, in0=av, in1=bv, op=ALU.subtract).then_inc(ctx.vc.h, 1)
            n += 1
            v.wait_ge(ctx.vc.h, n)
            v.scalar_tensor_tensor(out=av, in0=av, scalar=2.0, in1=bv,
                                   op0=ALU.mult, op1=ALU.subtract).then_inc(ctx.vc.h, 1)
            n += 1


def build_L1(stages=("ln1", "ffn", "ln2")):
    nc = bass.Bass("TRN2", target_bir_lowering=False, debug=False)
    x = nc.dram_tensor("x", [TOK, D], F32, kind="ExternalInput").ap()
    wu = nc.dram_tensor("wu", [32, 128, 2, DT, 128], F32, kind="ExternalInput").ap()
    wg = nc.dram_tensor("wg", [32, 128, 2, DT, 128], F32, kind="ExternalInput").ap()
    wd = nc.dram_tensor("wd", [DT, 128, IHI, 128], F32, kind="ExternalInput").ap()
    su_ug_d = nc.dram_tensor("su_ug", [128, 16], F32, kind="ExternalInput").ap()
    su_dp_d = nc.dram_tensor("su_dp", [128, 64], F32, kind="ExternalInput").ap()
    ws_up_d = nc.dram_tensor("ws_up", [128, 64], F32, kind="ExternalInput").ap()
    ws_gt_d = nc.dram_tensor("ws_gt", [128, 64], F32, kind="ExternalInput").ap()
    ws_dp_d = nc.dram_tensor("ws_dp", [128, 16], F32, kind="ExternalInput").ap()
    h128_d = nc.dram_tensor("h128s", [128, 128], F32, kind="ExternalInput").ap()
    id_d = nc.dram_tensor("ident", [128, 128], F32, kind="ExternalInput").ap()
    cv_d = nc.dram_tensor("consts", [128, 4], F32, kind="ExternalInput").ap()
    x1_d = nc.dram_tensor("x1", [TOK, D], F32, kind="ExternalOutput").ap()
    xn2_d = nc.dram_tensor("xn2", [TOK, D], BF16, kind="ExternalOutput").ap()
    xn_i = nc.dram_tensor("xn_i", [TOK, D], BF16, kind="Internal").ap()

    with ExitStack() as ES:
        ctx = Ctx(nc, ES)
        block = ctx.block
        h128s = ctx.sb(ES, "h128", (128, 128), BF16)
        ident = ctx.sb(ES, "id", (128, 128), BF16)
        cvec = ctx.sb(ES, "cv", (128, 4))
        lc = ctx.lg.plan(3 * 16)

        @block.gpsimd
        def _(g):
            g.dma_start(out=h128s[:], in_=h128_d[:]).then_inc(ctx.lg.h, 16)
            g.dma_start(out=ident[:], in_=id_d[:]).then_inc(ctx.lg.h, 16)
            g.dma_start(out=cvec[:], in_=cv_d[:]).then_inc(ctx.lg.h, 16)

        ctx.join()

        if "ln1" in stages:
            with ExitStack() as S1:
                emit_ln_stream(ctx, S1, x, xn_i, NT, cvec)
                ctx.join()
        if "ffn" in stages:
            fp = tuple(s[1:] for s in stages if s.startswith("_"))
            parts = fp if fp else ("y1", "z", "fwht", "down")
            with ExitStack() as S2:
                emit_ffn0(ctx, S2, x, x1_d, xn_i, NT, wu, wg, wd, su_ug_d, su_dp_d,
                          ws_up_d, ws_gt_d, ws_dp_d, h128s, ident, parts=parts)
                ctx.join()
        if "ln2" in stages:
            with ExitStack() as S3:
                emit_ln_stream(ctx, S3, x1_d, xn2_d, NT, cvec)
                ctx.join()

        ctx.join()

    return nc


SEQ = 4096
NTS = SEQ // 128


def build_L2():
    nc = bass.Bass("TRN2", target_bir_lowering=False, debug=False)
    xq_d = nc.dram_tensor("xq", [NTS, DT, 128, 128], BF16, kind="ExternalInput").ap()
    qkvw_d = nc.dram_tensor("qkvw", [4, DT, 128, 384], F32, kind="ExternalInput").ap()
    slope_d = nc.dram_tensor("slope", [1, 4], F32, kind="ExternalInput").ap()
    ones_d = nc.dram_tensor("ones1", [1, 128], F32, kind="ExternalInput").ap()
    arrq_d = nc.dram_tensor("arrq", [128, 2], F32, kind="ExternalInput").ap()
    arrk_d = nc.dram_tensor("arrk", [128, 2], F32, kind="ExternalInput").ap()
    idxT_d = nc.dram_tensor("idxT", [2, 128, 256], F32, kind="ExternalInput").ap()
    mskT_d = nc.dram_tensor("mskT", [2, 128, 256], F32, kind="ExternalInput").ap()
    id_d = nc.dram_tensor("ident", [128, 128], F32, kind="ExternalInput").ap()
    out_d = nc.dram_tensor("outT", [512, SEQ], F32, kind="ExternalOutput").ap()

    NBLK = 16

    with ExitStack() as ES:
        ctx = Ctx(nc, ES)
        block = ctx.block
        ident = ctx.sb(ES, "id", (128, 128), BF16)
        slope = ctx.sb(ES, "slp", (1, 4))
        ones1 = ctx.sb(ES, "on1", (1, 128))
        arrq = ctx.sb(ES, "arq", (128, 2))
        arrk = ctx.sb(ES, "ark", (128, 2))
        idxT = ctx.sb(ES, "idx", (128, 2 * 256))
        mskT = ctx.sb(ES, "msk", (128, 2 * 256))
        negs = ctx.sb(ES, "ngs", (128, 4))
        xqb = ctx.sb(ES, "xqb", (128, 2 * DT * 128), BF16)
        qwb = ctx.sb(ES, "qwb", (128, 2 * DT * 384), BF16)
        qkvt = ctx.sb(ES, "qkvt", (128, NTS * 384), BF16)
        qsc = ctx.sb(ES, "qsc", (128, 256), BF16)
        ksc = ctx.sb(ES, "ksc", (128, 256), BF16)
        qT = ctx.sb(ES, "qT", (128, 256), BF16)
        qsT = ctx.sb(ES, "qsT", (128, 256), BF16)
        kT = ctx.sb(ES, "kT", (128, 256), BF16)
        ssc = ctx.sb(ES, "ssc", (128, 2 * 256), BF16)
        decT = ctx.sb(ES, "decT", (128, 2 * 256))
        qdec = ctx.sb(ES, "qdec", (128, 2))
        kdec = ctx.sb(ES, "kdec", (128, 2))
        bdcol = ctx.sb(ES, "bdc", (128, 1))
        kv = ctx.sb(ES, "kv", (128, 128))
        kvbf = ctx.sb(ES, "kvbf", (128, 128), BF16)
        outh = ctx.sb(ES, "outh", (128, SEQ))
        psQ = ctx.ps(ES, "psQ", (128, 2 * 512), F32)
        psTt = ctx.ps(ES, "psTt", (128, 2 * 1024), BF16)
        psS = ctx.ps(ES, "psS", (128, 2 * 512), F32)
        psO = ctx.ps(ES, "psO", (128, 512), F32)
        psKV = ctx.ps(ES, "psKV", (128, 512), F32)

        sxq = [ctx.new_sem("xqA"), ctx.new_sem("xqB")]
        sqw = [ctx.new_sem("qwA"), ctx.new_sem("qwB")]
        sout = ctx.new_sem("out")
        lcs = ctx.new_sem("lcs")
        lc0 = lcs.plan(6 * 16)
        lc = ctx.lg.plan(16)

        @block.sync
        def _(sy):
            sy.dma_start(out=slope[:], in_=slope_d[:]).then_inc(lcs.h, 16)
            sy.dma_start(out=ones1[:], in_=ones_d[:]).then_inc(lcs.h, 16)
            sy.dma_start(out=arrq[:], in_=arrq_d[:]).then_inc(lcs.h, 16)
            sy.dma_start(out=arrk[:], in_=arrk_d[:]).then_inc(lcs.h, 16)
            sy.dma_start(out=idxT[:].rearrange("p (u m) -> p u m", u=2),
                         in_=idxT_d.rearrange("u p m -> p u m")).then_inc(lcs.h, 16)
            sy.dma_start(out=mskT[:].rearrange("p (u m) -> p u m", u=2),
                         in_=mskT_d.rearrange("u p m -> p u m")).then_inc(lcs.h, 16)

        @block.gpsimd
        def _(g):
            g.dma_start(out=ident[:], in_=id_d[:]).then_inc(ctx.lg.h, 16)

        ctx.join()

        tb = ctx.tc.plan(1)
        vbb = ctx.vc.plan(1)

        @block.tensor
        def _(te):
            te.matmul(psO[:, 0:4], lhsT=ones1[:], rhs=slope[:], start=True,
                      stop=True).then_inc(ctx.tc.h, 1)

        @block.vector
        def _(v):
            v.wait_ge(ctx.tc.h, tb + 1)
            v.tensor_scalar(out=negs[:], in0=psO[:, 0:4], scalar1=-1.0, scalar2=None,
                            op0=ALU.mult).then_inc(ctx.vc.h, 1)

        ctx.join()

        hq_base = []
        for h in range(4):
            head_tc0 = ctx.tc.v
            head_vc0 = ctx.vc.v
            head_sc0 = ctx.sc.v

            c0 = ctx.sc.plan(4)
            v0 = ctx.vc.plan(2)

            @block.scalar
            def _(s, h=h, head_vc0=head_vc0):
                s.wait_ge(ctx.vc.h, head_vc0)
                s.activation(out=qdec[:], in_=arrq[:], func=AF.Exp, bias=0.0,
                             scale=negs[:, h:h + 1]).then_inc(ctx.sc.h, 1)
                s.activation(out=kdec[:], in_=arrk[:], func=AF.Exp, bias=0.0,
                             scale=negs[:, h:h + 1]).then_inc(ctx.sc.h, 1)
                s.activation(out=bdcol[:], in_=negs[:, h:h + 1], func=AF.Exp, bias=0.0,
                             scale=256.0).then_inc(ctx.sc.h, 1)
                s.activation(out=decT[:], in_=idxT[:], func=AF.Exp, bias=0.0,
                             scale=negs[:, h:h + 1]).then_inc(ctx.sc.h, 1)

            @block.vector
            def _(v, c0=c0, v0=v0):
                v.wait_ge(ctx.sc.h, c0 + 4)
                v.tensor_tensor(out=decT[:], in0=decT[:], in1=mskT[:],
                                op=ALU.mult).then_inc(ctx.vc.h, 1)
                v.wait_ge(ctx.vc.h, v0 + 1)
                v.memset(kv[:], 0.0)
                v.nop().then_inc(ctx.vc.h, 1)

            hw = sqw[h % 2].plan(16)

            @block.gpsimd
            def _(g, h=h):
                par = h % 2
                if h >= 2:
                    g.wait_ge(ctx.tc.h, hq_base[h - 2] + NTS)
                g.dma_start(out=qwb[:, par * DT * 384:(par + 1) * DT * 384].rearrange(
                                "p (a q) -> p a q", a=DT),
                            in_=qkvw_d[h].rearrange("a p q -> p a q")
                            ).then_inc(sqw[par].h, 16)

            tq = ctx.tc.plan(NTS)
            hq_base.append(tq)
            cq = ctx.sc.plan(NTS)

            @block.sync
            def _(sy, h=h):
                for tt in range(NTS):
                    par = tt % 2
                    seq_no = h * (NTS // 2) + tt // 2
                    if seq_no > 0:
                        if tt >= 2:
                            need = hq_base[h] + tt - 1
                        else:
                            need = hq_base[h - 1] + NTS - 2 + par + 1
                        sy.wait_ge(ctx.tc.h, need)
                    sy.dma_start(out=xqb[:, par * DT * 128:(par + 1) * DT * 128].rearrange(
                                     "p (a q) -> p a q", a=DT),
                                 in_=xq_d[tt].rearrange("a p q -> p a q")
                                 ).then_inc(sxq[par].h, 16)
            for tt in range(NTS):
                sxq[tt % 2].plan(16)

            @block.tensor
            def _(te, h=h, tq=tq, cq=cq):
                par_h = h % 2
                te.wait_ge(sqw[h % 2].h, (h // 2 + 1) * 16)
                for tt in range(NTS):
                    par = tt % 2
                    te.wait_ge(sxq[par].h, (h * (NTS // 2) + tt // 2 + 1) * 16)
                    if tt >= 2:
                        te.wait_ge(ctx.sc.h, cq + tt - 1)
                    pq = psQ[:, (tt % 2) * 512:(tt % 2) * 512 + 384]
                    for T in range(DT):
                        mm = te.matmul(pq, lhsT=xqb[:, par * DT * 128 + T * 128: par * DT * 128 + (T + 1) * 128],
                                       rhs=qwb[:, par_h * DT * 384 + T * 384: par_h * DT * 384 + (T + 1) * 384],
                                       start=(T == 0), stop=(T == DT - 1))
                    mm.then_inc(ctx.tc.h, 1)

            @block.scalar
            def _(s, h=h, tq=tq, head_tc0=head_tc0):
                if h > 0:
                    s.wait_ge(ctx.tc.h, head_tc0)
                for tt in range(NTS):
                    s.wait_ge(ctx.tc.h, tq + tt + 1)
                    pq = psQ[:, (tt % 2) * 512:(tt % 2) * 512 + 384]
                    s.activation(out=qkvt[:, tt * 384:(tt + 1) * 384], in_=pq,
                                 func=AF.Silu, bias=0.0, scale=1.0).then_inc(ctx.sc.h, 1)

            for j in range(NBLK):
                tj = ctx.tc.plan(10)
                vj = ctx.vc.plan(14)

                @block.vector
                def _(v, h=h, j=j, tj=tj, vj=vj, cq=cq):
                    v.wait_ge(ctx.tc.h, tj)
                    v.wait_ge(ctx.sc.h, cq + 2 * j + 2)
                    for u in range(2):
                        col = (2 * j + u) * 384
                        v.tensor_scalar(out=qsc[:, u * 128:(u + 1) * 128],
                                        in0=qkvt[:, col:col + 128],
                                        scalar1=qdec[:, u:u + 1], scalar2=None,
                                        op0=ALU.mult).then_inc(ctx.vc.h, 1)
                        v.tensor_scalar(out=ksc[:, u * 128:(u + 1) * 128],
                                        in0=qkvt[:, col + 128:col + 256],
                                        scalar1=kdec[:, u:u + 1], scalar2=None,
                                        op0=ALU.mult).then_inc(ctx.vc.h, 1)
                    for i6 in range(6):
                        v.wait_ge(ctx.tc.h, tj + i6 + 1)
                        dst = [qT[:, 0:128], qT[:, 128:256], qsT[:, 0:128],
                               qsT[:, 128:256], kT[:, 0:128], kT[:, 128:256]][i6]
                        v.tensor_copy(dst, psTt[:, (i6 % 2) * 1024:(i6 % 2) * 1024 + 128]
                                      ).then_inc(ctx.vc.h, 1)
                    v.wait_ge(ctx.tc.h, tj + 8)
                    v.tensor_tensor(out=ssc[:, 0:256], in0=psS[:, 0:256],
                                    in1=decT[:, 0:256], op=ALU.mult)
                    v.tensor_tensor(out=ssc[:, 256:512], in0=psS[:, 512:768],
                                    in1=decT[:, 256:512], op=ALU.mult).then_inc(ctx.vc.h, 1)
                    v.wait_ge(ctx.tc.h, tj + 10)
                    v.scalar_tensor_tensor(out=kv[:], in0=kv[:], scalar=bdcol[:],
                                           in1=psKV[:, 0:128], op0=ALU.mult,
                                           op1=ALU.add).then_inc(ctx.vc.h, 1)
                    v.wait_ge(ctx.vc.h, vj + 12)
                    v.tensor_copy(kvbf[:], kv[:]).then_inc(ctx.vc.h, 1)
                    v.wait_ge(ctx.tc.h, tj + 9)
                    if j == 0:
                        v.wait_ge(sout.h, h * 16)
                    v.tensor_copy(outh[:, j * 256:(j + 1) * 256],
                                  psO[:, 0:256]).then_inc(ctx.vc.h, 1)

                @block.tensor
                def _(te, h=h, j=j, tj=tj, vj=vj, cq=cq):
                    te.wait_ge(ctx.sc.h, cq + 2 * j + 2)
                    srcs = [qkvt[:, (2 * j) * 384:(2 * j) * 384 + 128],
                            qkvt[:, (2 * j + 1) * 384:(2 * j + 1) * 384 + 128],
                            qsc[:, 0:128], qsc[:, 128:256],
                            qkvt[:, (2 * j) * 384 + 128:(2 * j) * 384 + 256],
                            qkvt[:, (2 * j + 1) * 384 + 128:(2 * j + 1) * 384 + 256]]
                    for i6, src in enumerate(srcs):
                        if i6 in (2, 3):
                            te.wait_ge(ctx.vc.h, vj + 3)
                        if i6 >= 2:
                            te.wait_ge(ctx.vc.h, vj + 4 + i6 - 1)
                        te.transpose(psTt[:, (i6 % 2) * 1024:(i6 % 2) * 1024 + 128],
                                     src, ident[:]).then_inc(ctx.tc.h, 1)
                    te.wait_ge(ctx.vc.h, vj + 10)
                    for u in range(2):
                        te.matmul(psS[:, u * 512:u * 512 + 256],
                                  lhsT=kT[:, u * 128:(u + 1) * 128],
                                  rhs=qT[:], start=True, stop=True).then_inc(ctx.tc.h, 1)
                    te.wait_ge(ctx.vc.h, vj + 11)
                    first = True
                    if j > 0:
                        te.matmul(psO[:, 0:256], lhsT=kvbf[:], rhs=qsT[:], start=True,
                                  stop=False)
                        first = False
                    for u in range(2):
                        mm = te.matmul(psO[:, 0:256],
                                       lhsT=qkvt[:, (2 * j + u) * 384 + 256:(2 * j + u) * 384 + 384],
                                       rhs=ssc[:, u * 256:(u + 1) * 256], start=first,
                                       stop=(u == 1))
                        first = False
                    mm.then_inc(ctx.tc.h, 1)
                    for u in range(2):
                        mm = te.matmul(psKV[:, 0:128], lhsT=ksc[:, u * 128:(u + 1) * 128],
                                       rhs=qkvt[:, (2 * j + u) * 384 + 256:(2 * j + u) * 384 + 384],
                                       start=(u == 0), stop=(u == 1))
                    mm.then_inc(ctx.tc.h, 1)

            so = sout.plan(16)
            hv_end = ctx.vc.v

            @block.sync
            def _(sy, h=h, hv_end=hv_end):
                sy.wait_ge(ctx.vc.h, hv_end)
                sy.dma_start(out=out_d[h * 128:(h + 1) * 128, :],
                             in_=outh[:]).then_inc(sout.h, 16)

        ctx.join()

    return nc




def emit_attn_finish(ctx, S, x1, outT_d, xn2v_d, wgt_d, wot_d, nw_d, x2, cvec, ident):
    block = ctx.block
    xn2v = ctx.sb(S, "xn2v", (128, DT * TOK), BF16)
    goT = ctx.sb(S, "goT", (128, DT * TOK), BF16)
    aoutT = ctx.sb(S, "aoutT", (128, DT * TOK), BF16)
    owin = ctx.sb(S, "owin", (128, 2 * TOK))
    sq = ctx.sb(S, "sqa", (128, 2 * TOK))
    tmp = ctx.sb(S, "tmpa", (128, TOK))
    rrow = ctx.sb(S, "rrow", (1, TOK))
    nw = ctx.sb(S, "nw", (128, 16))
    wgb = ctx.sb(S, "wgb", (128, 2 * DT * 128), BF16)
    wob = ctx.sb(S, "wob", (128, 2 * DT * 128), BF16)
    win = ctx.sb(S, "awin", (128, 2 * D))
    ones1 = ctx.sb(S, "aon1", (1, 128))
    onec = ctx.sb(S, "aonc", (128, 1))
    psW = ctx.ps(S, "psW", (128, 2 * 512), F32)
    psR = ctx.ps(S, "psR", (1, 2 * 512), F32)
    psB = ctx.ps(S, "psB", (128, 2 * 512), F32)
    psT = ctx.ps(S, "psT2", (128, 2 * 1024), BF16)

    sxv = ctx.new_sem("axv")
    swg = [ctx.new_sem("awg"), ctx.new_sem("awg")]
    swo = [ctx.new_sem("awo"), ctx.new_sem("awo")]
    sow = [ctx.new_sem("aow"), ctx.new_sem("aow")]
    srl = [ctx.new_sem("arl"), ctx.new_sem("arl")]
    srs = [ctx.new_sem("ars"), ctx.new_sem("ars")]

    lv = sxv.plan(2 * 16)

    @block.sync
    def _(sy):
        sy.dma_start(out=xn2v[:].rearrange("p (a q) -> p a q", a=DT),
                     in_=xn2v_d[:].rearrange("a p q -> p a q")).then_inc(sxv.h, 16)
        sy.dma_start(out=nw[:], in_=nw_d[:]).then_inc(sxv.h, 16)

    v_on = ctx.vc.plan(2)

    @block.vector
    def _(v):
        v.memset(ones1[:], 1.0).then_inc(ctx.vc.h, 1)
        v.memset(onec[:], 1.0).then_inc(ctx.vc.h, 1)

    tg = ctx.tc.plan(DT * 2)
    cg = ctx.sc.plan(DT * 2)

    @block.gpsimd
    def _(g):
        for F in range(DT):
            par = F % 2
            if F >= 2:
                g.wait_ge(ctx.tc.h, tg + 2 * (F - 1))
            g.dma_start(out=wgb[:, par * DT * 128:(par + 1) * DT * 128].rearrange(
                            "p (a q) -> p a q", a=DT),
                        in_=wgt_d[F].rearrange("a p q -> p a q")).then_inc(swg[par].h, 16)
    for F in range(DT):
        swg[F % 2].plan(16)

    @block.tensor
    def _(te):
        te.wait_ge(sxv.h, 32)
        for F in range(DT):
            par = F % 2
            te.wait_ge(swg[par].h, (F // 2 + 1) * 16)
            for half in range(2):
                n2 = 2 * F + half
                if n2 >= 2:
                    te.wait_ge(ctx.sc.h, cg + n2 - 1)
                pw = psW[:, (n2 % 2) * 512:(n2 % 2) * 512 + 512]
                for T in range(DT):
                    mm = te.matmul(pw, lhsT=wgb[:, par * DT * 128 + T * 128: par * DT * 128 + (T + 1) * 128],
                                   rhs=xn2v[:, T * TOK + half * 512: T * TOK + half * 512 + 512],
                                   start=(T == 0), stop=(T == DT - 1))
                mm.then_inc(ctx.tc.h, 1)

    @block.scalar
    def _(s):
        for F in range(DT):
            for half in range(2):
                n2 = 2 * F + half
                s.wait_ge(ctx.tc.h, tg + n2 + 1)
                pw = psW[:, (n2 % 2) * 512:(n2 % 2) * 512 + 512]
                s.activation(out=goT[:, F * TOK + half * 512: F * TOK + half * 512 + 512],
                             in_=pw, func=AF.Sigmoid, bias=0.0,
                             scale=1.0).then_inc(ctx.sc.h, 1)

    t_r = ctx.tc.plan(DT * 2)
    c_r = ctx.sc.plan(DT)
    v_r = ctx.vc.plan(0)

    @block.sync
    def _(sy):
        for T in range(DT):
            par = T % 2
            if T >= 2:
                sy.wait_ge(ctx.sc.h, c_r + T - 1)
            sy.dma_start(out=owin[:, par * TOK:(par + 1) * TOK],
                         in_=outT_d[T * 128:(T + 1) * 128, :]).then_inc(sow[par].h, 16)
    for T in range(DT):
        sow[T % 2].plan(16)

    @block.scalar
    def _(s):
        for T in range(DT):
            par = T % 2
            s.wait_ge(sow[par].h, (T // 2 + 1) * 16)
            if T >= 2:
                s.wait_ge(ctx.tc.h, t_r + 2 * (T - 1))
            s.activation(out=sq[:, par * TOK:(par + 1) * TOK],
                         in_=owin[:, par * TOK:(par + 1) * TOK],
                         func=AF.Square, bias=0.0, scale=1.0).then_inc(ctx.sc.h, 1)

    @block.tensor
    def _(te):
        te.wait_ge(ctx.vc.h, v_on + 2)
        for T in range(DT):
            par = T % 2
            te.wait_ge(ctx.sc.h, c_r + T + 1)
            for half in range(2):
                te.matmul(psR[0:1, half * 512:half * 512 + 512], lhsT=onec[:],
                          rhs=sq[:, par * TOK + half * 512: par * TOK + half * 512 + 512],
                          start=(T == 0), stop=(T == DT - 1)).then_inc(ctx.tc.h, 1)

    c_s = ctx.sc.plan(2)
    t_b = ctx.tc.plan(2)

    @block.scalar
    def _(s):
        s.wait_ge(ctx.tc.h, t_r + DT * 2)
        s.activation(out=rrow[:], in_=psR[0:1, :], func=AF.Ln, bias=cvec[0:1, 2:3],
                     scale=1.0 / D).then_inc(ctx.sc.h, 1)
        s.wait_ge(ctx.sc.h, c_s + 1)
        s.activation(out=rrow[:], in_=rrow[:], func=AF.Exp, bias=0.0,
                     scale=-0.5).then_inc(ctx.sc.h, 1)

    @block.tensor
    def _(te):
        te.wait_ge(ctx.sc.h, c_s + 2)
        for half in range(2):
            te.matmul(psB[:, half * 512:half * 512 + 512], lhsT=ones1[:],
                      rhs=rrow[0:1, half * 512:half * 512 + 512], start=True,
                      stop=True).then_inc(ctx.tc.h, 1)

    v_g = ctx.vc.plan(DT * 2)
    c_g2 = ctx.sc.plan(0)

    @block.sync
    def _(sy):
        sy.wait_ge(ctx.sc.h, c_r + DT)
        for T in range(DT):
            par = T % 2
            if T >= 2:
                sy.wait_ge(ctx.vc.h, v_g + 2 * (T - 1))
            sy.dma_start(out=owin[:, par * TOK:(par + 1) * TOK],
                         in_=outT_d[T * 128:(T + 1) * 128, :]).then_inc(sow[par].h, 16)
    for T in range(DT):
        sow[T % 2].plan(16)

    @block.vector
    def _(v):
        v.wait_ge(ctx.tc.h, t_b + 2)
        for T in range(DT):
            par = T % 2
            v.wait_ge(sow[par].h, (DT // 2 + T // 2 + 1) * 16)
            v.wait_ge(ctx.sc.h, cg + 2 * T + 2)
            if T >= 1:
                v.wait_ge(ctx.vc.h, v_g + 2 * T)
            v.scalar_tensor_tensor(out=tmp[:], in0=owin[:, par * TOK:(par + 1) * TOK],
                                   scalar=nw[:, T:T + 1], in1=psB[:],
                                   op0=ALU.mult, op1=ALU.mult).then_inc(ctx.vc.h, 1)
            v.wait_ge(ctx.vc.h, v_g + 2 * T + 1)
            v.tensor_tensor(out=goT[:, T * TOK:(T + 1) * TOK], in0=tmp[:],
                            in1=goT[:, T * TOK:(T + 1) * TOK],
                            op=ALU.mult).then_inc(ctx.vc.h, 1)

    t_w = ctx.tc.plan(DT * 2)
    v_w = ctx.vc.plan(DT * 2)

    @block.gpsimd
    def _(g):
        for dto in range(DT):
            par = dto % 2
            if dto >= 2:
                g.wait_ge(ctx.tc.h, t_w + 2 * (dto - 1))
            g.dma_start(out=wob[:, par * DT * 128:(par + 1) * DT * 128].rearrange(
                            "p (a q) -> p a q", a=DT),
                        in_=wot_d[dto]).then_inc(swo[par].h, 16)
    for dto in range(DT):
        swo[dto % 2].plan(16)

    @block.tensor
    def _(te):
        te.wait_ge(ctx.vc.h, v_w)
        for dto in range(DT):
            par = dto % 2
            te.wait_ge(swo[par].h, (dto // 2 + 1) * 16)
            for half in range(2):
                n2 = 2 * dto + half
                if n2 >= 2:
                    te.wait_ge(ctx.vc.h, v_w + n2 - 1)
                pw = psW[:, (n2 % 2) * 512:(n2 % 2) * 512 + 512]
                for T in range(DT):
                    mm = te.matmul(pw, lhsT=wob[:, par * DT * 128 + T * 128: par * DT * 128 + (T + 1) * 128],
                                   rhs=goT[:, T * TOK + half * 512: T * TOK + half * 512 + 512],
                                   start=(T == 0), stop=(T == DT - 1))
                mm.then_inc(ctx.tc.h, 1)

    @block.vector
    def _(v):
        for dto in range(DT):
            for half in range(2):
                n2 = 2 * dto + half
                v.wait_ge(ctx.tc.h, t_w + n2 + 1)
                pw = psW[:, (n2 % 2) * 512:(n2 % 2) * 512 + 512]
                v.tensor_copy(aoutT[:, dto * TOK + half * 512: dto * TOK + half * 512 + 512],
                              pw).then_inc(ctx.vc.h, 1)

    t_a = ctx.tc.plan(NT * DT)
    v_a = ctx.vc.plan(NT * DT)
    for tt in range(NT):
        srl[tt % 2].plan(16)
        srs[tt % 2].plan(16)

    @block.sync
    def _(sy):
        for tt in range(NT):
            par = tt % 2
            if tt >= 2:
                sy.wait_ge(srs[par].h, (tt // 2) * 16)
            sy.dma_start(out=win[:, par * D:(par + 1) * D],
                         in_=x1[tt * 128:(tt + 1) * 128, :]).then_inc(srl[par].h, 16)
            sy.wait_ge(ctx.vc.h, v_a + (tt + 1) * DT)
            sy.dma_start(out=x2[tt * 128:(tt + 1) * 128, :],
                         in_=win[:, par * D:(par + 1) * D]).then_inc(srs[par].h, 16)

    @block.tensor
    def _(te):
        te.wait_ge(ctx.vc.h, v_a)
        n = 0
        for tt in range(NT):
            for dto in range(DT):
                if n >= 2:
                    te.wait_ge(ctx.vc.h, v_a + n - 1)
                te.transpose(psT[:, (n % 2) * 1024:(n % 2) * 1024 + 128],
                             aoutT[:, dto * TOK + tt * 128: dto * TOK + (tt + 1) * 128],
                             ident[:]).then_inc(ctx.tc.h, 1)
                n += 1

    @block.vector
    def _(v):
        n = 0
        for tt in range(NT):
            par = tt % 2
            v.wait_ge(srl[par].h, (tt // 2 + 1) * 16)
            for dto in range(DT):
                n += 1
                v.wait_ge(ctx.tc.h, t_a + n)
                dst = win[:, par * D + dto * 128: par * D + (dto + 1) * 128]
                v.tensor_tensor(out=dst, in0=dst,
                                in1=psT[:, ((n - 1) % 2) * 1024:((n - 1) % 2) * 1024 + 128],
                                op=ALU.add).then_inc(ctx.vc.h, 1)


def emit_moe(ctx, S, x3, x4, xn4, w1_d, w2_d, b1_d, b2_d, cent_d, bias_d,
             iota_d, ident, ident32):
    block = ctx.block
    xnT = ctx.sb(S, "mxnT", (128, DT * TOK), BF16)
    hT = ctx.sb(S, "mhT", (128, FT * TOK), BF16)
    acc = ctx.sb(S, "macc", (128, NT * D))
    w1b = ctx.sb(S, "mw1", (128, 2 * DT * 128), BF16)
    w2b = ctx.sb(S, "mw2", (128, FT * D), BF16)
    b1c = ctx.sb(S, "mb1", (128, NEXP * FT))
    b2b = ctx.sb(S, "mb2", (18, D), BF16)
    cent = ctx.sb(S, "mcent", (128, DT * 16), BF16)
    biasr = ctx.sb(S, "mbias", (128, 16))
    iota = ctx.sb(S, "miota", (128, 16))
    sct = ctx.sb(S, "msct", (128, NT * 16))
    G = ctx.sb(S, "mG", (128, NT * NEXP))
    GT = ctx.sb(S, "mGT", (18, TOK), BF16)
    work = ctx.sb(S, "mwork", (128, 16))
    m1 = ctx.sb(S, "mm1", (128, 1))
    msk = ctx.sb(S, "mmsk", (128, 16))
    cand = ctx.sb(S, "mcand", (128, 16))
    idxs = ctx.sb(S, "midx", (128, 1))
    selm = ctx.sb(S, "mselm", (128, 16))
    gsum = ctx.sb(S, "mgsum", (128, 1))
    ginv = ctx.sb(S, "mginv", (128, 1))
    win = ctx.sb(S, "mwin", (128, 2 * D))
    S_xw = ExitStack()
    xw = ctx.sb(S_xw, "mxw", (128, 2 * D), BF16)
    psH = ctx.ps(S, "psH", (128, 2 * 512), F32)
    psY = ctx.ps(S, "psY", (128, 2 * 512), F32)
    psSc = ctx.ps(S, "psSc", (128, 512), F32)
    psG = ctx.ps(S, "psGt", (128, 512), F32)
    psT = ctx.ps(S, "psTm", (128, 2 * 1024), BF16)

    sxv = ctx.new_sem("mxv")
    sw1 = [ctx.new_sem("mw1"), ctx.new_sem("mw1")]
    s2f = ctx.new_sem("m2f")
    sxt = [ctx.new_sem("mxt"), ctx.new_sem("mxt")]
    srl = [ctx.new_sem("mrl"), ctx.new_sem("mrl")]
    srs = [ctx.new_sem("mrs"), ctx.new_sem("mrs")]

    lv = sxv.plan(3 * 16)
    lvg = ctx.lg.plan(2 * 16)

    @block.sync
    def _(sy):
        sy.dma_start(out=b1c[:].rearrange("p (e f) -> p e f", e=NEXP),
                     in_=b1_d[:].rearrange("e p f -> p e f")).then_inc(sxv.h, 16)
        sy.dma_start(out=biasr[:], in_=bias_d[:]).then_inc(sxv.h, 16)
        sy.dma_start(out=iota[:], in_=iota_d[:]).then_inc(sxv.h, 16)

    @block.gpsimd
    def _(g):
        g.dma_start(out=b2b[:], in_=b2_d[:]).then_inc(ctx.lg.h, 16)
        g.dma_start(out=cent[:].rearrange("p (a q) -> p a q", a=DT),
                    in_=cent_d[:].rearrange("a p q -> p a q")).then_inc(ctx.lg.h, 16)

    t_x = ctx.tc.plan(NT * DT)
    v_x = ctx.vc.plan(NT * (1 + DT))
    for tt in range(NT):
        sxt[tt % 2].plan(16)

    @block.sync
    def _(sy):
        for tt in range(NT):
            par = tt % 2
            if tt >= 2:
                sy.wait_ge(ctx.vc.h, v_x + (tt - 1) * (1 + DT))
            sy.dma_start(out=xw[:, par * D:(par + 1) * D],
                         in_=xn4[tt * 128:(tt + 1) * 128, :]).then_inc(sxt[par].h, 16)

    @block.vector
    def _(v):
        n = 0
        for tt in range(NT):
            par = tt % 2
            v.wait_ge(sxt[par].h, (tt // 2 + 1) * 16)
            v.tensor_scalar(out=acc[:, tt * D:(tt + 1) * D], in0=xw[:, par * D:(par + 1) * D],
                            scalar1=0.5, scalar2=None, op0=ALU.mult).then_inc(ctx.vc.h, 1)
            n += 1
            for d_ in range(DT):
                n += 1
                v.wait_ge(ctx.tc.h, t_x + tt * DT + d_ + 1)
                v.tensor_copy(xnT[:, d_ * TOK + tt * 128: d_ * TOK + (tt + 1) * 128],
                              psT[:, ((tt * DT + d_) % 2) * 1024:((tt * DT + d_) % 2) * 1024 + 128]
                              ).then_inc(ctx.vc.h, 1)

    @block.tensor
    def _(te):
        te.wait_ge(ctx.vc.h, t_x * 0 + v_x)
        n = 0
        for tt in range(NT):
            par = tt % 2
            te.wait_ge(sxt[par].h, (tt // 2 + 1) * 16)
            for d_ in range(DT):
                if n >= 2:
                    prev = n - 2
                    ptt, pd = prev // DT, prev % DT
                    te.wait_ge(ctx.vc.h, v_x + ptt * (1 + DT) + 1 + pd + 1)
                te.transpose(psT[:, (n % 2) * 1024:(n % 2) * 1024 + 128],
                             xw[:, par * D + d_ * 128: par * D + (d_ + 1) * 128],
                             ident[:]).then_inc(ctx.tc.h, 1)
                n += 1

    S_xw.close()
    w2f = ctx.sb(S, "mw2f", (128, D))

    xnT_done = ctx.vc.v
    t_s = ctx.tc.plan(NT)
    c_s = ctx.sc.plan(NT)
    v_s = ctx.vc.plan(NT * 39)

    @block.tensor
    def _(te):
        te.wait_ge(sxv.h, 48)
        te.wait_ge(ctx.lg.h, lvg + 32)
        te.wait_ge(ctx.vc.h, xnT_done)
        for tt in range(NT):
            if tt >= 1:
                te.wait_ge(ctx.sc.h, c_s + tt)
            for T in range(DT):
                mm = te.matmul(psSc[:, 0:16],
                               lhsT=xnT[:, T * TOK + tt * 128: T * TOK + (tt + 1) * 128],
                               rhs=cent[:, T * 16:(T + 1) * 16],
                               start=(T == 0), stop=(T == DT - 1))
            mm.then_inc(ctx.tc.h, 1)

    @block.scalar
    def _(s):
        for tt in range(NT):
            s.wait_ge(ctx.tc.h, t_s + tt + 1)
            s.activation(out=sct[:, tt * 16:(tt + 1) * 16], in_=psSc[:, 0:16],
                         func=AF.Sigmoid, bias=0.0, scale=1.0).then_inc(ctx.sc.h, 1)

    @block.vector
    def _(v):
        nv = v_s
        v.wait_ge(sxv.h, 48)
        v.wait_ge(ctx.lg.h, lvg + 32)
        for tt in range(NT):
            gt = G[:, tt * NEXP:(tt + 1) * NEXP]
            v.wait_ge(ctx.sc.h, c_s + tt + 1)

            def step(instr):
                nonlocal nv
                instr.then_inc(ctx.vc.h, 1)
                nv += 1
                v.wait_ge(ctx.vc.h, nv)
            step(v.tensor_tensor(out=work[:], in0=sct[:, tt * 16:(tt + 1) * 16],
                                 in1=biasr[:], op=ALU.add))
            step(v.memset(gt, 0.0))
            step(v.tensor_scalar(out=gt[:, 0:2], in0=gt[:, 0:2], scalar1=0.0,
                                 scalar2=0.5, op0=ALU.mult, op1=ALU.add))
            for k in range(4):
                step(v.tensor_reduce(m1[:], work[:], axis=AX.X, op=ALU.max))
                step(v.tensor_scalar(out=msk[:], in0=work[:], scalar1=m1[:],
                                     scalar2=None, op0=ALU.is_equal))
                step(v.scalar_tensor_tensor(out=cand[:], in0=iota[:], scalar=-999.0,
                                            in1=msk[:], op0=ALU.add, op1=ALU.mult))
                step(v.tensor_scalar(out=cand[:], in0=cand[:], scalar1=999.0,
                                     scalar2=None, op0=ALU.add))
                step(v.tensor_reduce(idxs[:], cand[:], axis=AX.X, op=ALU.min))
                step(v.tensor_scalar(out=selm[:], in0=iota[:], scalar1=idxs[:],
                                     scalar2=None, op0=ALU.is_equal))
                step(v.scalar_tensor_tensor(out=gt[:, 2:18], in0=selm[:], scalar=m1[:],
                                            in1=gt[:, 2:18], op0=ALU.mult, op1=ALU.add))
                if k < 3:
                    step(v.scalar_tensor_tensor(out=work[:], in0=selm[:], scalar=-1e9,
                                                in1=work[:], op0=ALU.mult, op1=ALU.add))
                else:
                    step(v.nop())
            step(v.tensor_reduce(gsum[:], gt[:, 2:18], axis=AX.X, op=ALU.add))
            step(v.tensor_scalar(out=gsum[:], in0=gsum[:], scalar1=1e-8, scalar2=None,
                                 op0=ALU.max))
            step(v.reciprocal(ginv[:], gsum[:]))
            step(v.tensor_scalar(out=gt[:, 2:18], in0=gt[:, 2:18], scalar1=ginv[:],
                                 scalar2=0.5, op0=ALU.mult, op1=ALU.mult))

    t_g = ctx.tc.plan(NT)
    v_gt = ctx.vc.plan(NT)

    @block.tensor
    def _(te):
        te.wait_ge(ctx.vc.h, v_gt)
        for tt in range(NT):
            if tt >= 1:
                te.wait_ge(ctx.vc.h, v_gt + tt)
            te.transpose(psG[0:NEXP, 0:128], G[:, tt * NEXP:(tt + 1) * NEXP],
                         ident32[:]).then_inc(ctx.tc.h, 1)

    @block.vector
    def _(v):
        for tt in range(NT):
            v.wait_ge(ctx.tc.h, t_g + tt + 1)
            v.tensor_copy(GT[:, tt * 128:(tt + 1) * 128],
                          psG[0:NEXP, 0:128]).then_inc(ctx.vc.h, 1)


    sw1_cnt = [0, 0]
    prev_ty = None
    for e in range(NEXP):
        t_h = ctx.tc.plan(FT * 2)
        c_h = ctx.sc.plan(FT * 2)
        sw1_need = []
        for ft in range(FT):
            sw1[ft % 2].plan(16)
            sw1_cnt[ft % 2] += 1
            sw1_need.append(sw1_cnt[ft % 2])

        @block.gpsimd
        def _(g, e=e, t_h=t_h, prev_ty=prev_ty):
            for ft in range(FT):
                par = ft % 2
                if ft >= 2:
                    g.wait_ge(ctx.tc.h, t_h + 2 * (ft - 1))
                elif e > 0:
                    g.wait_ge(ctx.tc.h, prev_ty)
                g.dma_start(out=w1b[:, par * DT * 128:(par + 1) * DT * 128].rearrange(
                                "p (a q) -> p a q", a=DT),
                            in_=w1_d[e, ft]).then_inc(sw1[par].h, 16)

        v_w2 = ctx.vc.plan(FT)
        s2f.plan(FT * 16)

        @block.sync
        def _(sy, e=e, v_w2=v_w2):
            for ft in range(FT):
                need = v_w2 if ft == 0 else v_w2 + ft
                sy.wait_ge(ctx.vc.h, need)
                sy.dma_start(out=w2f[:, 0:D],
                             in_=w2_d[e, ft * 128:(ft + 1) * 128, :]
                             ).then_inc(s2f.h, 16)

        @block.vector
        def _(v, e=e, t_h=t_h, v_w2=v_w2):
            v.wait_ge(ctx.tc.h, t_h)
            for ft in range(FT):
                v.wait_ge(s2f.h, (e * FT + ft + 1) * 16)
                v.tensor_copy(w2b[:, ft * D:(ft + 1) * D],
                              w2f[:, 0:D]).then_inc(ctx.vc.h, 1)

        @block.tensor
        def _(te, e=e, t_h=t_h, c_h=c_h, sw1_need=sw1_need):
            te.wait_ge(ctx.sc.h, c_h)
            for ft in range(FT):
                par = ft % 2
                te.wait_ge(sw1[par].h, sw1_need[ft] * 16)
                for half in range(2):
                    n2 = 2 * ft + half
                    if n2 >= 2:
                        te.wait_ge(ctx.sc.h, c_h + n2 - 1)
                    ph = psH[:, (n2 % 2) * 512:(n2 % 2) * 512 + 512]
                    for T in range(DT):
                        mm = te.matmul(ph, lhsT=w1b[:, par * DT * 128 + T * 128: par * DT * 128 + (T + 1) * 128],
                                       rhs=xnT[:, T * TOK + half * 512: T * TOK + half * 512 + 512],
                                       start=(T == 0), stop=(T == DT - 1))
                    mm.then_inc(ctx.tc.h, 1)

        @block.scalar
        def _(s, e=e, t_h=t_h):
            s.wait_ge(ctx.tc.h, t_h)
            for ft in range(FT):
                for half in range(2):
                    n2 = 2 * ft + half
                    s.wait_ge(ctx.tc.h, t_h + n2 + 1)
                    ph = psH[:, (n2 % 2) * 512:(n2 % 2) * 512 + 512]
                    s.activation(out=hT[:, ft * TOK + half * 512: ft * TOK + half * 512 + 512],
                                 in_=ph, func=AF.Relu, bias=b1c[:, e * FT + ft:e * FT + ft + 1],
                                 scale=1.0).then_inc(ctx.sc.h, 1)

        prev_ty = t_h + FT * 2
        t_y = ctx.tc.plan(NT * 4)
        v_y = ctx.vc.plan(NT * 4)

        @block.tensor
        def _(te, e=e, t_y=t_y, v_y=v_y, c_h=c_h, v_w2=v_w2):
            te.wait_ge(ctx.sc.h, c_h + FT * 2)
            te.wait_ge(ctx.vc.h, v_y)
            te.wait_ge(ctx.vc.h, v_w2 + FT)
            n = 0
            for tt in range(NT):
                for chk in range(4):
                    if n >= 2:
                        te.wait_ge(ctx.vc.h, v_y + n - 1)
                    py = psY[:, (n % 2) * 512:(n % 2) * 512 + 512]
                    for ft in range(FT):
                        mm = te.matmul(py, lhsT=hT[:, ft * TOK + tt * 128: ft * TOK + (tt + 1) * 128],
                                       rhs=w2b[:, ft * D + chk * 512: ft * D + (chk + 1) * 512],
                                       start=(ft == 0), stop=(ft == FT - 1))
                    mm.then_inc(ctx.tc.h, 1)
                    n += 1

        @block.vector
        def _(v, e=e, t_y=t_y, v_y=v_y):
            n = 0
            for tt in range(NT):
                for chk in range(4):
                    n += 1
                    v.wait_ge(ctx.tc.h, t_y + n)
                    py = psY[:, ((n - 1) % 2) * 512:((n - 1) % 2) * 512 + 512]
                    dst = acc[:, tt * D + chk * 512: tt * D + (chk + 1) * 512]
                    v.scalar_tensor_tensor(out=dst, in0=py,
                                           scalar=G[:, tt * NEXP + e:tt * NEXP + e + 1],
                                           in1=dst, op0=ALU.mult,
                                           op1=ALU.add).then_inc(ctx.vc.h, 1)

    t_b2 = ctx.tc.plan(NT * 4)
    v_b2 = ctx.vc.plan(NT * 4)

    @block.tensor
    def _(te):
        te.wait_ge(ctx.vc.h, v_b2)
        n = 0
        for tt in range(NT):
            for chk in range(4):
                if n >= 2:
                    te.wait_ge(ctx.vc.h, v_b2 + n - 1)
                py = psY[:, (n % 2) * 512:(n % 2) * 512 + 512]
                te.matmul(py, lhsT=GT[:, tt * 128:(tt + 1) * 128],
                          rhs=b2b[:, chk * 512:(chk + 1) * 512], start=True,
                          stop=True).then_inc(ctx.tc.h, 1)
                n += 1

    @block.vector
    def _(v):
        n = 0
        for tt in range(NT):
            for chk in range(4):
                n += 1
                v.wait_ge(ctx.tc.h, t_b2 + n)
                py = psY[:, ((n - 1) % 2) * 512:((n - 1) % 2) * 512 + 512]
                dst = acc[:, tt * D + chk * 512: tt * D + (chk + 1) * 512]
                v.tensor_tensor(out=dst, in0=dst, in1=py, op=ALU.add).then_inc(ctx.vc.h, 1)

    v_f = ctx.vc.plan(NT)
    for tt in range(NT):
        srl[tt % 2].plan(16)
        srs[tt % 2].plan(16)

    @block.sync
    def _(sy):
        for tt in range(NT):
            par = tt % 2
            if tt >= 2:
                sy.wait_ge(srs[par].h, (tt // 2) * 16)
            sy.dma_start(out=win[:, par * D:(par + 1) * D],
                         in_=x3[tt * 128:(tt + 1) * 128, :]).then_inc(srl[par].h, 16)
            sy.wait_ge(ctx.vc.h, v_f + tt + 1)
            sy.dma_start(out=x4[tt * 128:(tt + 1) * 128, :],
                         in_=win[:, par * D:(par + 1) * D]).then_inc(srs[par].h, 16)

    @block.vector
    def _(v):
        v.wait_ge(ctx.vc.h, v_f)
        for tt in range(NT):
            par = tt % 2
            v.wait_ge(srl[par].h, (tt // 2 + 1) * 16)
            v.tensor_tensor(out=win[:, par * D:(par + 1) * D],
                            in0=win[:, par * D:(par + 1) * D],
                            in1=acc[:, tt * D:(tt + 1) * D], op=ALU.add).then_inc(ctx.vc.h, 1)



def emit_xdbl(ctx, S, xn5, wx_d, xdbl_d, ident):
    block = ctx.block
    xw = ctx.sb(S, "xdw", (128, 2 * D), BF16)
    xnT = ctx.sb(S, "xdT", (128, DT * 128), BF16)
    wxb = ctx.sb(S, "xwx", (128, DT * 96), BF16)
    ob = ctx.sb(S, "xdo", (128, 2 * 96))
    psX = ctx.ps(S, "psX", (128, 512), F32)
    psT = ctx.ps(S, "psTx", (128, 2 * 1024), BF16)
    sx = [ctx.new_sem("xdl"), ctx.new_sem("xdl")]
    so = [ctx.new_sem("xds"), ctx.new_sem("xds")]
    lw = ctx.lg.plan(16)

    @block.gpsimd
    def _(g):
        g.dma_start(out=wxb[:].rearrange("p (a q) -> p a q", a=DT),
                    in_=wx_d.rearrange("a p q -> p a q")).then_inc(ctx.lg.h, 16)

    t_x = ctx.tc.plan(NT * (DT + 1))
    v_x = ctx.vc.plan(NT * (DT + 1))
    c_x = ctx.sc.plan(0)
    for tt in range(NT):
        sx[tt % 2].plan(16)
        so[tt % 2].plan(16)

    @block.sync
    def _(sy):
        for tt in range(NT):
            par = tt % 2
            if tt >= 2:
                sy.wait_ge(ctx.tc.h, t_x + (tt - 1) * (DT + 1))
            sy.dma_start(out=xw[:, par * D:(par + 1) * D],
                         in_=xn5[tt * 128:(tt + 1) * 128, :]).then_inc(sx[par].h, 16)
            sy.wait_ge(ctx.vc.h, v_x + tt * (DT + 1) + DT + 1)
            sy.dma_start(out=xdbl_d[tt * 128:(tt + 1) * 128, :],
                         in_=ob[:, par * 96:(par + 1) * 96]).then_inc(so[par].h, 16)

    @block.tensor
    def _(te):
        te.wait_ge(ctx.lg.h, ctx.lg.v)
        n = 0
        for tt in range(NT):
            par = tt % 2
            te.wait_ge(sx[par].h, (tt // 2 + 1) * 16)
            for d_ in range(DT):
                if n >= 2:
                    prev = n - 2
                    ptt, pd = prev // DT, prev % DT
                    te.wait_ge(ctx.vc.h, v_x + ptt * (DT + 1) + pd + 1)
                te.transpose(psT[:, (n % 2) * 1024:(n % 2) * 1024 + 128],
                             xw[:, par * D + d_ * 128: par * D + (d_ + 1) * 128],
                             ident[:]).then_inc(ctx.tc.h, 1)
                n += 1
            te.wait_ge(ctx.vc.h, v_x + tt * (DT + 1) + DT)
            if tt >= 1:
                te.wait_ge(ctx.vc.h, v_x + (tt - 1) * (DT + 1) + DT + 1)
            for T in range(DT):
                mm = te.matmul(psX[:, 0:96], lhsT=xnT[:, T * 128:(T + 1) * 128],
                               rhs=wxb[:, T * 96:(T + 1) * 96],
                               start=(T == 0), stop=(T == DT - 1))
            mm.then_inc(ctx.tc.h, 1)

    @block.vector
    def _(v):
        n = 0
        for tt in range(NT):
            par = tt % 2
            for d_ in range(DT):
                n += 1
                v.wait_ge(ctx.tc.h, t_x + tt * (DT + 1) + d_ + 1)
                v.tensor_copy(xnT[:, d_ * 128:(d_ + 1) * 128],
                              psT[:, ((n - 1 - tt) % 2) * 1024:((n - 1 - tt) % 2) * 1024 + 128]
                              ).then_inc(ctx.vc.h, 1)
            n += 1
            v.wait_ge(ctx.tc.h, t_x + (tt + 1) * (DT + 1))
            if tt >= 2:
                v.wait_ge(so[par].h, (tt // 2) * 16)
            v.tensor_copy(ob[:, par * 96:(par + 1) * 96], psX[:, 0:96]).then_inc(ctx.vc.h, 1)


def build_L3(stages=("att", "ln3", "ffn", "ln4", "moe", "ln5", "xdbl")):
    nc = bass.Bass("TRN2", target_bir_lowering=False, debug=False)
    x1 = nc.dram_tensor("x1", [TOK, D], F32, kind="ExternalInput").ap()
    outT_d = nc.dram_tensor("outT", [D, TOK], F32, kind="ExternalInput").ap()
    xn2v_d = nc.dram_tensor("xn2v", [DT, 128, TOK], BF16, kind="ExternalInput").ap()
    wgt_d = nc.dram_tensor("wgt", [DT, DT, 128, 128], F32, kind="ExternalInput").ap()
    wot_d = nc.dram_tensor("wot", [DT, 128, DT, 128], F32, kind="ExternalInput").ap()
    nw_d = nc.dram_tensor("nw", [128, 16], F32, kind="ExternalInput").ap()
    wu = nc.dram_tensor("wu", [32, 128, 2, DT, 128], F32, kind="ExternalInput").ap()
    wg = nc.dram_tensor("wg", [32, 128, 2, DT, 128], F32, kind="ExternalInput").ap()
    wd = nc.dram_tensor("wd", [DT, 128, IHI, 128], F32, kind="ExternalInput").ap()
    su_ug_d = nc.dram_tensor("su_ug", [128, 16], F32, kind="ExternalInput").ap()
    su_dp_d = nc.dram_tensor("su_dp", [128, 64], F32, kind="ExternalInput").ap()
    ws_up_d = nc.dram_tensor("ws_up", [128, 64], F32, kind="ExternalInput").ap()
    ws_gt_d = nc.dram_tensor("ws_gt", [128, 64], F32, kind="ExternalInput").ap()
    ws_dp_d = nc.dram_tensor("ws_dp", [128, 16], F32, kind="ExternalInput").ap()
    w1_d = nc.dram_tensor("w1", [NEXP, FT, 128, DT, 128], F32, kind="ExternalInput").ap()
    w2_d = nc.dram_tensor("w2", [NEXP, DFF, D], F32, kind="ExternalInput").ap()
    b1_d = nc.dram_tensor("b1", [NEXP, 128, FT], F32, kind="ExternalInput").ap()
    b2_d = nc.dram_tensor("b2", [NEXP, D], F32, kind="ExternalInput").ap()
    cent_d = nc.dram_tensor("cent", [DT, 128, 16], F32, kind="ExternalInput").ap()
    bias_d = nc.dram_tensor("biasr", [128, 16], F32, kind="ExternalInput").ap()
    iota_d = nc.dram_tensor("iota16", [128, 16], F32, kind="ExternalInput").ap()
    wx_d = nc.dram_tensor("wx", [DT, 128, 96], F32, kind="ExternalInput").ap()
    h128_d = nc.dram_tensor("h128s", [128, 128], F32, kind="ExternalInput").ap()
    id_d = nc.dram_tensor("ident", [128, 128], F32, kind="ExternalInput").ap()
    cv_d = nc.dram_tensor("consts", [128, 4], F32, kind="ExternalInput").ap()
    x4_d = nc.dram_tensor("x4", [TOK, D], F32, kind="ExternalOutput").ap()
    xn5_d = nc.dram_tensor("xn5", [TOK, D], BF16, kind="ExternalOutput").ap()
    xdbl_d = nc.dram_tensor("xdbl", [TOK, 96], F32, kind="ExternalOutput").ap()
    x2_i = nc.dram_tensor("x2_i", [TOK, D], F32, kind="Internal").ap()
    xn3_i = nc.dram_tensor("xn3_i", [TOK, D], BF16, kind="Internal").ap()
    x3_i = nc.dram_tensor("x3_i", [TOK, D], F32, kind="Internal").ap()
    xn4_i = nc.dram_tensor("xn4_i", [TOK, D], BF16, kind="Internal").ap()

    with ExitStack() as ES:
        ctx = Ctx(nc, ES)
        block = ctx.block
        h128s = ctx.sb(ES, "h128", (128, 128), BF16)
        ident = ctx.sb(ES, "id", (128, 128), BF16)
        ident32 = ctx.sb(ES, "id32", (128, 128))
        cvec = ctx.sb(ES, "cv", (128, 4))
        lc = ctx.lg.plan(4 * 16)

        @block.gpsimd
        def _(g):
            g.dma_start(out=h128s[:], in_=h128_d[:]).then_inc(ctx.lg.h, 16)
            g.dma_start(out=ident[:], in_=id_d[:]).then_inc(ctx.lg.h, 16)
            g.dma_start(out=ident32[:], in_=id_d[:]).then_inc(ctx.lg.h, 16)
            g.dma_start(out=cvec[:], in_=cv_d[:]).then_inc(ctx.lg.h, 16)

        ctx.join()

        if "att" in stages:
            with ExitStack() as S:
                emit_attn_finish(ctx, S, x1, outT_d, xn2v_d, wgt_d, wot_d, nw_d, x2_i,
                                 cvec, ident)
                ctx.join()
        else:
            x2_i = x1
        if "ln3" in stages:
            with ExitStack() as S:
                emit_ln_stream(ctx, S, x2_i, xn3_i, NT, cvec)
                ctx.join()
        if "ffn" in stages:
            with ExitStack() as S:
                emit_ffn0(ctx, S, x2_i, x3_i, xn3_i, NT, wu, wg, wd, su_ug_d, su_dp_d,
                          ws_up_d, ws_gt_d, ws_dp_d, h128s, ident)
                ctx.join()
        else:
            x3_i = x2_i
        if "ln4" in stages:
            with ExitStack() as S:
                emit_ln_stream(ctx, S, x3_i, xn4_i, NT, cvec)
                ctx.join()
        if "moe" in stages:
            with ExitStack() as S:
                emit_moe(ctx, S, x3_i, x4_d, xn4_i, w1_d, w2_d, b1_d, b2_d, cent_d,
                         bias_d, iota_d, ident, ident32)
                ctx.join()
        else:
            with ExitStack() as S:
                emit_copy_dram(ctx, S, x3_i, x4_d, NT)
                ctx.join()
        if "ln5" in stages:
            with ExitStack() as S:
                emit_ln_stream(ctx, S, x4_d, xn5_d, NT, cvec)
                ctx.join()
        if "xdbl" in stages:
            with ExitStack() as S:
                emit_xdbl(ctx, S, xn5_d, wx_d, xdbl_d, ident)
                ctx.join()
        ctx.join()

    return nc


def emit_copy_dram(ctx, S, src_d, dst_d, ntile):
    block = ctx.block
    buf = ctx.sb(S, "cpy", (128, 2 * D))
    sl = [ctx.new_sem("cpl"), ctx.new_sem("cpl")]
    ss = [ctx.new_sem("cps"), ctx.new_sem("cps")]
    for t in range(ntile):
        sl[t % 2].plan(16)
        ss[t % 2].plan(16)

    @block.sync
    def _(sy):
        for t in range(ntile):
            par = t % 2
            if t >= 2:
                sy.wait_ge(ss[par].h, (t // 2) * 16)
            sy.dma_start(out=buf[:, par * D:(par + 1) * D],
                         in_=src_d[t * 128:(t + 1) * 128, :]).then_inc(sl[par].h, 16)
            sy.wait_ge(sl[par].h, (t // 2 + 1) * 16)
            sy.dma_start(out=dst_d[t * 128:(t + 1) * 128, :],
                         in_=buf[:, par * D:(par + 1) * D]).then_inc(ss[par].h, 16)


def build_L4():
    nc = bass.Bass("TRN2", target_bir_lowering=False, debug=False)
    xnv_d = nc.dram_tensor("xnv", [4, 128, SEQ], BF16, kind="ExternalInput").ap()
    dr_d = nc.dram_tensor("drows", [64, SEQ], F32, kind="ExternalInput").ap()
    br_d = nc.dram_tensor("brows", [16, SEQ], F32, kind="ExternalInput").ap()
    cr_d = nc.dram_tensor("crows", [16, SEQ], F32, kind="ExternalInput").ap()
    wdt_d = nc.dram_tensor("wdt", [64, 4, 128], F32, kind="ExternalInput").ap()
    bdt_d = nc.dram_tensor("bdt", [128, 4], F32, kind="ExternalInput").ap()
    alog_d = nc.dram_tensor("alog", [4, 128, 16], F32, kind="ExternalInput").ap()
    dpar_d = nc.dram_tensor("dpar", [128, 4], F32, kind="ExternalInput").ap()
    ones_d = nc.dram_tensor("ones1", [1, 128], F32, kind="ExternalInput").ap()
    sel_d = nc.dram_tensor("sels", [16, 16, 128], F32, kind="ExternalInput").ap()
    y_d = nc.dram_tensor("yT", [512, SEQ], F32, kind="ExternalOutput").ap()

    LCH = 1024
    NCH = SEQ // LCH

    with ExitStack() as ES:
        ctx = Ctx(nc, ES)
        block = ctx.block
        xnv = ctx.sb(ES, "sxnv", (128, 4 * SEQ), BF16)
        drbf = ctx.sb(ES, "sdr", (64, SEQ), BF16)
        brow = ctx.sb(ES, "sbr", (16, SEQ))
        crow = ctx.sb(ES, "scr", (16, SEQ))
        sels = ctx.sb(ES, "ssel", (16, 16 * 128))
        wdt = ctx.sb(ES, "swdt", (64, 4 * 128), BF16)
        bdt = ctx.sb(ES, "sbdt", (128, 4))
        acols = ctx.sb(ES, "sac", (128, 4 * 16))
        dpar = ctx.sb(ES, "sdp", (128, 4))
        ones1 = ctx.sb(ES, "son", (1, 128))
        Bbc = ctx.sb(ES, "sBb", (128, 16 * LCH), BF16)
        Cbc = ctx.sb(ES, "sCb", (128, 16 * LCH), BF16)
        delta = ctx.sb(ES, "sdl", (128, LCH))
        absb = ctx.sb(ES, "sab", (128, LCH))
        l1p = ctx.sb(ES, "sl1", (128, LCH))
        relu = ctx.sb(ES, "srl", (128, LCH))
        dxb = ctx.sb(ES, "sdx", (128, LCH))
        dAw = ctx.sb(ES, "sdA", (128, 2 * LCH))
        dBw = ctx.sb(ES, "sdB", (128, 2 * LCH))
        hw = ctx.sb(ES, "shw", (128, LCH))
        hcw = ctx.sb(ES, "shc", (128, LCH))
        yac = ctx.sb(ES, "syac", (128, 2 * LCH))
        hst = ctx.sb(ES, "shst", (128, 64))
        psD2 = ctx.ps(ES, "psD2", (128, 2 * 512), F32)
        psB2 = ctx.ps(ES, "psB2", (128, 2 * 512), F32)

        sld = ctx.new_sem("s4l")
        sbc = ctx.new_sem("s4b")
        sst = [ctx.new_sem("s4s"), ctx.new_sem("s4s")]

        l0 = sld.plan(7 * 16)
        lg0 = ctx.lg.plan(2 * 16)
        lb0 = sbc.plan(16)

        @block.sync
        def _(sy):
            sy.dma_start(out=xnv[:].rearrange("p (a q) -> p a q", a=4),
                         in_=xnv_d.rearrange("a p q -> p a q")).then_inc(sld.h, 16)
            sy.dma_start(out=brow[:], in_=br_d[:]).then_inc(sld.h, 16)
            sy.dma_start(out=crow[:], in_=cr_d[:]).then_inc(sld.h, 16)
            sy.dma_start(out=bdt[:], in_=bdt_d[:]).then_inc(sld.h, 16)
            sy.dma_start(out=acols[:].rearrange("p (a q) -> p a q", a=4),
                         in_=alog_d.rearrange("a p q -> p a q")).then_inc(sld.h, 16)
            sy.dma_start(out=dpar[:], in_=dpar_d[:]).then_inc(sld.h, 16)
            sy.dma_start(out=ones1[:], in_=ones_d[:]).then_inc(sld.h, 16)
            sy.dma_start(out=sels[:].rearrange("p (a q) -> p a q", a=16),
                         in_=sel_d.rearrange("a p q -> p a q")).then_inc(sbc.h, 16)

        @block.gpsimd
        def _(g):
            g.dma_start(out=drbf[:], in_=dr_d[:]).then_inc(ctx.lg.h, 16)
            g.dma_start(out=wdt[:].rearrange("p (a q) -> p a q", a=4),
                        in_=wdt_d.rearrange("p a q -> p a q")).then_inc(ctx.lg.h, 16)

        c_a = ctx.sc.plan(1)
        v_a = ctx.vc.plan(1)

        @block.scalar
        def _(s):
            s.wait_ge(sld.h, l0 + 7 * 16)
            s.activation(out=acols[:], in_=acols[:], func=AF.Exp, bias=0.0,
                         scale=1.0).then_inc(ctx.sc.h, 1)

        @block.vector
        def _(v):
            v.wait_ge(ctx.sc.h, c_a + 1)
            v.tensor_scalar(out=acols[:], in0=acols[:], scalar1=-1.0, scalar2=None,
                            op0=ALU.mult).then_inc(ctx.vc.h, 1)

        ctx.join()

        for lc in range(NCH):
            o = lc * LCH
            t_b = ctx.tc.plan(64)
            v_b = ctx.vc.plan(64)

            @block.tensor
            def _(te, lc=lc, t_b=t_b, v_b=v_b, gprev=ctx.gc.v):
                te.wait_ge(ctx.vc.h, v_b)
                te.wait_ge(ctx.gc.h, gprev)
                te.wait_ge(sbc.h, 16)
                n4 = 0
                for n in range(16):
                    for half in range(2):
                        for which in range(2):
                            src = brow if which == 0 else crow
                            if n4 >= 2:
                                te.wait_ge(ctx.vc.h, v_b + n4 - 1)
                            te.matmul(psB2[:, (n4 % 2) * 512:(n4 % 2) * 512 + 512],
                                      lhsT=sels[:, n * 128:(n + 1) * 128],
                                      rhs=src[:, o + half * 512: o + half * 512 + 512],
                                      start=True, stop=True).then_inc(ctx.tc.h, 1)
                            n4 += 1

            @block.vector
            def _(v, lc=lc, t_b=t_b, v_b=v_b):
                n4 = 0
                for n in range(16):
                    for half in range(2):
                        for which in range(2):
                            dst = (Bbc if which == 0 else Cbc)
                            v.wait_ge(ctx.tc.h, t_b + n4 + 1)
                            v.tensor_copy(dst[:, n * LCH + half * 512: n * LCH + half * 512 + 512],
                                          psB2[:, (n4 % 2) * 512:(n4 % 2) * 512 + 512]
                                          ).then_inc(ctx.vc.h, 1)
                            n4 += 1

            for dt4 in range(4):
                it = lc * 4 + dt4
                t_d = ctx.tc.plan(2)
                c_d = ctx.sc.plan(4 * 2)
                v_d = ctx.vc.plan(2)

                @block.tensor
                def _(te, dt4=dt4, o=o, t_d=t_d, v_d=v_d, c_d=c_d, it=it):
                    if it > 0:
                        te.wait_ge(ctx.sc.h, c_d)
                    for half in range(2):
                        mm = te.matmul(psD2[:, half * 512:half * 512 + 512],
                                       lhsT=wdt[:, dt4 * 128:(dt4 + 1) * 128],
                                       rhs=drbf[:, o + half * 512: o + half * 512 + 512],
                                       start=True, stop=True)
                    mm.then_inc(ctx.tc.h, 2)

                @block.scalar
                def _(s, dt4=dt4, t_d=t_d, c_d=c_d, v_d=v_d):
                    s.wait_ge(ctx.tc.h, t_d + 2)
                    for half in range(2):
                        pd = psD2[:, half * 512:half * 512 + 512]
                        sl = slice(half * 512, half * 512 + 512)
                        s.activation(out=absb[:, sl], in_=pd, func=AF.Abs,
                                     bias=bdt[:, dt4:dt4 + 1], scale=1.0).then_inc(ctx.sc.h, 1)
                        s.wait_ge(ctx.sc.h, c_d + 4 * half + 1)
                        s.activation(out=l1p[:, sl], in_=absb[:, sl], func=AF.Exp,
                                     bias=0.0, scale=-1.0).then_inc(ctx.sc.h, 1)
                        s.wait_ge(ctx.sc.h, c_d + 4 * half + 2)
                        s.activation(out=l1p[:, sl], in_=l1p[:, sl], func=AF.Ln,
                                     bias=1.0, scale=1.0).then_inc(ctx.sc.h, 1)
                        s.activation(out=relu[:, sl], in_=pd, func=AF.Relu,
                                     bias=bdt[:, dt4:dt4 + 1], scale=1.0).then_inc(ctx.sc.h, 1)

                @block.vector
                def _(v, dt4=dt4, o=o, c_d=c_d, v_d=v_d, gprev=ctx.gc.v):
                    v.wait_ge(ctx.sc.h, c_d + 8)
                    v.wait_ge(ctx.gc.h, gprev)
                    v.tensor_tensor(out=delta[:], in0=relu[:], in1=l1p[:],
                                    op=ALU.add).then_inc(ctx.vc.h, 1)
                    v.wait_ge(ctx.vc.h, v_d + 1)
                    v.tensor_tensor(out=dxb[:], in0=delta[:],
                                    in1=xnv[:, dt4 * SEQ + o: dt4 * SEQ + o + LCH],
                                    op=ALU.mult).then_inc(ctx.vc.h, 1)
                    v.wait_ge(ctx.vc.h, v_d + 2)

                c_n = ctx.sc.plan(16)
                v_n = ctx.vc.plan(16 * 5)

                @block.scalar
                def _(s, dt4=dt4, c_n=c_n, v_n=v_n, v_d=v_d):
                    s.wait_ge(ctx.vc.h, v_d + 1)
                    for n in range(16):
                        if n >= 2:
                            s.wait_ge(ctx.vc.h, v_n + (n - 2) * 5 + 2)
                        s.activation(out=dAw[:, (n % 2) * LCH:(n % 2) * LCH + LCH],
                                     in_=delta[:], func=AF.Exp, bias=0.0,
                                     scale=acols[:, dt4 * 16 + n: dt4 * 16 + n + 1]
                                     ).then_inc(ctx.sc.h, 1)

                @block.vector
                def _(v, dt4=dt4, lc=lc, o=o, c_n=c_n, v_n=v_n, it=it):
                    nv = v_n
                    if it >= 2:
                        v.wait_ge(sst[it % 2].h, (it // 2) * 16)
                    for n in range(16):
                        v.tensor_tensor(out=dBw[:, (n % 2) * LCH:(n % 2) * LCH + LCH],
                                        in0=dxb[:], in1=Bbc[:, n * LCH:(n + 1) * LCH],
                                        op=ALU.mult).then_inc(ctx.vc.h, 1)
                        nv += 1
                        v.wait_ge(ctx.vc.h, nv)
                        v.wait_ge(ctx.sc.h, c_n + n + 1)
                        v.tensor_tensor_scan(out=hw[:],
                                             data0=dAw[:, (n % 2) * LCH:(n % 2) * LCH + LCH],
                                             data1=dBw[:, (n % 2) * LCH:(n % 2) * LCH + LCH],
                                             initial=0.0 if lc == 0 else hst[:, dt4 * 16 + n: dt4 * 16 + n + 1],
                                             op0=ALU.mult, op1=ALU.add).then_inc(ctx.vc.h, 1)
                        nv += 1
                        v.wait_ge(ctx.vc.h, nv)
                        v.tensor_copy(hst[:, dt4 * 16 + n: dt4 * 16 + n + 1],
                                      hw[:, LCH - 1:LCH]).then_inc(ctx.vc.h, 1)
                        nv += 1
                        if n == 0:
                            v.tensor_tensor(out=yac[:, (it % 2) * LCH:(it % 2) * LCH + LCH],
                                            in0=hw[:], in1=Cbc[:, 0:LCH],
                                            op=ALU.mult).then_inc(ctx.vc.h, 1)
                            nv += 1
                            v.wait_ge(ctx.vc.h, nv)
                            v.nop().then_inc(ctx.vc.h, 1)
                            nv += 1
                        else:
                            v.tensor_tensor(out=hcw[:], in0=hw[:],
                                            in1=Cbc[:, n * LCH:(n + 1) * LCH],
                                            op=ALU.mult).then_inc(ctx.vc.h, 1)
                            nv += 1
                            v.wait_ge(ctx.vc.h, nv)
                            v.tensor_tensor(out=yac[:, (it % 2) * LCH:(it % 2) * LCH + LCH],
                                            in0=yac[:, (it % 2) * LCH:(it % 2) * LCH + LCH],
                                            in1=hcw[:], op=ALU.add).then_inc(ctx.vc.h, 1)
                            nv += 1
                        v.wait_ge(ctx.vc.h, nv)

                v_y = ctx.vc.plan(1)
                st0 = sst[it % 2].plan(16)

                @block.vector
                def _(v, dt4=dt4, it=it, v_y=v_y):
                    v.scalar_tensor_tensor(out=yac[:, (it % 2) * LCH:(it % 2) * LCH + LCH],
                                           in0=xnv[:, dt4 * SEQ + o: dt4 * SEQ + o + LCH],
                                           scalar=dpar[:, dt4:dt4 + 1],
                                           in1=yac[:, (it % 2) * LCH:(it % 2) * LCH + LCH],
                                           op0=ALU.mult, op1=ALU.add).then_inc(ctx.vc.h, 1)

                @block.sync
                def _(sy, dt4=dt4, o=o, it=it, v_y=v_y):
                    sy.wait_ge(ctx.vc.h, v_y + 1)
                    sy.dma_start(out=y_d[dt4 * 128:(dt4 + 1) * 128, o:o + LCH],
                                 in_=yac[:, (it % 2) * LCH:(it % 2) * LCH + LCH]
                                 ).then_inc(sst[it % 2].h, 16)

        ctx.join()

    return nc


def build_L5():
    nc = bass.Bass("TRN2", target_bir_lowering=False, debug=False)
    x4 = nc.dram_tensor("x4", [TOK, D], F32, kind="ExternalInput").ap()
    yv_d = nc.dram_tensor("yv", [TOK, D], F32, kind="ExternalInput").ap()
    w1_d = nc.dram_tensor("w1", [NEXP, FT, 128, DT, 128], F32, kind="ExternalInput").ap()
    w2_d = nc.dram_tensor("w2", [NEXP, DFF, D], F32, kind="ExternalInput").ap()
    b1_d = nc.dram_tensor("b1", [NEXP, 128, FT], F32, kind="ExternalInput").ap()
    b2_d = nc.dram_tensor("b2", [NEXP, D], F32, kind="ExternalInput").ap()
    cent_d = nc.dram_tensor("cent", [DT, 128, 16], F32, kind="ExternalInput").ap()
    bias_d = nc.dram_tensor("biasr", [128, 16], F32, kind="ExternalInput").ap()
    iota_d = nc.dram_tensor("iota16", [128, 16], F32, kind="ExternalInput").ap()
    id_d = nc.dram_tensor("ident", [128, 128], F32, kind="ExternalInput").ap()
    cv_d = nc.dram_tensor("consts", [128, 4], F32, kind="ExternalInput").ap()
    out_d = nc.dram_tensor("out", [TOK, D], F32, kind="ExternalOutput").ap()
    x5_i = nc.dram_tensor("x5_i", [TOK, D], F32, kind="Internal").ap()
    xn6_i = nc.dram_tensor("xn6_i", [TOK, D], BF16, kind="Internal").ap()

    with ExitStack() as ES:
        ctx = Ctx(nc, ES)
        block = ctx.block
        ident = ctx.sb(ES, "id", (128, 128), BF16)
        ident32 = ctx.sb(ES, "id32", (128, 128))
        cvec = ctx.sb(ES, "cv", (128, 4))
        lc = ctx.lg.plan(3 * 16)

        @block.gpsimd
        def _(g):
            g.dma_start(out=ident[:], in_=id_d[:]).then_inc(ctx.lg.h, 16)
            g.dma_start(out=ident32[:], in_=id_d[:]).then_inc(ctx.lg.h, 16)
            g.dma_start(out=cvec[:], in_=cv_d[:]).then_inc(ctx.lg.h, 16)

        ctx.join()

        with ExitStack() as S:
            buf = ctx.sb(S, "ya", (128, 2 * D))
            yb = ctx.sb(S, "yb", (128, 2 * D))
            sl = [ctx.new_sem("y5l"), ctx.new_sem("y5l")]
            sy_ = [ctx.new_sem("y5y"), ctx.new_sem("y5y")]
            ss = [ctx.new_sem("y5s"), ctx.new_sem("y5s")]
            v5 = ctx.vc.plan(NT)
            for t in range(NT):
                sl[t % 2].plan(16)
                sy_[t % 2].plan(16)
                ss[t % 2].plan(16)

            @block.sync
            def _(sy):
                for t in range(NT):
                    par = t % 2
                    if t >= 2:
                        sy.wait_ge(ss[par].h, (t // 2) * 16)
                    sy.dma_start(out=buf[:, par * D:(par + 1) * D],
                                 in_=x4[t * 128:(t + 1) * 128, :]).then_inc(sl[par].h, 16)
                    sy.dma_start(out=yb[:, par * D:(par + 1) * D],
                                 in_=yv_d[t * 128:(t + 1) * 128, :]).then_inc(sy_[par].h, 16)
                    sy.wait_ge(ctx.vc.h, v5 + t + 1)
                    sy.dma_start(out=x5_i[t * 128:(t + 1) * 128, :],
                                 in_=buf[:, par * D:(par + 1) * D]).then_inc(ss[par].h, 16)

            @block.vector
            def _(v):
                for t in range(NT):
                    par = t % 2
                    v.wait_ge(sl[par].h, (t // 2 + 1) * 16)
                    v.wait_ge(sy_[par].h, (t // 2 + 1) * 16)
                    v.tensor_tensor(out=buf[:, par * D:(par + 1) * D],
                                    in0=buf[:, par * D:(par + 1) * D],
                                    in1=yb[:, par * D:(par + 1) * D],
                                    op=ALU.add).then_inc(ctx.vc.h, 1)
            ctx.join()

        with ExitStack() as S:
            emit_ln_stream(ctx, S, x5_i, xn6_i, NT, cvec)
            ctx.join()
        with ExitStack() as S:
            emit_moe(ctx, S, x5_i, out_d, xn6_i, w1_d, w2_d, b1_d, b2_d, cent_d,
                     bias_d, iota_d, ident, ident32)
            ctx.join()
        ctx.join()

    return nc


def prep_ffn0_weights(W_up, W_gate, W_down):
    def up_tile(W):
        return _c(W.reshape(32, 2, 128, DT, 128).transpose(0, 4, 1, 3, 2))
    wu = up_tile(W_up)
    wg = up_tile(W_gate)
    wd = _c(W_down.reshape(DT, 128, IHI, 128).transpose(0, 3, 2, 1))
    return wu, wg, wd


def prep_attn_weights(Wqkv, Wg, Wo, norm_w):
    WqkvT = np.ascontiguousarray(Wqkv.T)
    qkvw_all = np.stack([WqkvT[:, h * 384:(h + 1) * 384].reshape(DT, 128, 384)
                         for h in range(16)])
    WgT = Wg.T
    wgt = _c(WgT.reshape(DT, 128, DT, 128).transpose(2, 0, 1, 3))
    wot = _c(Wo.reshape(DT, 128, DT, 128).transpose(0, 3, 2, 1))
    nw = _c(norm_w.reshape(16, 128).T)
    return _c(qkvw_all), wgt, wot, nw


def prep_moe_weights(sw1, sb1, sw2, sb2, rw1, rb1, rw2, rb2, centroids, bias):
    w1 = np.concatenate([sw1, rw1], axis=0)
    w2 = np.concatenate([sw2, rw2], axis=0)
    b1 = np.concatenate([sb1, rb1], axis=0)
    b2 = np.concatenate([sb2, rb2], axis=0)
    w1t = _c(np.stack([w1[e].reshape(DT, 128, FT, 128).transpose(2, 1, 0, 3)
                       for e in range(NEXP)]))
    b1c = _c(b1.reshape(NEXP, FT, 128).transpose(0, 2, 1))
    cent = _c(centroids.T.reshape(DT, 128, 16))
    biasr = _c(np.broadcast_to(bias.astype(np.float32), (128, 16)))
    return w1t, _c(w2), b1c, _c(b2), cent, biasr


IOTA16 = _c(np.broadcast_to(np.arange(16, dtype=np.float32), (128, 16)))


def prep_ffn0_small(SU_ug, SU_dp, Wscale_ug, Wscale_dp):
    su_ug = _c(SU_ug.reshape(16, 128).T)
    su_dp = _c(SU_dp.reshape(64, 128).T)
    ws_up = _c(Wscale_ug[:INTER].reshape(64, 128).T)
    ws_gt = _c(Wscale_ug[INTER:].reshape(64, 128).T)
    ws_dp = _c(Wscale_dp.reshape(16, 128).T)
    return su_ug, su_dp, ws_up, ws_gt, ws_dp


_NC_CACHE = {}


def _nc(name, builder):
    if name not in _NC_CACHE:
        _NC_CACHE[name] = builder()
    return _NC_CACHE[name]


def kernel(**inputs):
    ins = inputs
    x = np.ascontiguousarray(ins["L0_attn_x"].reshape(8192, D), dtype=np.float32)

    wu, wg, wd = prep_ffn0_weights(ins["L0_ffn_W_up"], ins["L0_ffn_W_gate"],
                                   ins["L0_ffn_W_down"])
    su_ug, su_dp, ws_up, ws_gt, ws_dp = prep_ffn0_small(
        ins["L0_ffn_SU_ug"], ins["L0_ffn_SU_dp"], ins["L0_ffn_Wscale_ug"],
        ins["L0_ffn_Wscale_dp"])
    qkvw_all, wgt, wot, nw = prep_attn_weights(ins["L0_attn_Wqkv"], ins["L0_attn_Wg"],
                                               ins["L0_attn_Wo"], ins["L0_attn_norm_w"])
    w1t, w2c, b1c, b2c, cent, biasr = prep_moe_weights(
        ins["L1_ffn_shared_w1"], ins["L1_ffn_shared_b1"], ins["L1_ffn_shared_w2"],
        ins["L1_ffn_shared_b2"], ins["L1_ffn_routed_w1"], ins["L1_ffn_routed_b1"],
        ins["L1_ffn_routed_w2"], ins["L1_ffn_routed_b2"], ins["L1_ffn_centroids"],
        ins["L1_ffn_bias"])
    wx = _c(ins["L1_attn_W_xproj"].T.reshape(DT, 128, 96))
    slope_row = _c(ins["L0_attn_slope_rate"].reshape(1, 16))
    ones1 = np.ones((1, 128), np.float32)
    p_ = np.arange(128, dtype=np.float32)
    arrq = _c(np.stack([p_ + 1, p_ + 129], axis=1))
    arrk = _c(np.stack([255 - p_, 255 - (p_ + 128)], axis=1))
    idxT = np.zeros((2, 128, 256), np.float32)
    mskT = np.zeros((2, 128, 256), np.float32)
    m_ = np.arange(256)
    for u_ in range(2):
        for pp in range(128):
            d_ = m_ - (u_ * 128 + pp)
            idxT[u_, pp] = np.where(d_ >= 0, d_, 1e9)
            mskT[u_, pp] = (d_ >= 0).astype(np.float32)
    sels = _c(np.eye(16, dtype=np.float32)[:, :, None]
              * np.ones((1, 1, 128), np.float32)).transpose(1, 0, 2).copy()
    WdtT = np.ascontiguousarray(ins["L1_attn_W_dt"].T)

    nc1 = _nc("L1", build_L1)
    common1 = {"wu": wu, "wg": wg, "wd": wd, "su_ug": su_ug, "su_dp": su_dp,
               "ws_up": ws_up, "ws_gt": ws_gt, "ws_dp": ws_dp,
               "h128s": H128S, "ident": IDENT, "consts": CONSTS}
    maps = [dict(common1, x=np.ascontiguousarray(x[c * TOK:(c + 1) * TOK]))
            for c in range(8)]
    r1 = run_bass_kernel_spmd(nc1, maps, core_ids=list(range(8))).results
    x1 = [r1[c]["x1"] for c in range(8)]
    xn2 = [r1[c]["xn2"] for c in range(8)]

    nc2 = _nc("L2", build_L2)
    maps = []
    xq_b = []
    for b in range(2):
        xn2_b = np.concatenate(xn2[4 * b:4 * b + 4], axis=0)
        xq_b.append(np.ascontiguousarray(
            xn2_b.reshape(NTS, 128, DT, 128).transpose(0, 2, 3, 1)))
    for cs in range(8):
        b, hg = cs // 4, cs % 4
        maps.append({"xq": xq_b[b],
                     "qkvw": np.ascontiguousarray(qkvw_all[hg * 4:(hg + 1) * 4]),
                     "slope": np.ascontiguousarray(slope_row[:, hg * 4:(hg + 1) * 4]),
                     "ones1": ones1, "arrq": arrq, "arrk": arrk,
                     "idxT": idxT, "mskT": mskT, "ident": IDENT})
    r2 = run_bass_kernel_spmd(nc2, maps, core_ids=list(range(8))).results
    outT_h = [r2[cs]["outT"] for cs in range(8)]

    nc3 = _nc("L3", build_L3)
    common3 = {"wgt": wgt, "wot": wot, "nw": nw, "wu": wu, "wg": wg, "wd": wd,
               "su_ug": su_ug, "su_dp": su_dp, "ws_up": ws_up, "ws_gt": ws_gt,
               "ws_dp": ws_dp, "w1": w1t, "w2": w2c, "b1": b1c, "b2": b2c,
               "cent": cent, "biasr": biasr, "iota16": IOTA16, "wx": wx,
               "h128s": H128S, "ident": IDENT, "consts": CONSTS}
    maps = []
    for c in range(8):
        b, pos = c // 4, c % 4
        outT_home = np.ascontiguousarray(np.concatenate(
            [outT_h[b * 4 + g][:, pos * TOK:(pos + 1) * TOK] for g in range(4)], axis=0))
        xn2v = np.ascontiguousarray(
            xn2[c].reshape(NT, 128, DT, 128).transpose(2, 3, 0, 1).reshape(DT, 128, TOK))
        maps.append(dict(common3, x1=x1[c], outT=outT_home, xn2v=xn2v))
    r3 = run_bass_kernel_spmd(nc3, maps, core_ids=list(range(8))).results
    x4 = [r3[c]["x4"] for c in range(8)]
    xn5 = [r3[c]["xn5"] for c in range(8)]
    xdbl = [r3[c]["xdbl"] for c in range(8)]

    nc4 = _nc("L4", build_L4)
    maps = []
    xn5T_b, xdblT_b = [], []
    for b in range(2):
        xn5_b = np.concatenate(xn5[4 * b:4 * b + 4], axis=0)
        xn5T_b.append(np.ascontiguousarray(xn5_b.T))
        xdbl_b = np.concatenate(xdbl[4 * b:4 * b + 4], axis=0)
        xdblT_b.append(np.ascontiguousarray(xdbl_b.T.astype(np.float32)))
    for cs in range(8):
        b, k = cs // 4, cs % 4
        ds = k * 512
        maps.append({
            "xnv": np.ascontiguousarray(xn5T_b[b][ds:ds + 512].reshape(4, 128, SEQ)),
            "drows": np.ascontiguousarray(xdblT_b[b][0:64]),
            "brows": np.ascontiguousarray(xdblT_b[b][64:80]),
            "crows": np.ascontiguousarray(xdblT_b[b][80:96]),
            "wdt": np.ascontiguousarray(WdtT[:, ds:ds + 512].reshape(64, 4, 128)),
            "bdt": np.ascontiguousarray(ins["L1_attn_b_dt"][ds:ds + 512].reshape(4, 128).T),
            "alog": np.ascontiguousarray(ins["L1_attn_A_log"][ds:ds + 512].reshape(4, 128, 16)),
            "dpar": np.ascontiguousarray(ins["L1_attn_D_param"][ds:ds + 512].reshape(4, 128).T),
            "ones1": ones1, "sels": sels})
    r4 = run_bass_kernel_spmd(nc4, maps, core_ids=list(range(8))).results
    yT = [r4[cs]["yT"] for cs in range(8)]

    nc5 = _nc("L5", build_L5)
    common5 = {"w1": w1t, "w2": w2c, "b1": b1c, "b2": b2c, "cent": cent,
               "biasr": biasr, "iota16": IOTA16, "ident": IDENT, "consts": CONSTS}
    maps = []
    for c in range(8):
        b, pos = c // 4, c % 4
        y_home = np.concatenate(
            [yT[b * 4 + k][:, pos * TOK:(pos + 1) * TOK] for k in range(4)], axis=0)
        yv = np.ascontiguousarray(y_home.T)
        maps.append(dict(common5, x4=x4[c], yv=yv))
    r5 = run_bass_kernel_spmd(nc5, maps, core_ids=list(range(8))).results
    out = np.concatenate([r5[c]["out"] for c in range(8)], axis=0)
    return out.reshape(2, 4096, D)
